# version 03
# baseline (speedup 1.0000x reference)
import numpy as np
import ml_dtypes
from contextlib import ExitStack

import concourse.bass as bass
import concourse.mybir as mybir
from concourse.bass_utils import run_bass_kernel_spmd

F32 = mybir.dt.float32
BF16 = mybir.dt.bfloat16
AF = mybir.ActivationFunctionType
ALU = mybir.AluOpType
AX = mybir.AxisListType
BF = ml_dtypes.bfloat16

NCORES = 8
B, T, D = 2, 8192, 1024
TPC = 2048
NCH = D // 128
NTT = TPC // 512

_NC_CACHE = {}


class Prog:

    ENGS = ("T", "V", "S", "G", "Y")

    def __init__(self, nc, es):
        self.nc = nc
        self.es = es
        self.q = {e: [] for e in self.ENGS}
        self.sem = {e: es.enter_context(nc.semaphore(f"sem{e}")) for e in self.ENGS}
        self.cnt = {e: 0 for e in self.ENGS}
        self.gsem = {}

    def op(self, eng, fn, waits=(), inc=True):
        waits = [w for w in waits if w is not None]
        tok = None
        if inc:
            self.cnt[eng] += 1
            tok = (eng, self.cnt[eng])
        self.q[eng].append((fn, tuple(waits), tok))
        return tok

    def dma(self, eng, out, in_, waits=(), group=None, **kw):
        assert group is not None
        waits = [w for w in waits if w is not None]
        if group not in self.gsem:
            self.gsem[group] = [self.es.enter_context(self.nc.semaphore(f"g_{group}")), 0]
        ent = self.gsem[group]
        ent[1] += 16
        tok = ("g:" + group, ent[1])
        sem = ent[0]

        def fn(e, out=out, in_=in_, kw=kw):
            return e.dma_start(out=out, in_=in_, **kw).then_inc(sem, 16)

        self.q[eng].append((fn, tuple(waits), None))
        return tok

    def collective(self, kind, ins, outs, waits=()):
        waits = [w for w in waits if w is not None]
        self.cnt["G"] += 1
        tok = ("G", self.cnt["G"])
        sem = self.sem["G"]

        def fn(e, ins=ins, outs=outs):
            return e.collective_compute(
                kind, ALU.bypass, replica_groups=[list(range(NCORES))],
                ins=ins, outs=outs).then_inc(sem, 1)

        self.q["G"].append((fn, tuple(waits), None))
        return tok

    def raw_dma(self, eng, fn, waits=(), group=None):
        assert group is not None
        waits = [w for w in waits if w is not None]
        if group not in self.gsem:
            self.gsem[group] = [self.es.enter_context(self.nc.semaphore(f"g_{group}")), 0]
        ent = self.gsem[group]
        ent[1] += 16
        tok = ("g:" + group, ent[1])
        sem = ent[0]

        def fn2(e, fn=fn, sem=sem):
            return fn(e).then_inc(sem, 16)

        self.q[eng].append((fn2, tuple(waits), None))
        return tok

    def gtotal(self, group):
        return ("g:" + group, self.gsem[group][1])

    def barrier(self):
        waits = tuple(("" + e, self.cnt[e]) for e in self.ENGS if self.cnt[e] > 0)
        gwaits = tuple(("g:" + g, ent[1]) for g, ent in self.gsem.items())
        for e in self.ENGS:
            self.q[e].append(((lambda engine: None), waits + gwaits, None))

    def _resolve(self, t):
        name, v = t
        if name.startswith("g:"):
            return self.gsem[name[2:]][0], v
        return self.sem[name], v

    def emit(self):
        for gname, (sem, cnt) in self.gsem.items():
            self.q["Y"].append(((lambda e: None), (("g:" + gname, cnt),), None))
        block = self.es.enter_context(self.nc.Block())

        def body(engine, key):
            for fn, waits, tok in self.q[key]:
                for t in waits:
                    s, v = self._resolve(t)
                    engine.wait_ge(s, v)
                ins = fn(engine)
                if tok is not None:
                    ins.then_inc(self.sem[tok[0]], 1)

        block.tensor(lambda e: body(e, "T"))
        block.vector(lambda e: body(e, "V"))
        block.scalar(lambda e: body(e, "S"))
        block.gpsimd(lambda e: body(e, "G"))
        block.sync(lambda e: body(e, "Y"))


class NCProxy:

    def __init__(self, nc, tag):
        self._nc = nc
        self._tag = tag

    def sbuf_tensor(self, name, shape, dtype, **kw):
        return self._nc.sbuf_tensor(self._tag + name, shape, dtype, **kw)

    def psum_tensor(self, name, shape, dtype=F32, **kw):
        return self._nc.psum_tensor(self._tag + name, shape, dtype, **kw)

    def __getattr__(self, k):
        return getattr(self._nc, k)



def alloc_ln_bufs(nc, es, tag="", reuse=None):
    b = {}
    b["ln_sq"] = es.enter_context(nc.sbuf_tensor(f"lnsq{tag}", [128, 512], F32))
    b["ln_stat"] = es.enter_context(nc.sbuf_tensor(f"lnst{tag}", [1, 1024], F32))
    b["ln_tmp"] = es.enter_context(nc.sbuf_tensor(f"lntmp{tag}", [128, 512], F32))
    b["ones_col"] = es.enter_context(nc.sbuf_tensor(f"lnoc{tag}", [128, 1], F32))
    b["ones1"] = es.enter_context(nc.sbuf_tensor(f"lno1{tag}", [1, 128], F32))
    if reuse is None:
        b["ln_psS"] = es.enter_context(nc.psum_tensor(f"lnpsS{tag}", [1, 512], F32))[:]
        b["ln_psS2"] = es.enter_context(nc.psum_tensor(f"lnpsS2{tag}", [1, 512], F32))[:]
        b["ln_psB"] = es.enter_context(nc.psum_tensor(f"lnpsB{tag}", [128, 512], F32))[:]
        b["ln_psB2"] = es.enter_context(nc.psum_tensor(f"lnpsB2{tag}", [128, 512], F32))[:]
    else:
        b["ln_psS"] = reuse["psS"]
        b["ln_psS2"] = reuse["psS2"]
        b["ln_psB"] = reuse["psB"]
        b["ln_psB2"] = reuse["psB2"]
    return b


def emit_ln_init(p, b):
    t1 = p.op("V", lambda v: v.memset(b["ones_col"][:], 1.0))
    t2 = p.op("V", lambda v: v.memset(b["ones1"][:], 1.0))
    return [t1, t2]


def emit_ln(p, b, x_ap_fn, n_tok, out_fn, gamma_scale=None, gamma_bias=None,
            eps=1e-5, pre_waits=(), extra_out_fn=None, init_toks=()):
    sq, stat, tmp = b["ln_sq"], b["ln_stat"], b["ln_tmp"]
    psS, psS2, psB, psB2 = b["ln_psS"], b["ln_psS2"], b["ln_psB"], b["ln_psB2"]
    ones_col, ones1 = b["ones_col"], b["ones1"]
    n_tt = n_tok // 512
    out_toks = {}
    prev_apply = []
    prev_stats = []
    last = None
    for tt in range(n_tt):
        mu = stat[:, 0:512]
        va = stat[:, 512:1024]
        s_tok = None
        for c in range(NCH):
            w = list(pre_waits) + list(init_toks) + prev_stats if c == 0 else []
            s_tok = p.op("T", (lambda t, c=c, tt=tt: t.matmul(
                psS[:], ones_col[:], x_ap_fn(c, tt), start=(c == 0), stop=(c == NCH - 1))), w)
        prev_mt = None
        for c in range(NCH):
            w = (list(pre_waits) + prev_stats) if c == 0 else [prev_mt]
            st = p.op("S", (lambda s, c=c, tt=tt: s.activation(sq[:], x_ap_fn(c, tt), AF.Square)), w)
            prev_mt = p.op("T", (lambda t, c=c: t.matmul(
                psS2[:], ones_col[:], sq[:], start=(c == 0), stop=(c == NCH - 1))), [st])
        t1 = p.op("V", (lambda v, mu=mu: v.tensor_scalar_mul(mu, psS, 1.0 / D)), [s_tok])
        t2 = p.op("V", (lambda v, va=va: v.tensor_scalar_mul(va, psS2, 1.0 / D)), [prev_mt])
        t3 = p.op("V", (lambda v, mu=mu: v.tensor_mul(sq[0:1, 0:512], mu, mu)), [t1, t2])
        t4 = p.op("V", (lambda v, va=va: v.tensor_sub(va, va, sq[0:1, 0:512])), [t3])
        t4b = p.op("V", (lambda v, va=va: v.tensor_scalar_add(va, va, float(eps))), [t4])
        t5 = p.op("S", (lambda s, va=va: s.activation(va, va, AF.Sqrt)), [t4b])
        t6 = p.op("V", (lambda v, va=va: v.reciprocal(va, va)), [t5])
        b1 = p.op("T", (lambda t, mu=mu: t.matmul(psB, ones1[:], mu, start=True, stop=True)),
                  [t6] + prev_apply)
        b2 = p.op("T", (lambda t, va=va: t.matmul(psB2, ones1[:], va, start=True, stop=True)), [b1])
        prev_stats = [t3, t6]
        prev_apply = []
        for c in range(NCH):
            tA = p.op("V", (lambda v, c=c, tt=tt: v.tensor_sub(tmp[:], x_ap_fn(c, tt), psB)),
                      [b2] + ([prev_apply[-1]] if prev_apply else []))
            if gamma_scale is None:
                tB = p.op("V", (lambda v, c=c, tt=tt: v.tensor_mul(out_fn(c, tt), tmp[:], psB2)), [tA])
            else:
                tB0 = p.op("V", (lambda v: v.tensor_mul(tmp[:], tmp[:], psB2)), [tA])
                tB = p.op("V", (lambda v, c=c, tt=tt: v.tensor_scalar(
                    out_fn(c, tt), tmp[:], gamma_scale(c), gamma_bias(c), ALU.mult, ALU.add)), [tB0])
            if extra_out_fn is not None:
                tB = p.op("S", (lambda s, c=c, tt=tt: s.activation(
                    extra_out_fn(c, tt), out_fn(c, tt), AF.Copy)), [tB])
            out_toks[(c, tt)] = tB
            prev_apply.append(tB)
            last = tB
        prev_stats = prev_stats + [b1, b2]
    return out_toks, last



def emit_qkv(p, nc, es, xT, wq, wk, wv, bq, bk, bvrow, qx, kx, vx, tag=""):
    xbf = es.enter_context(nc.sbuf_tensor("xbf", [128, NCH * TPC], BF16))
    wqb = es.enter_context(nc.sbuf_tensor("wqb", [128, NCH * D], BF16))
    wkb = es.enter_context(nc.sbuf_tensor("wkb", [128, NCH * D], BF16))
    wvb = es.enter_context(nc.sbuf_tensor("wvb", [128, NCH * D], BF16))
    bqs = es.enter_context(nc.sbuf_tensor("bqs", [128, NCH], F32))
    bks = es.enter_context(nc.sbuf_tensor("bks", [128, NCH], F32))
    bvr = es.enter_context(nc.sbuf_tensor("bvr", [1, D], F32))
    bvb = es.enter_context(nc.sbuf_tensor("bvb", [128, D], F32))
    ones1 = es.enter_context(nc.sbuf_tensor("ones1", [1, 128], F32))
    osb = es.enter_context(nc.sbuf_tensor("osb", [128, 2 * TPC], BF16))
    vsb = es.enter_context(nc.sbuf_tensor("vsb", [128, 2 * D], BF16))
    ps = [es.enter_context(nc.psum_tensor(f"ps{i}", [128, 512], F32)) for i in range(2)]
    psv = [es.enter_context(nc.psum_tensor(f"psv{i}", [128, 512], F32)) for i in range(2)]
    psb = es.enter_context(nc.psum_tensor("psb", [128, 512], F32))

    ldx = p.dma("G", xbf[:].rearrange("p (c t) -> p c t", t=TPC),
                xT.rearrange("(c p) t -> p c t", p=128), group="ldx")
    ldwq = p.dma("G", wqb[:].rearrange("p (c o) -> p c o", o=D), wq.rearrange("(c p) o -> p c o", p=128), group="wq")
    ldwk = p.dma("G", wkb[:].rearrange("p (c o) -> p c o", o=D), wk.rearrange("(c p) o -> p c o", p=128), group="wk")
    ldwv = p.dma("G", wvb[:].rearrange("p (c o) -> p c o", o=D), wv.rearrange("(c p) o -> p c o", p=128), group="wv")
    ldbq = p.dma("Y", bqs[:], bq, group="bq")
    ldbk = p.dma("Y", bks[:], bk, group="bk")
    ldbv = p.dma("Y", bvr[:], bvrow, group="bv")
    onet = p.op("V", lambda v: v.memset(ones1[:], 1.0))
    prevc = onet
    bvb_toks = []
    for h in range(2):
        tb = p.op("T", (lambda t, h=h: t.matmul(psb[:], ones1[:], bvr[:, h * 512:(h + 1) * 512],
                                                start=True, stop=True)), [prevc, ldbv])
        prevc = p.op("V", (lambda v, h=h: v.tensor_copy(bvb[:, h * 512:(h + 1) * 512], psb[:])), [tb])
        bvb_toks.append(prevc)

    bank_free = [None, None]
    half_free = [None, None]
    for name, wb, bsbuf, ldwt, ldbt, out_dram in (
            ("q", wqb, bqs, ldwq, ldbq, qx), ("k", wkb, bks, ldwk, ldbk, kx)):
        for oc in range(NCH):
            half = oc % 2
            st_toks = []
            for tt in range(NTT):
                bi = tt % 2
                pt = ps[bi]
                w0 = [ldx, ldwt, ldbt, bank_free[bi], half_free[half]]
                mm = None
                for ic in range(NCH):
                    mm = p.op("T", (lambda t, ic=ic, oc=oc, tt=tt, pt=pt, wb=wb: t.matmul(
                        pt[:], wb[:, ic * D + oc * 128: ic * D + oc * 128 + 128],
                        xbf[:, ic * TPC + tt * 512: ic * TPC + (tt + 1) * 512],
                        start=(ic == 0), stop=(ic == NCH - 1))), w0 if ic == 0 else [])
                    w0 = []
                ev = p.op("V", (lambda v, tt=tt, pt=pt, oc=oc, half=half, bsbuf=bsbuf: v.tensor_scalar_add(
                    osb[:, half * TPC + tt * 512: half * TPC + (tt + 1) * 512], pt[:],
                    bsbuf[:, oc:oc + 1])), [mm])
                bank_free[bi] = ev
                st_toks.append(ev)
            dt = p.dma("G", out_dram.rearrange("(c p) t -> p c t", p=128)[:, oc, :],
                       osb[:, half * TPC: half * TPC + TPC], waits=st_toks,
                       group=f"oh{half}_{name}")
            half_free[half] = dt

    vbank_free = [None, None]
    vhalf_free = None
    for tc in range(TPC // 128):
        st_toks = []
        for half in range(2):
            pt = psv[half]
            w0 = [ldx, ldwv, vbank_free[half], vhalf_free]
            mm = None
            for ic in range(NCH):
                mm = p.op("T", (lambda t, ic=ic, tc=tc, half=half, pt=pt: t.matmul(
                    pt[:], xbf[:, ic * TPC + tc * 128: ic * TPC + (tc + 1) * 128],
                    wvb[:, ic * D + half * 512: ic * D + (half + 1) * 512],
                    start=(ic == 0), stop=(ic == NCH - 1))), w0 if ic == 0 else [])
                w0 = []
            ev = p.op("V", (lambda v, half=half, pt=pt: v.tensor_add(
                vsb[:, half * D // 2 + 0: 0] if False else vsb[:, half * 512:(half + 1) * 512],
                pt[:], bvb[:, half * 512:(half + 1) * 512])), [mm] + bvb_toks)
            vbank_free[half] = ev
            st_toks.append(ev)
        vhalf_free = p.dma("G", vx[tc * 128:(tc + 1) * 128, :], vsb[:, 0:D], waits=st_toks,
                           group="vout")

    return


def build_qkv_launch():
    nc = bass.Bass("TRN2", target_bir_lowering=False)
    names_i = [("xT", [D, TPC], F32), ("wq", [D, D], F32), ("wk", [D, D], F32),
               ("wv", [D, D], F32), ("bq", [128, NCH], F32), ("bk", [128, NCH], F32),
               ("bvrow", [1, D], F32)]
    names_o = [("qx", [D, TPC], BF16), ("kx", [D, TPC], BF16), ("vx", [TPC, D], BF16)]
    aps = {n: nc.dram_tensor(n, s, d, kind="ExternalInput").ap() for n, s, d in names_i}
    aps.update({n: nc.dram_tensor(n, s, d, kind="ExternalOutput").ap() for n, s, d in names_o})
    es = ExitStack()
    p = Prog(nc, es)
    emit_qkv(p, nc, es, **aps)
    p.emit()
    es.close()
    return nc



def emit_flash(p, nc, es, ident, trimask, o4, loader):
    SC = 0.125
    qe = es.enter_context(nc.sbuf_tensor("qe", [65, T], BF16))
    ke = es.enter_context(nc.sbuf_tensor("ke", [65, T], BF16))
    ve = es.enter_context(nc.sbuf_tensor("ve", [128, 65 * (T // 128)], BF16))
    idn = es.enter_context(nc.sbuf_tensor("idn", [128, 128], F32))
    tri = es.enter_context(nc.sbuf_tensor("tri", [128, 4 * 512], BF16))
    mneg = es.enter_context(nc.sbuf_tensor("mneg", [128, 4], F32))
    mtmp = es.enter_context(nc.sbuf_tensor("mtmp", [128, 1], F32))
    pt_sb = es.enter_context(nc.sbuf_tensor("pt_sb", [128, 2048], BF16))
    rb = es.enter_context(nc.sbuf_tensor("rb", [64, 512], F32))
    rec = es.enter_context(nc.sbuf_tensor("rec", [1, 512], F32))
    ones1 = es.enter_context(nc.sbuf_tensor("ones1", [1, 64], F32))
    ob = es.enter_context(nc.sbuf_tensor("ob", [64, 2 * 512], BF16))
    psA = es.enter_context(nc.psum_tensor("psA", [128, 512], F32))
    psM = es.enter_context(nc.psum_tensor("psM", [1, 128], F32))
    psT = es.enter_context(nc.psum_tensor("psT", [128, 2048], F32))
    psO = es.enter_context(nc.psum_tensor("psO", [65, 512], F32))
    psR = es.enter_context(nc.psum_tensor("psR", [64, 512], F32))

    ldi = p.dma("Y", idn[:], ident, group="ldi")
    ldt = p.dma("Y", tri[:].rearrange("p (s n) -> p s n", n=512), trimask.rearrange("s p n -> p s n"),
                group="ldt")
    one1 = p.op("V", lambda v: v.memset(ones1[:], 1.0))
    onek = p.op("V", lambda v: v.memset(ke[64:65, :], 1.0))
    onev = p.op("V", lambda v: v.memset(ve[:].rearrange("p (c e) -> p c e", e=65)[:, :, 64], 1.0))

    NQG = T // 512
    prev_pair_done = [onek, onev]
    prev_cpm = None
    ob_last = [None, None]
    for pair in range(4):
        ldq, ldk, ldv = loader(pair, prev_pair_done, qe, ke, ve)
        prev_ot = None
        for g in range(NQG):
            nkc = g + 1
            tr_toks = []
            for qt in range(4):
                qsl = qe[0:64, g * 512 + qt * 128: g * 512 + qt * 128 + 128]
                red_prev = None
                for kc in range(nkc):
                    w = [ldq, ldk] if (qt == 0 and kc == 0) else []
                    w = w + ([red_prev] if red_prev else [])
                    mm = p.op("T", (lambda t, qsl=qsl, kc=kc: t.matmul(
                        psA[:], qsl, ke[0:64, kc * 512:(kc + 1) * 512], start=True, stop=True)), w)
                    if kc == 0:
                        red_prev = p.op("V", (lambda v, qt=qt: v.reduce_max(
                            mneg[:, qt:qt + 1], psA[:], axis=AX.X, negate=True)), [mm])
                    else:
                        r1 = p.op("V", (lambda v: v.reduce_max(mtmp[:], psA[:], axis=AX.X,
                                                               negate=True)), [mm])
                        red_prev = p.op("V", (lambda v, qt=qt: v.tensor_tensor(
                            mneg[:, qt:qt + 1], mneg[:, qt:qt + 1], mtmp[:], ALU.min)), [r1])
                trm = p.op("T", (lambda t, qt=qt: t.transpose(psM[:], mneg[:, qt:qt + 1], idn[:])),
                           [red_prev, ldi, prev_cpm])
                cpm = p.op("V", (lambda v, g=g, qt=qt: v.tensor_copy(
                    qe[64:65, g * 512 + qt * 128: g * 512 + qt * 128 + 128], psM[:])), [trm])
                prev_cpm = cpm
                tr_toks.append(cpm)
            qg = qe[:, g * 512:(g + 1) * 512]
            first_st_waits = tr_toks + [prev_ot]
            ex_prev = None
            avm = None
            for kc in range(nkc):
                mm4 = None
                for s in range(4):
                    w = (first_st_waits + [ex_prev]) if s == 0 else []
                    mm4 = p.op("T", (lambda t, kc=kc, s=s, qg=qg: t.matmul(
                        psT[:, s * 512:(s + 1) * 512],
                        ke[:, kc * 512 + s * 128: kc * 512 + s * 128 + 128], qg,
                        start=True, stop=True)), w)
                first_st_waits = []
                ex = p.op("S", (lambda s_: s_.activation(pt_sb[:], psT[:], AF.Exp, scale=SC)),
                          [mm4, avm])
                ex_prev = ex
                dep = [ex]
                if kc == g:
                    for s in range(4):
                        dep = [p.op("V", (lambda v, s=s: v.tensor_mul(
                            pt_sb[:, s * 512:(s + 1) * 512], pt_sb[:, s * 512:(s + 1) * 512],
                            tri[:, s * 512:(s + 1) * 512])), dep + ([ldt] if s == 0 else []))]
                for s in range(4):
                    w = (dep + [ldv]) if s == 0 else []
                    avm = p.op("T", (lambda t, kc=kc, s=s, g=g: t.matmul(
                        psO[:], ve[:, (kc * 4 + s) * 65:(kc * 4 + s) * 65 + 65],
                        pt_sb[:, s * 512:(s + 1) * 512],
                        start=(kc == 0 and s == 0), stop=(kc == nkc - 1 and s == 3))), w)
            rc = p.op("V", (lambda v: v.reciprocal(rec[:], psO[64:65, :])), [avm])
            bc = p.op("T", (lambda t: t.matmul(psR[:], ones1[:], rec[:], start=True, stop=True)),
                      [rc, one1])
            cb = p.op("V", (lambda v: v.tensor_copy(rb[:], psR[:])), [bc])
            ot = p.op("V", (lambda v, g=g: v.tensor_mul(
                ob[:, (g % 2) * 512:(g % 2) * 512 + 512], psO[0:64, :], rb[:])),
                [cb, ob_last[g % 2]])
            ob_last[g % 2] = p.dma("G", o4[pair][:, g * 512:(g + 1) * 512],
                                   ob[:, (g % 2) * 512:(g % 2) * 512 + 512], waits=[ot],
                                   group=f"ob{g % 2}")
            prev_ot = ot
        prev_pair_done = [prev_ot, ob_last[0], ob_last[1]]
    return


def default_flash_loader(p, q4, k4, v4):
    def loader(pair, prev_done, qe, ke, ve):
        ldq = p.dma("G", qe[0:64, :], q4[pair], waits=prev_done, group="ldq")
        ldk = p.dma("G", ke[0:64, :], k4[pair], waits=prev_done, group="ldk")
        ldv = p.dma("G", ve[:].rearrange("p (c e) -> p c e", e=65)[:, :, 0:64],
                    v4[pair].rearrange("(c p) d -> p c d", p=128), waits=prev_done,
                    group="ldv")
        return ldq, ldk, ldv
    return loader


def build_flash_dense():
    nc = bass.Bass("TRN2", target_bir_lowering=False)
    q4 = nc.dram_tensor("q4", [4, 64, T], BF16, kind="ExternalInput").ap()
    k4 = nc.dram_tensor("k4", [4, 64, T], BF16, kind="ExternalInput").ap()
    v4 = nc.dram_tensor("v4", [4, T, 64], BF16, kind="ExternalInput").ap()
    ident = nc.dram_tensor("ident", [128, 128], F32, kind="ExternalInput").ap()
    trimask = nc.dram_tensor("trimask", [4, 128, 512], BF16, kind="ExternalInput").ap()
    o4 = nc.dram_tensor("o4", [4, 64, T], BF16, kind="ExternalOutput").ap()
    es = ExitStack()
    p = Prog(nc, es)
    emit_flash(p, nc, es, ident, trimask, o4, default_flash_loader(p, q4, k4, v4))
    p.emit()
    es.close()
    return nc



def emit_proj_ln(p, nc, es, oT, wo, bo, xT, xn, o_loader=None):
    obf = es.enter_context(nc.sbuf_tensor("obf", [128, NCH * TPC], BF16))
    wob = es.enter_context(nc.sbuf_tensor("wob", [128, NCH * D], BF16))
    bos = es.enter_context(nc.sbuf_tensor("bos", [128, NCH], F32))
    xr = es.enter_context(nc.sbuf_tensor("xr", [128, NCH * TPC], F32))
    xo = es.enter_context(nc.sbuf_tensor("xo", [128, 512], F32))
    ps = [es.enter_context(nc.psum_tensor(f"ps{i}", [128, 512], F32)) for i in range(2)]
    lnb = alloc_ln_bufs(nc, es)

    if o_loader is None:
        ldo = p.dma("G", obf[:].rearrange("p (c t) -> p c t", t=TPC),
                    oT.rearrange("(c p) t -> p c t", p=128), group="ldo")
    else:
        ldo = o_loader(obf)
    ldw = p.dma("G", wob[:].rearrange("p (c o) -> p c o", o=D), wo.rearrange("(c p) o -> p c o", p=128), group="ldw")
    ldb = p.dma("Y", bos[:], bo, group="ldb")
    ldxr = p.dma("G", xr[:].rearrange("p (c t) -> p c t", t=TPC),
                 xT.rearrange("(c p) t -> p c t", p=128), group="ldxr")
    init = emit_ln_init(p, lnb)

    bank_free = [None, None]
    res_toks = []
    for oc in range(NCH):
        for tt in range(NTT):
            bi = (oc * NTT + tt) % 2
            pt = ps[bi]
            w0 = [ldo, ldw, ldb, ldxr, bank_free[bi]]
            mm = None
            for ic in range(NCH):
                mm = p.op("T", (lambda t, ic=ic, oc=oc, tt=tt, pt=pt: t.matmul(
                    pt[:], wob[:, ic * D + oc * 128: ic * D + oc * 128 + 128],
                    obf[:, ic * TPC + tt * 512: ic * TPC + (tt + 1) * 512],
                    start=(ic == 0), stop=(ic == NCH - 1))), w0 if ic == 0 else [])
                w0 = []
            ev = p.op("V", (lambda v, pt=pt, oc=oc: v.tensor_scalar_add(
                xo[:], pt[:], bos[:, oc:oc + 1])), [mm])
            bank_free[bi] = ev
            ad = p.op("V", (lambda v, oc=oc, tt=tt: v.tensor_add(
                xr[:, oc * TPC + tt * 512: oc * TPC + (tt + 1) * 512],
                xr[:, oc * TPC + tt * 512: oc * TPC + (tt + 1) * 512], xo[:])), [ev])
            bank_free[bi] = ad
            res_toks.append(ad)

    def x_ap(c, tt):
        return xr[:, c * TPC + tt * 512: c * TPC + (tt + 1) * 512]

    toks, last = emit_ln(p, lnb, x_ap, TPC, out_fn=x_ap, pre_waits=res_toks, init_toks=init)
    for c in range(NCH):
        for tt in range(NTT):
            p.dma("G", xn.rearrange("(c p) t -> p c t", p=128)[:, c, tt * 512:(tt + 1) * 512],
                  x_ap(c, tt), waits=[toks[(c, tt)]], group="xout")
    return



INNER = 512
NQC = 4


def emit_lffn_qkv(p, nc, es, xT, xprevT, wqkv, qs2, ks2, tabs, E2, qx, kx, vx):
    W = 2049
    xbf = es.enter_context(nc.sbuf_tensor("xbf", [128, NCH * W], BF16))
    wb = es.enter_context(nc.sbuf_tensor("wb", [128, NCH * 3 * INNER], BF16))
    qss = es.enter_context(nc.sbuf_tensor("qss", [128, 1], F32))
    kss = es.enter_context(nc.sbuf_tensor("kss", [128, 1], F32))
    tb = {n: es.enter_context(nc.sbuf_tensor("t" + n, [128, TPC], F32)) for n in ("AQ", "BQ", "AK", "BK")}
    e2s = es.enter_context(nc.sbuf_tensor("e2s", [128, 128], F32))
    sqb = es.enter_context(nc.sbuf_tensor("sqb", [128, 512], F32))
    rstd = es.enter_context(nc.sbuf_tensor("rstd", [128, 512], F32))
    qsq = es.enter_context(nc.sbuf_tensor("qsq", [128, 512], F32))
    qrot = es.enter_context(nc.sbuf_tensor("qrot", [128, 512], F32))
    u = es.enter_context(nc.sbuf_tensor("u", [128, 512], F32))
    w_ = es.enter_context(nc.sbuf_tensor("w_", [128, 512], F32))
    osb = es.enter_context(nc.sbuf_tensor("osb", [128, 2 * TPC], BF16))
    vsb = es.enter_context(nc.sbuf_tensor("vsb", [128, 512], BF16))
    pQ = [es.enter_context(nc.psum_tensor(f"pQ{i}", [128, 512], F32)) for i in range(2)]
    pS = es.enter_context(nc.psum_tensor("pS", [128, 512], F32))
    pV = [es.enter_context(nc.psum_tensor(f"pV{i}", [128, 512], F32)) for i in range(2)]

    ldx = p.dma("G", xbf[:].rearrange("p (c t) -> p c t", t=W)[:, :, 1:W],
                xT.rearrange("(c p) t -> p c t", p=128), group="ldx")
    ldxp = p.dma("G", xbf[:].rearrange("p (c t) -> p c t", t=W)[:, :, 0:1],
                 xprevT.rearrange("(c p) t -> p c t", p=128), allow_slow_non_contiguous=True,
                 group="ldxp")
    ldw = p.dma("G", wb[:].rearrange("p (c o) -> p c o", o=3 * INNER),
                wqkv.rearrange("(c p) o -> p c o", p=128), group="ldw")
    ldqs = p.dma("Y", qss[:], qs2, group="ldqs")
    ldks = p.dma("Y", kss[:], ks2, group="ldks")
    ldtb = [p.dma("Y", tb[n][:], tabs[n], group="t" + n) for n in ("AQ", "BQ", "AK", "BK")]
    lde2 = p.dma("Y", e2s[:], E2, group="lde2")
    lds = [ldx, ldxp, ldw, ldqs, ldks, lde2] + ldtb

    def h_ap(ic, t0, n):
        off = 0 if ic >= 4 else 1
        return xbf[:, ic * W + off + t0: ic * W + off + t0 + n]

    Q0, Q1, Q2, Q3 = (slice(0, 32), slice(32, 64), slice(64, 96), slice(96, 128))
    QPART = [(Q0, Q1), (Q1, Q0), (Q2, Q3), (Q3, Q2)]

    bank_free = [None, None]
    half_free = [None, None]
    prev_t3 = None
    for name, qkoff, ssb, A, Bt, out_dram in (
            ("q", 0, qss, "AQ", "BQ", qx), ("k", 4, kss, "AK", "BK", kx)):
        for oc in range(NQC):
            half = oc % 2
            st_toks = []
            for tt in range(NTT):
                bi = tt % 2
                pt = pQ[bi]
                w0 = lds + [bank_free[bi], half_free[half]]
                mm = None
                for ic in range(NCH):
                    mm = p.op("T", (lambda t, ic=ic, oc=oc, tt=tt, pt=pt, qkoff=qkoff: t.matmul(
                        pt[:], wb[:, ic * 3 * INNER + (qkoff + oc) * 128: ic * 3 * INNER + (qkoff + oc) * 128 + 128],
                        h_ap(ic, tt * 512, 512),
                        start=(ic == 0), stop=(ic == NCH - 1))), w0 if ic == 0 else [])
                    w0 = []
                sg = p.op("S", (lambda s, pt=pt: s.activation(sqb[:], pt[:], AF.Square)), [mm])
                sm = p.op("T", (lambda t: t.matmul(pS[:], e2s[:], sqb[:], start=True, stop=True)), [sg])
                nr = p.op("S", (lambda s: s.activation(rstd[:], pS[:], AF.Sqrt)), [sm, prev_t3])
                nr2 = p.op("V", (lambda v: v.tensor_scalar_max(rstd[:], rstd[:], 1e-12)), [nr])
                nr3 = p.op("V", (lambda v: v.reciprocal(rstd[:], rstd[:])), [nr2])
                ev = p.op("V", (lambda v, pt=pt, ssb=ssb: v.tensor_scalar_mul(qsq[:], pt[:], ssb[:])), [mm])
                bank_free[bi] = ev
                t1 = p.op("V", (lambda v, tt=tt, A=A: v.tensor_mul(
                    u[:], qsq[:], tb[A][:, tt * 512:(tt + 1) * 512])), [ev])
                rt = None
                for (dst, srcq) in QPART:
                    rt = p.op("V", (lambda v, dst=dst, srcq=srcq: v.tensor_copy(
                        qrot[dst, :], qsq[srcq, :])), [ev])
                rt = p.op("V", (lambda v, tt=tt, Bt=Bt: v.tensor_mul(
                    w_[:], qrot[:], tb[Bt][:, tt * 512:(tt + 1) * 512])), [rt])
                t2 = p.op("V", (lambda v: v.tensor_add(u[:], u[:], w_[:])), [t1, rt])
                t3 = p.op("V", (lambda v, oc=oc, tt=tt, half=half: v.tensor_mul(
                    osb[:, half * TPC + tt * 512: half * TPC + (tt + 1) * 512], u[:], rstd[:])),
                    [t2, nr3])
                prev_t3 = t3
                st_toks.append(t3)
            dt = p.dma("G", out_dram.rearrange("(c p) t -> p c t", p=128)[:, oc, :],
                       osb[:, half * TPC: half * TPC + TPC], waits=st_toks,
                       group=f"oh{half}_{name}")
            half_free[half] = dt

    vbank_free = [None, None]
    vhalf_free = None
    for tc in range(TPC // 128):
        bi = tc % 2
        pt = pV[bi]
        w0 = lds + [vbank_free[bi], vhalf_free]
        mm = None
        for ic in range(NCH):
            mm = p.op("T", (lambda t, ic=ic, tc=tc, pt=pt: t.matmul(
                pt[:], h_ap(ic, tc * 128, 128),
                wb[:, ic * 3 * INNER + 2 * INNER: ic * 3 * INNER + 3 * INNER],
                start=(ic == 0), stop=(ic == NCH - 1))), w0 if ic == 0 else [])
            w0 = []
        ev = p.op("V", (lambda v, pt=pt: v.tensor_copy(vsb[:], pt[:])), [mm])
        vbank_free[bi] = ev
        vhalf_free = p.dma("G", vx[tc * 128:(tc + 1) * 128, :], vsb[:], waits=[ev], group="vout")
    return


def build_lffn_qkv():
    nc = bass.Bass("TRN2", target_bir_lowering=False)
    xT = nc.dram_tensor("xT", [D, TPC], F32, kind="ExternalInput").ap()
    xprevT = nc.dram_tensor("xprevT", [D, 1], F32, kind="ExternalInput").ap()
    wqkv = nc.dram_tensor("wqkv", [D, 3 * INNER], F32, kind="ExternalInput").ap()
    qs2 = nc.dram_tensor("qs2", [128, 1], F32, kind="ExternalInput").ap()
    ks2 = nc.dram_tensor("ks2", [128, 1], F32, kind="ExternalInput").ap()
    tabs = {n: nc.dram_tensor(n, [128, TPC], F32, kind="ExternalInput").ap()
            for n in ("AQ", "BQ", "AK", "BK")}
    E2 = nc.dram_tensor("E2", [128, 128], F32, kind="ExternalInput").ap()
    qx = nc.dram_tensor("qx", [INNER, TPC], BF16, kind="ExternalOutput").ap()
    kx = nc.dram_tensor("kx", [INNER, TPC], BF16, kind="ExternalOutput").ap()
    vx = nc.dram_tensor("vx", [TPC, INNER], BF16, kind="ExternalOutput").ap()
    es = ExitStack()
    p = Prog(nc, es)
    emit_lffn_qkv(p, nc, es, xT, xprevT, wqkv, qs2, ks2, tabs, E2, qx, kx, vx)
    p.emit()
    es.close()
    return nc



def emit_lffn_attn(p, nc, es, qx, kfb, vfb, masks, ident, ox, kv_loader=None):
    KL = TPC + 512
    qe = es.enter_context(nc.sbuf_tensor("qe", [65, TPC], BF16))
    ke = es.enter_context(nc.sbuf_tensor("ke", [65, KL], BF16))
    ve = es.enter_context(nc.sbuf_tensor("ve", [128, 65 * (KL // 128)], BF16))
    msk = es.enter_context(nc.sbuf_tensor("msk", [128, 32 * 512], BF16))
    idn = es.enter_context(nc.sbuf_tensor("idn", [128, 128], F32))
    mneg = es.enter_context(nc.sbuf_tensor("mneg", [128, 4], F32))
    mtmp = es.enter_context(nc.sbuf_tensor("mtmp", [128, 1], F32))
    pt_sb = es.enter_context(nc.sbuf_tensor("pt_sb", [128, 2048], BF16))
    rb = es.enter_context(nc.sbuf_tensor("rb", [64, 512], F32))
    rec = es.enter_context(nc.sbuf_tensor("rec", [1, 512], F32))
    ones1 = es.enter_context(nc.sbuf_tensor("ones1", [1, 64], F32))
    ob = es.enter_context(nc.sbuf_tensor("ob", [64, 2 * 512], BF16))
    psA = es.enter_context(nc.psum_tensor("psA", [128, 512], F32))
    psM = es.enter_context(nc.psum_tensor("psM", [1, 128], F32))
    psT = es.enter_context(nc.psum_tensor("psT", [128, 2048], F32))
    psO = es.enter_context(nc.psum_tensor("psO", [65, 512], F32))
    psR = es.enter_context(nc.psum_tensor("psR", [64, 512], F32))

    ldi = p.dma("Y", idn[:], ident, group="ldi")
    ldm = p.dma("Y", msk[:].rearrange("p (s n) -> p s n", n=512), masks.rearrange("s p n -> p s n"),
                group="ldm")
    one1 = p.op("V", lambda v: v.memset(ones1[:], 1.0))
    onek = p.op("V", lambda v: v.memset(ke[64:65, :], 1.0))
    onev = p.op("V", lambda v: v.memset(ve[:].rearrange("p (c e) -> p c e", e=65)[:, :, 64], 1.0))
    prev_head_done = [onek, onev]
    prev_cpm = None
    ob_last = [None, None]
    for h in range(8):
        ldq = p.dma("G", qe[0:64, :], qx[h * 64:(h + 1) * 64, :], waits=prev_head_done, group="ldq")
        if kv_loader is None:
            ldk = p.dma("G", ke[0:64, :], kfb[h * 64:(h + 1) * 64, :], waits=prev_head_done, group="ldk")
            ldv = p.dma("G", ve[:].rearrange("p (c e) -> p c e", e=65)[:, :, 0:64],
                        vfb.rearrange("(c p) d -> p c d", p=128)[:, :, h * 64:(h + 1) * 64],
                        waits=prev_head_done, group="ldv")
        else:
            ldk, ldv = kv_loader(h, prev_head_done, ke, ve)
        prev_ot = None
        for wi in range(4):
            tr_toks = []
            for qt in range(4):
                qsl = qe[0:64, wi * 512 + qt * 128: wi * 512 + qt * 128 + 128]
                red_prev = None
                for kc in range(2):
                    w = [ldq, ldk] if (qt == 0 and kc == 0) else []
                    w = w + ([red_prev] if red_prev else [])
                    mm = p.op("T", (lambda t, qsl=qsl, kc=kc, wi=wi: t.matmul(
                        psA[:], qsl, ke[0:64, wi * 512 + kc * 512: wi * 512 + (kc + 1) * 512],
                        start=True, stop=True)), w)
                    if kc == 0:
                        red_prev = p.op("V", (lambda v, qt=qt: v.reduce_max(
                            mneg[:, qt:qt + 1], psA[:], axis=AX.X, negate=True)), [mm])
                    else:
                        r1 = p.op("V", (lambda v: v.reduce_max(mtmp[:], psA[:], axis=AX.X,
                                                               negate=True)), [mm])
                        red_prev = p.op("V", (lambda v, qt=qt: v.tensor_tensor(
                            mneg[:, qt:qt + 1], mneg[:, qt:qt + 1], mtmp[:], ALU.min)), [r1])
                trm = p.op("T", (lambda t, qt=qt: t.transpose(psM[:], mneg[:, qt:qt + 1], idn[:])),
                           [red_prev, ldi, prev_cpm])
                cpm = p.op("V", (lambda v, wi=wi, qt=qt: v.tensor_copy(
                    qe[64:65, wi * 512 + qt * 128: wi * 512 + qt * 128 + 128], psM[:])), [trm])
                prev_cpm = cpm
                tr_toks.append(cpm)
            qg = qe[:, wi * 512:(wi + 1) * 512]
            first_st_waits = tr_toks + [prev_ot]
            ex_prev = None
            avm = None
            for kc in range(2):
                mm4 = None
                for s in range(4):
                    w = (first_st_waits + [ex_prev]) if s == 0 else []
                    mm4 = p.op("T", (lambda t, kc=kc, s=s, qg=qg, wi=wi: t.matmul(
                        psT[:, s * 512:(s + 1) * 512],
                        ke[:, wi * 512 + kc * 512 + s * 128: wi * 512 + kc * 512 + s * 128 + 128], qg,
                        start=True, stop=True)), w)
                first_st_waits = []
                ex = p.op("S", (lambda s_: s_.activation(pt_sb[:], psT[:], AF.Exp, scale=8.0)),
                          [mm4, avm])
                ex_prev = ex
                dep = [ex]
                for s in range(4):
                    mi = wi * 8 + kc * 4 + s
                    dep = [p.op("V", (lambda v, s=s, mi=mi: v.tensor_mul(
                        pt_sb[:, s * 512:(s + 1) * 512], pt_sb[:, s * 512:(s + 1) * 512],
                        msk[:, mi * 512:(mi + 1) * 512])), dep + ([ldm] if s == 0 else []))]
                for s in range(4):
                    w = (dep + [ldv]) if s == 0 else []
                    avm = p.op("T", (lambda t, kc=kc, s=s, wi=wi: t.matmul(
                        psO[:], ve[:, (wi * 4 + kc * 4 + s) * 65:(wi * 4 + kc * 4 + s) * 65 + 65],
                        pt_sb[:, s * 512:(s + 1) * 512],
                        start=(kc == 0 and s == 0), stop=(kc == 1 and s == 3))), w)
            rc = p.op("V", (lambda v: v.reciprocal(rec[:], psO[64:65, :])), [avm])
            bc = p.op("T", (lambda t: t.matmul(psR[:], ones1[:], rec[:], start=True, stop=True)),
                      [rc, one1])
            cb = p.op("V", (lambda v: v.tensor_copy(rb[:], psR[:])), [bc])
            ot = p.op("V", (lambda v, wi=wi: v.tensor_mul(
                ob[:, (wi % 2) * 512:(wi % 2) * 512 + 512], psO[0:64, :], rb[:])),
                [cb, ob_last[wi % 2]])
            ob_last[wi % 2] = p.dma("G", ox[h * 64:(h + 1) * 64, wi * 512:(wi + 1) * 512],
                                    ob[:, (wi % 2) * 512:(wi % 2) * 512 + 512], waits=[ot],
                                    group=f"ob{wi % 2}")
            prev_ot = ot
        prev_head_done = [prev_ot, ob_last[0], ob_last[1]]
    return


def build_lffn_attn():
    nc = bass.Bass("TRN2", target_bir_lowering=False)
    KL = TPC + 512
    qx = nc.dram_tensor("qx", [INNER, TPC], BF16, kind="ExternalInput").ap()
    kfb = nc.dram_tensor("kfb", [INNER, KL], BF16, kind="ExternalInput").ap()
    vfb = nc.dram_tensor("vfb", [KL, INNER], BF16, kind="ExternalInput").ap()
    masks = nc.dram_tensor("masks", [32, 128, 512], BF16, kind="ExternalInput").ap()
    ident = nc.dram_tensor("ident", [128, 128], F32, kind="ExternalInput").ap()
    ox = nc.dram_tensor("ox", [INNER, TPC], BF16, kind="ExternalOutput").ap()
    es = ExitStack()
    p = Prog(nc, es)
    emit_lffn_attn(p, nc, es, qx, kfb, vfb, masks, ident, ox)
    p.emit()
    es.close()
    return nc



def emit_lffn_proj(p, nc, es, oT, wo, xT, x2):
    obf = es.enter_context(nc.sbuf_tensor("obf", [128, 4 * TPC], BF16))
    wob = es.enter_context(nc.sbuf_tensor("wob", [128, 4 * D], BF16))
    xr = es.enter_context(nc.sbuf_tensor("xr", [128, NCH * TPC], F32))
    xo = es.enter_context(nc.sbuf_tensor("xo", [128, 512], F32))
    ps = [es.enter_context(nc.psum_tensor(f"ps{i}", [128, 512], F32)) for i in range(2)]
    ldo = p.dma("G", obf[:].rearrange("p (c t) -> p c t", t=TPC),
                oT.rearrange("(c p) t -> p c t", p=128), group="ldo")
    ldw = p.dma("G", wob[:].rearrange("p (c o) -> p c o", o=D), wo.rearrange("(c p) o -> p c o", p=128), group="ldw")
    ldxr = p.dma("G", xr[:].rearrange("p (c t) -> p c t", t=TPC),
                 xT.rearrange("(c p) t -> p c t", p=128), group="ldxr")
    bank_free = [None, None]
    for oc in range(NCH):
        for tt in range(NTT):
            bi = (oc * NTT + tt) % 2
            pt = ps[bi]
            w0 = [ldo, ldw, ldxr, bank_free[bi]]
            mm = None
            for ic in range(4):
                mm = p.op("T", (lambda t, ic=ic, oc=oc, tt=tt, pt=pt: t.matmul(
                    pt[:], wob[:, ic * D + oc * 128: ic * D + oc * 128 + 128],
                    obf[:, ic * TPC + tt * 512: ic * TPC + (tt + 1) * 512],
                    start=(ic == 0), stop=(ic == 3))), w0 if ic == 0 else [])
                w0 = []
            ad = p.op("V", (lambda v, oc=oc, tt=tt, pt=pt: v.tensor_add(
                xr[:, oc * TPC + tt * 512: oc * TPC + (tt + 1) * 512],
                xr[:, oc * TPC + tt * 512: oc * TPC + (tt + 1) * 512], pt[:])), [mm])
            bank_free[bi] = ad
            p.dma("G", x2.rearrange("(c p) t -> p c t", p=128)[:, oc, tt * 512:(tt + 1) * 512],
                  xr[:, oc * TPC + tt * 512: oc * TPC + (tt + 1) * 512], waits=[ad], group="xout")
    return


def build_lffn_proj():
    nc = bass.Bass("TRN2", target_bir_lowering=False)
    oT = nc.dram_tensor("oT", [INNER, TPC], BF16, kind="ExternalInput").ap()
    wo = nc.dram_tensor("wo", [INNER, D], F32, kind="ExternalInput").ap()
    xT = nc.dram_tensor("xT", [D, TPC], F32, kind="ExternalInput").ap()
    x2 = nc.dram_tensor("x2", [D, TPC], F32, kind="ExternalOutput").ap()
    es = ExitStack()
    p = Prog(nc, es)
    emit_lffn_proj(p, nc, es, oT, wo, xT, x2)
    p.emit()
    es.close()
    return nc



HID_P = 2816


def emit_lffn_ffn(p, nc, es, x2T, x2prevT, lg, lb, w1p, b1p, w2p, b2, x3):
    W = 2049
    NA = HID_P // 128
    xsh = es.enter_context(nc.sbuf_tensor("xsh", [128, NCH * W], F32))
    hbf = es.enter_context(nc.sbuf_tensor("hbf", [128, NCH * TPC], BF16))
    lgs = es.enter_context(nc.sbuf_tensor("lgs", [128, NCH], F32))
    lbs = es.enter_context(nc.sbuf_tensor("lbs", [128, NCH], F32))
    b1s = es.enter_context(nc.sbuf_tensor("b1s", [128, 2 * NA], F32))
    b2s = es.enter_context(nc.sbuf_tensor("b2s", [128, NCH], F32))
    w1sb = es.enter_context(nc.sbuf_tensor("w1sb", [128, 2 * 2 * NCH * 128], BF16))
    w2sb = es.enter_context(nc.sbuf_tensor("w2sb", [128, 2 * NA * 128], BF16))
    pbuf = es.enter_context(nc.sbuf_tensor("pbuf", [128, NA * 512], BF16))
    gg = es.enter_context(nc.sbuf_tensor("gg", [128, 512], BF16))
    atmp = es.enter_context(nc.sbuf_tensor("atmp", [128, 512], F32))
    xo = es.enter_context(nc.sbuf_tensor("xo", [128, 512], F32))
    ps = [es.enter_context(nc.psum_tensor(f"ps{i}", [128, 512], F32)) for i in range(2)]
    lnb = alloc_ln_bufs(nc, es)

    ldx = p.dma("G", xsh[:].rearrange("p (c t) -> p c t", t=W)[:, :, 1:W],
                x2T.rearrange("(c p) t -> p c t", p=128), group="ldx")
    ldxp = p.dma("G", xsh[:].rearrange("p (c t) -> p c t", t=W)[:, :, 0:1],
                 x2prevT.rearrange("(c p) t -> p c t", p=128), allow_slow_non_contiguous=True,
                 group="ldxp")
    ldlg = p.dma("Y", lgs[:], lg, group="ldlg")
    ldlb = p.dma("Y", lbs[:], lb, group="ldlb")
    ldb1 = p.dma("Y", b1s[:], b1p, group="ldb1")
    ldb2 = p.dma("Y", b2s[:], b2, group="ldb2")
    init = emit_ln_init(p, lnb)

    def xs_ap(c, tt):
        off = 0 if c >= 4 else 1
        return xsh[:, c * W + off + tt * 512: c * W + off + tt * 512 + 512]

    def xu_ap(c, tt):
        return xsh[:, c * W + 1 + tt * 512: c * W + 1 + tt * 512 + 512]

    def hb_ap(c, tt):
        return hbf[:, c * TPC + tt * 512: c * TPC + (tt + 1) * 512]

    toks, last_ln = emit_ln(p, lnb, xs_ap, TPC, out_fn=hb_ap,
                            gamma_scale=lambda c: lgs[:, c:c + 1],
                            gamma_bias=lambda c: lbs[:, c:c + 1],
                            pre_waits=[ldx, ldxp, ldlg, ldlb], init_toks=init)
    ln_done = [toks[(c, tt)] for c in range(NCH) for tt in range(NTT)]

    w1r = w1p.rearrange("(c p) o -> p c o", p=128)
    w2r = w2p.rearrange("(c p) o -> p c o", p=128)
    w1v = w1sb[:].rearrange("p (b a c o) -> p b a c o", b=2, a=2, o=128)
    w2v = w2sb[:].rearrange("p (b c o) -> p b c o", b=2, o=128)
    prev_pb_use = None
    last_x3 = None
    w1_free = [None, None]
    w2_free = [None, None]
    gbank = [None, None]
    for tt in range(NTT):
        pb_toks = []
        for ocp in range(NA):
            par = ocp % 2
            lwa = p.dma("G", w1v[:, par, 0], w1r[:, :, ocp * 128:(ocp + 1) * 128],
                        waits=[w1_free[par]], group=f"w1_{par}")
            lwg = p.dma("G", w1v[:, par, 1], w1r[:, :, HID_P + ocp * 128: HID_P + ocp * 128 + 128],
                        group=f"w1_{par}")
            w0 = ln_done + [lwa, lwg, gbank[0], prev_pb_use]
            mmg = None
            for ic in range(NCH):
                mmg = p.op("T", (lambda t, ic=ic, par=par, tt=tt: t.matmul(
                    ps[0][:], w1v[:, par, 1, ic], hb_ap(ic, tt),
                    start=(ic == 0), stop=(ic == NCH - 1))), w0 if ic == 0 else [])
                w0 = []
            ge = p.op("S", (lambda s, ocp=ocp: s.activation(
                gg[:], ps[0][:], AF.Gelu, bias=b1s[:, NA + ocp:NA + ocp + 1])), [mmg, ldb1])
            gbank[0] = ge
            w0 = [gbank[1], ge]
            mma = None
            for ic in range(NCH):
                mma = p.op("T", (lambda t, ic=ic, par=par, tt=tt: t.matmul(
                    ps[1][:], w1v[:, par, 0, ic], hb_ap(ic, tt),
                    start=(ic == 0), stop=(ic == NCH - 1))), w0 if ic == 0 else [])
                w0 = []
            ae = p.op("V", (lambda v, ocp=ocp: v.tensor_scalar_add(
                atmp[:], ps[1][:], b1s[:, ocp:ocp + 1])), [mma, ldb1])
            gbank[1] = ae
            pm = p.op("V", (lambda v, ocp=ocp: v.tensor_mul(
                pbuf[:, ocp * 512:(ocp + 1) * 512], atmp[:], gg[:])), [ae, ge])
            w1_free[par] = pm
            pb_toks.append(pm)
        x3_toks = []
        last_w2mm = None
        for oc2 in range(NCH):
            par = oc2 % 2
            lw2 = p.dma("G", w2v[:, par], w2r[:, :, oc2 * 128:(oc2 + 1) * 128],
                        waits=[w2_free[par]], group=f"w2_{par}")
            w0 = pb_toks + [lw2, gbank[0], gbank[1]]
            mm = None
            for kc in range(NA):
                mm = p.op("T", (lambda t, kc=kc, par=par, tt=tt: t.matmul(
                    ps[0][:], w2v[:, par, kc], pbuf[:, kc * 512:(kc + 1) * 512],
                    start=(kc == 0), stop=(kc == NA - 1))), w0 if kc == 0 else [])
                w0 = []
            last_w2mm = mm
            w2_free[par] = mm
            ev = p.op("V", (lambda v, oc2=oc2: v.tensor_scalar_add(
                xo[:], ps[0][:], b2s[:, oc2:oc2 + 1])), [mm, ldb2])
            gbank[0] = ev
            ad = p.op("V", (lambda v, oc2=oc2, tt=tt: v.tensor_add(
                xo[:], xo[:], xu_ap(oc2, tt))), [ev])
            dt = p.dma("G", x3.rearrange("(c p) t -> p c t", p=128)[:, oc2, tt * 512:(tt + 1) * 512],
                       xo[:], waits=[ad], group="xout")
            gbank[0] = dt
            x3_toks.append(ad)
        prev_pb_use = last_w2mm
        last_x3 = x3_toks[-1]
    return


def build_lffn_ffn():
    nc = bass.Bass("TRN2", target_bir_lowering=False)
    x2T = nc.dram_tensor("x2T", [D, TPC], F32, kind="ExternalInput").ap()
    x2prevT = nc.dram_tensor("x2prevT", [D, 1], F32, kind="ExternalInput").ap()
    lg = nc.dram_tensor("lg", [128, NCH], F32, kind="ExternalInput").ap()
    lb = nc.dram_tensor("lb", [128, NCH], F32, kind="ExternalInput").ap()
    w1p = nc.dram_tensor("w1p", [D, 2 * HID_P], F32, kind="ExternalInput").ap()
    b1p = nc.dram_tensor("b1p", [128, 2 * HID_P // 128], F32, kind="ExternalInput").ap()
    w2p = nc.dram_tensor("w2p", [HID_P, D], F32, kind="ExternalInput").ap()
    b2 = nc.dram_tensor("b2", [128, NCH], F32, kind="ExternalInput").ap()
    x3 = nc.dram_tensor("x3", [D, TPC], F32, kind="ExternalOutput").ap()
    es = ExitStack()
    p = Prog(nc, es)
    emit_lffn_ffn(p, nc, es, x2T, x2prevT, lg, lb, w1p, b1p, w2p, b2, x3)
    p.emit()
    es.close()
    return nc





DFF = 4096
NE = 8


def emit_moe(p, nc, es, xT, rw, rbrow, upT, downT, ident, rowsel, x4):
    NDC = DFF // 128
    TH = 1024
    xbf = es.enter_context(nc.sbuf_tensor("xbf", [128, NCH * TPC], BF16))
    rwb = es.enter_context(nc.sbuf_tensor("rwb", [128, NCH * NE], F32))
    rbr = es.enter_context(nc.sbuf_tensor("rbr", [1, NE], F32))
    onesr = es.enter_context(nc.sbuf_tensor("onesr", [1, 128], F32))
    xf = es.enter_context(nc.sbuf_tensor("xf", [128, NCH * 128], F32))
    ones1 = es.enter_context(nc.sbuf_tensor("ones1f", [1, 128], F32))
    idn = es.enter_context(nc.sbuf_tensor("idn", [128, 128], F32))
    rsel = es.enter_context(nc.sbuf_tensor("rsel", [NE, NE * 128], F32))
    probs = es.enter_context(nc.sbuf_tensor("probs", [128, NE], F32))
    m8 = es.enter_context(nc.sbuf_tensor("m8", [128, NE], F32))
    mneg = es.enter_context(nc.sbuf_tensor("mnegM", [128, 1], F32))
    ssum = es.enter_context(nc.sbuf_tensor("ssum", [128, 1], F32))
    comb = es.enter_context(nc.sbuf_tensor("comb", [128, NE], F32))
    comb_fm = es.enter_context(nc.sbuf_tensor("comb_fm", [NE, TPC], F32))
    cbc = es.enter_context(nc.sbuf_tensor("cbc", [128, TH], F32))
    uph = es.enter_context(nc.sbuf_tensor("uph", [128, NCH * 2048], BF16))
    hb = es.enter_context(nc.sbuf_tensor("hb", [128, NDC * TH], BF16))
    dsl = es.enter_context(nc.sbuf_tensor("dsl", [128, NDC * 128], BF16))
    acc = es.enter_context(nc.sbuf_tensor("acc", [128, NCH * TH], F32))
    tmpv = es.enter_context(nc.sbuf_tensor("tmpv", [128, 512], F32))
    xtile = es.enter_context(nc.sbuf_tensor("xtile", [128, 2 * 512], F32))
    psL = es.enter_context(nc.psum_tensor("psL", [128, 512], F32))
    psC = es.enter_context(nc.psum_tensor("psC", [128, 512], F32))
    psU = [es.enter_context(nc.psum_tensor(f"psU{i}", [128, 512], F32)) for i in range(2)]
    psD = [es.enter_context(nc.psum_tensor(f"psD{i}", [128, 512], F32)) for i in range(2)]
    lnb = alloc_ln_bufs(nc, es, reuse=dict(psS=psL[0:1, 0:512], psS2=psC[0:1, 0:512],
                                           psB=psU[0][:], psB2=psU[1][:]))

    ldx = p.dma("G", xbf[:].rearrange("p (c t) -> p c t", t=TPC),
                xT.rearrange("(c p) t -> p c t", p=128), group="ldx")
    ldrw = p.dma("Y", rwb[:].rearrange("p (c o) -> p c o", o=NE),
                 rw.rearrange("(c p) o -> p c o", p=128), group="ldrw")
    ldrb = p.dma("Y", rbr[:], rbrow, group="ldrb")
    ldi = p.dma("Y", idn[:], ident, group="ldi")
    ldrs = p.dma("Y", rsel[:], rowsel, group="ldrs")
    o1 = p.op("V", lambda v: v.memset(onesr[:], 1.0))
    o2 = p.op("V", lambda v: v.memset(ones1[:], 1.0))
    init = emit_ln_init(p, lnb)

    prev_tr = None
    last_cf = None
    prev_t2 = None
    xf_free = None
    for tc in range(TPC // 128):
        lxf = p.dma("G", xf[:].rearrange("p (c o) -> p c o", o=128),
                    xT.rearrange("(c p) t -> p c t", p=128)[:, :, tc * 128:(tc + 1) * 128],
                    waits=[xf_free], group="xf")
        w0 = [lxf, ldrw, ldrb, o1, prev_t2]
        for ic in range(NCH):
            mmr = p.op("T", (lambda t, ic=ic, tc=tc: t.matmul(
                psL[:, 0:NE],
                xf[:, ic * 128:(ic + 1) * 128],
                rwb[:, ic * NE:(ic + 1) * NE], start=(ic == 0), stop=False)),
                w0 if ic == 0 else [])
            w0 = []
        xf_free = mmr
        mm = p.op("T", (lambda t: t.matmul(
            psL[:, 0:NE], onesr[:], rbr[:], start=False, stop=True)), [])
        t1 = p.op("V", (lambda v: v.reduce_max(mneg[:], psL[:, 0:NE], axis=AX.X, negate=True)),
                  [mm, prev_tr])
        t2 = p.op("S", (lambda s: s.activation(probs[:], psL[:, 0:NE], AF.Exp, bias=mneg[:])), [t1])
        prev_t2 = t2
        t3 = p.op("V", (lambda v: v.reduce_sum(ssum[:], probs[:], axis=AX.X)), [t2])
        t4 = p.op("V", (lambda v: v.reciprocal(ssum[:], ssum[:])), [t3])
        t5 = p.op("V", (lambda v: v.tensor_scalar_mul(probs[:], probs[:], ssum[:])), [t4])
        t6 = p.op("V", (lambda v: v.max(m8[:], probs[:])), [t5])
        t7 = p.op("V", (lambda v: v.scalar_tensor_tensor(
            comb[:], probs[:], m8[:, 1:2], probs[:], ALU.is_ge, ALU.mult)), [t6])
        tr = p.op("T", (lambda t: t.transpose(psC[0:NE, 0:128], comb[:], idn[:])),
                  [t7, ldi, prev_tr])
        cf = p.op("V", (lambda v, tc=tc: v.tensor_copy(
            comb_fm[:, tc * 128:(tc + 1) * 128], psC[0:NE, 0:128])), [tr])
        prev_tr = cf
        last_cf = cf

    upr = upT.rearrange("e (c p) f -> e p c f", p=128)
    dwr = downT.rearrange("e (c p) f -> e p c f", p=128)
    uphv = uph[:].rearrange("p (c f) -> p c f", f=2048)
    dslv = dsl[:].rearrange("p (c o) -> p c o", o=128)
    up_free = None
    dsl_free = [None, None]
    psD_reader = [None, None]
    psU_reader = [None, None]
    prev_cbc = None
    cbc_reader = None
    prev_acc_use = []
    x4_toks = []
    for th in range(2):
        acc_last = {}
        tmp_last = None
        for e in range(NE):
            h_last = []
            for dh in range(2):
                lu = p.dma("G", uphv, upr[e][:, :, dh * 2048:(dh + 1) * 2048], waits=[up_free],
                           group="lu")
                last_ev = None
                for dc in range(16):
                    for ts in range(2):
                        bi = (dc * 2 + ts) % 2
                        pt = psU[bi]
                        w0 = [ldx, lu, psU_reader[bi]] + \
                             (prev_acc_use if (e == 0 and dh == 0 and dc == 0 and ts == 0) else [])
                        mm = None
                        for ic in range(NCH):
                            mm = p.op("T", (lambda t, ic=ic, dc=dc, ts=ts, th=th, pt=pt: t.matmul(
                                pt[:], uphv[:, ic, dc * 128:(dc + 1) * 128],
                                xbf[:, ic * TPC + th * TH + ts * 512: ic * TPC + th * TH + ts * 512 + 512],
                                start=(ic == 0), stop=(ic == NCH - 1))), w0 if ic == 0 else [])
                            w0 = []
                        ev = p.op("S", (lambda s, dh=dh, dc=dc, ts=ts, pt=pt: s.activation(
                            hb[:, (dh * 16 + dc) * TH + ts * 512: (dh * 16 + dc) * TH + ts * 512 + 512],
                            pt[:], AF.Gelu)), [mm])
                        psU_reader[bi] = ev
                        last_ev = ev
                up_free = last_ev
                h_last.append(last_ev)
            cb_toks = []
            for ts in range(2):
                bcm = p.op("T", (lambda t, e=e, th=th, ts=ts: t.matmul(
                    psC[:, 0:512], rsel[:, e * 128:(e + 1) * 128],
                    comb_fm[0:NE, th * TH + ts * 512: th * TH + ts * 512 + 512],
                    start=True, stop=True)), [last_cf, ldrs, prev_cbc] + cb_toks +
                    ([cbc_reader] if cbc_reader else []))
                cbc_t = p.op("V", (lambda v, ts=ts: v.tensor_copy(
                    cbc[:, ts * 512:(ts + 1) * 512], psC[:, 0:512])), [bcm])
                prev_cbc = cbc_t
                cb_toks.append(cbc_t)
            last_mm = None
            for oc in range(NCH):
                par = 0
                ld2 = p.dma("G", dslv, dwr[e][:, :, oc * 128:(oc + 1) * 128],
                            waits=[dsl_free[par]], group="dsl")
                for ts in range(2):
                    pt = psD[ts]
                    w0 = (h_last if oc == 0 else []) + [ld2, psD_reader[ts]] + \
                         (prev_acc_use if (oc == 0 and e == 0) else [])
                    mm = None
                    for kc in range(NDC):
                        mm = p.op("T", (lambda t, kc=kc, ts=ts, pt=pt: t.matmul(
                            pt[:], dslv[:, kc],
                            hb[:, kc * TH + ts * 512: kc * TH + ts * 512 + 512],
                            start=(kc == 0), stop=(kc == NDC - 1))), w0 if kc == 0 else [])
                        w0 = []
                    last_mm = mm
                    aslice = acc[:, oc * TH + ts * 512: oc * TH + ts * 512 + 512]
                    if e == 0:
                        rd = p.op("V", (lambda v, aslice=aslice, ts=ts, pt=pt: v.tensor_mul(
                            aslice, pt[:], cbc[:, ts * 512:(ts + 1) * 512])), [mm] + cb_toks)
                        acc_last[(oc, ts)] = rd
                    else:
                        rd = p.op("V", (lambda v, ts=ts, pt=pt: v.tensor_mul(
                            tmpv[:], pt[:], cbc[:, ts * 512:(ts + 1) * 512])),
                            [mm, tmp_last] + cb_toks)
                        at = p.op("G", (lambda g, aslice=aslice: g.tensor_add(
                            aslice, aslice, tmpv[:])), [rd, acc_last[(oc, ts)]])
                        tmp_last = at
                        acc_last[(oc, ts)] = at
                    psD_reader[ts] = rd
                    cbc_reader = rd
                dsl_free[par] = last_mm
        def a_ap(c, tt, th=th):
            return acc[:, c * TH + tt * 512: c * TH + tt * 512 + 512]
        res_toks = []
        xl_prev = [None, None]
        for c in range(NCH):
            for tt in range(2):
                par = tt % 2
                lx = p.dma("G", xtile[:, par * 512:(par + 1) * 512],
                           xT.rearrange("(c p) t -> p c t", p=128)[:, c,
                                        th * TH + tt * 512: th * TH + tt * 512 + 512],
                           waits=[xl_prev[par]], group=f"xt{par}")
                ad = p.op("V", (lambda v, c=c, tt=tt, par=par: v.tensor_add(
                    a_ap(c, tt), a_ap(c, tt), xtile[:, par * 512:(par + 1) * 512])),
                    [lx, acc_last[(c, tt)]])
                xl_prev[par] = ad
                res_toks.append(ad)
        toks, last_ln = emit_ln(p, lnb, a_ap, TH, out_fn=a_ap, pre_waits=[res_toks[-1]],
                                init_toks=init)
        for c in range(NCH):
            for tt in range(2):
                p.dma("G", x4.rearrange("(c p) t -> p c t", p=128)[:, c,
                                        th * TH + tt * 512: th * TH + tt * 512 + 512],
                      a_ap(c, tt), waits=[toks[(c, tt)]], group="x4out")
        prev_acc_use = [last_ln, p.gtotal("x4out")]
    return


def build_moe():
    nc = bass.Bass("TRN2", target_bir_lowering=False)
    xT = nc.dram_tensor("xT", [D, TPC], F32, kind="ExternalInput").ap()
    rw = nc.dram_tensor("rw", [D, NE], F32, kind="ExternalInput").ap()
    rbrow = nc.dram_tensor("rbrow", [1, NE], F32, kind="ExternalInput").ap()
    upT = nc.dram_tensor("upT", [NE, D, DFF], F32, kind="ExternalInput").ap()
    downT = nc.dram_tensor("downT", [NE, DFF, D], F32, kind="ExternalInput").ap()
    ident = nc.dram_tensor("ident", [128, 128], F32, kind="ExternalInput").ap()
    rowsel = nc.dram_tensor("rowsel", [NE, NE * 128], F32, kind="ExternalInput").ap()
    x4 = nc.dram_tensor("x4", [D, TPC], F32, kind="ExternalOutput").ap()
    es = ExitStack()
    p = Prog(nc, es)
    emit_moe(p, nc, es, xT, rw, rbrow, upT, downT, ident, rowsel, x4)
    p.emit()
    es.close()
    return nc



def emit_addln(p, nc, es, xT, yT, xn):
    xr = es.enter_context(nc.sbuf_tensor("xr", [128, NCH * TPC], F32))
    yb = es.enter_context(nc.sbuf_tensor("yb", [128, NCH * TPC], F32))
    lnb = alloc_ln_bufs(nc, es)
    ldx = p.dma("G", xr[:].rearrange("p (c t) -> p c t", t=TPC), xT.rearrange("(c p) t -> p c t", p=128), group="ldx")
    ldy = p.dma("G", yb[:].rearrange("p (c t) -> p c t", t=TPC), yT.rearrange("(c p) t -> p c t", p=128), group="ldy")
    init = emit_ln_init(p, lnb)
    add_toks = []
    for c in range(NCH):
        for tt in range(NTT):
            ad = p.op("V", (lambda v, c=c, tt=tt: v.tensor_add(
                xr[:, c * TPC + tt * 512: c * TPC + (tt + 1) * 512],
                xr[:, c * TPC + tt * 512: c * TPC + (tt + 1) * 512],
                yb[:, c * TPC + tt * 512: c * TPC + (tt + 1) * 512])),
                [ldx, ldy] if (c == 0 and tt == 0) else [])
            add_toks.append(ad)

    def x_ap(c, tt):
        return xr[:, c * TPC + tt * 512: c * TPC + (tt + 1) * 512]

    toks, last = emit_ln(p, lnb, x_ap, TPC, out_fn=x_ap, pre_waits=[add_toks[-1]], init_toks=init)
    for c in range(NCH):
        for tt in range(NTT):
            p.dma("G", xn.rearrange("(c p) t -> p c t", p=128)[:, c, tt * 512:(tt + 1) * 512],
                  x_ap(c, tt), waits=[toks[(c, tt)]], group="xout")
    return


def build_addln():
    nc = bass.Bass("TRN2", target_bir_lowering=False)
    xT = nc.dram_tensor("xT", [D, TPC], F32, kind="ExternalInput").ap()
    yT = nc.dram_tensor("yT", [D, TPC], F32, kind="ExternalInput").ap()
    xn = nc.dram_tensor("xn", [D, TPC], F32, kind="ExternalOutput").ap()
    es = ExitStack()
    p = Prog(nc, es)
    emit_addln(p, nc, es, xT, yT, xn)
    p.emit()
    es.close()
    return nc


def build_proj_ln():
    nc = bass.Bass("TRN2", target_bir_lowering=False)
    oT = nc.dram_tensor("oT", [D, TPC], BF16, kind="ExternalInput").ap()
    wo = nc.dram_tensor("wo", [D, D], F32, kind="ExternalInput").ap()
    bo = nc.dram_tensor("bo", [128, NCH], F32, kind="ExternalInput").ap()
    xT = nc.dram_tensor("xT", [D, TPC], F32, kind="ExternalInput").ap()
    xn = nc.dram_tensor("xn", [D, TPC], F32, kind="ExternalOutput").ap()
    es = ExitStack()
    p = Prog(nc, es)
    emit_proj_ln(p, nc, es, oT, wo, bo, xT, xn)
    p.emit()
    es.close()
    return nc



MEGA_DEBUG = False


def build_mega():
    nc = bass.Bass("TRN2", target_bir_lowering=False)
    EI, EO = "ExternalInput", "ExternalOutput"
    A = lambda n, s, d=F32, k=EI: nc.dram_tensor(n, s, d, kind=k).ap()
    x0T = A("x0T", [D, TPC])
    out = A("out", [D, TPC], F32, EO)
    xdbg = A("xdbg", [5, D, TPC], F32, EO) if MEGA_DEBUG else None
    if MEGA_DEBUG:
        bdbg = A("bdbg", [D, 1], F32, EO)
        qdbg = A("qdbg", [INNER, TPC], BF16, EO)
        kdbg = A("kdbg", [INNER, TPC + 512], BF16, EO)
        vdbg = A("vdbg", [TPC + 512, INNER], BF16, EO)
        odbg = A("odbg", [INNER, TPC], BF16, EO)
        x2dbg = A("x2dbg", [D, TPC], F32, EO)
    dw = {}
    for l in ("0", "1"):
        for n in ("wq", "wk", "wv", "wo"):
            dw[n + l] = A(n + l, [D, D])
        for n in ("bq", "bk", "bo"):
            dw[n + l] = A(n + l, [128, NCH])
        dw["bvrow" + l] = A("bvrow" + l, [1, D])
    lw = {}
    for l in ("0", "1"):
        lw["wqkv" + l] = A("wqkv" + l, [D, 3 * INNER])
        lw["qs2" + l] = A("qs2" + l, [128, 1])
        lw["ks2" + l] = A("ks2" + l, [128, 1])
        lw["lwo" + l] = A("lwo" + l, [INNER, D])
        lw["lg" + l] = A("lg" + l, [128, NCH])
        lw["lb" + l] = A("lb" + l, [128, NCH])
        lw["w1p" + l] = A("w1p" + l, [D, 2 * HID_P])
        lw["b1p" + l] = A("b1p" + l, [128, 2 * HID_P // 128])
        lw["w2p" + l] = A("w2p" + l, [HID_P, D])
        lw["b2" + l] = A("b2" + l, [128, NCH])
    tabs = {n: A("t" + n, [128, TPC]) for n in ("AQ", "BQ", "AK", "BK")}
    E2 = A("E2", [128, 128])
    lmasks = A("lmasks", [32, 128, 512], BF16)
    tri = A("tri", [4, 128, 512], BF16)
    ident = A("ident", [128, 128])
    rowsel = A("rowsel", [NE, NE * 128])
    rw = A("rw", [D, NE])
    rbrow = A("rbrow", [1, NE])
    upT = A("upT", [NE, D, DFF])
    downT = A("downT", [NE, DFF, D])
    bmask = A("bmask", [128, NCH])
    ix = A("ix", [1, 64], mybir.dt.uint32)
    I = lambda n, s, d=F32: nc.dram_tensor(n, s, d, kind="Internal").ap()
    xcur = I("xcur", [D, TPC])
    xpre = I("xpre", [D, TPC])
    x2s = I("x2s", [D, TPC])
    qkc = I("qkc", [2 * D, TPC], BF16)
    vc = I("vc", [TPC, D], BF16)
    qkg = nc.dram_tensor("qkg", [8 * 2 * D, TPC], BF16, kind="Internal", addr_space="Shared").ap()
    vg = nc.dram_tensor("vg", [8 * TPC, D], BF16, kind="Internal", addr_space="Shared").ap()
    oc4 = I("oc4", [4 * 64, T], BF16)
    og = nc.dram_tensor("og", [8 * 4 * 64, T], BF16, kind="Internal", addr_space="Shared").ap()
    bndc = I("bndc", [128, NCH])
    bndg = nc.dram_tensor("bndg", [8 * 128, NCH], F32, kind="Internal", addr_space="Shared").ap()
    bndloc = I("bndloc", [D, 1])
    kvwc = I("kvwc", [2 * 512, 512], BF16)
    kvwg = nc.dram_tensor("kvwg", [8 * 2 * 512, 512], BF16, kind="Internal", addr_space="Shared").ap()
    qkl = I("qkl", [2 * 4 * 64, T], BF16)
    vloc = I("vloc", [4 * T, 64], BF16)
    oloc = I("oloc", [16 * 64, TPC], BF16)
    qlxs = I("qlxs", [INNER, TPC], BF16)
    kfbs = I("kfbs", [INNER, TPC + 512], BF16)
    vfbs = I("vfbs", [TPC + 512, INNER], BF16)
    oxl = I("oxl", [INNER, TPC], BF16)
    qkg5 = qkg.rearrange("(r a h p) t -> r a h p t", r=8, a=2, h=16)
    vg4 = vg.rearrange("(r t) (h e) -> r h t e", r=8, h=16)
    og6 = og.rearrange("(r s p) (b t) -> b r s p t", r=8, s=4, t=TPC)
    qkl5 = qkl.rearrange("(a h p) (r t) -> a h p r t", a=2, h=4, r=4)
    vloc4 = vloc.rearrange("(h r t) e -> h r t e", h=4, r=4)
    oloc4 = oloc.rearrange("(c b p) t -> c b p t", c=8, b=2)
    bndg3 = bndg.rearrange("(r p) c -> r p c", r=8)
    kvwg3 = kvwg.rearrange("(r a p) t -> r a p t", r=8, a=2)

    es = ExitStack()
    p = Prog(nc, es)
    ixs = es.enter_context(nc.sbuf_tensor("ixs", [1, 64], mybir.dt.uint32))
    bnds = es.enter_context(nc.sbuf_tensor("bnds", [128, NCH], F32))
    bmsks = es.enter_context(nc.sbuf_tensor("bmsks", [128, NCH], F32))
    ldix = p.dma("Y", ixs[:], ix, group="ldix")
    ldbm = p.dma("Y", bmsks[:], bmask, group="ldbm")

    RG = [list(range(NCORES))]

    _dyncache = {}

    def dyn(g, k, ekey="G"):
        if (ekey, k) not in _dyncache:
            _dyncache[(ekey, k)] = g.value_load(ixs[0:1, k:k + 1])
        return _dyncache[(ekey, k)]

    def repack_qkv(coll_tok):
        first = [coll_tok, ldix]
        for a in range(2):
            def fqk(e, a=a):
                bb = dyn(e, 0, "G")
                hb = dyn(e, 1, "G")
                return e.dma_start(
                    out=qkl5[a].transpose([2, 0, 1, 3]),
                    in_=qkg5[bass.ts(bb, 4)][:, a][:, bass.ts(hb, 4)])
            p.raw_dma("G", fqk, waits=first, group="rpqk")
            first = []
        for ri in range(4):
            def fv(e, ri=ri):
                bb = dyn(e, 0, "S")
                hb = dyn(e, 1, "S")
                return e.dma_start(
                    out=vloc4[:, ri],
                    in_=vg4[bass.ts(bb, 4)][ri][bass.ts(hb, 4)])
            p.raw_dma("S", fv, waits=[coll_tok, ldix] if ri == 0 else (), group="rpv")
        return [p.gtotal("rpqk"), p.gtotal("rpv")]

    def mega_flash_loader(rp_toks):
        def loader(pair, prev_done, qe, ke, ve):
            w = list(prev_done) + list(rp_toks)
            ldq = p.dma("G", qe[0:64, :], qkl5[0, pair].rearrange("p r t -> p (r t)"),
                        waits=w, group="ldq")
            ldk = p.dma("G", ke[0:64, :], qkl5[1, pair].rearrange("p r t -> p (r t)"),
                        waits=(), group="ldk")
            ldv = p.dma("G", ve[:].rearrange("p (c e2) -> p c e2", e2=65)[:, :, 0:64],
                        vloc4[pair].rearrange("r (c p) e -> p (r c) e", p=128),
                        waits=(), group="ldv")
            return ldq, ldk, ldv
        return loader

    def mega_o_loader(coll_tok):
        def o_loader(obf):
            def fo(e):
                bb = dyn(e, 0, "G")
                so = dyn(e, 3, "G")
                return e.dma_start(
                    out=oloc.rearrange("(r s p) t -> r s p t", r=4, s=4),
                    in_=og6[so][bass.ts(bb, 4)])
            t1 = p.raw_dma("G", fo, waits=[coll_tok, ldix], group="ldo1")
            p.dma("G", obf[0:64, :].rearrange("p (c t) -> p c t", t=TPC),
                  oloc4[:, 0].transpose([1, 0, 2]), waits=[t1], group="ldo")
            tokl = p.dma("G", obf[64:128, :].rearrange("p (c t) -> p c t", t=TPC),
                         oloc4[:, 1].transpose([1, 0, 2]), waits=(), group="ldo")
            return tokl
        return o_loader

    def dense_block(lbl, x_src):
        pn = NCProxy(nc, f"dq{lbl}_")
        with ExitStack() as pes:
            emit_qkv(p, pn, pes, x_src, dw["wq" + lbl], dw["wk" + lbl], dw["wv" + lbl],
                     dw["bq" + lbl], dw["bk" + lbl], dw["bvrow" + lbl],
                     qkc[0:D, :], qkc[D:2 * D, :], vc)
            p.barrier()
        wv_ = [p.gtotal("oh0_q"), p.gtotal("oh1_q"), p.gtotal("oh0_k"),
               p.gtotal("oh1_k"), p.gtotal("vout")]
        p.collective("AllGather", [qkc], [qkg], waits=wv_)
        ct = p.collective("AllGather", [vc], [vg], waits=[])
        rp = repack_qkv(ct)
        pn = NCProxy(nc, f"df{lbl}_")
        with ExitStack() as pes:
            emit_flash(p, pn, pes, ident, tri, oc4.rearrange("(f p) t -> f p t", f=4),
                       mega_flash_loader(rp))
            p.barrier()
        ct2 = p.collective("AllGather", [oc4], [og], waits=[p.gtotal("ob0"), p.gtotal("ob1")])
        pn = NCProxy(nc, f"dp{lbl}_")
        with ExitStack() as pes:
            emit_proj_ln(p, pn, pes, None, dw["wo" + lbl], dw["bo" + lbl], x_src, xcur,
                         o_loader=mega_o_loader(ct2))
            p.barrier()

    def stage_boundary(src_dram):
        t1 = p.dma("G", bndc, src_dram.rearrange("(c p) t -> p c t", p=128)[:, :, TPC - 1],
                   group="bstg", allow_slow_non_contiguous=True)
        ct = p.collective("AllGather", [bndc], [bndg], waits=[t1])

        def fb(e):
            rk = dyn(e, 2, "S")
            return e.dma_start(out=bnds[:], in_=bndg3[rk])
        t2 = p.raw_dma("S", fb, waits=[ct, ldix], group="bld")
        t3 = p.op("V", lambda v: v.tensor_mul(bnds[:], bnds[:], bmsks[:]), [t2, ldbm])
        t4 = p.dma("G", bndloc.rearrange("(c p) o -> p c o", p=128), bnds[:].unsqueeze(2),
                   waits=[t3], group="bst2", allow_slow_non_contiguous=True)
        p.barrier()

    def lffn_layer(lbl):
        dbg = MEGA_DEBUG and lbl == "0"
        stage_boundary(xcur)
        if dbg:
            p.dma("G", bdbg, bndloc, group="xdbg", allow_slow_non_contiguous=True)
            p.barrier()
        pn = NCProxy(nc, f"lq{lbl}_")
        with ExitStack() as pes:
            emit_lffn_qkv(p, pn, pes, xcur, bndloc, lw["wqkv" + lbl], lw["qs2" + lbl],
                          lw["ks2" + lbl], tabs, E2, qlxs, kfbs[:, 512:TPC + 512],
                          vfbs[512:TPC + 512, :])
            p.barrier()
        if dbg:
            p.dma("G", qdbg, qlxs, group="xdbg")
            p.barrier()
        t1 = p.dma("G", kvwc[0:512, :], kfbs[:, TPC:TPC + 512], group="kvstg")
        t2 = p.dma("G", kvwc[512:1024, :], vfbs[TPC:TPC + 512, :], group="kvstg")
        ct = p.collective("AllGather", [kvwc], [kvwg], waits=[t1, t2])

        kes = ExitStack()
        kvtmp = kes.enter_context(nc.sbuf_tensor(f"kvtmp{lbl}", [128, 4 * 512], BF16))

        def fk(e):
            rk = dyn(e, 2, "Y")
            return e.dma_start(out=kvtmp[:].rearrange("p (a t) -> p a t", t=512),
                               in_=kvwg3[rk][0].rearrange("(a p) t -> p a t", p=128))
        def fv(e):
            rk = dyn(e, 2, "Y")
            return e.dma_start(out=vfbs[0:512, :].rearrange("(a p) t -> p a t", p=128),
                               in_=kvwg3[rk][1].rearrange("(a p) t -> p a t", p=128))
        t3 = p.raw_dma("Y", fk, waits=[ct, ldix], group="kvld")
        t4 = p.raw_dma("Y", fv, waits=(), group="kvld")
        tz = p.op("V", lambda v: v.tensor_scalar_mul(kvtmp[:], kvtmp[:], bmsks[:, 0:1]),
                  [t3, ldbm])
        p.dma("G", kfbs[:, 0:512].rearrange("(a p) t -> p a t", p=128),
              kvtmp[:].rearrange("p (a t) -> p a t", t=512), waits=[tz], group="kvld2")
        p.barrier()
        kes.close()
        if dbg:
            p.dma("G", kdbg, kfbs, group="xdbg")
            p.dma("G", vdbg, vfbs, group="xdbg")
            p.barrier()
        pn = NCProxy(nc, f"la{lbl}_")
        with ExitStack() as pes:
            emit_lffn_attn(p, pn, pes, qlxs, kfbs, vfbs, lmasks, ident, oxl)
            p.barrier()
        if dbg:
            p.dma("G", odbg, oxl, group="xdbg")
            p.barrier()
        pn = NCProxy(nc, f"lp{lbl}_")
        with ExitStack() as pes:
            emit_lffn_proj(p, pn, pes, oxl, lw["lwo" + lbl], xcur, x2s)
            p.barrier()
        if dbg:
            p.dma("G", x2dbg, x2s, group="xdbg")
            p.barrier()
        stage_boundary(x2s)
        pn = NCProxy(nc, f"lf{lbl}_")
        with ExitStack() as pes:
            emit_lffn_ffn(p, pn, pes, x2s, bndloc, lw["lg" + lbl], lw["lb" + lbl],
                          lw["w1p" + lbl], lw["b1p" + lbl], lw["w2p" + lbl],
                          lw["b2" + lbl], xcur)
            p.barrier()

    def tap(i):
        if MEGA_DEBUG:
            p.dma("G", xdbg[i], xcur, group="xdbg")
            p.barrier()

    dense_block("0", x0T)
    cp = p.dma("G", xpre, xcur, group="xpre")
    p.barrier()
    tap(0)
    lffn_layer("0")
    tap(1)
    lffn_layer("1")
    tap(2)
    pn = NCProxy(nc, "al_")
    with ExitStack() as pes:
        emit_addln(p, pn, pes, xpre, xcur, xcur)
        p.barrier()
    tap(3)
    dense_block("1", xcur)
    tap(4)
    pn = NCProxy(nc, "mo_")
    with ExitStack() as pes:
        emit_moe(p, pn, pes, xcur, rw, rbrow, upT, downT, ident, rowsel, out)
        p.barrier()
    p.emit()
    es.close()
    return nc



def _cache_build(key, fn):
    if key not in _NC_CACHE:
        _NC_CACHE[key] = fn()
    return _NC_CACHE[key]


def _fm(x):
    return np.ascontiguousarray(x.T)


def host_tri_masks():
    ks = np.arange(128)
    qs = np.arange(512)
    m = np.zeros((4, 128, 512), dtype=BF)
    for s in range(4):
        m[s] = ((128 * s + ks[:, None]) <= qs[None, :]).astype(BF)
    return m


def host_local_masks(first_window_of_seq):
    ks = np.arange(128)
    qs = np.arange(512)
    out = np.zeros((4, 8, 128, 512), dtype=BF)
    for w in range(4):
        for kc in range(2):
            for s in range(4):
                klocal = kc * 512 + s * 128 + ks[:, None]
                ql = qs[None, :]
                ok = (klocal > ql) & (klocal <= ql + 512)
                if w == 0 and first_window_of_seq:
                    ok = ok & (klocal >= 512)
                out[w, kc * 4 + s] = ok.astype(BF)
    return out.reshape(32, 128, 512)


def host_rotary_tables():
    DH, WIN = 64, 512
    half = np.arange(0, DH, 2, dtype=np.float64)
    inv_freq = 1.0 / (10000.0 ** (half / DH))
    t = np.arange(T, dtype=np.float64)
    freqs = t[:, None] * inv_freq[None, :]
    freqs = np.concatenate([freqs, freqs], axis=1)
    base = (half + 0.4 * DH) / (1.4 * DH)
    power = (t - T // 2) / (WIN // 2)
    scale = base[None, :] ** power[:, None]
    scale = np.concatenate([scale, scale], axis=1)
    cos, sin = np.cos(freqs), np.sin(freqs)
    AQ = (cos * scale).T.astype(np.float32)
    BQr = (sin * scale).T.astype(np.float32)
    AK = (cos / scale).T.astype(np.float32)
    BKr = (sin / scale).T.astype(np.float32)
    sgn = np.concatenate([-np.ones(32), np.ones(32)]).astype(np.float32)[:, None]
    return AQ, BQr * sgn, AK, BKr * sgn


def host_E2():
    e = np.zeros((128, 128), dtype=np.float32)
    e[0:64, 0:64] = 1.0
    e[64:128, 64:128] = 1.0
    return e


_IDENT = np.eye(128, dtype=np.float32)
_ROWSEL = np.zeros((8, 8 * 128), dtype=np.float32)
for _e in range(8):
    _ROWSEL[_e, _e * 128:(_e + 1) * 128] = 1.0


def run_dense_attn_block(x, Wq, bq, Wk, bk, Wv, bv, Wo, bo):
    nc1 = _cache_build("qkv", build_qkv_launch)
    in1 = []
    for c in range(NCORES):
        b, s = divmod(c, 4)
        sl = x[b, s * TPC:(s + 1) * TPC]
        in1.append(dict(xT=_fm(sl), wq=Wq, wk=Wk, wv=Wv,
                        bq=np.ascontiguousarray(bq.reshape(NCH, 128).T),
                        bk=np.ascontiguousarray(bk.reshape(NCH, 128).T),
                        bvrow=bv.reshape(1, D)))
    r1 = run_bass_kernel_spmd(nc1, in1, core_ids=list(range(NCORES)))
    QF = np.zeros((B, D, T), dtype=BF)
    KF = np.zeros((B, D, T), dtype=BF)
    VT = np.zeros((B, T, D), dtype=BF)
    for c in range(NCORES):
        b, s = divmod(c, 4)
        QF[b][:, s * TPC:(s + 1) * TPC] = r1.results[c]["qx"]
        KF[b][:, s * TPC:(s + 1) * TPC] = r1.results[c]["kx"]
        VT[b][s * TPC:(s + 1) * TPC] = r1.results[c]["vx"]
    nc2 = _cache_build("flash", build_flash_dense)
    tri = host_tri_masks()
    in2 = []
    for c in range(NCORES):
        pairs = [(c * 4 + i) for i in range(4)]
        q4 = np.stack([np.ascontiguousarray(QF[pr // 16][(pr % 16) * 64:(pr % 16) * 64 + 64]) for pr in pairs])
        k4 = np.stack([np.ascontiguousarray(KF[pr // 16][(pr % 16) * 64:(pr % 16) * 64 + 64]) for pr in pairs])
        v4 = np.stack([np.ascontiguousarray(VT[pr // 16][:, (pr % 16) * 64:(pr % 16) * 64 + 64]) for pr in pairs])
        in2.append(dict(q4=q4, k4=k4, v4=v4, ident=_IDENT, trimask=tri))
    r2 = run_bass_kernel_spmd(nc2, in2, core_ids=list(range(NCORES)))
    OF = np.zeros((B, D, T), dtype=BF)
    for c in range(NCORES):
        for i in range(4):
            pr = c * 4 + i
            OF[pr // 16][(pr % 16) * 64:(pr % 16) * 64 + 64] = r2.results[c]["o4"][i]
    nc3 = _cache_build("projln", build_proj_ln)
    in3 = []
    for c in range(NCORES):
        b, s = divmod(c, 4)
        in3.append(dict(oT=np.ascontiguousarray(OF[b][:, s * TPC:(s + 1) * TPC]),
                        wo=Wo, bo=np.ascontiguousarray(bo.reshape(NCH, 128).T),
                        xT=_fm(x[b, s * TPC:(s + 1) * TPC])))
    r3 = run_bass_kernel_spmd(nc3, in3, core_ids=list(range(NCORES)))
    xo = np.zeros((B, T, D), dtype=np.float32)
    for c in range(NCORES):
        b, s = divmod(c, 4)
        xo[b, s * TPC:(s + 1) * TPC] = r3.results[c]["xn"].T
    return xo


def run_lffn_layer(x, wqkv_l, qs_l, ks_l, wo_l, lg_l, lb_l, w1_l, b1_l, w2_l, b2_l):
    AQ, BQ, AK, BK = host_rotary_tables()
    E2 = host_E2()
    HID = 2730
    w1p = np.zeros((D, 2 * HID_P), dtype=np.float32)
    w1p[:, 0:HID] = w1_l[:, 0:HID]
    w1p[:, HID_P:HID_P + HID] = w1_l[:, HID:2 * HID]
    b1p = np.zeros(2 * HID_P, dtype=np.float32)
    b1p[0:HID] = b1_l[0:HID]
    b1p[HID_P:HID_P + HID] = b1_l[HID:2 * HID]
    w2p = np.zeros((HID_P, D), dtype=np.float32)
    w2p[0:HID] = w2_l

    nc = _cache_build("lffn_qkv", build_lffn_qkv)
    ins = []
    for c in range(NCORES):
        b, s = divmod(c, 4)
        sl = x[b, s * TPC:(s + 1) * TPC]
        xprev = (x[b, s * TPC - 1] if s > 0 else np.zeros(D, dtype=np.float32)).reshape(D, 1)
        t0 = s * TPC
        tiles = {n: np.ascontiguousarray(np.tile(tab[:, t0:t0 + TPC], (2, 1)))
                 for n, tab in (("AQ", AQ), ("BQ", BQ), ("AK", AK), ("BK", BK))}
        ins.append(dict(xT=_fm(sl), xprevT=np.ascontiguousarray(xprev), wqkv=wqkv_l,
                        qs2=np.tile(qs_l, 2).reshape(128, 1).astype(np.float32),
                        ks2=np.tile(ks_l, 2).reshape(128, 1).astype(np.float32),
                        E2=E2, **tiles))
    r1 = run_bass_kernel_spmd(nc, ins, core_ids=list(range(NCORES)))
    QF = np.zeros((B, INNER, T), dtype=BF)
    KF = np.zeros((B, INNER, T), dtype=BF)
    VT = np.zeros((B, T, INNER), dtype=BF)
    for c in range(NCORES):
        b, s = divmod(c, 4)
        QF[b][:, s * TPC:(s + 1) * TPC] = r1.results[c]["qx"]
        KF[b][:, s * TPC:(s + 1) * TPC] = r1.results[c]["kx"]
        VT[b][s * TPC:(s + 1) * TPC] = r1.results[c]["vx"]

    nc2 = _cache_build("lffn_attn", build_lffn_attn)
    ins2 = []
    for c in range(NCORES):
        b, s = divmod(c, 4)
        t0 = s * TPC
        if s > 0:
            kfb = KF[b][:, t0 - 512: t0 + TPC]
            vfb = VT[b][t0 - 512: t0 + TPC]
        else:
            kfb = np.concatenate([np.zeros((INNER, 512), dtype=BF), KF[b][:, 0:TPC]], axis=1)
            vfb = np.concatenate([np.zeros((512, INNER), dtype=BF), VT[b][0:TPC]], axis=0)
        ins2.append(dict(qx=np.ascontiguousarray(QF[b][:, t0:t0 + TPC]),
                         kfb=np.ascontiguousarray(kfb), vfb=np.ascontiguousarray(vfb),
                         masks=host_local_masks(first_window_of_seq=(s == 0)),
                         ident=_IDENT))
    r2 = run_bass_kernel_spmd(nc2, ins2, core_ids=list(range(NCORES)))

    nc3 = _cache_build("lffn_proj", build_lffn_proj)
    ins3 = []
    for c in range(NCORES):
        b, s = divmod(c, 4)
        ins3.append(dict(oT=r2.results[c]["ox"], wo=wo_l,
                         xT=_fm(x[b, s * TPC:(s + 1) * TPC])))
    r3 = run_bass_kernel_spmd(nc3, ins3, core_ids=list(range(NCORES)))
    x2 = np.zeros((B, T, D), dtype=np.float32)
    for c in range(NCORES):
        b, s = divmod(c, 4)
        x2[b, s * TPC:(s + 1) * TPC] = r3.results[c]["x2"].T

    nc4 = _cache_build("lffn_ffn", build_lffn_ffn)
    ins4 = []
    for c in range(NCORES):
        b, s = divmod(c, 4)
        xprev = (x2[b, s * TPC - 1] if s > 0 else np.zeros(D, dtype=np.float32)).reshape(D, 1)
        ins4.append(dict(x2T=_fm(x2[b, s * TPC:(s + 1) * TPC]),
                         x2prevT=np.ascontiguousarray(xprev),
                         lg=np.ascontiguousarray(lg_l.reshape(NCH, 128).T),
                         lb=np.ascontiguousarray(lb_l.reshape(NCH, 128).T),
                         w1p=w1p,
                         b1p=np.ascontiguousarray(b1p.reshape(2 * HID_P // 128, 128).T),
                         w2p=w2p,
                         b2=np.ascontiguousarray(b2_l.reshape(NCH, 128).T)))
    r4 = run_bass_kernel_spmd(nc4, ins4, core_ids=list(range(NCORES)))
    x3 = np.zeros((B, T, D), dtype=np.float32)
    for c in range(NCORES):
        b, s = divmod(c, 4)
        x3[b, s * TPC:(s + 1) * TPC] = r4.results[c]["x3"].T
    return x3


def run_moe_block(x, router_w, router_b, expert_up, expert_down):
    upTh = np.ascontiguousarray(expert_up.transpose(0, 2, 1))
    downTh = np.ascontiguousarray(expert_down.transpose(0, 2, 1))
    nc = _cache_build("moe", build_moe)
    ins = []
    for c in range(NCORES):
        b, s = divmod(c, 4)
        ins.append(dict(xT=_fm(x[b, s * TPC:(s + 1) * TPC]), rw=router_w,
                        rbrow=router_b.reshape(1, NE), upT=upTh, downT=downTh,
                        ident=_IDENT, rowsel=_ROWSEL))
    r = run_bass_kernel_spmd(nc, ins, core_ids=list(range(NCORES)))
    xo = np.zeros((B, T, D), dtype=np.float32)
    for c in range(NCORES):
        b, s = divmod(c, 4)
        xo[b, s * TPC:(s + 1) * TPC] = r.results[c]["x4"].T
    return xo


def run_addln(x, y):
    nc = _cache_build("addln", build_addln)
    ins = []
    for c in range(NCORES):
        b, s = divmod(c, 4)
        ins.append(dict(xT=_fm(x[b, s * TPC:(s + 1) * TPC]), yT=_fm(y[b, s * TPC:(s + 1) * TPC])))
    r = run_bass_kernel_spmd(nc, ins, core_ids=list(range(NCORES)))
    xo = np.zeros((B, T, D), dtype=np.float32)
    for c in range(NCORES):
        b, s = divmod(c, 4)
        xo[b, s * TPC:(s + 1) * TPC] = r.results[c]["xn"].T
    return xo


def kernel_fused(inp):
    g = {k: np.asarray(v) for k, v in inp.items()}
    x0 = np.asarray(g["L0_attn_x"], dtype=np.float32)
    wA = np.asarray(g["L1_attn_w_attn"], dtype=np.float32)
    bA = np.asarray(g["L1_attn_b_attn"], dtype=np.float32)
    AQ, BQ, AK, BK = host_rotary_tables()
    E2h = host_E2()
    HID = 2730
    trih = host_tri_masks()
    upTh = np.ascontiguousarray(np.asarray(g["L1_ffn_expert_up"], dtype=np.float32).transpose(0, 2, 1))
    downTh = np.ascontiguousarray(np.asarray(g["L1_ffn_expert_down"], dtype=np.float32).transpose(0, 2, 1))

    def colmat(v):
        return np.ascontiguousarray(np.asarray(v, dtype=np.float32).reshape(NCH, 128).T)

    lws = []
    for l in range(2):
        w1_l = np.asarray(g["L0_ffn_w1"][l], dtype=np.float32)
        b1_l = np.asarray(g["L0_ffn_b1"][l], dtype=np.float32)
        w2_l = np.asarray(g["L0_ffn_w2"][l], dtype=np.float32)
        w1p = np.zeros((D, 2 * HID_P), dtype=np.float32)
        w1p[:, 0:HID] = w1_l[:, 0:HID]
        w1p[:, HID_P:HID_P + HID] = w1_l[:, HID:2 * HID]
        b1p = np.zeros(2 * HID_P, dtype=np.float32)
        b1p[0:HID] = b1_l[0:HID]
        b1p[HID_P:HID_P + HID] = b1_l[HID:2 * HID]
        w2p = np.zeros((HID_P, D), dtype=np.float32)
        w2p[0:HID] = w2_l
        sl = str(l)
        lws.append({
            "wqkv" + sl: np.ascontiguousarray(np.asarray(g["L0_ffn_wqkv"][l], dtype=np.float32)),
            "qs2" + sl: np.tile(np.asarray(g["L0_ffn_q_scale"][l], dtype=np.float32), 2).reshape(128, 1),
            "ks2" + sl: np.tile(np.asarray(g["L0_ffn_k_scale"][l], dtype=np.float32), 2).reshape(128, 1),
            "lwo" + sl: np.ascontiguousarray(np.asarray(g["L0_ffn_wo"][l], dtype=np.float32)),
            "lg" + sl: colmat(g["L0_ffn_ln_g"][l]),
            "lb" + sl: colmat(g["L0_ffn_ln_b"][l]),
            "w1p" + sl: w1p,
            "b1p" + sl: np.ascontiguousarray(b1p.reshape(2 * HID_P // 128, 128).T),
            "w2p" + sl: w2p,
            "b2" + sl: colmat(g["L0_ffn_b2"][l]),
        })

    dense0 = dict(
        wq0=np.asarray(g["L0_attn_Wq"], dtype=np.float32), wk0=np.asarray(g["L0_attn_Wk"], dtype=np.float32),
        wv0=np.asarray(g["L0_attn_Wv"], dtype=np.float32), wo0=np.asarray(g["L0_attn_Wo"], dtype=np.float32),
        bq0=colmat(g["L0_attn_bq"]), bk0=colmat(g["L0_attn_bk"]), bo0=colmat(g["L0_attn_bo"]),
        bvrow0=np.asarray(g["L0_attn_bv"], dtype=np.float32).reshape(1, D))
    dense1 = dict(
        wq1=np.ascontiguousarray(wA[:, 0:D]), wk1=np.ascontiguousarray(wA[:, D:2 * D]),
        wv1=np.ascontiguousarray(wA[:, 2 * D:3 * D]),
        wo1=np.asarray(g["L1_attn_w_proj"], dtype=np.float32),
        bq1=colmat(bA[0:D]), bk1=colmat(bA[D:2 * D]), bo1=colmat(g["L1_attn_b_proj"]),
        bvrow1=bA[2 * D:3 * D].reshape(1, D))

    nc = _cache_build("mega", build_mega)
    ins = []
    for c in range(NCORES):
        b, s = divmod(c, 4)
        t0 = s * TPC
        tiles = {"t" + n: np.ascontiguousarray(np.tile(tab[:, t0:t0 + TPC], (2, 1)))
                 for n, tab in (("AQ", AQ), ("BQ", BQ), ("AK", AK), ("BK", BK))}
        ixv = np.zeros((1, 64), dtype=np.uint32)
        ixv[0, 0] = b
        ixv[0, 1] = s
        ixv[0, 2] = (c + 7) % 8
        ixv[0, 3] = s
        bm = np.ones((128, NCH), dtype=np.float32) if s > 0 else np.zeros((128, NCH), dtype=np.float32)
        m = dict(x0T=_fm(x0[b, t0:t0 + TPC]), **dense0, **dense1, **lws[0], **lws[1],
                 **tiles, E2=E2h, lmasks=host_local_masks(first_window_of_seq=(s == 0)),
                 tri=trih, ident=_IDENT, rowsel=_ROWSEL,
                 rw=np.asarray(g["L1_ffn_router_w"], dtype=np.float32),
                 rbrow=np.asarray(g["L1_ffn_router_b"], dtype=np.float32).reshape(1, NE),
                 upT=upTh, downT=downTh, bmask=bm, ix=ixv)
        ins.append(m)
    r = run_bass_kernel_spmd(nc, ins, core_ids=list(range(NCORES)))
    xo = np.zeros((B, T, D), dtype=np.float32)
    for c in range(NCORES):
        b, s = divmod(c, 4)
        xo[b, s * TPC:(s + 1) * TPC] = r.results[c]["out"].T
    if MEGA_DEBUG:
        global MEGA_TAPS, MEGA_RAW
        MEGA_TAPS = np.zeros((5, B, T, D), dtype=np.float32)
        for c in range(NCORES):
            b, s = divmod(c, 4)
            for i in range(5):
                MEGA_TAPS[i, b, s * TPC:(s + 1) * TPC] = r.results[c]["xdbg"][i].T
        MEGA_RAW = r.results
    return xo


def kernel(L0_attn_x, L0_attn_Wq, L0_attn_bq, L0_attn_Wk, L0_attn_bk, L0_attn_Wv,
           L0_attn_bv, L0_attn_Wo, L0_attn_bo, L0_ffn_wqkv, L0_ffn_q_scale,
           L0_ffn_k_scale, L0_ffn_wo, L0_ffn_ln_g, L0_ffn_ln_b, L0_ffn_w1, L0_ffn_b1,
           L0_ffn_w2, L0_ffn_b2, L1_attn_w_attn, L1_attn_b_attn, L1_attn_w_proj,
           L1_attn_b_proj, L1_ffn_router_w, L1_ffn_router_b, L1_ffn_expert_up,
           L1_ffn_expert_down):
    g = {k: np.ascontiguousarray(np.asarray(v), dtype=None) for k, v in list(locals().items())
         if k != "g"}
    return kernel_fused(g)
    x = np.asarray(g["L0_attn_x"], dtype=np.float32)
    x = run_dense_attn_block(x, g["L0_attn_Wq"], g["L0_attn_bq"], g["L0_attn_Wk"],
                             g["L0_attn_bk"], g["L0_attn_Wv"], g["L0_attn_bv"],
                             g["L0_attn_Wo"], g["L0_attn_bo"])
    y = x
    for l in range(2):
        y = run_lffn_layer(y, g["L0_ffn_wqkv"][l], g["L0_ffn_q_scale"][l],
                           g["L0_ffn_k_scale"][l], g["L0_ffn_wo"][l], g["L0_ffn_ln_g"][l],
                           g["L0_ffn_ln_b"][l], g["L0_ffn_w1"][l], g["L0_ffn_b1"][l],
                           g["L0_ffn_w2"][l], g["L0_ffn_b2"][l])
    x = run_addln(x, y)
    wA = np.asarray(g["L1_attn_w_attn"], dtype=np.float32)
    bA = np.asarray(g["L1_attn_b_attn"], dtype=np.float32)
    x = run_dense_attn_block(x, np.ascontiguousarray(wA[:, 0:D]), bA[0:D],
                             np.ascontiguousarray(wA[:, D:2 * D]), bA[D:2 * D],
                             np.ascontiguousarray(wA[:, 2 * D:3 * D]), bA[2 * D:3 * D],
                             g["L1_attn_w_proj"], g["L1_attn_b_proj"])
    x = run_moe_block(x, g["L1_ffn_router_w"], g["L1_ffn_router_b"],
                      np.asarray(g["L1_ffn_expert_up"], dtype=np.float32),
                      np.asarray(g["L1_ffn_expert_down"], dtype=np.float32))
    return x


# version 04
# speedup vs baseline: 1.1059x; 1.1059x over previous
import numpy as np
import ml_dtypes
from contextlib import ExitStack

import concourse.bass as bass
import concourse.mybir as mybir
from concourse.bass_utils import run_bass_kernel_spmd

F32 = mybir.dt.float32
BF16 = mybir.dt.bfloat16
AF = mybir.ActivationFunctionType
ALU = mybir.AluOpType
AX = mybir.AxisListType
BF = ml_dtypes.bfloat16

NCORES = 8
B, T, D = 2, 8192, 1024
TPC = 2048
NCH = D // 128
NTT = TPC // 512

_NC_CACHE = {}


class Prog:

    ENGS = ("T", "V", "S", "G", "Y")

    def __init__(self, nc, es):
        self.nc = nc
        self.es = es
        self.q = {e: [] for e in self.ENGS}
        self.sem = {e: es.enter_context(nc.semaphore(f"sem{e}")) for e in self.ENGS}
        self.cnt = {e: 0 for e in self.ENGS}
        self.gsem = {}

    def op(self, eng, fn, waits=(), inc=True):
        waits = [w for w in waits if w is not None]
        tok = None
        if inc:
            self.cnt[eng] += 1
            tok = (eng, self.cnt[eng])
        self.q[eng].append((fn, tuple(waits), tok))
        return tok

    def dma(self, eng, out, in_, waits=(), group=None, **kw):
        assert group is not None
        waits = [w for w in waits if w is not None]
        if group not in self.gsem:
            self.gsem[group] = [self.es.enter_context(self.nc.semaphore(f"g_{group}")), 0]
        ent = self.gsem[group]
        ent[1] += 16
        tok = ("g:" + group, ent[1])
        sem = ent[0]

        def fn(e, out=out, in_=in_, kw=kw):
            return e.dma_start(out=out, in_=in_, **kw).then_inc(sem, 16)

        self.q[eng].append((fn, tuple(waits), None))
        return tok

    def collective(self, kind, ins, outs, waits=()):
        waits = [w for w in waits if w is not None]
        self.cnt["G"] += 1
        tok = ("G", self.cnt["G"])
        sem = self.sem["G"]

        def fn(e, ins=ins, outs=outs):
            return e.collective_compute(
                kind, ALU.bypass, replica_groups=[list(range(NCORES))],
                ins=ins, outs=outs).then_inc(sem, 1)

        self.q["G"].append((fn, tuple(waits), None))
        return tok

    def raw_dma(self, eng, fn, waits=(), group=None):
        assert group is not None
        waits = [w for w in waits if w is not None]
        if group not in self.gsem:
            self.gsem[group] = [self.es.enter_context(self.nc.semaphore(f"g_{group}")), 0]
        ent = self.gsem[group]
        ent[1] += 16
        tok = ("g:" + group, ent[1])
        sem = ent[0]

        def fn2(e, fn=fn, sem=sem):
            return fn(e).then_inc(sem, 16)

        self.q[eng].append((fn2, tuple(waits), None))
        return tok

    def gtotal(self, group):
        return ("g:" + group, self.gsem[group][1])

    def barrier(self):
        waits = tuple(("" + e, self.cnt[e]) for e in self.ENGS if self.cnt[e] > 0)
        gwaits = tuple(("g:" + g, ent[1]) for g, ent in self.gsem.items())
        for e in self.ENGS:
            self.q[e].append(((lambda engine: None), waits + gwaits, None))

    def _resolve(self, t):
        name, v = t
        if name.startswith("g:"):
            return self.gsem[name[2:]][0], v
        return self.sem[name], v

    def emit(self):
        for gname, (sem, cnt) in self.gsem.items():
            self.q["Y"].append(((lambda e: None), (("g:" + gname, cnt),), None))
        block = self.es.enter_context(self.nc.Block())

        def body(engine, key):
            for fn, waits, tok in self.q[key]:
                for t in waits:
                    s, v = self._resolve(t)
                    engine.wait_ge(s, v)
                ins = fn(engine)
                if tok is not None:
                    ins.then_inc(self.sem[tok[0]], 1)

        block.tensor(lambda e: body(e, "T"))
        block.vector(lambda e: body(e, "V"))
        block.scalar(lambda e: body(e, "S"))
        block.gpsimd(lambda e: body(e, "G"))
        block.sync(lambda e: body(e, "Y"))


class NCProxy:

    def __init__(self, nc, tag):
        self._nc = nc
        self._tag = tag

    def sbuf_tensor(self, name, shape, dtype, **kw):
        return self._nc.sbuf_tensor(self._tag + name, shape, dtype, **kw)

    def psum_tensor(self, name, shape, dtype=F32, **kw):
        return self._nc.psum_tensor(self._tag + name, shape, dtype, **kw)

    def __getattr__(self, k):
        return getattr(self._nc, k)



def alloc_ln_bufs(nc, es, tag="", reuse=None):
    b = {}
    b["ln_sq"] = es.enter_context(nc.sbuf_tensor(f"lnsq{tag}", [128, 512], F32))
    b["ln_stat"] = es.enter_context(nc.sbuf_tensor(f"lnst{tag}", [1, 1024], F32))
    b["ln_tmp"] = es.enter_context(nc.sbuf_tensor(f"lntmp{tag}", [128, 512], F32))
    b["ones_col"] = es.enter_context(nc.sbuf_tensor(f"lnoc{tag}", [128, 1], F32))
    b["ones1"] = es.enter_context(nc.sbuf_tensor(f"lno1{tag}", [1, 128], F32))
    if reuse is None:
        b["ln_psS"] = es.enter_context(nc.psum_tensor(f"lnpsS{tag}", [1, 512], F32))[:]
        b["ln_psS2"] = es.enter_context(nc.psum_tensor(f"lnpsS2{tag}", [1, 512], F32))[:]
        b["ln_psB"] = es.enter_context(nc.psum_tensor(f"lnpsB{tag}", [128, 512], F32))[:]
        b["ln_psB2"] = es.enter_context(nc.psum_tensor(f"lnpsB2{tag}", [128, 512], F32))[:]
    else:
        b["ln_psS"] = reuse["psS"]
        b["ln_psS2"] = reuse["psS2"]
        b["ln_psB"] = reuse["psB"]
        b["ln_psB2"] = reuse["psB2"]
    return b


def emit_ln_init(p, b):
    t1 = p.op("V", lambda v: v.memset(b["ones_col"][:], 1.0))
    t2 = p.op("V", lambda v: v.memset(b["ones1"][:], 1.0))
    return [t1, t2]


def emit_ln(p, b, x_ap_fn, n_tok, out_fn, gamma_scale=None, gamma_bias=None,
            eps=1e-5, pre_waits=(), extra_out_fn=None, init_toks=()):
    sq, stat, tmp = b["ln_sq"], b["ln_stat"], b["ln_tmp"]
    psS, psS2, psB, psB2 = b["ln_psS"], b["ln_psS2"], b["ln_psB"], b["ln_psB2"]
    ones_col, ones1 = b["ones_col"], b["ones1"]
    n_tt = n_tok // 512
    out_toks = {}
    prev_apply = []
    prev_stats = []
    last = None
    for tt in range(n_tt):
        mu = stat[:, 0:512]
        va = stat[:, 512:1024]
        s_tok = None
        for c in range(NCH):
            w = list(pre_waits) + list(init_toks) + prev_stats if c == 0 else []
            s_tok = p.op("T", (lambda t, c=c, tt=tt: t.matmul(
                psS[:], ones_col[:], x_ap_fn(c, tt), start=(c == 0), stop=(c == NCH - 1))), w)
        prev_mt = None
        for c in range(NCH):
            w = (list(pre_waits) + prev_stats) if c == 0 else [prev_mt]
            st = p.op("S", (lambda s, c=c, tt=tt: s.activation(sq[:], x_ap_fn(c, tt), AF.Square)), w)
            prev_mt = p.op("T", (lambda t, c=c: t.matmul(
                psS2[:], ones_col[:], sq[:], start=(c == 0), stop=(c == NCH - 1))), [st])
        t1 = p.op("V", (lambda v, mu=mu: v.tensor_scalar_mul(mu, psS, 1.0 / D)), [s_tok])
        t2 = p.op("V", (lambda v, va=va: v.tensor_scalar_mul(va, psS2, 1.0 / D)), [prev_mt])
        t3 = p.op("V", (lambda v, mu=mu: v.tensor_mul(sq[0:1, 0:512], mu, mu)), [t1, t2])
        t4 = p.op("V", (lambda v, va=va: v.tensor_sub(va, va, sq[0:1, 0:512])), [t3])
        t4b = p.op("V", (lambda v, va=va: v.tensor_scalar_add(va, va, float(eps))), [t4])
        t5 = p.op("S", (lambda s, va=va: s.activation(va, va, AF.Sqrt)), [t4b])
        t6 = p.op("V", (lambda v, va=va: v.reciprocal(va, va)), [t5])
        b1 = p.op("T", (lambda t, mu=mu: t.matmul(psB, ones1[:], mu, start=True, stop=True)),
                  [t6] + prev_apply)
        b2 = p.op("T", (lambda t, va=va: t.matmul(psB2, ones1[:], va, start=True, stop=True)), [b1])
        prev_stats = [t3, t6]
        prev_apply = []
        for c in range(NCH):
            tA = p.op("V", (lambda v, c=c, tt=tt: v.tensor_sub(tmp[:], x_ap_fn(c, tt), psB)),
                      [b2] + ([prev_apply[-1]] if prev_apply else []))
            if gamma_scale is None:
                tB = p.op("V", (lambda v, c=c, tt=tt: v.tensor_mul(out_fn(c, tt), tmp[:], psB2)), [tA])
            else:
                tB0 = p.op("V", (lambda v: v.tensor_mul(tmp[:], tmp[:], psB2)), [tA])
                tB = p.op("V", (lambda v, c=c, tt=tt: v.tensor_scalar(
                    out_fn(c, tt), tmp[:], gamma_scale(c), gamma_bias(c), ALU.mult, ALU.add)), [tB0])
            if extra_out_fn is not None:
                tB = p.op("S", (lambda s, c=c, tt=tt: s.activation(
                    extra_out_fn(c, tt), out_fn(c, tt), AF.Copy)), [tB])
            out_toks[(c, tt)] = tB
            prev_apply.append(tB)
            last = tB
        prev_stats = prev_stats + [b1, b2]
    return out_toks, last



def emit_qkv(p, nc, es, xT, wq, wk, wv, bq, bk, bvrow, qx, kx, vx, tag=""):
    xbf = es.enter_context(nc.sbuf_tensor("xbf", [128, NCH * TPC], BF16))
    wqb = es.enter_context(nc.sbuf_tensor("wqb", [128, NCH * D], BF16))
    wkb = es.enter_context(nc.sbuf_tensor("wkb", [128, NCH * D], BF16))
    wvb = es.enter_context(nc.sbuf_tensor("wvb", [128, NCH * D], BF16))
    bqs = es.enter_context(nc.sbuf_tensor("bqs", [128, NCH], F32))
    bks = es.enter_context(nc.sbuf_tensor("bks", [128, NCH], F32))
    bvr = es.enter_context(nc.sbuf_tensor("bvr", [1, D], F32))
    bvb = es.enter_context(nc.sbuf_tensor("bvb", [128, D], F32))
    ones1 = es.enter_context(nc.sbuf_tensor("ones1", [1, 128], F32))
    osb = es.enter_context(nc.sbuf_tensor("osb", [128, 2 * TPC], BF16))
    vsb = es.enter_context(nc.sbuf_tensor("vsb", [128, 2 * D], BF16))
    ps = [es.enter_context(nc.psum_tensor(f"ps{i}", [128, 512], F32)) for i in range(2)]
    psv = [es.enter_context(nc.psum_tensor(f"psv{i}", [128, 512], F32)) for i in range(2)]
    psb = es.enter_context(nc.psum_tensor("psb", [128, 512], F32))

    ldx = p.dma("G", xbf[:].rearrange("p (c t) -> p c t", t=TPC),
                xT.rearrange("(c p) t -> p c t", p=128), group="ldx")
    ldwq = p.dma("G", wqb[:].rearrange("p (c o) -> p c o", o=D), wq.rearrange("(c p) o -> p c o", p=128), group="wq")
    ldwk = p.dma("G", wkb[:].rearrange("p (c o) -> p c o", o=D), wk.rearrange("(c p) o -> p c o", p=128), group="wk")
    ldwv = p.dma("G", wvb[:].rearrange("p (c o) -> p c o", o=D), wv.rearrange("(c p) o -> p c o", p=128), group="wv")
    ldbq = p.dma("Y", bqs[:], bq, group="bq")
    ldbk = p.dma("Y", bks[:], bk, group="bk")
    ldbv = p.dma("Y", bvr[:], bvrow, group="bv")
    onet = p.op("V", lambda v: v.memset(ones1[:], 1.0))
    prevc = onet
    bvb_toks = []
    for h in range(2):
        tb = p.op("T", (lambda t, h=h: t.matmul(psb[:], ones1[:], bvr[:, h * 512:(h + 1) * 512],
                                                start=True, stop=True)), [prevc, ldbv])
        prevc = p.op("V", (lambda v, h=h: v.tensor_copy(bvb[:, h * 512:(h + 1) * 512], psb[:])), [tb])
        bvb_toks.append(prevc)

    bank_free = [None, None]
    half_free = [None, None]
    for name, wb, bsbuf, ldwt, ldbt, out_dram in (
            ("q", wqb, bqs, ldwq, ldbq, qx), ("k", wkb, bks, ldwk, ldbk, kx)):
        for oc in range(NCH):
            half = oc % 2
            st_toks = []
            for tt in range(NTT):
                bi = tt % 2
                pt = ps[bi]
                w0 = [ldx, ldwt, ldbt, bank_free[bi], half_free[half]]
                mm = None
                for ic in range(NCH):
                    mm = p.op("T", (lambda t, ic=ic, oc=oc, tt=tt, pt=pt, wb=wb: t.matmul(
                        pt[:], wb[:, ic * D + oc * 128: ic * D + oc * 128 + 128],
                        xbf[:, ic * TPC + tt * 512: ic * TPC + (tt + 1) * 512],
                        start=(ic == 0), stop=(ic == NCH - 1))), w0 if ic == 0 else [])
                    w0 = []
                ev = p.op("V", (lambda v, tt=tt, pt=pt, oc=oc, half=half, bsbuf=bsbuf: v.tensor_scalar_add(
                    osb[:, half * TPC + tt * 512: half * TPC + (tt + 1) * 512], pt[:],
                    bsbuf[:, oc:oc + 1])), [mm])
                bank_free[bi] = ev
                st_toks.append(ev)
            dt = p.dma("G", out_dram.rearrange("(c p) t -> p c t", p=128)[:, oc, :],
                       osb[:, half * TPC: half * TPC + TPC], waits=st_toks,
                       group=f"oh{half}_{name}")
            half_free[half] = dt

    vbank_free = [None, None]
    vhalf_free = None
    for tc in range(TPC // 128):
        st_toks = []
        for half in range(2):
            pt = psv[half]
            w0 = [ldx, ldwv, vbank_free[half], vhalf_free]
            mm = None
            for ic in range(NCH):
                mm = p.op("T", (lambda t, ic=ic, tc=tc, half=half, pt=pt: t.matmul(
                    pt[:], xbf[:, ic * TPC + tc * 128: ic * TPC + (tc + 1) * 128],
                    wvb[:, ic * D + half * 512: ic * D + (half + 1) * 512],
                    start=(ic == 0), stop=(ic == NCH - 1))), w0 if ic == 0 else [])
                w0 = []
            ev = p.op("V", (lambda v, half=half, pt=pt: v.tensor_add(
                vsb[:, half * D // 2 + 0: 0] if False else vsb[:, half * 512:(half + 1) * 512],
                pt[:], bvb[:, half * 512:(half + 1) * 512])), [mm] + bvb_toks)
            vbank_free[half] = ev
            st_toks.append(ev)
        vhalf_free = p.dma("G", vx[tc * 128:(tc + 1) * 128, :], vsb[:, 0:D], waits=st_toks,
                           group="vout")

    return


def build_qkv_launch():
    nc = bass.Bass("TRN2", target_bir_lowering=False)
    names_i = [("xT", [D, TPC], F32), ("wq", [D, D], F32), ("wk", [D, D], F32),
               ("wv", [D, D], F32), ("bq", [128, NCH], F32), ("bk", [128, NCH], F32),
               ("bvrow", [1, D], F32)]
    names_o = [("qx", [D, TPC], BF16), ("kx", [D, TPC], BF16), ("vx", [TPC, D], BF16)]
    aps = {n: nc.dram_tensor(n, s, d, kind="ExternalInput").ap() for n, s, d in names_i}
    aps.update({n: nc.dram_tensor(n, s, d, kind="ExternalOutput").ap() for n, s, d in names_o})
    es = ExitStack()
    p = Prog(nc, es)
    emit_qkv(p, nc, es, **aps)
    p.emit()
    es.close()
    return nc



def emit_flash(p, nc, es, ident, trimask, o4, loader):
    SC = 0.125
    qe = es.enter_context(nc.sbuf_tensor("qe", [65, T], BF16))
    ke = es.enter_context(nc.sbuf_tensor("ke", [65, T], BF16))
    ve = es.enter_context(nc.sbuf_tensor("ve", [128, 65 * (T // 128)], BF16))
    idn = es.enter_context(nc.sbuf_tensor("idn", [128, 128], F32))
    tri = es.enter_context(nc.sbuf_tensor("tri", [128, 4 * 512], BF16))
    mneg = es.enter_context(nc.sbuf_tensor("mneg", [128, 4], F32))
    mtmp = es.enter_context(nc.sbuf_tensor("mtmp", [128, 2], F32))
    pt_sb = es.enter_context(nc.sbuf_tensor("pt_sb", [128, 2048], BF16))
    rb = es.enter_context(nc.sbuf_tensor("rb", [64, 512], F32))
    rec = es.enter_context(nc.sbuf_tensor("rec", [1, 512], F32))
    ones1 = es.enter_context(nc.sbuf_tensor("ones1", [1, 64], F32))
    ob = es.enter_context(nc.sbuf_tensor("ob", [64, 2 * 512], BF16))
    psA = [es.enter_context(nc.psum_tensor(f"psA{i}", [128, 512], F32)) for i in range(2)]
    psM = es.enter_context(nc.psum_tensor("psM", [1, 128], F32))
    psT = es.enter_context(nc.psum_tensor("psT", [128, 2048], F32))
    psO = es.enter_context(nc.psum_tensor("psO", [65, 512], F32))
    psR = psT[0:64, 0:512]
    p_ = p

    ldi = p.dma("Y", idn[:], ident, group="ldi")
    ldt = p.dma("Y", tri[:].rearrange("p (s n) -> p s n", n=512), trimask.rearrange("s p n -> p s n"),
                group="ldt")
    one1 = p.op("V", lambda v: v.memset(ones1[:], 1.0))
    onek = p.op("V", lambda v: v.memset(ke[64:65, :], 1.0))
    onev = p.op("V", lambda v: v.memset(ve[:].rearrange("p (c e) -> p c e", e=65)[:, :, 64], 1.0))

    NQG = T // 512
    prev_pair_done = [onek, onev]
    prev_cpm = None
    ob_last = [None, None]
    pending_norm = [None]
    last_st0 = [None]

    for pair in range(4):
        ldq, ldk, ldv = loader(pair, prev_pair_done, qe, ke, ve)
        prev_avm = [None]
        prev_red = [None, None]
        for g in range(NQG):
            nkc = g + 1
            tr_toks = []
            for qt in range(4):
                qsl = qe[0:64, g * 512 + qt * 128: g * 512 + qt * 128 + 128]
                red_prev = None
                for kc in range(nkc):
                    bi = kc % 2
                    w = [ldq, ldk] if (qt == 0 and kc == 0) else []
                    w = w + [prev_red[bi]]
                    mm = p.op("T", (lambda t, qsl=qsl, kc=kc, bi=bi: t.matmul(
                        psA[bi][:], qsl, ke[0:64, kc * 512:(kc + 1) * 512],
                        start=True, stop=True)), w)
                    if kc == 0:
                        red_prev = p.op("V", (lambda v, qt=qt, bi=bi: v.reduce_max(
                            mneg[:, qt:qt + 1], psA[bi][:], axis=AX.X, negate=True)), [mm])
                    else:
                        r1 = p.op("V", (lambda v, bi=bi: v.reduce_max(
                            mtmp[:, bi:bi + 1], psA[bi][:], axis=AX.X, negate=True)), [mm])
                        red_prev = p.op("V", (lambda v, qt=qt, bi=bi: v.tensor_tensor(
                            mneg[:, qt:qt + 1], mneg[:, qt:qt + 1], mtmp[:, bi:bi + 1],
                            ALU.min)), [r1])
                    prev_red[bi] = red_prev
                trm = p.op("T", (lambda t, qt=qt: t.transpose(psM[:], mneg[:, qt:qt + 1], idn[:])),
                           [red_prev, ldi, prev_cpm])
                cpm = p.op("V", (lambda v, g=g, qt=qt: v.tensor_copy(
                    qe[64:65, g * 512 + qt * 128: g * 512 + qt * 128 + 128], psM[:])), [trm])
                prev_cpm = cpm
                tr_toks.append(cpm)
            if pending_norm[0] is not None:
                pending_norm[0]()
                pending_norm[0] = None
            qg = qe[:, g * 512:(g + 1) * 512]
            first_st_waits = tr_toks + [prev_avm[0]]
            ex_prev = [None, None]
            avm = None
            for kc in range(nkc):
                for hf in range(2):
                    mm2 = None
                    for s2 in range(2):
                        s = hf * 2 + s2
                        w = list(first_st_waits) + [ex_prev[hf]]
                        if s == 0:
                            w = w + [last_st0[0]]
                        first_st_waits = []
                        mm2 = p.op("T", (lambda t, kc=kc, s=s, qg=qg: t.matmul(
                            psT[:, s * 512:(s + 1) * 512],
                            ke[:, kc * 512 + s * 128: kc * 512 + s * 128 + 128], qg,
                            start=True, stop=True)), w)
                        if s == 0:
                            last_st0[0] = mm2
                    ex = p.op("S", (lambda s_, hf=hf: s_.activation(
                        pt_sb[:, hf * 1024:(hf + 1) * 1024],
                        psT[:, hf * 1024:(hf + 1) * 1024], AF.Exp, scale=SC)),
                        [mm2, ("avdep", None)] if False else [mm2, prev_avm[0]])
                    ex_prev[hf] = ex
                    dep = [ex]
                    if kc == g:
                        for s2 in range(2):
                            s = hf * 2 + s2
                            dep = [p.op("V", (lambda v, s=s: v.tensor_mul(
                                pt_sb[:, s * 512:(s + 1) * 512], pt_sb[:, s * 512:(s + 1) * 512],
                                tri[:, s * 512:(s + 1) * 512])), dep + ([ldt] if s2 == 0 else []))]
                    for s2 in range(2):
                        s = hf * 2 + s2
                        w = (dep + [ldv]) if s2 == 0 else []
                        avm = p.op("T", (lambda t, kc=kc, s=s, g=g: t.matmul(
                            psO[:], ve[:, (kc * 4 + s) * 65:(kc * 4 + s) * 65 + 65],
                            pt_sb[:, s * 512:(s + 1) * 512],
                            start=(kc == 0 and s == 0), stop=(kc == nkc - 1 and s == 3))), w)
            prev_avm[0] = avm

            def make_norm(g=g, pair=pair, avm=avm):
                def donorm():
                    rc = p_.op("V", (lambda v: v.reciprocal(rec[:], psO[64:65, :])), [avm])
                    bc = p_.op("T", (lambda t: t.matmul(psR, ones1[:], rec[:],
                                                        start=True, stop=True)),
                               [rc, one1, last_st0[0], ex_prev[0]])
                    last_st0[0] = None
                    cb = p_.op("V", (lambda v: v.tensor_copy(rb[:], psR)), [bc])
                    ot = p_.op("V", (lambda v, g=g: v.tensor_mul(
                        ob[:, (g % 2) * 512:(g % 2) * 512 + 512], psO[0:64, :], rb[:])),
                        [cb, ob_last[g % 2]])
                    ob_last[g % 2] = p_.dma("G", o4[pair][:, g * 512:(g + 1) * 512],
                                            ob[:, (g % 2) * 512:(g % 2) * 512 + 512],
                                            waits=[ot], group=f"ob{g % 2}")
                    prev_avm[0] = ot
                    last_st0[0] = cb
                return donorm
            pending_norm[0] = make_norm()
        pending_norm[0]()
        pending_norm[0] = None
        prev_pair_done = [prev_avm[0], ob_last[0], ob_last[1], prev_cpm]
    return


def default_flash_loader(p, q4, k4, v4):
    def loader(pair, prev_done, qe, ke, ve):
        ldq = p.dma("G", qe[0:64, :], q4[pair], waits=prev_done, group="ldq")
        ldk = p.dma("G", ke[0:64, :], k4[pair], waits=prev_done, group="ldk")
        ldv = p.dma("G", ve[:].rearrange("p (c e) -> p c e", e=65)[:, :, 0:64],
                    v4[pair].rearrange("(c p) d -> p c d", p=128), waits=prev_done,
                    group="ldv")
        return ldq, ldk, ldv
    return loader


def build_flash_dense():
    nc = bass.Bass("TRN2", target_bir_lowering=False)
    q4 = nc.dram_tensor("q4", [4, 64, T], BF16, kind="ExternalInput").ap()
    k4 = nc.dram_tensor("k4", [4, 64, T], BF16, kind="ExternalInput").ap()
    v4 = nc.dram_tensor("v4", [4, T, 64], BF16, kind="ExternalInput").ap()
    ident = nc.dram_tensor("ident", [128, 128], F32, kind="ExternalInput").ap()
    trimask = nc.dram_tensor("trimask", [4, 128, 512], BF16, kind="ExternalInput").ap()
    o4 = nc.dram_tensor("o4", [4, 64, T], BF16, kind="ExternalOutput").ap()
    es = ExitStack()
    p = Prog(nc, es)
    emit_flash(p, nc, es, ident, trimask, o4, default_flash_loader(p, q4, k4, v4))
    p.emit()
    es.close()
    return nc



def emit_proj_ln(p, nc, es, oT, wo, bo, xT, xn, o_loader=None):
    obf = es.enter_context(nc.sbuf_tensor("obf", [128, NCH * TPC], BF16))
    wob = es.enter_context(nc.sbuf_tensor("wob", [128, NCH * D], BF16))
    bos = es.enter_context(nc.sbuf_tensor("bos", [128, NCH], F32))
    xr = es.enter_context(nc.sbuf_tensor("xr", [128, NCH * TPC], F32))
    xo = es.enter_context(nc.sbuf_tensor("xo", [128, 512], F32))
    ps = [es.enter_context(nc.psum_tensor(f"ps{i}", [128, 512], F32)) for i in range(2)]
    lnb = alloc_ln_bufs(nc, es)

    if o_loader is None:
        ldo = p.dma("G", obf[:].rearrange("p (c t) -> p c t", t=TPC),
                    oT.rearrange("(c p) t -> p c t", p=128), group="ldo")
    else:
        ldo = o_loader(obf)
    ldw = p.dma("G", wob[:].rearrange("p (c o) -> p c o", o=D), wo.rearrange("(c p) o -> p c o", p=128), group="ldw")
    ldb = p.dma("Y", bos[:], bo, group="ldb")
    ldxr = p.dma("G", xr[:].rearrange("p (c t) -> p c t", t=TPC),
                 xT.rearrange("(c p) t -> p c t", p=128), group="ldxr")
    init = emit_ln_init(p, lnb)

    bank_free = [None, None]
    res_toks = []
    for oc in range(NCH):
        for tt in range(NTT):
            bi = (oc * NTT + tt) % 2
            pt = ps[bi]
            w0 = [ldo, ldw, ldb, ldxr, bank_free[bi]]
            mm = None
            for ic in range(NCH):
                mm = p.op("T", (lambda t, ic=ic, oc=oc, tt=tt, pt=pt: t.matmul(
                    pt[:], wob[:, ic * D + oc * 128: ic * D + oc * 128 + 128],
                    obf[:, ic * TPC + tt * 512: ic * TPC + (tt + 1) * 512],
                    start=(ic == 0), stop=(ic == NCH - 1))), w0 if ic == 0 else [])
                w0 = []
            ev = p.op("V", (lambda v, pt=pt, oc=oc: v.tensor_scalar_add(
                xo[:], pt[:], bos[:, oc:oc + 1])), [mm])
            bank_free[bi] = ev
            ad = p.op("V", (lambda v, oc=oc, tt=tt: v.tensor_add(
                xr[:, oc * TPC + tt * 512: oc * TPC + (tt + 1) * 512],
                xr[:, oc * TPC + tt * 512: oc * TPC + (tt + 1) * 512], xo[:])), [ev])
            bank_free[bi] = ad
            res_toks.append(ad)

    def x_ap(c, tt):
        return xr[:, c * TPC + tt * 512: c * TPC + (tt + 1) * 512]

    toks, last = emit_ln(p, lnb, x_ap, TPC, out_fn=x_ap, pre_waits=res_toks, init_toks=init)
    for c in range(NCH):
        for tt in range(NTT):
            p.dma("G", xn.rearrange("(c p) t -> p c t", p=128)[:, c, tt * 512:(tt + 1) * 512],
                  x_ap(c, tt), waits=[toks[(c, tt)]], group="xout")
    return



INNER = 512
NQC = 4


def emit_lffn_qkv(p, nc, es, xT, xprevT, wqkv, qs2, ks2, tabs, E2, qx, kx, vx):
    W = 2049
    xbf = es.enter_context(nc.sbuf_tensor("xbf", [128, NCH * W], BF16))
    wb = es.enter_context(nc.sbuf_tensor("wb", [128, NCH * 3 * INNER], BF16))
    qss = es.enter_context(nc.sbuf_tensor("qss", [128, 1], F32))
    kss = es.enter_context(nc.sbuf_tensor("kss", [128, 1], F32))
    tb = {n: es.enter_context(nc.sbuf_tensor("t" + n, [128, TPC], F32)) for n in ("AQ", "BQ", "AK", "BK")}
    e2s = es.enter_context(nc.sbuf_tensor("e2s", [128, 128], F32))
    sqb = es.enter_context(nc.sbuf_tensor("sqb", [128, 512], F32))
    rstd = es.enter_context(nc.sbuf_tensor("rstd", [128, 512], F32))
    qsq = es.enter_context(nc.sbuf_tensor("qsq", [128, 512], F32))
    qrot = es.enter_context(nc.sbuf_tensor("qrot", [128, 512], F32))
    u = es.enter_context(nc.sbuf_tensor("u", [128, 512], F32))
    w_ = es.enter_context(nc.sbuf_tensor("w_", [128, 512], F32))
    osb = es.enter_context(nc.sbuf_tensor("osb", [128, 2 * TPC], BF16))
    vsb = es.enter_context(nc.sbuf_tensor("vsb", [128, 512], BF16))
    pQ = [es.enter_context(nc.psum_tensor(f"pQ{i}", [128, 512], F32)) for i in range(2)]
    pS = es.enter_context(nc.psum_tensor("pS", [128, 512], F32))
    pV = [es.enter_context(nc.psum_tensor(f"pV{i}", [128, 512], F32)) for i in range(2)]

    ldx = p.dma("G", xbf[:].rearrange("p (c t) -> p c t", t=W)[:, :, 1:W],
                xT.rearrange("(c p) t -> p c t", p=128), group="ldx")
    ldxp = p.dma("G", xbf[:].rearrange("p (c t) -> p c t", t=W)[:, :, 0:1],
                 xprevT.rearrange("(c p) t -> p c t", p=128), allow_slow_non_contiguous=True,
                 group="ldxp")
    ldw = p.dma("G", wb[:].rearrange("p (c o) -> p c o", o=3 * INNER),
                wqkv.rearrange("(c p) o -> p c o", p=128), group="ldw")
    ldqs = p.dma("Y", qss[:], qs2, group="ldqs")
    ldks = p.dma("Y", kss[:], ks2, group="ldks")
    ldtb = [p.dma("Y", tb[n][:], tabs[n], group="t" + n) for n in ("AQ", "BQ", "AK", "BK")]
    lde2 = p.dma("Y", e2s[:], E2, group="lde2")
    lds = [ldx, ldxp, ldw, ldqs, ldks, lde2] + ldtb

    def h_ap(ic, t0, n):
        off = 0 if ic >= 4 else 1
        return xbf[:, ic * W + off + t0: ic * W + off + t0 + n]

    Q0, Q1, Q2, Q3 = (slice(0, 32), slice(32, 64), slice(64, 96), slice(96, 128))
    QPART = [(Q0, Q1), (Q1, Q0), (Q2, Q3), (Q3, Q2)]

    bank_free = [None, None]
    half_free = [None, None]
    prev_t3 = None
    for name, qkoff, ssb, A, Bt, out_dram in (
            ("q", 0, qss, "AQ", "BQ", qx), ("k", 4, kss, "AK", "BK", kx)):
        for oc in range(NQC):
            half = oc % 2
            st_toks = []
            for tt in range(NTT):
                bi = tt % 2
                pt = pQ[bi]
                w0 = lds + [bank_free[bi], half_free[half]]
                mm = None
                for ic in range(NCH):
                    mm = p.op("T", (lambda t, ic=ic, oc=oc, tt=tt, pt=pt, qkoff=qkoff: t.matmul(
                        pt[:], wb[:, ic * 3 * INNER + (qkoff + oc) * 128: ic * 3 * INNER + (qkoff + oc) * 128 + 128],
                        h_ap(ic, tt * 512, 512),
                        start=(ic == 0), stop=(ic == NCH - 1))), w0 if ic == 0 else [])
                    w0 = []
                sg = p.op("S", (lambda s, pt=pt: s.activation(sqb[:], pt[:], AF.Square)), [mm])
                sm = p.op("T", (lambda t: t.matmul(pS[:], e2s[:], sqb[:], start=True, stop=True)), [sg])
                nr = p.op("S", (lambda s: s.activation(rstd[:], pS[:], AF.Sqrt)), [sm, prev_t3])
                nr2 = p.op("V", (lambda v: v.tensor_scalar_max(rstd[:], rstd[:], 1e-12)), [nr])
                nr3 = p.op("V", (lambda v: v.reciprocal(rstd[:], rstd[:])), [nr2])
                ev = p.op("V", (lambda v, pt=pt, ssb=ssb: v.tensor_scalar_mul(qsq[:], pt[:], ssb[:])), [mm])
                bank_free[bi] = ev
                t1 = p.op("V", (lambda v, tt=tt, A=A: v.tensor_mul(
                    u[:], qsq[:], tb[A][:, tt * 512:(tt + 1) * 512])), [ev])
                rt = None
                for (dst, srcq) in QPART:
                    rt = p.op("V", (lambda v, dst=dst, srcq=srcq: v.tensor_copy(
                        qrot[dst, :], qsq[srcq, :])), [ev])
                rt = p.op("V", (lambda v, tt=tt, Bt=Bt: v.tensor_mul(
                    w_[:], qrot[:], tb[Bt][:, tt * 512:(tt + 1) * 512])), [rt])
                t2 = p.op("V", (lambda v: v.tensor_add(u[:], u[:], w_[:])), [t1, rt])
                t3 = p.op("V", (lambda v, oc=oc, tt=tt, half=half: v.tensor_mul(
                    osb[:, half * TPC + tt * 512: half * TPC + (tt + 1) * 512], u[:], rstd[:])),
                    [t2, nr3])
                prev_t3 = t3
                st_toks.append(t3)
            dt = p.dma("G", out_dram.rearrange("(c p) t -> p c t", p=128)[:, oc, :],
                       osb[:, half * TPC: half * TPC + TPC], waits=st_toks,
                       group=f"oh{half}_{name}")
            half_free[half] = dt

    vbank_free = [None, None]
    vhalf_free = None
    for tc in range(TPC // 128):
        bi = tc % 2
        pt = pV[bi]
        w0 = lds + [vbank_free[bi], vhalf_free]
        mm = None
        for ic in range(NCH):
            mm = p.op("T", (lambda t, ic=ic, tc=tc, pt=pt: t.matmul(
                pt[:], h_ap(ic, tc * 128, 128),
                wb[:, ic * 3 * INNER + 2 * INNER: ic * 3 * INNER + 3 * INNER],
                start=(ic == 0), stop=(ic == NCH - 1))), w0 if ic == 0 else [])
            w0 = []
        ev = p.op("V", (lambda v, pt=pt: v.tensor_copy(vsb[:], pt[:])), [mm])
        vbank_free[bi] = ev
        vhalf_free = p.dma("G", vx[tc * 128:(tc + 1) * 128, :], vsb[:], waits=[ev], group="vout")
    return


def build_lffn_qkv():
    nc = bass.Bass("TRN2", target_bir_lowering=False)
    xT = nc.dram_tensor("xT", [D, TPC], F32, kind="ExternalInput").ap()
    xprevT = nc.dram_tensor("xprevT", [D, 1], F32, kind="ExternalInput").ap()
    wqkv = nc.dram_tensor("wqkv", [D, 3 * INNER], F32, kind="ExternalInput").ap()
    qs2 = nc.dram_tensor("qs2", [128, 1], F32, kind="ExternalInput").ap()
    ks2 = nc.dram_tensor("ks2", [128, 1], F32, kind="ExternalInput").ap()
    tabs = {n: nc.dram_tensor(n, [128, TPC], F32, kind="ExternalInput").ap()
            for n in ("AQ", "BQ", "AK", "BK")}
    E2 = nc.dram_tensor("E2", [128, 128], F32, kind="ExternalInput").ap()
    qx = nc.dram_tensor("qx", [INNER, TPC], BF16, kind="ExternalOutput").ap()
    kx = nc.dram_tensor("kx", [INNER, TPC], BF16, kind="ExternalOutput").ap()
    vx = nc.dram_tensor("vx", [TPC, INNER], BF16, kind="ExternalOutput").ap()
    es = ExitStack()
    p = Prog(nc, es)
    emit_lffn_qkv(p, nc, es, xT, xprevT, wqkv, qs2, ks2, tabs, E2, qx, kx, vx)
    p.emit()
    es.close()
    return nc



def emit_lffn_attn(p, nc, es, qx, kfb, vfb, masks, ident, ox, kv_loader=None):
    KL = TPC + 512
    qe = es.enter_context(nc.sbuf_tensor("qe", [65, TPC], BF16))
    ke = es.enter_context(nc.sbuf_tensor("ke", [65, KL], BF16))
    ve = es.enter_context(nc.sbuf_tensor("ve", [128, 65 * (KL // 128)], BF16))
    msk = es.enter_context(nc.sbuf_tensor("msk", [128, 32 * 512], BF16))
    idn = es.enter_context(nc.sbuf_tensor("idn", [128, 128], F32))
    mneg = es.enter_context(nc.sbuf_tensor("mneg", [128, 4], F32))
    mtmp = es.enter_context(nc.sbuf_tensor("mtmp", [128, 2], F32))
    pt_sb = es.enter_context(nc.sbuf_tensor("pt_sb", [128, 2048], BF16))
    rb = es.enter_context(nc.sbuf_tensor("rb", [64, 512], F32))
    rec = es.enter_context(nc.sbuf_tensor("rec", [1, 512], F32))
    ones1 = es.enter_context(nc.sbuf_tensor("ones1", [1, 64], F32))
    ob = es.enter_context(nc.sbuf_tensor("ob", [64, 2 * 512], BF16))
    psA = [es.enter_context(nc.psum_tensor(f"psA{i}", [128, 512], F32)) for i in range(2)]
    psM = es.enter_context(nc.psum_tensor("psM", [1, 128], F32))
    psT = es.enter_context(nc.psum_tensor("psT", [128, 2048], F32))
    psO = es.enter_context(nc.psum_tensor("psO", [65, 512], F32))
    psR = psT[0:64, 0:512]
    p_ = p

    ldi = p.dma("Y", idn[:], ident, group="ldi")
    ldm = p.dma("Y", msk[:].rearrange("p (s n) -> p s n", n=512), masks.rearrange("s p n -> p s n"),
                group="ldm")
    one1 = p.op("V", lambda v: v.memset(ones1[:], 1.0))
    onek = p.op("V", lambda v: v.memset(ke[64:65, :], 1.0))
    onev = p.op("V", lambda v: v.memset(ve[:].rearrange("p (c e) -> p c e", e=65)[:, :, 64], 1.0))
    prev_head_done = [onek, onev]
    prev_cpm = None
    ob_last = [None, None]
    pending_norm = [None]
    last_st0 = [None]
    for h in range(8):
        ldq = p.dma("G", qe[0:64, :], qx[h * 64:(h + 1) * 64, :], waits=prev_head_done, group="ldq")
        if kv_loader is None:
            ldk = p.dma("G", ke[0:64, :], kfb[h * 64:(h + 1) * 64, :], waits=prev_head_done, group="ldk")
            ldv = p.dma("G", ve[:].rearrange("p (c e) -> p c e", e=65)[:, :, 0:64],
                        vfb.rearrange("(c p) d -> p c d", p=128)[:, :, h * 64:(h + 1) * 64],
                        waits=prev_head_done, group="ldv")
        else:
            ldk, ldv = kv_loader(h, prev_head_done, ke, ve)
        prev_avm = [None]
        prev_red = [None, None]
        for wi in range(4):
            tr_toks = []
            for qt in range(4):
                qsl = qe[0:64, wi * 512 + qt * 128: wi * 512 + qt * 128 + 128]
                red_prev = None
                for kc in range(2):
                    bi = kc
                    w = [ldq, ldk] if (qt == 0 and kc == 0) else []
                    w = w + [prev_red[bi]]
                    mm = p.op("T", (lambda t, qsl=qsl, kc=kc, wi=wi, bi=bi: t.matmul(
                        psA[bi][:], qsl, ke[0:64, wi * 512 + kc * 512: wi * 512 + (kc + 1) * 512],
                        start=True, stop=True)), w)
                    if kc == 0:
                        red_prev = p.op("V", (lambda v, qt=qt, bi=bi: v.reduce_max(
                            mneg[:, qt:qt + 1], psA[bi][:], axis=AX.X, negate=True)), [mm])
                    else:
                        r1 = p.op("V", (lambda v, bi=bi: v.reduce_max(
                            mtmp[:, bi:bi + 1], psA[bi][:], axis=AX.X, negate=True)), [mm])
                        red_prev = p.op("V", (lambda v, qt=qt, bi=bi: v.tensor_tensor(
                            mneg[:, qt:qt + 1], mneg[:, qt:qt + 1], mtmp[:, bi:bi + 1],
                            ALU.min)), [r1])
                    prev_red[bi] = red_prev
                trm = p.op("T", (lambda t, qt=qt: t.transpose(psM[:], mneg[:, qt:qt + 1], idn[:])),
                           [red_prev, ldi, prev_cpm])
                cpm = p.op("V", (lambda v, wi=wi, qt=qt: v.tensor_copy(
                    qe[64:65, wi * 512 + qt * 128: wi * 512 + qt * 128 + 128], psM[:])), [trm])
                prev_cpm = cpm
                tr_toks.append(cpm)
            if pending_norm[0] is not None:
                pending_norm[0]()
                pending_norm[0] = None
            qg = qe[:, wi * 512:(wi + 1) * 512]
            first_st_waits = tr_toks + [prev_avm[0]]
            ex_prev = [None, None]
            avm = None
            for kc in range(2):
                for hf in range(2):
                    mm2 = None
                    for s2 in range(2):
                        s = hf * 2 + s2
                        w = list(first_st_waits) + [ex_prev[hf]]
                        if s == 0:
                            w = w + [last_st0[0]]
                        first_st_waits = []
                        mm2 = p.op("T", (lambda t, kc=kc, s=s, qg=qg, wi=wi: t.matmul(
                            psT[:, s * 512:(s + 1) * 512],
                            ke[:, wi * 512 + kc * 512 + s * 128: wi * 512 + kc * 512 + s * 128 + 128],
                            qg, start=True, stop=True)), w)
                        if s == 0:
                            last_st0[0] = mm2
                    ex = p.op("S", (lambda s_, hf=hf: s_.activation(
                        pt_sb[:, hf * 1024:(hf + 1) * 1024],
                        psT[:, hf * 1024:(hf + 1) * 1024], AF.Exp, scale=8.0)),
                        [mm2, prev_avm[0]])
                    ex_prev[hf] = ex
                    dep = [ex]
                    for s2 in range(2):
                        s = hf * 2 + s2
                        mi = wi * 8 + kc * 4 + s
                        dep = [p.op("V", (lambda v, s=s, mi=mi: v.tensor_mul(
                            pt_sb[:, s * 512:(s + 1) * 512], pt_sb[:, s * 512:(s + 1) * 512],
                            msk[:, mi * 512:(mi + 1) * 512])), dep + ([ldm] if s2 == 0 else []))]
                    for s2 in range(2):
                        s = hf * 2 + s2
                        w = (dep + [ldv]) if s2 == 0 else []
                        avm = p.op("T", (lambda t, kc=kc, s=s, wi=wi: t.matmul(
                            psO[:], ve[:, (wi * 4 + kc * 4 + s) * 65:(wi * 4 + kc * 4 + s) * 65 + 65],
                            pt_sb[:, s * 512:(s + 1) * 512],
                            start=(kc == 0 and s == 0), stop=(kc == 1 and s == 3))), w)
            prev_avm[0] = avm

            def make_norm(wi=wi, h=h, avm=avm, ex_prev=ex_prev):
                def donorm():
                    rc = p_.op("V", (lambda v: v.reciprocal(rec[:], psO[64:65, :])), [avm])
                    bc = p_.op("T", (lambda t: t.matmul(psR, ones1[:], rec[:],
                                                        start=True, stop=True)),
                               [rc, one1, last_st0[0], ex_prev[0]])
                    cb = p_.op("V", (lambda v: v.tensor_copy(rb[:], psR)), [bc])
                    ot = p_.op("V", (lambda v, wi=wi: v.tensor_mul(
                        ob[:, (wi % 2) * 512:(wi % 2) * 512 + 512], psO[0:64, :], rb[:])),
                        [cb, ob_last[wi % 2]])
                    ob_last[wi % 2] = p_.dma("G", ox[h * 64:(h + 1) * 64, wi * 512:(wi + 1) * 512],
                                             ob[:, (wi % 2) * 512:(wi % 2) * 512 + 512],
                                             waits=[ot], group=f"ob{wi % 2}")
                    prev_avm[0] = ot
                    last_st0[0] = cb
                return donorm
            pending_norm[0] = make_norm()
        pending_norm[0]()
        pending_norm[0] = None
        prev_head_done = [prev_avm[0], ob_last[0], ob_last[1], prev_cpm]
    return


def build_lffn_attn():
    nc = bass.Bass("TRN2", target_bir_lowering=False)
    KL = TPC + 512
    qx = nc.dram_tensor("qx", [INNER, TPC], BF16, kind="ExternalInput").ap()
    kfb = nc.dram_tensor("kfb", [INNER, KL], BF16, kind="ExternalInput").ap()
    vfb = nc.dram_tensor("vfb", [KL, INNER], BF16, kind="ExternalInput").ap()
    masks = nc.dram_tensor("masks", [32, 128, 512], BF16, kind="ExternalInput").ap()
    ident = nc.dram_tensor("ident", [128, 128], F32, kind="ExternalInput").ap()
    ox = nc.dram_tensor("ox", [INNER, TPC], BF16, kind="ExternalOutput").ap()
    es = ExitStack()
    p = Prog(nc, es)
    emit_lffn_attn(p, nc, es, qx, kfb, vfb, masks, ident, ox)
    p.emit()
    es.close()
    return nc



def emit_lffn_proj(p, nc, es, oT, wo, xT, x2):
    obf = es.enter_context(nc.sbuf_tensor("obf", [128, 4 * TPC], BF16))
    wob = es.enter_context(nc.sbuf_tensor("wob", [128, 4 * D], BF16))
    xr = es.enter_context(nc.sbuf_tensor("xr", [128, NCH * TPC], F32))
    xo = es.enter_context(nc.sbuf_tensor("xo", [128, 512], F32))
    ps = [es.enter_context(nc.psum_tensor(f"ps{i}", [128, 512], F32)) for i in range(2)]
    ldo = p.dma("G", obf[:].rearrange("p (c t) -> p c t", t=TPC),
                oT.rearrange("(c p) t -> p c t", p=128), group="ldo")
    ldw = p.dma("G", wob[:].rearrange("p (c o) -> p c o", o=D), wo.rearrange("(c p) o -> p c o", p=128), group="ldw")
    ldxr = p.dma("G", xr[:].rearrange("p (c t) -> p c t", t=TPC),
                 xT.rearrange("(c p) t -> p c t", p=128), group="ldxr")
    bank_free = [None, None]
    for oc in range(NCH):
        for tt in range(NTT):
            bi = (oc * NTT + tt) % 2
            pt = ps[bi]
            w0 = [ldo, ldw, ldxr, bank_free[bi]]
            mm = None
            for ic in range(4):
                mm = p.op("T", (lambda t, ic=ic, oc=oc, tt=tt, pt=pt: t.matmul(
                    pt[:], wob[:, ic * D + oc * 128: ic * D + oc * 128 + 128],
                    obf[:, ic * TPC + tt * 512: ic * TPC + (tt + 1) * 512],
                    start=(ic == 0), stop=(ic == 3))), w0 if ic == 0 else [])
                w0 = []
            ad = p.op("V", (lambda v, oc=oc, tt=tt, pt=pt: v.tensor_add(
                xr[:, oc * TPC + tt * 512: oc * TPC + (tt + 1) * 512],
                xr[:, oc * TPC + tt * 512: oc * TPC + (tt + 1) * 512], pt[:])), [mm])
            bank_free[bi] = ad
            p.dma("G", x2.rearrange("(c p) t -> p c t", p=128)[:, oc, tt * 512:(tt + 1) * 512],
                  xr[:, oc * TPC + tt * 512: oc * TPC + (tt + 1) * 512], waits=[ad], group="xout")
    return


def build_lffn_proj():
    nc = bass.Bass("TRN2", target_bir_lowering=False)
    oT = nc.dram_tensor("oT", [INNER, TPC], BF16, kind="ExternalInput").ap()
    wo = nc.dram_tensor("wo", [INNER, D], F32, kind="ExternalInput").ap()
    xT = nc.dram_tensor("xT", [D, TPC], F32, kind="ExternalInput").ap()
    x2 = nc.dram_tensor("x2", [D, TPC], F32, kind="ExternalOutput").ap()
    es = ExitStack()
    p = Prog(nc, es)
    emit_lffn_proj(p, nc, es, oT, wo, xT, x2)
    p.emit()
    es.close()
    return nc



HID_P = 2816


def emit_lffn_ffn(p, nc, es, x2T, x2prevT, lg, lb, w1p, b1p, w2p, b2, x3):
    W = 2049
    NA = HID_P // 128
    xsh = es.enter_context(nc.sbuf_tensor("xsh", [128, NCH * W], F32))
    hbf = es.enter_context(nc.sbuf_tensor("hbf", [128, NCH * TPC], BF16))
    lgs = es.enter_context(nc.sbuf_tensor("lgs", [128, NCH], F32))
    lbs = es.enter_context(nc.sbuf_tensor("lbs", [128, NCH], F32))
    b1s = es.enter_context(nc.sbuf_tensor("b1s", [128, 2 * NA], F32))
    b2s = es.enter_context(nc.sbuf_tensor("b2s", [128, NCH], F32))
    w1sb = es.enter_context(nc.sbuf_tensor("w1sb", [128, 2 * 2 * NCH * 128], BF16))
    w2sb = es.enter_context(nc.sbuf_tensor("w2sb", [128, 2 * NA * 128], BF16))
    pbuf = es.enter_context(nc.sbuf_tensor("pbuf", [128, NA * 512], BF16))
    gg = es.enter_context(nc.sbuf_tensor("gg", [128, 512], BF16))
    atmp = es.enter_context(nc.sbuf_tensor("atmp", [128, 512], F32))
    xo = es.enter_context(nc.sbuf_tensor("xo", [128, 512], F32))
    ps = [es.enter_context(nc.psum_tensor(f"ps{i}", [128, 512], F32)) for i in range(2)]
    lnb = alloc_ln_bufs(nc, es)

    ldx = p.dma("G", xsh[:].rearrange("p (c t) -> p c t", t=W)[:, :, 1:W],
                x2T.rearrange("(c p) t -> p c t", p=128), group="ldx")
    ldxp = p.dma("G", xsh[:].rearrange("p (c t) -> p c t", t=W)[:, :, 0:1],
                 x2prevT.rearrange("(c p) t -> p c t", p=128), allow_slow_non_contiguous=True,
                 group="ldxp")
    ldlg = p.dma("Y", lgs[:], lg, group="ldlg")
    ldlb = p.dma("Y", lbs[:], lb, group="ldlb")
    ldb1 = p.dma("Y", b1s[:], b1p, group="ldb1")
    ldb2 = p.dma("Y", b2s[:], b2, group="ldb2")
    init = emit_ln_init(p, lnb)

    def xs_ap(c, tt):
        off = 0 if c >= 4 else 1
        return xsh[:, c * W + off + tt * 512: c * W + off + tt * 512 + 512]

    def xu_ap(c, tt):
        return xsh[:, c * W + 1 + tt * 512: c * W + 1 + tt * 512 + 512]

    def hb_ap(c, tt):
        return hbf[:, c * TPC + tt * 512: c * TPC + (tt + 1) * 512]

    toks, last_ln = emit_ln(p, lnb, xs_ap, TPC, out_fn=hb_ap,
                            gamma_scale=lambda c: lgs[:, c:c + 1],
                            gamma_bias=lambda c: lbs[:, c:c + 1],
                            pre_waits=[ldx, ldxp, ldlg, ldlb], init_toks=init)
    ln_done = [toks[(c, tt)] for c in range(NCH) for tt in range(NTT)]

    w1r = w1p.rearrange("(c p) o -> p c o", p=128)
    w2r = w2p.rearrange("(c p) o -> p c o", p=128)
    w1v = w1sb[:].rearrange("p (b a c o) -> p b a c o", b=2, a=2, o=128)
    w2v = w2sb[:].rearrange("p (b c o) -> p b c o", b=2, o=128)
    prev_pb_use = None
    last_x3 = None
    w1_free = [None, None]
    w2_free = [None, None]
    gbank = [None, None]
    for tt in range(NTT):
        pb_toks = []
        for ocp in range(NA):
            par = ocp % 2
            lwa = p.dma("G", w1v[:, par, 0], w1r[:, :, ocp * 128:(ocp + 1) * 128],
                        waits=[w1_free[par]], group=f"w1_{par}")
            lwg = p.dma("G", w1v[:, par, 1], w1r[:, :, HID_P + ocp * 128: HID_P + ocp * 128 + 128],
                        group=f"w1_{par}")
            w0 = ln_done + [lwa, lwg, gbank[0], prev_pb_use]
            mmg = None
            for ic in range(NCH):
                mmg = p.op("T", (lambda t, ic=ic, par=par, tt=tt: t.matmul(
                    ps[0][:], w1v[:, par, 1, ic], hb_ap(ic, tt),
                    start=(ic == 0), stop=(ic == NCH - 1))), w0 if ic == 0 else [])
                w0 = []
            ge = p.op("S", (lambda s, ocp=ocp: s.activation(
                gg[:], ps[0][:], AF.Gelu, bias=b1s[:, NA + ocp:NA + ocp + 1])), [mmg, ldb1])
            gbank[0] = ge
            w0 = [gbank[1], ge]
            mma = None
            for ic in range(NCH):
                mma = p.op("T", (lambda t, ic=ic, par=par, tt=tt: t.matmul(
                    ps[1][:], w1v[:, par, 0, ic], hb_ap(ic, tt),
                    start=(ic == 0), stop=(ic == NCH - 1))), w0 if ic == 0 else [])
                w0 = []
            ae = p.op("V", (lambda v, ocp=ocp: v.tensor_scalar_add(
                atmp[:], ps[1][:], b1s[:, ocp:ocp + 1])), [mma, ldb1])
            gbank[1] = ae
            pm = p.op("V", (lambda v, ocp=ocp: v.tensor_mul(
                pbuf[:, ocp * 512:(ocp + 1) * 512], atmp[:], gg[:])), [ae, ge])
            w1_free[par] = pm
            pb_toks.append(pm)
        x3_toks = []
        last_w2mm = None
        for oc2 in range(NCH):
            par = oc2 % 2
            lw2 = p.dma("G", w2v[:, par], w2r[:, :, oc2 * 128:(oc2 + 1) * 128],
                        waits=[w2_free[par]], group=f"w2_{par}")
            w0 = pb_toks + [lw2, gbank[0], gbank[1]]
            mm = None
            for kc in range(NA):
                mm = p.op("T", (lambda t, kc=kc, par=par, tt=tt: t.matmul(
                    ps[0][:], w2v[:, par, kc], pbuf[:, kc * 512:(kc + 1) * 512],
                    start=(kc == 0), stop=(kc == NA - 1))), w0 if kc == 0 else [])
                w0 = []
            last_w2mm = mm
            w2_free[par] = mm
            ev = p.op("V", (lambda v, oc2=oc2: v.tensor_scalar_add(
                xo[:], ps[0][:], b2s[:, oc2:oc2 + 1])), [mm, ldb2])
            gbank[0] = ev
            ad = p.op("V", (lambda v, oc2=oc2, tt=tt: v.tensor_add(
                xo[:], xo[:], xu_ap(oc2, tt))), [ev])
            dt = p.dma("G", x3.rearrange("(c p) t -> p c t", p=128)[:, oc2, tt * 512:(tt + 1) * 512],
                       xo[:], waits=[ad], group="xout")
            gbank[0] = dt
            x3_toks.append(ad)
        prev_pb_use = last_w2mm
        last_x3 = x3_toks[-1]
    return


def build_lffn_ffn():
    nc = bass.Bass("TRN2", target_bir_lowering=False)
    x2T = nc.dram_tensor("x2T", [D, TPC], F32, kind="ExternalInput").ap()
    x2prevT = nc.dram_tensor("x2prevT", [D, 1], F32, kind="ExternalInput").ap()
    lg = nc.dram_tensor("lg", [128, NCH], F32, kind="ExternalInput").ap()
    lb = nc.dram_tensor("lb", [128, NCH], F32, kind="ExternalInput").ap()
    w1p = nc.dram_tensor("w1p", [D, 2 * HID_P], F32, kind="ExternalInput").ap()
    b1p = nc.dram_tensor("b1p", [128, 2 * HID_P // 128], F32, kind="ExternalInput").ap()
    w2p = nc.dram_tensor("w2p", [HID_P, D], F32, kind="ExternalInput").ap()
    b2 = nc.dram_tensor("b2", [128, NCH], F32, kind="ExternalInput").ap()
    x3 = nc.dram_tensor("x3", [D, TPC], F32, kind="ExternalOutput").ap()
    es = ExitStack()
    p = Prog(nc, es)
    emit_lffn_ffn(p, nc, es, x2T, x2prevT, lg, lb, w1p, b1p, w2p, b2, x3)
    p.emit()
    es.close()
    return nc





DFF = 4096
NE = 8


def emit_moe(p, nc, es, xT, rw, rbrow, upT, downT, ident, rowsel, x4):
    NDC = DFF // 128
    TH = 1024
    xbf = es.enter_context(nc.sbuf_tensor("xbf", [128, NCH * TPC], BF16))
    rwb = es.enter_context(nc.sbuf_tensor("rwb", [128, NCH * NE], F32))
    rbr = es.enter_context(nc.sbuf_tensor("rbr", [1, NE], F32))
    onesr = es.enter_context(nc.sbuf_tensor("onesr", [1, 128], F32))
    xf = es.enter_context(nc.sbuf_tensor("xf", [128, NCH * 128], F32))
    ones1 = es.enter_context(nc.sbuf_tensor("ones1f", [1, 128], F32))
    idn = es.enter_context(nc.sbuf_tensor("idn", [128, 128], F32))
    rsel = es.enter_context(nc.sbuf_tensor("rsel", [NE, NE * 128], BF16))
    probs = es.enter_context(nc.sbuf_tensor("probs", [128, NE], F32))
    m8 = es.enter_context(nc.sbuf_tensor("m8", [128, NE], F32))
    mneg = es.enter_context(nc.sbuf_tensor("mnegM", [128, 1], F32))
    ssum = es.enter_context(nc.sbuf_tensor("ssum", [128, 1], F32))
    comb = es.enter_context(nc.sbuf_tensor("comb", [128, NE], F32))
    comb_fm = es.enter_context(nc.sbuf_tensor("comb_fm", [NE, TPC], BF16))
    cbc = es.enter_context(nc.sbuf_tensor("cbc", [128, TH], F32))
    uph = es.enter_context(nc.sbuf_tensor("uph", [128, 2 * NCH * 1024], BF16))
    hb = es.enter_context(nc.sbuf_tensor("hb", [128, NDC * TH], BF16))
    dsl = es.enter_context(nc.sbuf_tensor("dsl", [128, 2 * NDC * 128], BF16))
    acc = es.enter_context(nc.sbuf_tensor("acc", [128, NCH * TH], F32))
    tmpv = es.enter_context(nc.sbuf_tensor("tmpv", [128, 512], F32))
    xtile = es.enter_context(nc.sbuf_tensor("xtile", [128, 2 * 512], F32))
    psL = es.enter_context(nc.psum_tensor("psL", [128, 512], F32))
    psC = es.enter_context(nc.psum_tensor("psC", [128, 512], F32))
    psU = [es.enter_context(nc.psum_tensor(f"psU{i}", [128, 512], F32)) for i in range(2)]
    psD = [es.enter_context(nc.psum_tensor(f"psD{i}", [128, 512], F32)) for i in range(2)]
    lnb = alloc_ln_bufs(nc, es, reuse=dict(psS=psL[0:1, 0:512], psS2=psC[0:1, 0:512],
                                           psB=psU[0][:], psB2=psU[1][:]))

    ldx = p.dma("G", xbf[:].rearrange("p (c t) -> p c t", t=TPC),
                xT.rearrange("(c p) t -> p c t", p=128), group="ldx")
    ldrw = p.dma("Y", rwb[:].rearrange("p (c o) -> p c o", o=NE),
                 rw.rearrange("(c p) o -> p c o", p=128), group="ldrw")
    ldrb = p.dma("Y", rbr[:], rbrow, group="ldrb")
    ldi = p.dma("Y", idn[:], ident, group="ldi")
    ldrs = p.dma("G", rsel[:], rowsel, group="ldrs")
    o1 = p.op("V", lambda v: v.memset(onesr[:], 1.0))
    o2 = p.op("V", lambda v: v.memset(ones1[:], 1.0))
    init = emit_ln_init(p, lnb)

    prev_tr = None
    last_cf = None
    prev_t2 = None
    xf_free = None
    for tc in range(TPC // 128):
        lxf = p.dma("G", xf[:].rearrange("p (c o) -> p c o", o=128),
                    xT.rearrange("(c p) t -> p c t", p=128)[:, :, tc * 128:(tc + 1) * 128],
                    waits=[xf_free], group="xf")
        w0 = [lxf, ldrw, ldrb, o1, prev_t2]
        for ic in range(NCH):
            mmr = p.op("T", (lambda t, ic=ic, tc=tc: t.matmul(
                psL[:, 0:NE],
                xf[:, ic * 128:(ic + 1) * 128],
                rwb[:, ic * NE:(ic + 1) * NE], start=(ic == 0), stop=False)),
                w0 if ic == 0 else [])
            w0 = []
        xf_free = mmr
        mm = p.op("T", (lambda t: t.matmul(
            psL[:, 0:NE], onesr[:], rbr[:], start=False, stop=True)), [])
        t1 = p.op("V", (lambda v: v.reduce_max(mneg[:], psL[:, 0:NE], axis=AX.X, negate=True)),
                  [mm, prev_tr])
        t2 = p.op("S", (lambda s: s.activation(probs[:], psL[:, 0:NE], AF.Exp, bias=mneg[:])), [t1])
        prev_t2 = t2
        t3 = p.op("V", (lambda v: v.reduce_sum(ssum[:], probs[:], axis=AX.X)), [t2])
        t4 = p.op("V", (lambda v: v.reciprocal(ssum[:], ssum[:])), [t3])
        t5 = p.op("V", (lambda v: v.tensor_scalar_mul(probs[:], probs[:], ssum[:])), [t4])
        t6 = p.op("V", (lambda v: v.max(m8[:], probs[:])), [t5])
        t7 = p.op("V", (lambda v: v.scalar_tensor_tensor(
            comb[:], probs[:], m8[:, 1:2], probs[:], ALU.is_ge, ALU.mult)), [t6])
        tr = p.op("T", (lambda t: t.transpose(psC[0:NE, 0:128], comb[:], idn[:])),
                  [t7, ldi, prev_tr])
        cf = p.op("V", (lambda v, tc=tc: v.tensor_copy(
            comb_fm[:, tc * 128:(tc + 1) * 128], psC[0:NE, 0:128])), [tr])
        prev_tr = cf
        last_cf = cf

    upr = upT.rearrange("e (c p) f -> e p c f", p=128)
    dwr = downT.rearrange("e (c p) f -> e p c f", p=128)
    uphv = uph[:].rearrange("p (b c f) -> p b c f", b=2, f=1024)
    dslv = dsl[:].rearrange("p (b c o) -> p b c o", b=2, o=128)
    up_free = None
    dsl_free = [None, None]
    psD_reader = [None, None]
    psU_reader = [None, None]
    prev_cbc = None
    cbc_reader = None
    prev_acc_use = []
    x4_toks = []
    up_mm_last = {0: None, 1: None}
    for th in range(2):
        acc_last = {}
        tmp_last = None
        for e in range(NE):
            h_last = []
            for dq in range(4):
                qb = dq % 2
                lu = p.dma("G", uphv[:, qb], upr[e][:, :, dq * 1024:(dq + 1) * 1024],
                           waits=[up_mm_last[qb]], group="lu")
                last_ev = None
                for dc in range(8):
                    for ts in range(2):
                        bi = (dc * 2 + ts) % 2
                        pt = psU[bi]
                        w0 = [ldx, lu, psU_reader[bi]] + \
                             (prev_acc_use if (e == 0 and dq == 0 and dc == 0 and ts == 0) else [])
                        mm = None
                        for ic in range(NCH):
                            mm = p.op("T", (lambda t, ic=ic, dc=dc, ts=ts, th=th, pt=pt, qb=qb: t.matmul(
                                pt[:], uphv[:, qb, ic, dc * 128:(dc + 1) * 128],
                                xbf[:, ic * TPC + th * TH + ts * 512: ic * TPC + th * TH + ts * 512 + 512],
                                start=(ic == 0), stop=(ic == NCH - 1))), w0 if ic == 0 else [])
                            w0 = []
                        up_mm_last[qb] = mm
                        ev = p.op("S", (lambda s, dq=dq, dc=dc, ts=ts, pt=pt: s.activation(
                            hb[:, (dq * 8 + dc) * TH + ts * 512: (dq * 8 + dc) * TH + ts * 512 + 512],
                            pt[:], AF.Gelu)), [mm])
                        psU_reader[bi] = ev
                        last_ev = ev
                h_last.append(last_ev)
            h_last = h_last[-2:]
            cb_toks = []
            for ts in range(2):
                bcm = p.op("T", (lambda t, e=e, th=th, ts=ts: t.matmul(
                    psC[:, 0:512], rsel[:, e * 128:(e + 1) * 128],
                    comb_fm[0:NE, th * TH + ts * 512: th * TH + ts * 512 + 512],
                    start=True, stop=True)), [last_cf, ldrs, prev_cbc] + cb_toks +
                    ([cbc_reader] if cbc_reader else []))
                cbc_t = p.op("V", (lambda v, ts=ts: v.tensor_copy(
                    cbc[:, ts * 512:(ts + 1) * 512], psC[:, 0:512])), [bcm])
                prev_cbc = cbc_t
                cb_toks.append(cbc_t)
            last_mm = None
            for oc in range(NCH):
                par = oc % 2
                ld2 = p.dma("G", dslv[:, par], dwr[e][:, :, oc * 128:(oc + 1) * 128],
                            waits=[dsl_free[par]], group="dsl")
                for ts in range(2):
                    pt = psD[ts]
                    w0 = (h_last if oc == 0 else []) + [ld2, psD_reader[ts]] + \
                         (prev_acc_use if (oc == 0 and e == 0) else [])
                    mm = None
                    for kc in range(NDC):
                        mm = p.op("T", (lambda t, kc=kc, ts=ts, pt=pt, par=par: t.matmul(
                            pt[:], dslv[:, par, kc],
                            hb[:, kc * TH + ts * 512: kc * TH + ts * 512 + 512],
                            start=(kc == 0), stop=(kc == NDC - 1))), w0 if kc == 0 else [])
                        w0 = []
                    last_mm = mm
                    aslice = acc[:, oc * TH + ts * 512: oc * TH + ts * 512 + 512]
                    if e == 0:
                        rd = p.op("V", (lambda v, aslice=aslice, ts=ts, pt=pt: v.tensor_mul(
                            aslice, pt[:], cbc[:, ts * 512:(ts + 1) * 512])), [mm] + cb_toks)
                        acc_last[(oc, ts)] = rd
                    else:
                        rd = p.op("V", (lambda v, ts=ts, pt=pt: v.tensor_mul(
                            tmpv[:], pt[:], cbc[:, ts * 512:(ts + 1) * 512])),
                            [mm, tmp_last] + cb_toks)
                        at = p.op("G", (lambda g, aslice=aslice: g.tensor_add(
                            aslice, aslice, tmpv[:])), [rd, acc_last[(oc, ts)]])
                        tmp_last = at
                        acc_last[(oc, ts)] = at
                    psD_reader[ts] = rd
                    cbc_reader = rd
                dsl_free[par] = last_mm
        def a_ap(c, tt, th=th):
            return acc[:, c * TH + tt * 512: c * TH + tt * 512 + 512]
        res_toks = []
        xl_prev = [None, None]
        for c in range(NCH):
            for tt in range(2):
                par = tt % 2
                lx = p.dma("G", xtile[:, par * 512:(par + 1) * 512],
                           xT.rearrange("(c p) t -> p c t", p=128)[:, c,
                                        th * TH + tt * 512: th * TH + tt * 512 + 512],
                           waits=[xl_prev[par]], group=f"xt{par}")
                ad = p.op("V", (lambda v, c=c, tt=tt, par=par: v.tensor_add(
                    a_ap(c, tt), a_ap(c, tt), xtile[:, par * 512:(par + 1) * 512])),
                    [lx, acc_last[(c, tt)]])
                xl_prev[par] = ad
                res_toks.append(ad)
        toks, last_ln = emit_ln(p, lnb, a_ap, TH, out_fn=a_ap, pre_waits=[res_toks[-1]],
                                init_toks=init)
        for c in range(NCH):
            for tt in range(2):
                p.dma("G", x4.rearrange("(c p) t -> p c t", p=128)[:, c,
                                        th * TH + tt * 512: th * TH + tt * 512 + 512],
                      a_ap(c, tt), waits=[toks[(c, tt)]], group="x4out")
        prev_acc_use = [last_ln, p.gtotal("x4out")]
    return


def build_moe():
    nc = bass.Bass("TRN2", target_bir_lowering=False)
    xT = nc.dram_tensor("xT", [D, TPC], F32, kind="ExternalInput").ap()
    rw = nc.dram_tensor("rw", [D, NE], F32, kind="ExternalInput").ap()
    rbrow = nc.dram_tensor("rbrow", [1, NE], F32, kind="ExternalInput").ap()
    upT = nc.dram_tensor("upT", [NE, D, DFF], F32, kind="ExternalInput").ap()
    downT = nc.dram_tensor("downT", [NE, DFF, D], F32, kind="ExternalInput").ap()
    ident = nc.dram_tensor("ident", [128, 128], F32, kind="ExternalInput").ap()
    rowsel = nc.dram_tensor("rowsel", [NE, NE * 128], F32, kind="ExternalInput").ap()
    x4 = nc.dram_tensor("x4", [D, TPC], F32, kind="ExternalOutput").ap()
    es = ExitStack()
    p = Prog(nc, es)
    emit_moe(p, nc, es, xT, rw, rbrow, upT, downT, ident, rowsel, x4)
    p.emit()
    es.close()
    return nc



def emit_addln(p, nc, es, xT, yT, xn):
    xr = es.enter_context(nc.sbuf_tensor("xr", [128, NCH * TPC], F32))
    yb = es.enter_context(nc.sbuf_tensor("yb", [128, NCH * TPC], F32))
    lnb = alloc_ln_bufs(nc, es)
    ldx = p.dma("G", xr[:].rearrange("p (c t) -> p c t", t=TPC), xT.rearrange("(c p) t -> p c t", p=128), group="ldx")
    ldy = p.dma("G", yb[:].rearrange("p (c t) -> p c t", t=TPC), yT.rearrange("(c p) t -> p c t", p=128), group="ldy")
    init = emit_ln_init(p, lnb)
    add_toks = []
    for c in range(NCH):
        for tt in range(NTT):
            ad = p.op("V", (lambda v, c=c, tt=tt: v.tensor_add(
                xr[:, c * TPC + tt * 512: c * TPC + (tt + 1) * 512],
                xr[:, c * TPC + tt * 512: c * TPC + (tt + 1) * 512],
                yb[:, c * TPC + tt * 512: c * TPC + (tt + 1) * 512])),
                [ldx, ldy] if (c == 0 and tt == 0) else [])
            add_toks.append(ad)

    def x_ap(c, tt):
        return xr[:, c * TPC + tt * 512: c * TPC + (tt + 1) * 512]

    toks, last = emit_ln(p, lnb, x_ap, TPC, out_fn=x_ap, pre_waits=[add_toks[-1]], init_toks=init)
    for c in range(NCH):
        for tt in range(NTT):
            p.dma("G", xn.rearrange("(c p) t -> p c t", p=128)[:, c, tt * 512:(tt + 1) * 512],
                  x_ap(c, tt), waits=[toks[(c, tt)]], group="xout")
    return


def build_addln():
    nc = bass.Bass("TRN2", target_bir_lowering=False)
    xT = nc.dram_tensor("xT", [D, TPC], F32, kind="ExternalInput").ap()
    yT = nc.dram_tensor("yT", [D, TPC], F32, kind="ExternalInput").ap()
    xn = nc.dram_tensor("xn", [D, TPC], F32, kind="ExternalOutput").ap()
    es = ExitStack()
    p = Prog(nc, es)
    emit_addln(p, nc, es, xT, yT, xn)
    p.emit()
    es.close()
    return nc


def build_proj_ln():
    nc = bass.Bass("TRN2", target_bir_lowering=False)
    oT = nc.dram_tensor("oT", [D, TPC], BF16, kind="ExternalInput").ap()
    wo = nc.dram_tensor("wo", [D, D], F32, kind="ExternalInput").ap()
    bo = nc.dram_tensor("bo", [128, NCH], F32, kind="ExternalInput").ap()
    xT = nc.dram_tensor("xT", [D, TPC], F32, kind="ExternalInput").ap()
    xn = nc.dram_tensor("xn", [D, TPC], F32, kind="ExternalOutput").ap()
    es = ExitStack()
    p = Prog(nc, es)
    emit_proj_ln(p, nc, es, oT, wo, bo, xT, xn)
    p.emit()
    es.close()
    return nc



MEGA_DEBUG = False


def build_mega():
    nc = bass.Bass("TRN2", target_bir_lowering=False)
    EI, EO = "ExternalInput", "ExternalOutput"
    A = lambda n, s, d=F32, k=EI: nc.dram_tensor(n, s, d, kind=k).ap()
    x0T = A("x0T", [D, TPC])
    out = A("out", [D, TPC], F32, EO)
    xdbg = A("xdbg", [5, D, TPC], F32, EO) if MEGA_DEBUG else None
    if MEGA_DEBUG:
        bdbg = A("bdbg", [D, 1], F32, EO)
        qdbg = A("qdbg", [INNER, TPC], BF16, EO)
        kdbg = A("kdbg", [INNER, TPC + 512], BF16, EO)
        vdbg = A("vdbg", [TPC + 512, INNER], BF16, EO)
        odbg = A("odbg", [INNER, TPC], BF16, EO)
        x2dbg = A("x2dbg", [D, TPC], F32, EO)
    dw = {}
    for l in ("0", "1"):
        for n in ("wq", "wk", "wv", "wo"):
            dw[n + l] = A(n + l, [D, D])
        for n in ("bq", "bk", "bo"):
            dw[n + l] = A(n + l, [128, NCH])
        dw["bvrow" + l] = A("bvrow" + l, [1, D])
    lw = {}
    for l in ("0", "1"):
        lw["wqkv" + l] = A("wqkv" + l, [D, 3 * INNER])
        lw["qs2" + l] = A("qs2" + l, [128, 1])
        lw["ks2" + l] = A("ks2" + l, [128, 1])
        lw["lwo" + l] = A("lwo" + l, [INNER, D])
        lw["lg" + l] = A("lg" + l, [128, NCH])
        lw["lb" + l] = A("lb" + l, [128, NCH])
        lw["w1p" + l] = A("w1p" + l, [D, 2 * HID_P])
        lw["b1p" + l] = A("b1p" + l, [128, 2 * HID_P // 128])
        lw["w2p" + l] = A("w2p" + l, [HID_P, D])
        lw["b2" + l] = A("b2" + l, [128, NCH])
    tabs = {n: A("t" + n, [128, TPC]) for n in ("AQ", "BQ", "AK", "BK")}
    E2 = A("E2", [128, 128])
    lmasks = A("lmasks", [32, 128, 512], BF16)
    tri = A("tri", [4, 128, 512], BF16)
    ident = A("ident", [128, 128])
    rowsel = A("rowsel", [NE, NE * 128])
    rw = A("rw", [D, NE])
    rbrow = A("rbrow", [1, NE])
    upT = A("upT", [NE, D, DFF])
    downT = A("downT", [NE, DFF, D])
    bmask = A("bmask", [128, NCH])
    ix = A("ix", [1, 64], mybir.dt.uint32)
    I = lambda n, s, d=F32: nc.dram_tensor(n, s, d, kind="Internal").ap()
    xcur = I("xcur", [D, TPC])
    xpre = I("xpre", [D, TPC])
    x2s = I("x2s", [D, TPC])
    qkc = I("qkc", [2 * D, TPC], BF16)
    vc = I("vc", [TPC, D], BF16)
    qkg = nc.dram_tensor("qkg", [8 * 2 * D, TPC], BF16, kind="Internal", addr_space="Shared").ap()
    vg = nc.dram_tensor("vg", [8 * TPC, D], BF16, kind="Internal", addr_space="Shared").ap()
    oc4 = I("oc4", [4 * 64, T], BF16)
    og = nc.dram_tensor("og", [8 * 4 * 64, T], BF16, kind="Internal", addr_space="Shared").ap()
    bndc = I("bndc", [128, NCH])
    bndg = nc.dram_tensor("bndg", [8 * 128, NCH], F32, kind="Internal", addr_space="Shared").ap()
    bndloc = I("bndloc", [D, 1])
    kvwc = I("kvwc", [2 * 512, 512], BF16)
    kvwg = nc.dram_tensor("kvwg", [8 * 2 * 512, 512], BF16, kind="Internal", addr_space="Shared").ap()
    qkl = I("qkl", [2 * 4 * 64, T], BF16)
    vloc = I("vloc", [4 * T, 64], BF16)
    oloc = I("oloc", [16 * 64, TPC], BF16)
    qlxs = I("qlxs", [INNER, TPC], BF16)
    kfbs = I("kfbs", [INNER, TPC + 512], BF16)
    vfbs = I("vfbs", [TPC + 512, INNER], BF16)
    oxl = I("oxl", [INNER, TPC], BF16)
    qkg5 = qkg.rearrange("(r a h p) t -> r a h p t", r=8, a=2, h=16)
    vg4 = vg.rearrange("(r t) (h e) -> r h t e", r=8, h=16)
    og6 = og.rearrange("(r s p) (b t) -> b r s p t", r=8, s=4, t=TPC)
    qkl5 = qkl.rearrange("(a h p) (r t) -> a h p r t", a=2, h=4, r=4)
    vloc4 = vloc.rearrange("(h r t) e -> h r t e", h=4, r=4)
    oloc4 = oloc.rearrange("(c b p) t -> c b p t", c=8, b=2)
    bndg3 = bndg.rearrange("(r p) c -> r p c", r=8)
    kvwg3 = kvwg.rearrange("(r a p) t -> r a p t", r=8, a=2)

    es = ExitStack()
    p = Prog(nc, es)
    ixs = es.enter_context(nc.sbuf_tensor("ixs", [1, 64], mybir.dt.uint32))
    bnds = es.enter_context(nc.sbuf_tensor("bnds", [128, NCH], F32))
    bmsks = es.enter_context(nc.sbuf_tensor("bmsks", [128, NCH], F32))
    ldix = p.dma("Y", ixs[:], ix, group="ldix")
    ldbm = p.dma("Y", bmsks[:], bmask, group="ldbm")

    RG = [list(range(NCORES))]

    _dyncache = {}

    def dyn(g, k, ekey="G"):
        if (ekey, k) not in _dyncache:
            _dyncache[(ekey, k)] = g.value_load(ixs[0:1, k:k + 1])
        return _dyncache[(ekey, k)]

    def repack_qkv(coll_tok):
        first = [coll_tok, ldix]
        for a in range(2):
            def fqk(e, a=a):
                bb = dyn(e, 0, "G")
                hb = dyn(e, 1, "G")
                return e.dma_start(
                    out=qkl5[a].transpose([2, 0, 1, 3]),
                    in_=qkg5[bass.ts(bb, 4)][:, a][:, bass.ts(hb, 4)])
            p.raw_dma("G", fqk, waits=first, group="rpqk")
            first = []
        for ri in range(4):
            def fv(e, ri=ri):
                bb = dyn(e, 0, "S")
                hb = dyn(e, 1, "S")
                return e.dma_start(
                    out=vloc4[:, ri],
                    in_=vg4[bass.ts(bb, 4)][ri][bass.ts(hb, 4)])
            p.raw_dma("S", fv, waits=[coll_tok, ldix] if ri == 0 else (), group="rpv")
        return [p.gtotal("rpqk"), p.gtotal("rpv")]

    def mega_flash_loader(rp_toks):
        def loader(pair, prev_done, qe, ke, ve):
            w = list(prev_done) + list(rp_toks)
            ldq = p.dma("G", qe[0:64, :], qkl5[0, pair].rearrange("p r t -> p (r t)"),
                        waits=w, group="ldq")
            ldk = p.dma("G", ke[0:64, :], qkl5[1, pair].rearrange("p r t -> p (r t)"),
                        waits=(), group="ldk")
            ldv = p.dma("G", ve[:].rearrange("p (c e2) -> p c e2", e2=65)[:, :, 0:64],
                        vloc4[pair].rearrange("r (c p) e -> p (r c) e", p=128),
                        waits=(), group="ldv")
            return ldq, ldk, ldv
        return loader

    def mega_o_loader(coll_tok):
        def o_loader(obf):
            def fo(e):
                bb = dyn(e, 0, "G")
                so = dyn(e, 3, "G")
                return e.dma_start(
                    out=oloc.rearrange("(r s p) t -> r s p t", r=4, s=4),
                    in_=og6[so][bass.ts(bb, 4)])
            t1 = p.raw_dma("G", fo, waits=[coll_tok, ldix], group="ldo1")
            p.dma("G", obf[0:64, :].rearrange("p (c t) -> p c t", t=TPC),
                  oloc4[:, 0].transpose([1, 0, 2]), waits=[t1], group="ldo")
            tokl = p.dma("G", obf[64:128, :].rearrange("p (c t) -> p c t", t=TPC),
                         oloc4[:, 1].transpose([1, 0, 2]), waits=(), group="ldo")
            return tokl
        return o_loader

    def dense_block(lbl, x_src):
        pn = NCProxy(nc, f"dq{lbl}_")
        with ExitStack() as pes:
            emit_qkv(p, pn, pes, x_src, dw["wq" + lbl], dw["wk" + lbl], dw["wv" + lbl],
                     dw["bq" + lbl], dw["bk" + lbl], dw["bvrow" + lbl],
                     qkc[0:D, :], qkc[D:2 * D, :], vc)
            p.barrier()
        wv_ = [p.gtotal("oh0_q"), p.gtotal("oh1_q"), p.gtotal("oh0_k"),
               p.gtotal("oh1_k"), p.gtotal("vout")]
        p.collective("AllGather", [qkc], [qkg], waits=wv_)
        ct = p.collective("AllGather", [vc], [vg], waits=[])
        rp = repack_qkv(ct)
        pn = NCProxy(nc, f"df{lbl}_")
        with ExitStack() as pes:
            emit_flash(p, pn, pes, ident, tri, oc4.rearrange("(f p) t -> f p t", f=4),
                       mega_flash_loader(rp))
            p.barrier()
        ct2 = p.collective("AllGather", [oc4], [og], waits=[p.gtotal("ob0"), p.gtotal("ob1")])
        pn = NCProxy(nc, f"dp{lbl}_")
        with ExitStack() as pes:
            emit_proj_ln(p, pn, pes, None, dw["wo" + lbl], dw["bo" + lbl], x_src, xcur,
                         o_loader=mega_o_loader(ct2))
            p.barrier()

    def stage_boundary(src_dram):
        t1 = p.dma("G", bndc, src_dram.rearrange("(c p) t -> p c t", p=128)[:, :, TPC - 1],
                   group="bstg", allow_slow_non_contiguous=True)
        ct = p.collective("AllGather", [bndc], [bndg], waits=[t1])

        def fb(e):
            rk = dyn(e, 2, "S")
            return e.dma_start(out=bnds[:], in_=bndg3[rk])
        t2 = p.raw_dma("S", fb, waits=[ct, ldix], group="bld")
        t3 = p.op("V", lambda v: v.tensor_mul(bnds[:], bnds[:], bmsks[:]), [t2, ldbm])
        t4 = p.dma("G", bndloc.rearrange("(c p) o -> p c o", p=128), bnds[:].unsqueeze(2),
                   waits=[t3], group="bst2", allow_slow_non_contiguous=True)
        p.barrier()

    def lffn_layer(lbl):
        dbg = MEGA_DEBUG and lbl == "0"
        stage_boundary(xcur)
        if dbg:
            p.dma("G", bdbg, bndloc, group="xdbg", allow_slow_non_contiguous=True)
            p.barrier()
        pn = NCProxy(nc, f"lq{lbl}_")
        with ExitStack() as pes:
            emit_lffn_qkv(p, pn, pes, xcur, bndloc, lw["wqkv" + lbl], lw["qs2" + lbl],
                          lw["ks2" + lbl], tabs, E2, qlxs, kfbs[:, 512:TPC + 512],
                          vfbs[512:TPC + 512, :])
            p.barrier()
        if dbg:
            p.dma("G", qdbg, qlxs, group="xdbg")
            p.barrier()
        t1 = p.dma("G", kvwc[0:512, :], kfbs[:, TPC:TPC + 512], group="kvstg")
        t2 = p.dma("G", kvwc[512:1024, :], vfbs[TPC:TPC + 512, :], group="kvstg")
        ct = p.collective("AllGather", [kvwc], [kvwg], waits=[t1, t2])

        kes = ExitStack()
        kvtmp = kes.enter_context(nc.sbuf_tensor(f"kvtmp{lbl}", [128, 4 * 512], BF16))

        def fk(e):
            rk = dyn(e, 2, "Y")
            return e.dma_start(out=kvtmp[:].rearrange("p (a t) -> p a t", t=512),
                               in_=kvwg3[rk][0].rearrange("(a p) t -> p a t", p=128))
        def fv(e):
            rk = dyn(e, 2, "Y")
            return e.dma_start(out=vfbs[0:512, :].rearrange("(a p) t -> p a t", p=128),
                               in_=kvwg3[rk][1].rearrange("(a p) t -> p a t", p=128))
        t3 = p.raw_dma("Y", fk, waits=[ct, ldix], group="kvld")
        t4 = p.raw_dma("Y", fv, waits=(), group="kvld")
        tz = p.op("V", lambda v: v.tensor_scalar_mul(kvtmp[:], kvtmp[:], bmsks[:, 0:1]),
                  [t3, ldbm])
        p.dma("G", kfbs[:, 0:512].rearrange("(a p) t -> p a t", p=128),
              kvtmp[:].rearrange("p (a t) -> p a t", t=512), waits=[tz], group="kvld2")
        p.barrier()
        kes.close()
        if dbg:
            p.dma("G", kdbg, kfbs, group="xdbg")
            p.dma("G", vdbg, vfbs, group="xdbg")
            p.barrier()
        pn = NCProxy(nc, f"la{lbl}_")
        with ExitStack() as pes:
            emit_lffn_attn(p, pn, pes, qlxs, kfbs, vfbs, lmasks, ident, oxl)
            p.barrier()
        if dbg:
            p.dma("G", odbg, oxl, group="xdbg")
            p.barrier()
        pn = NCProxy(nc, f"lp{lbl}_")
        with ExitStack() as pes:
            emit_lffn_proj(p, pn, pes, oxl, lw["lwo" + lbl], xcur, x2s)
            p.barrier()
        if dbg:
            p.dma("G", x2dbg, x2s, group="xdbg")
            p.barrier()
        stage_boundary(x2s)
        pn = NCProxy(nc, f"lf{lbl}_")
        with ExitStack() as pes:
            emit_lffn_ffn(p, pn, pes, x2s, bndloc, lw["lg" + lbl], lw["lb" + lbl],
                          lw["w1p" + lbl], lw["b1p" + lbl], lw["w2p" + lbl],
                          lw["b2" + lbl], xcur)
            p.barrier()

    def tap(i):
        if MEGA_DEBUG:
            p.dma("G", xdbg[i], xcur, group="xdbg")
            p.barrier()

    dense_block("0", x0T)
    cp = p.dma("G", xpre, xcur, group="xpre")
    p.barrier()
    tap(0)
    lffn_layer("0")
    tap(1)
    lffn_layer("1")
    tap(2)
    pn = NCProxy(nc, "al_")
    with ExitStack() as pes:
        emit_addln(p, pn, pes, xpre, xcur, xcur)
        p.barrier()
    tap(3)
    dense_block("1", xcur)
    tap(4)
    pn = NCProxy(nc, "mo_")
    with ExitStack() as pes:
        emit_moe(p, pn, pes, xcur, rw, rbrow, upT, downT, ident, rowsel, out)
        p.barrier()
    p.emit()
    es.close()
    return nc



def _cache_build(key, fn):
    if key not in _NC_CACHE:
        _NC_CACHE[key] = fn()
    return _NC_CACHE[key]


def _fm(x):
    return np.ascontiguousarray(x.T)


def host_tri_masks():
    ks = np.arange(128)
    qs = np.arange(512)
    m = np.zeros((4, 128, 512), dtype=BF)
    for s in range(4):
        m[s] = ((128 * s + ks[:, None]) <= qs[None, :]).astype(BF)
    return m


def host_local_masks(first_window_of_seq):
    ks = np.arange(128)
    qs = np.arange(512)
    out = np.zeros((4, 8, 128, 512), dtype=BF)
    for w in range(4):
        for kc in range(2):
            for s in range(4):
                klocal = kc * 512 + s * 128 + ks[:, None]
                ql = qs[None, :]
                ok = (klocal > ql) & (klocal <= ql + 512)
                if w == 0 and first_window_of_seq:
                    ok = ok & (klocal >= 512)
                out[w, kc * 4 + s] = ok.astype(BF)
    return out.reshape(32, 128, 512)


def host_rotary_tables():
    DH, WIN = 64, 512
    half = np.arange(0, DH, 2, dtype=np.float64)
    inv_freq = 1.0 / (10000.0 ** (half / DH))
    t = np.arange(T, dtype=np.float64)
    freqs = t[:, None] * inv_freq[None, :]
    freqs = np.concatenate([freqs, freqs], axis=1)
    base = (half + 0.4 * DH) / (1.4 * DH)
    power = (t - T // 2) / (WIN // 2)
    scale = base[None, :] ** power[:, None]
    scale = np.concatenate([scale, scale], axis=1)
    cos, sin = np.cos(freqs), np.sin(freqs)
    AQ = (cos * scale).T.astype(np.float32)
    BQr = (sin * scale).T.astype(np.float32)
    AK = (cos / scale).T.astype(np.float32)
    BKr = (sin / scale).T.astype(np.float32)
    sgn = np.concatenate([-np.ones(32), np.ones(32)]).astype(np.float32)[:, None]
    return AQ, BQr * sgn, AK, BKr * sgn


def host_E2():
    e = np.zeros((128, 128), dtype=np.float32)
    e[0:64, 0:64] = 1.0
    e[64:128, 64:128] = 1.0
    return e


_IDENT = np.eye(128, dtype=np.float32)
_ROWSEL = np.zeros((8, 8 * 128), dtype=np.float32)
for _e in range(8):
    _ROWSEL[_e, _e * 128:(_e + 1) * 128] = 1.0


def run_dense_attn_block(x, Wq, bq, Wk, bk, Wv, bv, Wo, bo):
    nc1 = _cache_build("qkv", build_qkv_launch)
    in1 = []
    for c in range(NCORES):
        b, s = divmod(c, 4)
        sl = x[b, s * TPC:(s + 1) * TPC]
        in1.append(dict(xT=_fm(sl), wq=Wq, wk=Wk, wv=Wv,
                        bq=np.ascontiguousarray(bq.reshape(NCH, 128).T),
                        bk=np.ascontiguousarray(bk.reshape(NCH, 128).T),
                        bvrow=bv.reshape(1, D)))
    r1 = run_bass_kernel_spmd(nc1, in1, core_ids=list(range(NCORES)))
    QF = np.zeros((B, D, T), dtype=BF)
    KF = np.zeros((B, D, T), dtype=BF)
    VT = np.zeros((B, T, D), dtype=BF)
    for c in range(NCORES):
        b, s = divmod(c, 4)
        QF[b][:, s * TPC:(s + 1) * TPC] = r1.results[c]["qx"]
        KF[b][:, s * TPC:(s + 1) * TPC] = r1.results[c]["kx"]
        VT[b][s * TPC:(s + 1) * TPC] = r1.results[c]["vx"]
    nc2 = _cache_build("flash", build_flash_dense)
    tri = host_tri_masks()
    in2 = []
    for c in range(NCORES):
        pairs = [(c * 4 + i) for i in range(4)]
        q4 = np.stack([np.ascontiguousarray(QF[pr // 16][(pr % 16) * 64:(pr % 16) * 64 + 64]) for pr in pairs])
        k4 = np.stack([np.ascontiguousarray(KF[pr // 16][(pr % 16) * 64:(pr % 16) * 64 + 64]) for pr in pairs])
        v4 = np.stack([np.ascontiguousarray(VT[pr // 16][:, (pr % 16) * 64:(pr % 16) * 64 + 64]) for pr in pairs])
        in2.append(dict(q4=q4, k4=k4, v4=v4, ident=_IDENT, trimask=tri))
    r2 = run_bass_kernel_spmd(nc2, in2, core_ids=list(range(NCORES)))
    OF = np.zeros((B, D, T), dtype=BF)
    for c in range(NCORES):
        for i in range(4):
            pr = c * 4 + i
            OF[pr // 16][(pr % 16) * 64:(pr % 16) * 64 + 64] = r2.results[c]["o4"][i]
    nc3 = _cache_build("projln", build_proj_ln)
    in3 = []
    for c in range(NCORES):
        b, s = divmod(c, 4)
        in3.append(dict(oT=np.ascontiguousarray(OF[b][:, s * TPC:(s + 1) * TPC]),
                        wo=Wo, bo=np.ascontiguousarray(bo.reshape(NCH, 128).T),
                        xT=_fm(x[b, s * TPC:(s + 1) * TPC])))
    r3 = run_bass_kernel_spmd(nc3, in3, core_ids=list(range(NCORES)))
    xo = np.zeros((B, T, D), dtype=np.float32)
    for c in range(NCORES):
        b, s = divmod(c, 4)
        xo[b, s * TPC:(s + 1) * TPC] = r3.results[c]["xn"].T
    return xo


def run_lffn_layer(x, wqkv_l, qs_l, ks_l, wo_l, lg_l, lb_l, w1_l, b1_l, w2_l, b2_l):
    AQ, BQ, AK, BK = host_rotary_tables()
    E2 = host_E2()
    HID = 2730
    w1p = np.zeros((D, 2 * HID_P), dtype=np.float32)
    w1p[:, 0:HID] = w1_l[:, 0:HID]
    w1p[:, HID_P:HID_P + HID] = w1_l[:, HID:2 * HID]
    b1p = np.zeros(2 * HID_P, dtype=np.float32)
    b1p[0:HID] = b1_l[0:HID]
    b1p[HID_P:HID_P + HID] = b1_l[HID:2 * HID]
    w2p = np.zeros((HID_P, D), dtype=np.float32)
    w2p[0:HID] = w2_l

    nc = _cache_build("lffn_qkv", build_lffn_qkv)
    ins = []
    for c in range(NCORES):
        b, s = divmod(c, 4)
        sl = x[b, s * TPC:(s + 1) * TPC]
        xprev = (x[b, s * TPC - 1] if s > 0 else np.zeros(D, dtype=np.float32)).reshape(D, 1)
        t0 = s * TPC
        tiles = {n: np.ascontiguousarray(np.tile(tab[:, t0:t0 + TPC], (2, 1)))
                 for n, tab in (("AQ", AQ), ("BQ", BQ), ("AK", AK), ("BK", BK))}
        ins.append(dict(xT=_fm(sl), xprevT=np.ascontiguousarray(xprev), wqkv=wqkv_l,
                        qs2=np.tile(qs_l, 2).reshape(128, 1).astype(np.float32),
                        ks2=np.tile(ks_l, 2).reshape(128, 1).astype(np.float32),
                        E2=E2, **tiles))
    r1 = run_bass_kernel_spmd(nc, ins, core_ids=list(range(NCORES)))
    QF = np.zeros((B, INNER, T), dtype=BF)
    KF = np.zeros((B, INNER, T), dtype=BF)
    VT = np.zeros((B, T, INNER), dtype=BF)
    for c in range(NCORES):
        b, s = divmod(c, 4)
        QF[b][:, s * TPC:(s + 1) * TPC] = r1.results[c]["qx"]
        KF[b][:, s * TPC:(s + 1) * TPC] = r1.results[c]["kx"]
        VT[b][s * TPC:(s + 1) * TPC] = r1.results[c]["vx"]

    nc2 = _cache_build("lffn_attn", build_lffn_attn)
    ins2 = []
    for c in range(NCORES):
        b, s = divmod(c, 4)
        t0 = s * TPC
        if s > 0:
            kfb = KF[b][:, t0 - 512: t0 + TPC]
            vfb = VT[b][t0 - 512: t0 + TPC]
        else:
            kfb = np.concatenate([np.zeros((INNER, 512), dtype=BF), KF[b][:, 0:TPC]], axis=1)
            vfb = np.concatenate([np.zeros((512, INNER), dtype=BF), VT[b][0:TPC]], axis=0)
        ins2.append(dict(qx=np.ascontiguousarray(QF[b][:, t0:t0 + TPC]),
                         kfb=np.ascontiguousarray(kfb), vfb=np.ascontiguousarray(vfb),
                         masks=host_local_masks(first_window_of_seq=(s == 0)),
                         ident=_IDENT))
    r2 = run_bass_kernel_spmd(nc2, ins2, core_ids=list(range(NCORES)))

    nc3 = _cache_build("lffn_proj", build_lffn_proj)
    ins3 = []
    for c in range(NCORES):
        b, s = divmod(c, 4)
        ins3.append(dict(oT=r2.results[c]["ox"], wo=wo_l,
                         xT=_fm(x[b, s * TPC:(s + 1) * TPC])))
    r3 = run_bass_kernel_spmd(nc3, ins3, core_ids=list(range(NCORES)))
    x2 = np.zeros((B, T, D), dtype=np.float32)
    for c in range(NCORES):
        b, s = divmod(c, 4)
        x2[b, s * TPC:(s + 1) * TPC] = r3.results[c]["x2"].T

    nc4 = _cache_build("lffn_ffn", build_lffn_ffn)
    ins4 = []
    for c in range(NCORES):
        b, s = divmod(c, 4)
        xprev = (x2[b, s * TPC - 1] if s > 0 else np.zeros(D, dtype=np.float32)).reshape(D, 1)
        ins4.append(dict(x2T=_fm(x2[b, s * TPC:(s + 1) * TPC]),
                         x2prevT=np.ascontiguousarray(xprev),
                         lg=np.ascontiguousarray(lg_l.reshape(NCH, 128).T),
                         lb=np.ascontiguousarray(lb_l.reshape(NCH, 128).T),
                         w1p=w1p,
                         b1p=np.ascontiguousarray(b1p.reshape(2 * HID_P // 128, 128).T),
                         w2p=w2p,
                         b2=np.ascontiguousarray(b2_l.reshape(NCH, 128).T)))
    r4 = run_bass_kernel_spmd(nc4, ins4, core_ids=list(range(NCORES)))
    x3 = np.zeros((B, T, D), dtype=np.float32)
    for c in range(NCORES):
        b, s = divmod(c, 4)
        x3[b, s * TPC:(s + 1) * TPC] = r4.results[c]["x3"].T
    return x3


def run_moe_block(x, router_w, router_b, expert_up, expert_down):
    upTh = np.ascontiguousarray(expert_up.transpose(0, 2, 1))
    downTh = np.ascontiguousarray(expert_down.transpose(0, 2, 1))
    nc = _cache_build("moe", build_moe)
    ins = []
    for c in range(NCORES):
        b, s = divmod(c, 4)
        ins.append(dict(xT=_fm(x[b, s * TPC:(s + 1) * TPC]), rw=router_w,
                        rbrow=router_b.reshape(1, NE), upT=upTh, downT=downTh,
                        ident=_IDENT, rowsel=_ROWSEL))
    r = run_bass_kernel_spmd(nc, ins, core_ids=list(range(NCORES)))
    xo = np.zeros((B, T, D), dtype=np.float32)
    for c in range(NCORES):
        b, s = divmod(c, 4)
        xo[b, s * TPC:(s + 1) * TPC] = r.results[c]["x4"].T
    return xo


def run_addln(x, y):
    nc = _cache_build("addln", build_addln)
    ins = []
    for c in range(NCORES):
        b, s = divmod(c, 4)
        ins.append(dict(xT=_fm(x[b, s * TPC:(s + 1) * TPC]), yT=_fm(y[b, s * TPC:(s + 1) * TPC])))
    r = run_bass_kernel_spmd(nc, ins, core_ids=list(range(NCORES)))
    xo = np.zeros((B, T, D), dtype=np.float32)
    for c in range(NCORES):
        b, s = divmod(c, 4)
        xo[b, s * TPC:(s + 1) * TPC] = r.results[c]["xn"].T
    return xo


def kernel_fused(inp):
    g = {k: np.asarray(v) for k, v in inp.items()}
    x0 = np.asarray(g["L0_attn_x"], dtype=np.float32)
    wA = np.asarray(g["L1_attn_w_attn"], dtype=np.float32)
    bA = np.asarray(g["L1_attn_b_attn"], dtype=np.float32)
    AQ, BQ, AK, BK = host_rotary_tables()
    E2h = host_E2()
    HID = 2730
    trih = host_tri_masks()
    upTh = np.ascontiguousarray(np.asarray(g["L1_ffn_expert_up"], dtype=np.float32).transpose(0, 2, 1))
    downTh = np.ascontiguousarray(np.asarray(g["L1_ffn_expert_down"], dtype=np.float32).transpose(0, 2, 1))

    def colmat(v):
        return np.ascontiguousarray(np.asarray(v, dtype=np.float32).reshape(NCH, 128).T)

    lws = []
    for l in range(2):
        w1_l = np.asarray(g["L0_ffn_w1"][l], dtype=np.float32)
        b1_l = np.asarray(g["L0_ffn_b1"][l], dtype=np.float32)
        w2_l = np.asarray(g["L0_ffn_w2"][l], dtype=np.float32)
        w1p = np.zeros((D, 2 * HID_P), dtype=np.float32)
        w1p[:, 0:HID] = w1_l[:, 0:HID]
        w1p[:, HID_P:HID_P + HID] = w1_l[:, HID:2 * HID]
        b1p = np.zeros(2 * HID_P, dtype=np.float32)
        b1p[0:HID] = b1_l[0:HID]
        b1p[HID_P:HID_P + HID] = b1_l[HID:2 * HID]
        w2p = np.zeros((HID_P, D), dtype=np.float32)
        w2p[0:HID] = w2_l
        sl = str(l)
        lws.append({
            "wqkv" + sl: np.ascontiguousarray(np.asarray(g["L0_ffn_wqkv"][l], dtype=np.float32)),
            "qs2" + sl: np.tile(np.asarray(g["L0_ffn_q_scale"][l], dtype=np.float32), 2).reshape(128, 1),
            "ks2" + sl: np.tile(np.asarray(g["L0_ffn_k_scale"][l], dtype=np.float32), 2).reshape(128, 1),
            "lwo" + sl: np.ascontiguousarray(np.asarray(g["L0_ffn_wo"][l], dtype=np.float32)),
            "lg" + sl: colmat(g["L0_ffn_ln_g"][l]),
            "lb" + sl: colmat(g["L0_ffn_ln_b"][l]),
            "w1p" + sl: w1p,
            "b1p" + sl: np.ascontiguousarray(b1p.reshape(2 * HID_P // 128, 128).T),
            "w2p" + sl: w2p,
            "b2" + sl: colmat(g["L0_ffn_b2"][l]),
        })

    dense0 = dict(
        wq0=np.asarray(g["L0_attn_Wq"], dtype=np.float32), wk0=np.asarray(g["L0_attn_Wk"], dtype=np.float32),
        wv0=np.asarray(g["L0_attn_Wv"], dtype=np.float32), wo0=np.asarray(g["L0_attn_Wo"], dtype=np.float32),
        bq0=colmat(g["L0_attn_bq"]), bk0=colmat(g["L0_attn_bk"]), bo0=colmat(g["L0_attn_bo"]),
        bvrow0=np.asarray(g["L0_attn_bv"], dtype=np.float32).reshape(1, D))
    dense1 = dict(
        wq1=np.ascontiguousarray(wA[:, 0:D]), wk1=np.ascontiguousarray(wA[:, D:2 * D]),
        wv1=np.ascontiguousarray(wA[:, 2 * D:3 * D]),
        wo1=np.asarray(g["L1_attn_w_proj"], dtype=np.float32),
        bq1=colmat(bA[0:D]), bk1=colmat(bA[D:2 * D]), bo1=colmat(g["L1_attn_b_proj"]),
        bvrow1=bA[2 * D:3 * D].reshape(1, D))

    nc = _cache_build("mega", build_mega)
    ins = []
    for c in range(NCORES):
        b, s = divmod(c, 4)
        t0 = s * TPC
        tiles = {"t" + n: np.ascontiguousarray(np.tile(tab[:, t0:t0 + TPC], (2, 1)))
                 for n, tab in (("AQ", AQ), ("BQ", BQ), ("AK", AK), ("BK", BK))}
        ixv = np.zeros((1, 64), dtype=np.uint32)
        ixv[0, 0] = b
        ixv[0, 1] = s
        ixv[0, 2] = (c + 7) % 8
        ixv[0, 3] = s
        bm = np.ones((128, NCH), dtype=np.float32) if s > 0 else np.zeros((128, NCH), dtype=np.float32)
        m = dict(x0T=_fm(x0[b, t0:t0 + TPC]), **dense0, **dense1, **lws[0], **lws[1],
                 **tiles, E2=E2h, lmasks=host_local_masks(first_window_of_seq=(s == 0)),
                 tri=trih, ident=_IDENT, rowsel=_ROWSEL,
                 rw=np.asarray(g["L1_ffn_router_w"], dtype=np.float32),
                 rbrow=np.asarray(g["L1_ffn_router_b"], dtype=np.float32).reshape(1, NE),
                 upT=upTh, downT=downTh, bmask=bm, ix=ixv)
        ins.append(m)
    r = run_bass_kernel_spmd(nc, ins, core_ids=list(range(NCORES)))
    xo = np.zeros((B, T, D), dtype=np.float32)
    for c in range(NCORES):
        b, s = divmod(c, 4)
        xo[b, s * TPC:(s + 1) * TPC] = r.results[c]["out"].T
    if MEGA_DEBUG:
        global MEGA_TAPS, MEGA_RAW
        MEGA_TAPS = np.zeros((5, B, T, D), dtype=np.float32)
        for c in range(NCORES):
            b, s = divmod(c, 4)
            for i in range(5):
                MEGA_TAPS[i, b, s * TPC:(s + 1) * TPC] = r.results[c]["xdbg"][i].T
        MEGA_RAW = r.results
    return xo


def kernel(L0_attn_x, L0_attn_Wq, L0_attn_bq, L0_attn_Wk, L0_attn_bk, L0_attn_Wv,
           L0_attn_bv, L0_attn_Wo, L0_attn_bo, L0_ffn_wqkv, L0_ffn_q_scale,
           L0_ffn_k_scale, L0_ffn_wo, L0_ffn_ln_g, L0_ffn_ln_b, L0_ffn_w1, L0_ffn_b1,
           L0_ffn_w2, L0_ffn_b2, L1_attn_w_attn, L1_attn_b_attn, L1_attn_w_proj,
           L1_attn_b_proj, L1_ffn_router_w, L1_ffn_router_b, L1_ffn_expert_up,
           L1_ffn_expert_down):
    g = {k: np.ascontiguousarray(np.asarray(v), dtype=None) for k, v in list(locals().items())
         if k != "g"}
    return kernel_fused(g)
    x = np.asarray(g["L0_attn_x"], dtype=np.float32)
    x = run_dense_attn_block(x, g["L0_attn_Wq"], g["L0_attn_bq"], g["L0_attn_Wk"],
                             g["L0_attn_bk"], g["L0_attn_Wv"], g["L0_attn_bv"],
                             g["L0_attn_Wo"], g["L0_attn_bo"])
    y = x
    for l in range(2):
        y = run_lffn_layer(y, g["L0_ffn_wqkv"][l], g["L0_ffn_q_scale"][l],
                           g["L0_ffn_k_scale"][l], g["L0_ffn_wo"][l], g["L0_ffn_ln_g"][l],
                           g["L0_ffn_ln_b"][l], g["L0_ffn_w1"][l], g["L0_ffn_b1"][l],
                           g["L0_ffn_w2"][l], g["L0_ffn_b2"][l])
    x = run_addln(x, y)
    wA = np.asarray(g["L1_attn_w_attn"], dtype=np.float32)
    bA = np.asarray(g["L1_attn_b_attn"], dtype=np.float32)
    x = run_dense_attn_block(x, np.ascontiguousarray(wA[:, 0:D]), bA[0:D],
                             np.ascontiguousarray(wA[:, D:2 * D]), bA[D:2 * D],
                             np.ascontiguousarray(wA[:, 2 * D:3 * D]), bA[2 * D:3 * D],
                             g["L1_attn_w_proj"], g["L1_attn_b_proj"])
    x = run_moe_block(x, g["L1_ffn_router_w"], g["L1_ffn_router_b"],
                      np.asarray(g["L1_ffn_expert_up"], dtype=np.float32),
                      np.asarray(g["L1_ffn_expert_down"], dtype=np.float32))
    return x


# version 06
# speedup vs baseline: 1.2087x; 1.0929x over previous
import numpy as np
import ml_dtypes
from contextlib import ExitStack

import concourse.bass as bass
import concourse.mybir as mybir
from concourse.bass_utils import run_bass_kernel_spmd

F32 = mybir.dt.float32
BF16 = mybir.dt.bfloat16
AF = mybir.ActivationFunctionType
ALU = mybir.AluOpType
AX = mybir.AxisListType
BF = ml_dtypes.bfloat16

NCORES = 8
B, T, D = 2, 8192, 1024
TPC = 2048
NCH = D // 128
NTT = TPC // 512

_NC_CACHE = {}


class Prog:

    ENGS = ("T", "V", "S", "G", "Y")

    def __init__(self, nc, es):
        self.nc = nc
        self.es = es
        self.q = {e: [] for e in self.ENGS}
        self.sem = {e: es.enter_context(nc.semaphore(f"sem{e}")) for e in self.ENGS}
        self.cnt = {e: 0 for e in self.ENGS}
        self.gsem = {}

    def op(self, eng, fn, waits=(), inc=True):
        waits = [w for w in waits if w is not None]
        tok = None
        if inc:
            self.cnt[eng] += 1
            tok = (eng, self.cnt[eng])
        self.q[eng].append((fn, tuple(waits), tok))
        return tok

    def dma(self, eng, out, in_, waits=(), group=None, **kw):
        assert group is not None
        waits = [w for w in waits if w is not None]
        if group not in self.gsem:
            self.gsem[group] = [self.es.enter_context(self.nc.semaphore(f"g_{group}")), 0]
        ent = self.gsem[group]
        ent[1] += 16
        tok = ("g:" + group, ent[1])
        sem = ent[0]

        def fn(e, out=out, in_=in_, kw=kw):
            return e.dma_start(out=out, in_=in_, **kw).then_inc(sem, 16)

        self.q[eng].append((fn, tuple(waits), None))
        return tok

    def collective(self, kind, ins, outs, waits=()):
        waits = [w for w in waits if w is not None]
        self.cnt["G"] += 1
        tok = ("G", self.cnt["G"])
        sem = self.sem["G"]

        def fn(e, ins=ins, outs=outs):
            return e.collective_compute(
                kind, ALU.bypass, replica_groups=[list(range(NCORES))],
                ins=ins, outs=outs).then_inc(sem, 1)

        self.q["G"].append((fn, tuple(waits), None))
        return tok

    def raw_dma(self, eng, fn, waits=(), group=None):
        assert group is not None
        waits = [w for w in waits if w is not None]
        if group not in self.gsem:
            self.gsem[group] = [self.es.enter_context(self.nc.semaphore(f"g_{group}")), 0]
        ent = self.gsem[group]
        ent[1] += 16
        tok = ("g:" + group, ent[1])
        sem = ent[0]

        def fn2(e, fn=fn, sem=sem):
            return fn(e).then_inc(sem, 16)

        self.q[eng].append((fn2, tuple(waits), None))
        return tok

    def gtotal(self, group):
        return ("g:" + group, self.gsem[group][1])

    def barrier(self):
        waits = tuple(("" + e, self.cnt[e]) for e in self.ENGS if self.cnt[e] > 0)
        gwaits = tuple(("g:" + g, ent[1]) for g, ent in self.gsem.items())
        for e in self.ENGS:
            self.q[e].append(((lambda engine: None), waits + gwaits, None))

    def _resolve(self, t):
        name, v = t
        if name.startswith("g:"):
            return self.gsem[name[2:]][0], v
        return self.sem[name], v

    def emit(self):
        for gname, (sem, cnt) in self.gsem.items():
            self.q["Y"].append(((lambda e: None), (("g:" + gname, cnt),), None))
        block = self.es.enter_context(self.nc.Block())

        def body(engine, key):
            for fn, waits, tok in self.q[key]:
                for t in waits:
                    s, v = self._resolve(t)
                    engine.wait_ge(s, v)
                ins = fn(engine)
                if tok is not None:
                    ins.then_inc(self.sem[tok[0]], 1)

        block.tensor(lambda e: body(e, "T"))
        block.vector(lambda e: body(e, "V"))
        block.scalar(lambda e: body(e, "S"))
        block.gpsimd(lambda e: body(e, "G"))
        block.sync(lambda e: body(e, "Y"))


class NCProxy:

    def __init__(self, nc, tag):
        self._nc = nc
        self._tag = tag

    def sbuf_tensor(self, name, shape, dtype, **kw):
        return self._nc.sbuf_tensor(self._tag + name, shape, dtype, **kw)

    def psum_tensor(self, name, shape, dtype=F32, **kw):
        return self._nc.psum_tensor(self._tag + name, shape, dtype, **kw)

    def __getattr__(self, k):
        return getattr(self._nc, k)



def alloc_ln_bufs(nc, es, tag="", reuse=None):
    b = {}
    b["ln_sq"] = es.enter_context(nc.sbuf_tensor(f"lnsq{tag}", [128, 512], F32))
    b["ln_stat"] = es.enter_context(nc.sbuf_tensor(f"lnst{tag}", [1, 1024], F32))
    b["ln_tmp"] = es.enter_context(nc.sbuf_tensor(f"lntmp{tag}", [128, 512], F32))
    b["ones_col"] = es.enter_context(nc.sbuf_tensor(f"lnoc{tag}", [128, 1], F32))
    b["ones1"] = es.enter_context(nc.sbuf_tensor(f"lno1{tag}", [1, 128], F32))
    if reuse is None:
        b["ln_psS"] = es.enter_context(nc.psum_tensor(f"lnpsS{tag}", [1, 512], F32))[:]
        b["ln_psS2"] = es.enter_context(nc.psum_tensor(f"lnpsS2{tag}", [1, 512], F32))[:]
        b["ln_psB"] = es.enter_context(nc.psum_tensor(f"lnpsB{tag}", [128, 512], F32))[:]
        b["ln_psB2"] = es.enter_context(nc.psum_tensor(f"lnpsB2{tag}", [128, 512], F32))[:]
    else:
        b["ln_psS"] = reuse["psS"]
        b["ln_psS2"] = reuse["psS2"]
        b["ln_psB"] = reuse["psB"]
        b["ln_psB2"] = reuse["psB2"]
    return b


def emit_ln_init(p, b):
    t1 = p.op("V", lambda v: v.memset(b["ones_col"][:], 1.0))
    t2 = p.op("V", lambda v: v.memset(b["ones1"][:], 1.0))
    return [t1, t2]


def emit_ln(p, b, x_ap_fn, n_tok, out_fn, gamma_scale=None, gamma_bias=None,
            eps=1e-5, pre_waits=(), extra_out_fn=None, init_toks=()):
    sq, stat, tmp = b["ln_sq"], b["ln_stat"], b["ln_tmp"]
    psS, psS2, psB, psB2 = b["ln_psS"], b["ln_psS2"], b["ln_psB"], b["ln_psB2"]
    ones_col, ones1 = b["ones_col"], b["ones1"]
    n_tt = n_tok // 512
    out_toks = {}
    prev_apply = []
    prev_stats = []
    last = None
    for tt in range(n_tt):
        mu = stat[:, 0:512]
        va = stat[:, 512:1024]
        s_tok = None
        for c in range(NCH):
            w = list(pre_waits) + list(init_toks) + prev_stats if c == 0 else []
            s_tok = p.op("T", (lambda t, c=c, tt=tt: t.matmul(
                psS[:], ones_col[:], x_ap_fn(c, tt), start=(c == 0), stop=(c == NCH - 1))), w)
        prev_mt = None
        for c in range(NCH):
            w = (list(pre_waits) + prev_stats) if c == 0 else [prev_mt]
            st = p.op("S", (lambda s, c=c, tt=tt: s.activation(sq[:], x_ap_fn(c, tt), AF.Square)), w)
            prev_mt = p.op("T", (lambda t, c=c: t.matmul(
                psS2[:], ones_col[:], sq[:], start=(c == 0), stop=(c == NCH - 1))), [st])
        t1 = p.op("V", (lambda v, mu=mu: v.tensor_scalar_mul(mu, psS, 1.0 / D)), [s_tok])
        t2 = p.op("V", (lambda v, va=va: v.tensor_scalar_mul(va, psS2, 1.0 / D)), [prev_mt])
        t3 = p.op("V", (lambda v, mu=mu: v.tensor_mul(sq[0:1, 0:512], mu, mu)), [t1, t2])
        t4 = p.op("V", (lambda v, va=va: v.tensor_sub(va, va, sq[0:1, 0:512])), [t3])
        t4b = p.op("V", (lambda v, va=va: v.tensor_scalar_add(va, va, float(eps))), [t4])
        t5 = p.op("S", (lambda s, va=va: s.activation(va, va, AF.Sqrt)), [t4b])
        t6 = p.op("V", (lambda v, va=va: v.reciprocal(va, va)), [t5])
        b1 = p.op("T", (lambda t, mu=mu: t.matmul(psB, ones1[:], mu, start=True, stop=True)),
                  [t6] + prev_apply)
        b2 = p.op("T", (lambda t, va=va: t.matmul(psB2, ones1[:], va, start=True, stop=True)), [b1])
        prev_stats = [t3, t6]
        prev_apply = []
        for c in range(NCH):
            tA = p.op("V", (lambda v, c=c, tt=tt: v.tensor_sub(tmp[:], x_ap_fn(c, tt), psB)),
                      [b2] + ([prev_apply[-1]] if prev_apply else []))
            if gamma_scale is None:
                tB = p.op("V", (lambda v, c=c, tt=tt: v.tensor_mul(out_fn(c, tt), tmp[:], psB2)), [tA])
            else:
                tB0 = p.op("V", (lambda v: v.tensor_mul(tmp[:], tmp[:], psB2)), [tA])
                tB = p.op("V", (lambda v, c=c, tt=tt: v.tensor_scalar(
                    out_fn(c, tt), tmp[:], gamma_scale(c), gamma_bias(c), ALU.mult, ALU.add)), [tB0])
            if extra_out_fn is not None:
                tB = p.op("S", (lambda s, c=c, tt=tt: s.activation(
                    extra_out_fn(c, tt), out_fn(c, tt), AF.Copy)), [tB])
            out_toks[(c, tt)] = tB
            prev_apply.append(tB)
            last = tB
        prev_stats = prev_stats + [b1, b2]
    return out_toks, last



def emit_qkv(p, nc, es, xT, wq, wk, wv, bq, bk, bvrow, qx, kx, vx, tag=""):
    xbf = es.enter_context(nc.sbuf_tensor("xbf", [128, NCH * TPC], BF16))
    wqb = es.enter_context(nc.sbuf_tensor("wqb", [128, NCH * D], BF16))
    wkb = es.enter_context(nc.sbuf_tensor("wkb", [128, NCH * D], BF16))
    wvb = es.enter_context(nc.sbuf_tensor("wvb", [128, NCH * D], BF16))
    bqs = es.enter_context(nc.sbuf_tensor("bqs", [128, NCH], F32))
    bks = es.enter_context(nc.sbuf_tensor("bks", [128, NCH], F32))
    bvr = es.enter_context(nc.sbuf_tensor("bvr", [1, D], F32))
    bvb = es.enter_context(nc.sbuf_tensor("bvb", [128, D], F32))
    ones1 = es.enter_context(nc.sbuf_tensor("ones1", [1, 128], F32))
    osb = es.enter_context(nc.sbuf_tensor("osb", [128, 2 * TPC], BF16))
    vsb = es.enter_context(nc.sbuf_tensor("vsb", [128, 2 * D], BF16))
    ps = [es.enter_context(nc.psum_tensor(f"ps{i}", [128, 512], F32)) for i in range(2)]
    psv = [es.enter_context(nc.psum_tensor(f"psv{i}", [128, 512], F32)) for i in range(2)]
    psb = es.enter_context(nc.psum_tensor("psb", [128, 512], F32))

    ldx = p.dma("G", xbf[:].rearrange("p (c t) -> p c t", t=TPC),
                xT.rearrange("(c p) t -> p c t", p=128), group="ldx")
    ldwq = p.dma("G", wqb[:].rearrange("p (c o) -> p c o", o=D), wq.rearrange("(c p) o -> p c o", p=128), group="wq")
    ldwk = p.dma("G", wkb[:].rearrange("p (c o) -> p c o", o=D), wk.rearrange("(c p) o -> p c o", p=128), group="wk")
    ldwv = p.dma("G", wvb[:].rearrange("p (c o) -> p c o", o=D), wv.rearrange("(c p) o -> p c o", p=128), group="wv")
    ldbq = p.dma("Y", bqs[:], bq, group="bq")
    ldbk = p.dma("Y", bks[:], bk, group="bk")
    ldbv = p.dma("Y", bvr[:], bvrow, group="bv")
    onet = p.op("V", lambda v: v.memset(ones1[:], 1.0))
    prevc = onet
    bvb_toks = []
    for h in range(2):
        tb = p.op("T", (lambda t, h=h: t.matmul(psb[:], ones1[:], bvr[:, h * 512:(h + 1) * 512],
                                                start=True, stop=True)), [prevc, ldbv])
        prevc = p.op("V", (lambda v, h=h: v.tensor_copy(bvb[:, h * 512:(h + 1) * 512], psb[:])), [tb])
        bvb_toks.append(prevc)

    bank_free = [None, None]
    half_free = [None, None]
    for name, wb, bsbuf, ldwt, ldbt, out_dram in (
            ("q", wqb, bqs, ldwq, ldbq, qx), ("k", wkb, bks, ldwk, ldbk, kx)):
        for oc in range(NCH):
            half = oc % 2
            st_toks = []
            for tt in range(NTT):
                bi = tt % 2
                pt = ps[bi]
                w0 = [ldx, ldwt, ldbt, bank_free[bi], half_free[half]]
                mm = None
                for ic in range(NCH):
                    mm = p.op("T", (lambda t, ic=ic, oc=oc, tt=tt, pt=pt, wb=wb: t.matmul(
                        pt[:], wb[:, ic * D + oc * 128: ic * D + oc * 128 + 128],
                        xbf[:, ic * TPC + tt * 512: ic * TPC + (tt + 1) * 512],
                        start=(ic == 0), stop=(ic == NCH - 1))), w0 if ic == 0 else [])
                    w0 = []
                ev = p.op("V", (lambda v, tt=tt, pt=pt, oc=oc, half=half, bsbuf=bsbuf: v.tensor_scalar_add(
                    osb[:, half * TPC + tt * 512: half * TPC + (tt + 1) * 512], pt[:],
                    bsbuf[:, oc:oc + 1])), [mm])
                bank_free[bi] = ev
                st_toks.append(ev)
            dt = p.dma("G", out_dram.rearrange("(c p) t -> p c t", p=128)[:, oc, :],
                       osb[:, half * TPC: half * TPC + TPC], waits=st_toks,
                       group=f"oh{half}_{name}")
            half_free[half] = dt

    vbank_free = [None, None]
    vhalf_free = None
    for tc in range(TPC // 128):
        st_toks = []
        for half in range(2):
            pt = psv[half]
            w0 = [ldx, ldwv, vbank_free[half], vhalf_free]
            mm = None
            for ic in range(NCH):
                mm = p.op("T", (lambda t, ic=ic, tc=tc, half=half, pt=pt: t.matmul(
                    pt[:], xbf[:, ic * TPC + tc * 128: ic * TPC + (tc + 1) * 128],
                    wvb[:, ic * D + half * 512: ic * D + (half + 1) * 512],
                    start=(ic == 0), stop=(ic == NCH - 1))), w0 if ic == 0 else [])
                w0 = []
            ev = p.op("V", (lambda v, half=half, pt=pt: v.tensor_add(
                vsb[:, half * D // 2 + 0: 0] if False else vsb[:, half * 512:(half + 1) * 512],
                pt[:], bvb[:, half * 512:(half + 1) * 512])), [mm] + bvb_toks)
            vbank_free[half] = ev
            st_toks.append(ev)
        vhalf_free = p.dma("G", vx[tc * 128:(tc + 1) * 128, :], vsb[:, 0:D], waits=st_toks,
                           group="vout")

    return


def build_qkv_launch():
    nc = bass.Bass("TRN2", target_bir_lowering=False)
    names_i = [("xT", [D, TPC], F32), ("wq", [D, D], F32), ("wk", [D, D], F32),
               ("wv", [D, D], F32), ("bq", [128, NCH], F32), ("bk", [128, NCH], F32),
               ("bvrow", [1, D], F32)]
    names_o = [("qx", [D, TPC], BF16), ("kx", [D, TPC], BF16), ("vx", [TPC, D], BF16)]
    aps = {n: nc.dram_tensor(n, s, d, kind="ExternalInput").ap() for n, s, d in names_i}
    aps.update({n: nc.dram_tensor(n, s, d, kind="ExternalOutput").ap() for n, s, d in names_o})
    es = ExitStack()
    p = Prog(nc, es)
    emit_qkv(p, nc, es, **aps)
    p.emit()
    es.close()
    return nc



def emit_flash(p, nc, es, ident, trimask, o4, loader):
    SC = 0.125
    qe = es.enter_context(nc.sbuf_tensor("qe", [65, T], BF16))
    ke = es.enter_context(nc.sbuf_tensor("ke", [65, T], BF16))
    ve = es.enter_context(nc.sbuf_tensor("ve", [128, 65 * (T // 128)], BF16))
    idn = es.enter_context(nc.sbuf_tensor("idn", [128, 128], F32))
    tri = es.enter_context(nc.sbuf_tensor("tri", [128, 4 * 512], BF16))
    mneg = es.enter_context(nc.sbuf_tensor("mneg", [128, 4], F32))
    mtmp = es.enter_context(nc.sbuf_tensor("mtmp", [128, 2], F32))
    pt_sb = es.enter_context(nc.sbuf_tensor("pt_sb", [128, 2048], BF16))
    rb = es.enter_context(nc.sbuf_tensor("rb", [64, 512], F32))
    rec = es.enter_context(nc.sbuf_tensor("rec", [1, 512], F32))
    ones1 = es.enter_context(nc.sbuf_tensor("ones1", [1, 64], F32))
    ob = es.enter_context(nc.sbuf_tensor("ob", [64, 2 * 512], BF16))
    psA = [es.enter_context(nc.psum_tensor(f"psA{i}", [128, 512], F32)) for i in range(2)]
    psM = es.enter_context(nc.psum_tensor("psM", [1, 128], F32))
    psT = es.enter_context(nc.psum_tensor("psT", [128, 2048], F32))
    psO = es.enter_context(nc.psum_tensor("psO", [65, 512], F32))
    psR = psT[0:64, 0:512]
    p_ = p

    ldi = p.dma("Y", idn[:], ident, group="ldi")
    ldt = p.dma("Y", tri[:].rearrange("p (s n) -> p s n", n=512), trimask.rearrange("s p n -> p s n"),
                group="ldt")
    one1 = p.op("V", lambda v: v.memset(ones1[:], 1.0))
    onek = p.op("V", lambda v: v.memset(ke[64:65, :], 1.0))
    onev = p.op("V", lambda v: v.memset(ve[:].rearrange("p (c e) -> p c e", e=65)[:, :, 64], 1.0))

    NQG = T // 512
    prev_pair_done = [onek, onev]
    prev_cpm = None
    ob_last = [None, None]
    pending_norm = [None]
    last_st0 = [None]

    for pair in range(4):
        ldq, ldk, ldv = loader(pair, prev_pair_done, qe, ke, ve)
        prev_avm = [None]
        prev_red = [None, None]
        for g in range(NQG):
            nkc = g + 1
            if pending_norm[0] is not None:
                pending_norm[0]()
                pending_norm[0] = None
            tr_toks = []
            for qt in range(4):
                qsl = qe[0:64, g * 512 + qt * 128: g * 512 + qt * 128 + 128]
                red_prev = None
                for kc in range(nkc):
                    bi = kc % 2
                    w = [ldq, ldk] if (qt == 0 and kc == 0) else []
                    w = w + [prev_red[bi]]
                    mm = p.op("T", (lambda t, qsl=qsl, kc=kc, bi=bi: t.matmul(
                        psA[bi][:], qsl, ke[0:64, kc * 512:(kc + 1) * 512],
                        start=True, stop=True)), w)
                    if kc == 0:
                        red_prev = p.op("V", (lambda v, qt=qt, bi=bi: v.reduce_max(
                            mneg[:, qt:qt + 1], psA[bi][:], axis=AX.X, negate=True)), [mm])
                    else:
                        r1 = p.op("V", (lambda v, bi=bi: v.reduce_max(
                            mtmp[:, bi:bi + 1], psA[bi][:], axis=AX.X, negate=True)), [mm])
                        red_prev = p.op("V", (lambda v, qt=qt, bi=bi: v.tensor_tensor(
                            mneg[:, qt:qt + 1], mneg[:, qt:qt + 1], mtmp[:, bi:bi + 1],
                            ALU.min)), [r1])
                    prev_red[bi] = red_prev
                trm = p.op("T", (lambda t, qt=qt: t.transpose(psM[:], mneg[:, qt:qt + 1], idn[:])),
                           [red_prev, ldi, prev_cpm])
                cpm = p.op("V", (lambda v, g=g, qt=qt: v.tensor_copy(
                    qe[64:65, g * 512 + qt * 128: g * 512 + qt * 128 + 128], psM[:])), [trm])
                prev_cpm = cpm
                tr_toks.append(cpm)
            qg = qe[:, g * 512:(g + 1) * 512]
            first_st_waits = tr_toks + [prev_avm[0]]
            ex_prev = [None, None]
            avm = None
            for kc in range(nkc):
                for hf in range(2):
                    mm2 = None
                    for s2 in range(2):
                        s = hf * 2 + s2
                        w = list(first_st_waits) + [ex_prev[hf]]
                        if s == 0:
                            w = w + [last_st0[0]]
                        first_st_waits = []
                        mm2 = p.op("T", (lambda t, kc=kc, s=s, qg=qg: t.matmul(
                            psT[:, s * 512:(s + 1) * 512],
                            ke[:, kc * 512 + s * 128: kc * 512 + s * 128 + 128], qg,
                            start=True, stop=True)), w)
                        if s == 0:
                            last_st0[0] = mm2
                    ex = p.op("S", (lambda s_, hf=hf: s_.activation(
                        pt_sb[:, hf * 1024:(hf + 1) * 1024],
                        psT[:, hf * 1024:(hf + 1) * 1024], AF.Exp, scale=SC)),
                        [mm2, ("avdep", None)] if False else [mm2, prev_avm[0]])
                    ex_prev[hf] = ex
                    dep = [ex]
                    if kc == g:
                        for s2 in range(2):
                            s = hf * 2 + s2
                            dep = [p.op("V", (lambda v, s=s: v.tensor_mul(
                                pt_sb[:, s * 512:(s + 1) * 512], pt_sb[:, s * 512:(s + 1) * 512],
                                tri[:, s * 512:(s + 1) * 512])), dep + ([ldt] if s2 == 0 else []))]
                    for s2 in range(2):
                        s = hf * 2 + s2
                        w = (dep + [ldv]) if s2 == 0 else []
                        avm = p.op("T", (lambda t, kc=kc, s=s, g=g: t.matmul(
                            psO[:], ve[:, (kc * 4 + s) * 65:(kc * 4 + s) * 65 + 65],
                            pt_sb[:, s * 512:(s + 1) * 512],
                            start=(kc == 0 and s == 0), stop=(kc == nkc - 1 and s == 3))), w)
            prev_avm[0] = avm

            def make_norm(g=g, pair=pair, avm=avm):
                def donorm():
                    rc = p_.op("V", (lambda v: v.reciprocal(rec[:], psO[64:65, :])), [avm])
                    bc = p_.op("T", (lambda t: t.matmul(psR, ones1[:], rec[:],
                                                        start=True, stop=True)),
                               [rc, one1, last_st0[0], ex_prev[0]])
                    last_st0[0] = None
                    cb = p_.op("V", (lambda v: v.tensor_copy(rb[:], psR)), [bc])
                    ot = p_.op("V", (lambda v, g=g: v.tensor_mul(
                        ob[:, (g % 2) * 512:(g % 2) * 512 + 512], psO[0:64, :], rb[:])),
                        [cb, ob_last[g % 2]])
                    ob_last[g % 2] = p_.dma("G", o4[pair][:, g * 512:(g + 1) * 512],
                                            ob[:, (g % 2) * 512:(g % 2) * 512 + 512],
                                            waits=[ot], group=f"ob{g % 2}")
                    prev_avm[0] = ot
                    last_st0[0] = cb
                return donorm
            pending_norm[0] = make_norm()
        pending_norm[0]()
        pending_norm[0] = None
        prev_pair_done = [prev_avm[0], ob_last[0], ob_last[1], prev_cpm]
    return


def default_flash_loader(p, q4, k4, v4):
    def loader(pair, prev_done, qe, ke, ve):
        ldq = p.dma("G", qe[0:64, :], q4[pair], waits=prev_done, group="ldq")
        ldk = p.dma("G", ke[0:64, :], k4[pair], waits=prev_done, group="ldk")
        ldv = p.dma("G", ve[:].rearrange("p (c e) -> p c e", e=65)[:, :, 0:64],
                    v4[pair].rearrange("(c p) d -> p c d", p=128), waits=prev_done,
                    group="ldv")
        return ldq, ldk, ldv
    return loader


def build_flash_dense():
    nc = bass.Bass("TRN2", target_bir_lowering=False)
    q4 = nc.dram_tensor("q4", [4, 64, T], BF16, kind="ExternalInput").ap()
    k4 = nc.dram_tensor("k4", [4, 64, T], BF16, kind="ExternalInput").ap()
    v4 = nc.dram_tensor("v4", [4, T, 64], BF16, kind="ExternalInput").ap()
    ident = nc.dram_tensor("ident", [128, 128], F32, kind="ExternalInput").ap()
    trimask = nc.dram_tensor("trimask", [4, 128, 512], BF16, kind="ExternalInput").ap()
    o4 = nc.dram_tensor("o4", [4, 64, T], BF16, kind="ExternalOutput").ap()
    es = ExitStack()
    p = Prog(nc, es)
    emit_flash(p, nc, es, ident, trimask, o4, default_flash_loader(p, q4, k4, v4))
    p.emit()
    es.close()
    return nc



def emit_proj_ln(p, nc, es, oT, wo, bo, xT, xn, o_loader=None):
    obf = es.enter_context(nc.sbuf_tensor("obf", [128, NCH * TPC], BF16))
    wob = es.enter_context(nc.sbuf_tensor("wob", [128, NCH * D], BF16))
    bos = es.enter_context(nc.sbuf_tensor("bos", [128, NCH], F32))
    xr = es.enter_context(nc.sbuf_tensor("xr", [128, NCH * TPC], F32))
    xo = es.enter_context(nc.sbuf_tensor("xo", [128, 512], F32))
    ps = [es.enter_context(nc.psum_tensor(f"ps{i}", [128, 512], F32)) for i in range(2)]
    lnb = alloc_ln_bufs(nc, es)

    if o_loader is None:
        ldo = p.dma("G", obf[:].rearrange("p (c t) -> p c t", t=TPC),
                    oT.rearrange("(c p) t -> p c t", p=128), group="ldo")
    else:
        ldo = o_loader(obf)
    ldw = p.dma("G", wob[:].rearrange("p (c o) -> p c o", o=D), wo.rearrange("(c p) o -> p c o", p=128), group="ldw")
    ldb = p.dma("Y", bos[:], bo, group="ldb")
    ldxr = p.dma("G", xr[:].rearrange("p (c t) -> p c t", t=TPC),
                 xT.rearrange("(c p) t -> p c t", p=128), group="ldxr")
    init = emit_ln_init(p, lnb)

    bank_free = [None, None]
    res_toks = []
    for oc in range(NCH):
        for tt in range(NTT):
            bi = (oc * NTT + tt) % 2
            pt = ps[bi]
            w0 = [ldo, ldw, ldb, ldxr, bank_free[bi]]
            mm = None
            for ic in range(NCH):
                mm = p.op("T", (lambda t, ic=ic, oc=oc, tt=tt, pt=pt: t.matmul(
                    pt[:], wob[:, ic * D + oc * 128: ic * D + oc * 128 + 128],
                    obf[:, ic * TPC + tt * 512: ic * TPC + (tt + 1) * 512],
                    start=(ic == 0), stop=(ic == NCH - 1))), w0 if ic == 0 else [])
                w0 = []
            ev = p.op("V", (lambda v, pt=pt, oc=oc: v.tensor_scalar_add(
                xo[:], pt[:], bos[:, oc:oc + 1])), [mm])
            bank_free[bi] = ev
            ad = p.op("V", (lambda v, oc=oc, tt=tt: v.tensor_add(
                xr[:, oc * TPC + tt * 512: oc * TPC + (tt + 1) * 512],
                xr[:, oc * TPC + tt * 512: oc * TPC + (tt + 1) * 512], xo[:])), [ev])
            bank_free[bi] = ad
            res_toks.append(ad)

    def x_ap(c, tt):
        return xr[:, c * TPC + tt * 512: c * TPC + (tt + 1) * 512]

    toks, last = emit_ln(p, lnb, x_ap, TPC, out_fn=x_ap, pre_waits=res_toks, init_toks=init)
    for c in range(NCH):
        for tt in range(NTT):
            p.dma("G", xn.rearrange("(c p) t -> p c t", p=128)[:, c, tt * 512:(tt + 1) * 512],
                  x_ap(c, tt), waits=[toks[(c, tt)]], group="xout")
    return



INNER = 512
NQC = 4


def emit_lffn_qkv(p, nc, es, xT, xprevT, wqkv, qs2, ks2, tabs, E2, qx, kx, vx):
    W = 2049
    xbf = es.enter_context(nc.sbuf_tensor("xbf", [128, NCH * W], BF16))
    wb = es.enter_context(nc.sbuf_tensor("wb", [128, NCH * 3 * INNER], BF16))
    qss = es.enter_context(nc.sbuf_tensor("qss", [128, 1], F32))
    kss = es.enter_context(nc.sbuf_tensor("kss", [128, 1], F32))
    tb = {n: es.enter_context(nc.sbuf_tensor("t" + n, [128, TPC], F32)) for n in ("AQ", "BQ", "AK", "BK")}
    e2s = es.enter_context(nc.sbuf_tensor("e2s", [128, 128], F32))
    sqb = es.enter_context(nc.sbuf_tensor("sqb", [128, 512], F32))
    rstd = es.enter_context(nc.sbuf_tensor("rstd", [128, 512], F32))
    qsq = es.enter_context(nc.sbuf_tensor("qsq", [128, 512], F32))
    qrot = es.enter_context(nc.sbuf_tensor("qrot", [128, 512], F32))
    u = es.enter_context(nc.sbuf_tensor("u", [128, 512], F32))
    w_ = es.enter_context(nc.sbuf_tensor("w_", [128, 512], F32))
    osb = es.enter_context(nc.sbuf_tensor("osb", [128, 2 * TPC], BF16))
    vsb = es.enter_context(nc.sbuf_tensor("vsb", [128, 512], BF16))
    pQ = [es.enter_context(nc.psum_tensor(f"pQ{i}", [128, 512], F32)) for i in range(2)]
    pS = es.enter_context(nc.psum_tensor("pS", [128, 512], F32))
    pV = [es.enter_context(nc.psum_tensor(f"pV{i}", [128, 512], F32)) for i in range(2)]

    ldx = p.dma("G", xbf[:].rearrange("p (c t) -> p c t", t=W)[:, :, 1:W],
                xT.rearrange("(c p) t -> p c t", p=128), group="ldx")
    ldxp = p.dma("G", xbf[:].rearrange("p (c t) -> p c t", t=W)[:, :, 0:1],
                 xprevT.rearrange("(c p) t -> p c t", p=128), allow_slow_non_contiguous=True,
                 group="ldxp")
    ldw = p.dma("G", wb[:].rearrange("p (c o) -> p c o", o=3 * INNER),
                wqkv.rearrange("(c p) o -> p c o", p=128), group="ldw")
    ldqs = p.dma("Y", qss[:], qs2, group="ldqs")
    ldks = p.dma("Y", kss[:], ks2, group="ldks")
    ldtb = [p.dma("Y", tb[n][:], tabs[n], group="t" + n) for n in ("AQ", "BQ", "AK", "BK")]
    lde2 = p.dma("Y", e2s[:], E2, group="lde2")
    lds = [ldx, ldxp, ldw, ldqs, ldks, lde2] + ldtb

    def h_ap(ic, t0, n):
        off = 0 if ic >= 4 else 1
        return xbf[:, ic * W + off + t0: ic * W + off + t0 + n]

    Q0, Q1, Q2, Q3 = (slice(0, 32), slice(32, 64), slice(64, 96), slice(96, 128))
    QPART = [(Q0, Q1), (Q1, Q0), (Q2, Q3), (Q3, Q2)]

    bank_free = [None, None]
    half_free = [None, None]
    prev_t3 = None
    for name, qkoff, ssb, A, Bt, out_dram in (
            ("q", 0, qss, "AQ", "BQ", qx), ("k", 4, kss, "AK", "BK", kx)):
        for oc in range(NQC):
            half = oc % 2
            st_toks = []
            for tt in range(NTT):
                bi = tt % 2
                pt = pQ[bi]
                w0 = lds + [bank_free[bi], half_free[half]]
                mm = None
                for ic in range(NCH):
                    mm = p.op("T", (lambda t, ic=ic, oc=oc, tt=tt, pt=pt, qkoff=qkoff: t.matmul(
                        pt[:], wb[:, ic * 3 * INNER + (qkoff + oc) * 128: ic * 3 * INNER + (qkoff + oc) * 128 + 128],
                        h_ap(ic, tt * 512, 512),
                        start=(ic == 0), stop=(ic == NCH - 1))), w0 if ic == 0 else [])
                    w0 = []
                sg = p.op("S", (lambda s, pt=pt: s.activation(sqb[:], pt[:], AF.Square)), [mm])
                sm = p.op("T", (lambda t: t.matmul(pS[:], e2s[:], sqb[:], start=True, stop=True)), [sg])
                nr = p.op("S", (lambda s: s.activation(rstd[:], pS[:], AF.Sqrt)), [sm, prev_t3])
                nr2 = p.op("V", (lambda v: v.tensor_scalar_max(rstd[:], rstd[:], 1e-12)), [nr])
                nr3 = p.op("V", (lambda v: v.reciprocal(rstd[:], rstd[:])), [nr2])
                ev = p.op("V", (lambda v, pt=pt, ssb=ssb: v.tensor_scalar_mul(qsq[:], pt[:], ssb[:])), [mm])
                bank_free[bi] = ev
                t1 = p.op("V", (lambda v, tt=tt, A=A: v.tensor_mul(
                    u[:], qsq[:], tb[A][:, tt * 512:(tt + 1) * 512])), [ev])
                rt = None
                for (dst, srcq) in QPART:
                    rt = p.op("V", (lambda v, dst=dst, srcq=srcq: v.tensor_copy(
                        qrot[dst, :], qsq[srcq, :])), [ev])
                rt = p.op("V", (lambda v, tt=tt, Bt=Bt: v.tensor_mul(
                    w_[:], qrot[:], tb[Bt][:, tt * 512:(tt + 1) * 512])), [rt])
                t2 = p.op("V", (lambda v: v.tensor_add(u[:], u[:], w_[:])), [t1, rt])
                t3 = p.op("V", (lambda v, oc=oc, tt=tt, half=half: v.tensor_mul(
                    osb[:, half * TPC + tt * 512: half * TPC + (tt + 1) * 512], u[:], rstd[:])),
                    [t2, nr3])
                prev_t3 = t3
                st_toks.append(t3)
            dt = p.dma("G", out_dram.rearrange("(c p) t -> p c t", p=128)[:, oc, :],
                       osb[:, half * TPC: half * TPC + TPC], waits=st_toks,
                       group=f"oh{half}_{name}")
            half_free[half] = dt

    vbank_free = [None, None]
    vhalf_free = None
    for tc in range(TPC // 128):
        bi = tc % 2
        pt = pV[bi]
        w0 = lds + [vbank_free[bi], vhalf_free]
        mm = None
        for ic in range(NCH):
            mm = p.op("T", (lambda t, ic=ic, tc=tc, pt=pt: t.matmul(
                pt[:], h_ap(ic, tc * 128, 128),
                wb[:, ic * 3 * INNER + 2 * INNER: ic * 3 * INNER + 3 * INNER],
                start=(ic == 0), stop=(ic == NCH - 1))), w0 if ic == 0 else [])
            w0 = []
        ev = p.op("V", (lambda v, pt=pt: v.tensor_copy(vsb[:], pt[:])), [mm])
        vbank_free[bi] = ev
        vhalf_free = p.dma("G", vx[tc * 128:(tc + 1) * 128, :], vsb[:], waits=[ev], group="vout")
    return


def build_lffn_qkv():
    nc = bass.Bass("TRN2", target_bir_lowering=False)
    xT = nc.dram_tensor("xT", [D, TPC], F32, kind="ExternalInput").ap()
    xprevT = nc.dram_tensor("xprevT", [D, 1], F32, kind="ExternalInput").ap()
    wqkv = nc.dram_tensor("wqkv", [D, 3 * INNER], F32, kind="ExternalInput").ap()
    qs2 = nc.dram_tensor("qs2", [128, 1], F32, kind="ExternalInput").ap()
    ks2 = nc.dram_tensor("ks2", [128, 1], F32, kind="ExternalInput").ap()
    tabs = {n: nc.dram_tensor(n, [128, TPC], F32, kind="ExternalInput").ap()
            for n in ("AQ", "BQ", "AK", "BK")}
    E2 = nc.dram_tensor("E2", [128, 128], F32, kind="ExternalInput").ap()
    qx = nc.dram_tensor("qx", [INNER, TPC], BF16, kind="ExternalOutput").ap()
    kx = nc.dram_tensor("kx", [INNER, TPC], BF16, kind="ExternalOutput").ap()
    vx = nc.dram_tensor("vx", [TPC, INNER], BF16, kind="ExternalOutput").ap()
    es = ExitStack()
    p = Prog(nc, es)
    emit_lffn_qkv(p, nc, es, xT, xprevT, wqkv, qs2, ks2, tabs, E2, qx, kx, vx)
    p.emit()
    es.close()
    return nc



def emit_lffn_attn(p, nc, es, qx, kfb, vfb, masks, ident, ox, kv_loader=None):
    KL = TPC + 512
    qe = es.enter_context(nc.sbuf_tensor("qe", [65, TPC], BF16))
    ke = es.enter_context(nc.sbuf_tensor("ke", [65, KL], BF16))
    ve = es.enter_context(nc.sbuf_tensor("ve", [128, 65 * (KL // 128)], BF16))
    msk = es.enter_context(nc.sbuf_tensor("msk", [128, 32 * 512], BF16))
    idn = es.enter_context(nc.sbuf_tensor("idn", [128, 128], F32))
    mneg = es.enter_context(nc.sbuf_tensor("mneg", [128, 4], F32))
    mtmp = es.enter_context(nc.sbuf_tensor("mtmp", [128, 2], F32))
    pt_sb = es.enter_context(nc.sbuf_tensor("pt_sb", [128, 2048], BF16))
    rb = es.enter_context(nc.sbuf_tensor("rb", [64, 512], F32))
    rec = es.enter_context(nc.sbuf_tensor("rec", [1, 512], F32))
    ones1 = es.enter_context(nc.sbuf_tensor("ones1", [1, 64], F32))
    ob = es.enter_context(nc.sbuf_tensor("ob", [64, 2 * 512], BF16))
    psA = [es.enter_context(nc.psum_tensor(f"psA{i}", [128, 512], F32)) for i in range(2)]
    psM = es.enter_context(nc.psum_tensor("psM", [1, 128], F32))
    psT = es.enter_context(nc.psum_tensor("psT", [128, 2048], F32))
    psO = es.enter_context(nc.psum_tensor("psO", [65, 512], F32))
    psR = psT[0:64, 0:512]
    p_ = p

    ldi = p.dma("Y", idn[:], ident, group="ldi")
    ldm = p.dma("Y", msk[:].rearrange("p (s n) -> p s n", n=512), masks.rearrange("s p n -> p s n"),
                group="ldm")
    one1 = p.op("V", lambda v: v.memset(ones1[:], 1.0))
    onek = p.op("V", lambda v: v.memset(ke[64:65, :], 1.0))
    onev = p.op("V", lambda v: v.memset(ve[:].rearrange("p (c e) -> p c e", e=65)[:, :, 64], 1.0))
    prev_head_done = [onek, onev]
    prev_cpm = None
    ob_last = [None, None]
    pending_norm = [None]
    last_st0 = [None]
    for h in range(8):
        ldq = p.dma("G", qe[0:64, :], qx[h * 64:(h + 1) * 64, :], waits=prev_head_done, group="ldq")
        if kv_loader is None:
            ldk = p.dma("G", ke[0:64, :], kfb[h * 64:(h + 1) * 64, :], waits=prev_head_done, group="ldk")
            ldv = p.dma("G", ve[:].rearrange("p (c e) -> p c e", e=65)[:, :, 0:64],
                        vfb.rearrange("(c p) d -> p c d", p=128)[:, :, h * 64:(h + 1) * 64],
                        waits=prev_head_done, group="ldv")
        else:
            ldk, ldv = kv_loader(h, prev_head_done, ke, ve)
        prev_avm = [None]
        prev_red = [None, None]
        for wi in range(4):
            if pending_norm[0] is not None:
                pending_norm[0]()
                pending_norm[0] = None
            tr_toks = []
            for qt in range(4):
                qsl = qe[0:64, wi * 512 + qt * 128: wi * 512 + qt * 128 + 128]
                red_prev = None
                for kc in range(2):
                    bi = kc
                    w = [ldq, ldk] if (qt == 0 and kc == 0) else []
                    w = w + [prev_red[bi]]
                    mm = p.op("T", (lambda t, qsl=qsl, kc=kc, wi=wi, bi=bi: t.matmul(
                        psA[bi][:], qsl, ke[0:64, wi * 512 + kc * 512: wi * 512 + (kc + 1) * 512],
                        start=True, stop=True)), w)
                    if kc == 0:
                        red_prev = p.op("V", (lambda v, qt=qt, bi=bi: v.reduce_max(
                            mneg[:, qt:qt + 1], psA[bi][:], axis=AX.X, negate=True)), [mm])
                    else:
                        r1 = p.op("V", (lambda v, bi=bi: v.reduce_max(
                            mtmp[:, bi:bi + 1], psA[bi][:], axis=AX.X, negate=True)), [mm])
                        red_prev = p.op("V", (lambda v, qt=qt, bi=bi: v.tensor_tensor(
                            mneg[:, qt:qt + 1], mneg[:, qt:qt + 1], mtmp[:, bi:bi + 1],
                            ALU.min)), [r1])
                    prev_red[bi] = red_prev
                trm = p.op("T", (lambda t, qt=qt: t.transpose(psM[:], mneg[:, qt:qt + 1], idn[:])),
                           [red_prev, ldi, prev_cpm])
                cpm = p.op("V", (lambda v, wi=wi, qt=qt: v.tensor_copy(
                    qe[64:65, wi * 512 + qt * 128: wi * 512 + qt * 128 + 128], psM[:])), [trm])
                prev_cpm = cpm
                tr_toks.append(cpm)
            qg = qe[:, wi * 512:(wi + 1) * 512]
            first_st_waits = tr_toks + [prev_avm[0]]
            ex_prev = [None, None]
            avm = None
            for kc in range(2):
                for hf in range(2):
                    mm2 = None
                    for s2 in range(2):
                        s = hf * 2 + s2
                        w = list(first_st_waits) + [ex_prev[hf]]
                        if s == 0:
                            w = w + [last_st0[0]]
                        first_st_waits = []
                        mm2 = p.op("T", (lambda t, kc=kc, s=s, qg=qg, wi=wi: t.matmul(
                            psT[:, s * 512:(s + 1) * 512],
                            ke[:, wi * 512 + kc * 512 + s * 128: wi * 512 + kc * 512 + s * 128 + 128],
                            qg, start=True, stop=True)), w)
                        if s == 0:
                            last_st0[0] = mm2
                    ex = p.op("S", (lambda s_, hf=hf: s_.activation(
                        pt_sb[:, hf * 1024:(hf + 1) * 1024],
                        psT[:, hf * 1024:(hf + 1) * 1024], AF.Exp, scale=8.0)),
                        [mm2, prev_avm[0]])
                    ex_prev[hf] = ex
                    dep = [ex]
                    for s2 in range(2):
                        s = hf * 2 + s2
                        mi = wi * 8 + kc * 4 + s
                        dep = [p.op("V", (lambda v, s=s, mi=mi: v.tensor_mul(
                            pt_sb[:, s * 512:(s + 1) * 512], pt_sb[:, s * 512:(s + 1) * 512],
                            msk[:, mi * 512:(mi + 1) * 512])), dep + ([ldm] if s2 == 0 else []))]
                    for s2 in range(2):
                        s = hf * 2 + s2
                        w = (dep + [ldv]) if s2 == 0 else []
                        avm = p.op("T", (lambda t, kc=kc, s=s, wi=wi: t.matmul(
                            psO[:], ve[:, (wi * 4 + kc * 4 + s) * 65:(wi * 4 + kc * 4 + s) * 65 + 65],
                            pt_sb[:, s * 512:(s + 1) * 512],
                            start=(kc == 0 and s == 0), stop=(kc == 1 and s == 3))), w)
            prev_avm[0] = avm

            def make_norm(wi=wi, h=h, avm=avm, ex_prev=ex_prev):
                def donorm():
                    rc = p_.op("V", (lambda v: v.reciprocal(rec[:], psO[64:65, :])), [avm])
                    bc = p_.op("T", (lambda t: t.matmul(psR, ones1[:], rec[:],
                                                        start=True, stop=True)),
                               [rc, one1, last_st0[0], ex_prev[0]])
                    cb = p_.op("V", (lambda v: v.tensor_copy(rb[:], psR)), [bc])
                    ot = p_.op("V", (lambda v, wi=wi: v.tensor_mul(
                        ob[:, (wi % 2) * 512:(wi % 2) * 512 + 512], psO[0:64, :], rb[:])),
                        [cb, ob_last[wi % 2]])
                    ob_last[wi % 2] = p_.dma("G", ox[h * 64:(h + 1) * 64, wi * 512:(wi + 1) * 512],
                                             ob[:, (wi % 2) * 512:(wi % 2) * 512 + 512],
                                             waits=[ot], group=f"ob{wi % 2}")
                    prev_avm[0] = ot
                    last_st0[0] = cb
                return donorm
            pending_norm[0] = make_norm()
        pending_norm[0]()
        pending_norm[0] = None
        prev_head_done = [prev_avm[0], ob_last[0], ob_last[1], prev_cpm]
    return


def build_lffn_attn():
    nc = bass.Bass("TRN2", target_bir_lowering=False)
    KL = TPC + 512
    qx = nc.dram_tensor("qx", [INNER, TPC], BF16, kind="ExternalInput").ap()
    kfb = nc.dram_tensor("kfb", [INNER, KL], BF16, kind="ExternalInput").ap()
    vfb = nc.dram_tensor("vfb", [KL, INNER], BF16, kind="ExternalInput").ap()
    masks = nc.dram_tensor("masks", [32, 128, 512], BF16, kind="ExternalInput").ap()
    ident = nc.dram_tensor("ident", [128, 128], F32, kind="ExternalInput").ap()
    ox = nc.dram_tensor("ox", [INNER, TPC], BF16, kind="ExternalOutput").ap()
    es = ExitStack()
    p = Prog(nc, es)
    emit_lffn_attn(p, nc, es, qx, kfb, vfb, masks, ident, ox)
    p.emit()
    es.close()
    return nc



def emit_lffn_proj(p, nc, es, oT, wo, xT, x2):
    obf = es.enter_context(nc.sbuf_tensor("obf", [128, 4 * TPC], BF16))
    wob = es.enter_context(nc.sbuf_tensor("wob", [128, 4 * D], BF16))
    xr = es.enter_context(nc.sbuf_tensor("xr", [128, NCH * TPC], F32))
    xo = es.enter_context(nc.sbuf_tensor("xo", [128, 512], F32))
    ps = [es.enter_context(nc.psum_tensor(f"ps{i}", [128, 512], F32)) for i in range(2)]
    ldo = p.dma("G", obf[:].rearrange("p (c t) -> p c t", t=TPC),
                oT.rearrange("(c p) t -> p c t", p=128), group="ldo")
    ldw = p.dma("G", wob[:].rearrange("p (c o) -> p c o", o=D), wo.rearrange("(c p) o -> p c o", p=128), group="ldw")
    ldxr = p.dma("G", xr[:].rearrange("p (c t) -> p c t", t=TPC),
                 xT.rearrange("(c p) t -> p c t", p=128), group="ldxr")
    bank_free = [None, None]
    for oc in range(NCH):
        for tt in range(NTT):
            bi = (oc * NTT + tt) % 2
            pt = ps[bi]
            w0 = [ldo, ldw, ldxr, bank_free[bi]]
            mm = None
            for ic in range(4):
                mm = p.op("T", (lambda t, ic=ic, oc=oc, tt=tt, pt=pt: t.matmul(
                    pt[:], wob[:, ic * D + oc * 128: ic * D + oc * 128 + 128],
                    obf[:, ic * TPC + tt * 512: ic * TPC + (tt + 1) * 512],
                    start=(ic == 0), stop=(ic == 3))), w0 if ic == 0 else [])
                w0 = []
            ad = p.op("V", (lambda v, oc=oc, tt=tt, pt=pt: v.tensor_add(
                xr[:, oc * TPC + tt * 512: oc * TPC + (tt + 1) * 512],
                xr[:, oc * TPC + tt * 512: oc * TPC + (tt + 1) * 512], pt[:])), [mm])
            bank_free[bi] = ad
            p.dma("G", x2.rearrange("(c p) t -> p c t", p=128)[:, oc, tt * 512:(tt + 1) * 512],
                  xr[:, oc * TPC + tt * 512: oc * TPC + (tt + 1) * 512], waits=[ad], group="xout")
    return


def build_lffn_proj():
    nc = bass.Bass("TRN2", target_bir_lowering=False)
    oT = nc.dram_tensor("oT", [INNER, TPC], BF16, kind="ExternalInput").ap()
    wo = nc.dram_tensor("wo", [INNER, D], F32, kind="ExternalInput").ap()
    xT = nc.dram_tensor("xT", [D, TPC], F32, kind="ExternalInput").ap()
    x2 = nc.dram_tensor("x2", [D, TPC], F32, kind="ExternalOutput").ap()
    es = ExitStack()
    p = Prog(nc, es)
    emit_lffn_proj(p, nc, es, oT, wo, xT, x2)
    p.emit()
    es.close()
    return nc



HID_P = 2816


def emit_lffn_ffn(p, nc, es, x2T, x2prevT, lg, lb, w1p, b1p, w2p, b2, x3):
    W = 2049
    NA = HID_P // 128
    xsh = es.enter_context(nc.sbuf_tensor("xsh", [128, NCH * W], F32))
    hbf = es.enter_context(nc.sbuf_tensor("hbf", [128, NCH * TPC], BF16))
    lgs = es.enter_context(nc.sbuf_tensor("lgs", [128, NCH], F32))
    lbs = es.enter_context(nc.sbuf_tensor("lbs", [128, NCH], F32))
    b1s = es.enter_context(nc.sbuf_tensor("b1s", [128, 2 * NA], F32))
    b2s = es.enter_context(nc.sbuf_tensor("b2s", [128, NCH], F32))
    w1sb = es.enter_context(nc.sbuf_tensor("w1sb", [128, 3 * 2 * NCH * 128], BF16))
    w2sb = es.enter_context(nc.sbuf_tensor("w2sb", [128, 3 * NA * 128], BF16))
    pbuf = es.enter_context(nc.sbuf_tensor("pbuf", [128, 2 * NA * 512], BF16))
    gg = es.enter_context(nc.sbuf_tensor("gg", [128, 512], BF16))
    atmp = es.enter_context(nc.sbuf_tensor("atmp", [128, 512], F32))
    xo = es.enter_context(nc.sbuf_tensor("xo", [128, 512], F32))
    xo2 = es.enter_context(nc.sbuf_tensor("xo2", [128, 512], F32))
    ps = [es.enter_context(nc.psum_tensor(f"ps{i}", [128, 512], F32)) for i in range(2)]
    lnb = alloc_ln_bufs(nc, es)

    ldx = p.dma("G", xsh[:].rearrange("p (c t) -> p c t", t=W)[:, :, 1:W],
                x2T.rearrange("(c p) t -> p c t", p=128), group="ldx")
    ldxp = p.dma("G", xsh[:].rearrange("p (c t) -> p c t", t=W)[:, :, 0:1],
                 x2prevT.rearrange("(c p) t -> p c t", p=128), allow_slow_non_contiguous=True,
                 group="ldxp")
    ldlg = p.dma("Y", lgs[:], lg, group="ldlg")
    ldlb = p.dma("Y", lbs[:], lb, group="ldlb")
    ldb1 = p.dma("Y", b1s[:], b1p, group="ldb1")
    ldb2 = p.dma("Y", b2s[:], b2, group="ldb2")
    init = emit_ln_init(p, lnb)

    def xs_ap(c, tt):
        off = 0 if c >= 4 else 1
        return xsh[:, c * W + off + tt * 512: c * W + off + tt * 512 + 512]

    def xu_ap(c, tt):
        return xsh[:, c * W + 1 + tt * 512: c * W + 1 + tt * 512 + 512]

    def hb_ap(c, tt):
        return hbf[:, c * TPC + tt * 512: c * TPC + (tt + 1) * 512]

    toks, last_ln = emit_ln(p, lnb, xs_ap, TPC, out_fn=hb_ap,
                            gamma_scale=lambda c: lgs[:, c:c + 1],
                            gamma_bias=lambda c: lbs[:, c:c + 1],
                            pre_waits=[ldx, ldxp, ldlg, ldlb], init_toks=init)
    ln_done = [toks[(c, tt)] for c in range(NCH) for tt in range(NTT)]

    w1v = w1sb[:].rearrange("p (b a c o) -> p b a c o", b=3, a=2, o=128)
    w2v = w2sb[:].rearrange("p (b c o) -> p b c o", b=3, o=128)
    prev_pb_use = None
    last_x3 = None
    w1_free = [None, None, None]
    w2_free = [None, None, None]
    gbank = [None, None]
    w1t = {}
    w2t = {}

    def issue_w1(ocp, wtok):
        b3 = ocp % 3
        la = p.dma("G", w1v[:, b3, 0], w1p[0][ocp], waits=[wtok], group=f"w1_{b3}")
        lg_ = p.dma("G", w1v[:, b3, 1], w1p[1][ocp], group=f"w1_{b3}")
        w1t[ocp] = (la, lg_)

    def issue_w2(oc2, wtok):
        w2t[oc2] = p.dma("G", w2v[:, oc2 % 3], w2p[oc2], waits=[wtok], group=f"w2_{oc2 % 3}")

    xov = [xo, xo2]
    x3r = x3.rearrange("(c p) t -> p c t", p=128)

    def pb_ap(j, kc):
        return pbuf[:, (j * NA + kc) * 512: (j * NA + kc) * 512 + 512]

    for tp in range(2):
        tts = (2 * tp, 2 * tp + 1)
        pbt = [[], []]
        for j3 in range(3):
            issue_w1(j3, w1_free[j3])
        for j3 in range(3):
            issue_w2(j3, w2_free[j3])
        for ocp in range(NA):
            par = ocp % 3
            lwa, lwg = w1t.pop(ocp)
            pm = None
            for j, tt in enumerate(tts):
                w0 = (ln_done + [lwa, lwg, prev_pb_use]) if j == 0 else []
                w0 = w0 + [gbank[0]]
                mmg = None
                for ic in range(NCH):
                    mmg = p.op("T", (lambda t, ic=ic, par=par, tt=tt: t.matmul(
                        ps[0][:], w1v[:, par, 1, ic], hb_ap(ic, tt),
                        start=(ic == 0), stop=(ic == NCH - 1))), w0 if ic == 0 else [])
                    w0 = []
                ge = p.op("S", (lambda s, ocp=ocp: s.activation(
                    gg[:], ps[0][:], AF.Gelu, bias=b1s[:, NA + ocp:NA + ocp + 1])), [mmg, ldb1])
                gbank[0] = ge
                w0 = [gbank[1], ge]
                mma = None
                for ic in range(NCH):
                    mma = p.op("T", (lambda t, ic=ic, par=par, tt=tt: t.matmul(
                        ps[1][:], w1v[:, par, 0, ic], hb_ap(ic, tt),
                        start=(ic == 0), stop=(ic == NCH - 1))), w0 if ic == 0 else [])
                    w0 = []
                ae = p.op("V", (lambda v, ocp=ocp: v.tensor_scalar_add(
                    atmp[:], ps[1][:], b1s[:, ocp:ocp + 1])), [mma, ldb1])
                gbank[1] = ae
                pm = p.op("V", (lambda v, ocp=ocp, j=j: v.tensor_mul(
                    pb_ap(j, ocp), atmp[:], gg[:])), [ae, ge])
                pbt[j].append(pm)
            w1_free[par] = pm
            if ocp + 3 < NA:
                issue_w1(ocp + 3, pm)
        x3_toks = []
        last_w2mm = None
        for oc2 in range(NCH):
            lw2 = w2t.pop(oc2)
            mm = None
            for j, tt in enumerate(tts):
                w0 = pbt[j] + [lw2, gbank[j]]
                for kc in range(NA):
                    mm = p.op("T", (lambda t, kc=kc, j=j, oc2=oc2: t.matmul(
                        ps[j][:], w2v[:, oc2 % 3, kc], pb_ap(j, kc),
                        start=(kc == 0), stop=(kc == NA - 1))), w0 if kc == 0 else [])
                    w0 = []
                ev = p.op("V", (lambda v, oc2=oc2, j=j: v.tensor_scalar_add(
                    xov[j][:], ps[j][:], b2s[:, oc2:oc2 + 1])), [mm, ldb2])
                ad = p.op("V", (lambda v, oc2=oc2, tt=tt, j=j: v.tensor_add(
                    xov[j][:], xov[j][:], xu_ap(oc2, tt))), [ev])
                dt = p.dma("Y", x3r[:, oc2, tt * 512:(tt + 1) * 512],
                           xov[j][:], waits=[ad], group="xout")
                gbank[j] = dt
                x3_toks.append(ad)
            last_w2mm = mm
            w2_free[oc2 % 3] = mm
            if oc2 + 3 < NCH:
                issue_w2(oc2 + 3, mm)
        prev_pb_use = last_w2mm
        last_x3 = x3_toks[-1]
        last_x3 = x3_toks[-1]
    return


def build_lffn_ffn():
    nc = bass.Bass("TRN2", target_bir_lowering=False)
    x2T = nc.dram_tensor("x2T", [D, TPC], F32, kind="ExternalInput").ap()
    x2prevT = nc.dram_tensor("x2prevT", [D, 1], F32, kind="ExternalInput").ap()
    lg = nc.dram_tensor("lg", [128, NCH], F32, kind="ExternalInput").ap()
    lb = nc.dram_tensor("lb", [128, NCH], F32, kind="ExternalInput").ap()
    w1p = nc.dram_tensor("w1p", [2, HID_P // 128, 128, NCH, 128], F32, kind="ExternalInput").ap()
    b1p = nc.dram_tensor("b1p", [128, 2 * HID_P // 128], F32, kind="ExternalInput").ap()
    w2p = nc.dram_tensor("w2p", [NCH, 128, HID_P // 128, 128], F32, kind="ExternalInput").ap()
    b2 = nc.dram_tensor("b2", [128, NCH], F32, kind="ExternalInput").ap()
    x3 = nc.dram_tensor("x3", [D, TPC], F32, kind="ExternalOutput").ap()
    es = ExitStack()
    p = Prog(nc, es)
    emit_lffn_ffn(p, nc, es, x2T, x2prevT, lg, lb, w1p, b1p, w2p, b2, x3)
    p.emit()
    es.close()
    return nc





DFF = 4096
NE = 8


def emit_moe(p, nc, es, xT, rw, rbrow, upT, downR, ident, rowsel, x4):
    NDC = DFF // 128
    TH = 1024
    xbf = es.enter_context(nc.sbuf_tensor("xbf", [128, NCH * TPC], BF16))
    rwb = es.enter_context(nc.sbuf_tensor("rwb", [128, NCH * NE], F32))
    rbr = es.enter_context(nc.sbuf_tensor("rbr", [1, NE], F32))
    onesr = es.enter_context(nc.sbuf_tensor("onesr", [1, 128], F32))
    xf = es.enter_context(nc.sbuf_tensor("xf", [128, NCH * 128], F32))
    ones1 = es.enter_context(nc.sbuf_tensor("ones1f", [1, 128], F32))
    idn = es.enter_context(nc.sbuf_tensor("idn", [128, 128], F32))
    rsel = es.enter_context(nc.sbuf_tensor("rsel", [NE, NE * 128], BF16))
    probs = es.enter_context(nc.sbuf_tensor("probs", [128, NE], F32))
    m8 = es.enter_context(nc.sbuf_tensor("m8", [128, NE], F32))
    mneg = es.enter_context(nc.sbuf_tensor("mnegM", [128, 1], F32))
    ssum = es.enter_context(nc.sbuf_tensor("ssum", [128, 1], F32))
    comb = es.enter_context(nc.sbuf_tensor("comb", [128, NE], F32))
    comb_fm = es.enter_context(nc.sbuf_tensor("comb_fm", [NE, TPC], BF16))
    cbc = es.enter_context(nc.sbuf_tensor("cbc", [128, TH], F32))
    uph = es.enter_context(nc.sbuf_tensor("uph", [128, 2 * NCH * 1024], BF16))
    hb = es.enter_context(nc.sbuf_tensor("hb", [128, NDC * TH], BF16))
    dsl = es.enter_context(nc.sbuf_tensor("dsl", [128, 2 * NDC * 128], BF16))
    acc = es.enter_context(nc.sbuf_tensor("acc", [128, NCH * TH], F32))
    tmpv = es.enter_context(nc.sbuf_tensor("tmpv", [128, 512], F32))
    xtile = es.enter_context(nc.sbuf_tensor("xtile", [128, 2 * 512], F32))
    psL = es.enter_context(nc.psum_tensor("psL", [128, 512], F32))
    psC = es.enter_context(nc.psum_tensor("psC", [128, 512], F32))
    psU = [es.enter_context(nc.psum_tensor(f"psU{i}", [128, 512], F32)) for i in range(2)]
    psD = [es.enter_context(nc.psum_tensor(f"psD{i}", [128, 512], F32)) for i in range(2)]
    lnb = alloc_ln_bufs(nc, es, reuse=dict(psS=psL[0:1, 0:512], psS2=psC[0:1, 0:512],
                                           psB=psU[0][:], psB2=psU[1][:]))

    ldx = p.dma("G", xbf[:].rearrange("p (c t) -> p c t", t=TPC),
                xT.rearrange("(c p) t -> p c t", p=128), group="ldx")
    ldrw = p.dma("Y", rwb[:].rearrange("p (c o) -> p c o", o=NE),
                 rw.rearrange("(c p) o -> p c o", p=128), group="ldrw")
    ldrb = p.dma("Y", rbr[:], rbrow, group="ldrb")
    ldi = p.dma("Y", idn[:], ident, group="ldi")
    ldrs = p.dma("G", rsel[:], rowsel, group="ldrs")
    o1 = p.op("V", lambda v: v.memset(onesr[:], 1.0))
    o2 = p.op("V", lambda v: v.memset(ones1[:], 1.0))
    init = emit_ln_init(p, lnb)

    prev_tr = None
    last_cf = None
    prev_t2 = None
    xf_free = None
    for tc in range(TPC // 128):
        lxf = p.dma("G", xf[:].rearrange("p (c o) -> p c o", o=128),
                    xT.rearrange("(c p) t -> p c t", p=128)[:, :, tc * 128:(tc + 1) * 128],
                    waits=[xf_free], group="xf")
        w0 = [lxf, ldrw, ldrb, o1, prev_t2]
        for ic in range(NCH):
            mmr = p.op("T", (lambda t, ic=ic, tc=tc: t.matmul(
                psL[:, 0:NE],
                xf[:, ic * 128:(ic + 1) * 128],
                rwb[:, ic * NE:(ic + 1) * NE], start=(ic == 0), stop=False)),
                w0 if ic == 0 else [])
            w0 = []
        xf_free = mmr
        mm = p.op("T", (lambda t: t.matmul(
            psL[:, 0:NE], onesr[:], rbr[:], start=False, stop=True)), [])
        t1 = p.op("V", (lambda v: v.reduce_max(mneg[:], psL[:, 0:NE], axis=AX.X, negate=True)),
                  [mm, prev_tr])
        t2 = p.op("S", (lambda s: s.activation(probs[:], psL[:, 0:NE], AF.Exp, bias=mneg[:])), [t1])
        prev_t2 = t2
        t3 = p.op("V", (lambda v: v.reduce_sum(ssum[:], probs[:], axis=AX.X)), [t2])
        t4 = p.op("V", (lambda v: v.reciprocal(ssum[:], ssum[:])), [t3])
        t5 = p.op("V", (lambda v: v.tensor_scalar_mul(probs[:], probs[:], ssum[:])), [t4])
        t6 = p.op("V", (lambda v: v.max(m8[:], probs[:])), [t5])
        t7 = p.op("V", (lambda v: v.scalar_tensor_tensor(
            comb[:], probs[:], m8[:, 1:2], probs[:], ALU.is_ge, ALU.mult)), [t6])
        tr = p.op("T", (lambda t: t.transpose(psC[0:NE, 0:128], comb[:], idn[:])),
                  [t7, ldi, prev_tr])
        cf = p.op("V", (lambda v, tc=tc: v.tensor_copy(
            comb_fm[:, tc * 128:(tc + 1) * 128], psC[0:NE, 0:128])), [tr])
        prev_tr = cf
        last_cf = cf

    upr = upT.rearrange("e (c p) f -> e p c f", p=128)
    dwr = downR
    uphv = uph[:].rearrange("p (b c f) -> p b c f", b=2, f=1024)
    dslv = dsl[:].rearrange("p (b c o) -> p b c o", b=2, o=128)
    up_free = None
    dsl_free = [None, None]
    psD_reader = [None, None]
    psU_reader = [None, None]
    prev_cbc = None
    cbc_reader = None
    prev_acc_use = []
    x4_toks = []
    up_mm_last = {0: None, 1: None}
    pref_lu = {}
    for th in range(2):
        acc_last = {}
        tmp_last = None
        for e in range(NE):
            h_last = []
            for dq in range(4):
                qb = dq % 2
                lu = pref_lu.pop((th, e, dq), None)
                if lu is None:
                    lu = p.dma("G", uphv[:, qb], upr[e][:, :, dq * 1024:(dq + 1) * 1024],
                               waits=[up_mm_last[qb]], group=f"lu{qb}")
                last_ev = None
                for dc in range(8):
                    for ts in range(2):
                        bi = (dc * 2 + ts) % 2
                        pt = psU[bi]
                        w0 = [ldx, lu, psU_reader[bi]] + \
                             (prev_acc_use if (e == 0 and dq == 0 and dc == 0 and ts == 0) else [])
                        mm = None
                        for ic in range(NCH):
                            mm = p.op("T", (lambda t, ic=ic, dc=dc, ts=ts, th=th, pt=pt, qb=qb: t.matmul(
                                pt[:], uphv[:, qb, ic, dc * 128:(dc + 1) * 128],
                                xbf[:, ic * TPC + th * TH + ts * 512: ic * TPC + th * TH + ts * 512 + 512],
                                start=(ic == 0), stop=(ic == NCH - 1))), w0 if ic == 0 else [])
                            w0 = []
                        up_mm_last[qb] = mm
                        ev = p.op("S", (lambda s, dq=dq, dc=dc, ts=ts, pt=pt: s.activation(
                            hb[:, (dq * 8 + dc) * TH + ts * 512: (dq * 8 + dc) * TH + ts * 512 + 512],
                            pt[:], AF.Gelu)), [mm])
                        psU_reader[bi] = ev
                        last_ev = ev
                h_last.append(last_ev)
            h_last = h_last[-2:]
            cb_toks = []
            for ts in range(2):
                bcm = p.op("T", (lambda t, e=e, th=th, ts=ts: t.matmul(
                    psC[:, 0:512], rsel[:, e * 128:(e + 1) * 128],
                    comb_fm[0:NE, th * TH + ts * 512: th * TH + ts * 512 + 512],
                    start=True, stop=True)), [last_cf, ldrs, prev_cbc] + cb_toks +
                    ([cbc_reader] if cbc_reader else []))
                cbc_t = p.op("V", (lambda v, ts=ts: v.tensor_copy(
                    cbc[:, ts * 512:(ts + 1) * 512], psC[:, 0:512])), [bcm])
                prev_cbc = cbc_t
                cb_toks.append(cbc_t)
            last_mm = None
            ldt = {0: p.dma("G", dslv[:, 0], dwr[e][0], waits=[dsl_free[0]], group="dsl0")}
            for oc in range(NCH):
                par = oc % 2
                if oc + 1 < NCH:
                    nb = (oc + 1) % 2
                    ldt[oc + 1] = p.dma("G", dslv[:, nb], dwr[e][oc + 1],
                                        waits=[dsl_free[nb]], group=f"dsl{nb}")
                if oc in (2, 3) and not (th == 1 and e == NE - 1):
                    ne_, nth = (e + 1, th) if e + 1 < NE else (0, th + 1)
                    qb_ = oc - 2
                    pref_lu[(nth, ne_, qb_)] = p.dma(
                        "G", uphv[:, qb_], upr[ne_][:, :, qb_ * 1024:(qb_ + 1) * 1024],
                        waits=[up_mm_last[qb_]], group=f"lu{qb_}")
                ld2 = ldt[oc]
                for ts in range(2):
                    pt = psD[ts]
                    w0 = (h_last if oc == 0 else []) + [ld2, psD_reader[ts]] + \
                         (prev_acc_use if (oc == 0 and e == 0) else [])
                    mm = None
                    for kc in range(NDC):
                        mm = p.op("T", (lambda t, kc=kc, ts=ts, pt=pt, par=par: t.matmul(
                            pt[:], dslv[:, par, kc],
                            hb[:, kc * TH + ts * 512: kc * TH + ts * 512 + 512],
                            start=(kc == 0), stop=(kc == NDC - 1))), w0 if kc == 0 else [])
                        w0 = []
                    last_mm = mm
                    aslice = acc[:, oc * TH + ts * 512: oc * TH + ts * 512 + 512]
                    if e == 0:
                        rd = p.op("V", (lambda v, aslice=aslice, ts=ts, pt=pt: v.tensor_mul(
                            aslice, pt[:], cbc[:, ts * 512:(ts + 1) * 512])), [mm] + cb_toks)
                        acc_last[(oc, ts)] = rd
                    else:
                        rd = p.op("V", (lambda v, ts=ts, pt=pt: v.tensor_mul(
                            tmpv[:], pt[:], cbc[:, ts * 512:(ts + 1) * 512])),
                            [mm, tmp_last] + cb_toks)
                        at = p.op("G", (lambda g, aslice=aslice: g.tensor_add(
                            aslice, aslice, tmpv[:])), [rd, acc_last[(oc, ts)]])
                        tmp_last = at
                        acc_last[(oc, ts)] = at
                    psD_reader[ts] = rd
                    cbc_reader = rd
                dsl_free[par] = last_mm
        def a_ap(c, tt, th=th):
            return acc[:, c * TH + tt * 512: c * TH + tt * 512 + 512]
        res_toks = []
        xl_prev = [None, None]
        for c in range(NCH):
            for tt in range(2):
                par = tt % 2
                lx = p.dma("G", xtile[:, par * 512:(par + 1) * 512],
                           xT.rearrange("(c p) t -> p c t", p=128)[:, c,
                                        th * TH + tt * 512: th * TH + tt * 512 + 512],
                           waits=[xl_prev[par]], group=f"xt{par}")
                ad = p.op("V", (lambda v, c=c, tt=tt, par=par: v.tensor_add(
                    a_ap(c, tt), a_ap(c, tt), xtile[:, par * 512:(par + 1) * 512])),
                    [lx, acc_last[(c, tt)]])
                xl_prev[par] = ad
                res_toks.append(ad)
        toks, last_ln = emit_ln(p, lnb, a_ap, TH, out_fn=a_ap, pre_waits=[res_toks[-1]],
                                init_toks=init)
        for c in range(NCH):
            for tt in range(2):
                p.dma("G", x4.rearrange("(c p) t -> p c t", p=128)[:, c,
                                        th * TH + tt * 512: th * TH + tt * 512 + 512],
                      a_ap(c, tt), waits=[toks[(c, tt)]], group="x4out")
        prev_acc_use = [last_ln, p.gtotal("x4out")]
    return


def build_moe():
    nc = bass.Bass("TRN2", target_bir_lowering=False)
    xT = nc.dram_tensor("xT", [D, TPC], F32, kind="ExternalInput").ap()
    rw = nc.dram_tensor("rw", [D, NE], F32, kind="ExternalInput").ap()
    rbrow = nc.dram_tensor("rbrow", [1, NE], F32, kind="ExternalInput").ap()
    upT = nc.dram_tensor("upT", [NE, D, DFF], F32, kind="ExternalInput").ap()
    downT = nc.dram_tensor("downT", [NE, NCH, 128, DFF // 128, 128], F32, kind="ExternalInput").ap()
    ident = nc.dram_tensor("ident", [128, 128], F32, kind="ExternalInput").ap()
    rowsel = nc.dram_tensor("rowsel", [NE, NE * 128], F32, kind="ExternalInput").ap()
    x4 = nc.dram_tensor("x4", [D, TPC], F32, kind="ExternalOutput").ap()
    es = ExitStack()
    p = Prog(nc, es)
    emit_moe(p, nc, es, xT, rw, rbrow, upT, downT, ident, rowsel, x4)
    p.emit()
    es.close()
    return nc



def emit_addln(p, nc, es, xT, yT, xn):
    xr = es.enter_context(nc.sbuf_tensor("xr", [128, NCH * TPC], F32))
    yb = es.enter_context(nc.sbuf_tensor("yb", [128, NCH * TPC], F32))
    lnb = alloc_ln_bufs(nc, es)
    ldx = p.dma("G", xr[:].rearrange("p (c t) -> p c t", t=TPC), xT.rearrange("(c p) t -> p c t", p=128), group="ldx")
    ldy = p.dma("G", yb[:].rearrange("p (c t) -> p c t", t=TPC), yT.rearrange("(c p) t -> p c t", p=128), group="ldy")
    init = emit_ln_init(p, lnb)
    add_toks = []
    for c in range(NCH):
        for tt in range(NTT):
            ad = p.op("V", (lambda v, c=c, tt=tt: v.tensor_add(
                xr[:, c * TPC + tt * 512: c * TPC + (tt + 1) * 512],
                xr[:, c * TPC + tt * 512: c * TPC + (tt + 1) * 512],
                yb[:, c * TPC + tt * 512: c * TPC + (tt + 1) * 512])),
                [ldx, ldy] if (c == 0 and tt == 0) else [])
            add_toks.append(ad)

    def x_ap(c, tt):
        return xr[:, c * TPC + tt * 512: c * TPC + (tt + 1) * 512]

    toks, last = emit_ln(p, lnb, x_ap, TPC, out_fn=x_ap, pre_waits=[add_toks[-1]], init_toks=init)
    for c in range(NCH):
        for tt in range(NTT):
            p.dma("G", xn.rearrange("(c p) t -> p c t", p=128)[:, c, tt * 512:(tt + 1) * 512],
                  x_ap(c, tt), waits=[toks[(c, tt)]], group="xout")
    return


def build_addln():
    nc = bass.Bass("TRN2", target_bir_lowering=False)
    xT = nc.dram_tensor("xT", [D, TPC], F32, kind="ExternalInput").ap()
    yT = nc.dram_tensor("yT", [D, TPC], F32, kind="ExternalInput").ap()
    xn = nc.dram_tensor("xn", [D, TPC], F32, kind="ExternalOutput").ap()
    es = ExitStack()
    p = Prog(nc, es)
    emit_addln(p, nc, es, xT, yT, xn)
    p.emit()
    es.close()
    return nc


def build_proj_ln():
    nc = bass.Bass("TRN2", target_bir_lowering=False)
    oT = nc.dram_tensor("oT", [D, TPC], BF16, kind="ExternalInput").ap()
    wo = nc.dram_tensor("wo", [D, D], F32, kind="ExternalInput").ap()
    bo = nc.dram_tensor("bo", [128, NCH], F32, kind="ExternalInput").ap()
    xT = nc.dram_tensor("xT", [D, TPC], F32, kind="ExternalInput").ap()
    xn = nc.dram_tensor("xn", [D, TPC], F32, kind="ExternalOutput").ap()
    es = ExitStack()
    p = Prog(nc, es)
    emit_proj_ln(p, nc, es, oT, wo, bo, xT, xn)
    p.emit()
    es.close()
    return nc



MEGA_DEBUG = False


def build_mega():
    nc = bass.Bass("TRN2", target_bir_lowering=False)
    EI, EO = "ExternalInput", "ExternalOutput"
    A = lambda n, s, d=F32, k=EI: nc.dram_tensor(n, s, d, kind=k).ap()
    x0T = A("x0T", [D, TPC])
    out = A("out", [D, TPC], F32, EO)
    xdbg = A("xdbg", [5, D, TPC], F32, EO) if MEGA_DEBUG else None
    if MEGA_DEBUG:
        bdbg = A("bdbg", [D, 1], F32, EO)
        qdbg = A("qdbg", [INNER, TPC], BF16, EO)
        kdbg = A("kdbg", [INNER, TPC + 512], BF16, EO)
        vdbg = A("vdbg", [TPC + 512, INNER], BF16, EO)
        odbg = A("odbg", [INNER, TPC], BF16, EO)
        x2dbg = A("x2dbg", [D, TPC], F32, EO)
    dw = {}
    for l in ("0", "1"):
        for n in ("wq", "wk", "wv", "wo"):
            dw[n + l] = A(n + l, [D, D])
        for n in ("bq", "bk", "bo"):
            dw[n + l] = A(n + l, [128, NCH])
        dw["bvrow" + l] = A("bvrow" + l, [1, D])
    lw = {}
    for l in ("0", "1"):
        lw["wqkv" + l] = A("wqkv" + l, [D, 3 * INNER])
        lw["qs2" + l] = A("qs2" + l, [128, 1])
        lw["ks2" + l] = A("ks2" + l, [128, 1])
        lw["lwo" + l] = A("lwo" + l, [INNER, D])
        lw["lg" + l] = A("lg" + l, [128, NCH])
        lw["lb" + l] = A("lb" + l, [128, NCH])
        lw["w1p" + l] = A("w1p" + l, [2, HID_P // 128, 128, NCH, 128])
        lw["b1p" + l] = A("b1p" + l, [128, 2 * HID_P // 128])
        lw["w2p" + l] = A("w2p" + l, [NCH, 128, HID_P // 128, 128])
        lw["b2" + l] = A("b2" + l, [128, NCH])
    tabs = {n: A("t" + n, [128, TPC]) for n in ("AQ", "BQ", "AK", "BK")}
    E2 = A("E2", [128, 128])
    lmasks = A("lmasks", [32, 128, 512], BF16)
    tri = A("tri", [4, 128, 512], BF16)
    ident = A("ident", [128, 128])
    rowsel = A("rowsel", [NE, NE * 128])
    rw = A("rw", [D, NE])
    rbrow = A("rbrow", [1, NE])
    upT = A("upT", [NE, D, DFF])
    downR = A("downR", [NE, NCH, 128, DFF // 128, 128])
    bmask = A("bmask", [128, NCH])
    ix = A("ix", [1, 64], mybir.dt.uint32)
    I = lambda n, s, d=F32: nc.dram_tensor(n, s, d, kind="Internal").ap()
    xcur = I("xcur", [D, TPC])
    xpre = I("xpre", [D, TPC])
    x2s = I("x2s", [D, TPC])
    qkc = I("qkc", [2 * D, TPC], BF16)
    vc = I("vc", [TPC, D], BF16)
    qkg = nc.dram_tensor("qkg", [8 * 2 * D, TPC], BF16, kind="Internal", addr_space="Shared").ap()
    vg = nc.dram_tensor("vg", [8 * TPC, D], BF16, kind="Internal", addr_space="Shared").ap()
    oc4 = I("oc4", [4 * 64, T], BF16)
    og = nc.dram_tensor("og", [8 * 4 * 64, T], BF16, kind="Internal", addr_space="Shared").ap()
    bndc = I("bndc", [128, NCH])
    bndg = nc.dram_tensor("bndg", [8 * 128, NCH], F32, kind="Internal", addr_space="Shared").ap()
    bndloc = I("bndloc", [D, 1])
    kvwc = I("kvwc", [2 * 512, 512], BF16)
    kvwg = nc.dram_tensor("kvwg", [8 * 2 * 512, 512], BF16, kind="Internal", addr_space="Shared").ap()
    qkl = I("qkl", [2 * 4 * 64, T], BF16)
    vloc = I("vloc", [4 * T, 64], BF16)
    oloc = I("oloc", [16 * 64, TPC], BF16)
    qlxs = I("qlxs", [INNER, TPC], BF16)
    kfbs = I("kfbs", [INNER, TPC + 512], BF16)
    vfbs = I("vfbs", [TPC + 512, INNER], BF16)
    oxl = I("oxl", [INNER, TPC], BF16)
    qkg5 = qkg.rearrange("(r a h p) t -> r a h p t", r=8, a=2, h=16)
    vg4 = vg.rearrange("(r t) (h e) -> r h t e", r=8, h=16)
    og6 = og.rearrange("(r s p) (b t) -> b r s p t", r=8, s=4, t=TPC)
    qkl5 = qkl.rearrange("(a h p) (r t) -> a h p r t", a=2, h=4, r=4)
    vloc4 = vloc.rearrange("(h r t) e -> h r t e", h=4, r=4)
    oloc4 = oloc.rearrange("(c b p) t -> c b p t", c=8, b=2)
    bndg3 = bndg.rearrange("(r p) c -> r p c", r=8)
    kvwg3 = kvwg.rearrange("(r a p) t -> r a p t", r=8, a=2)

    es = ExitStack()
    p = Prog(nc, es)
    ixs = es.enter_context(nc.sbuf_tensor("ixs", [1, 64], mybir.dt.uint32))
    bnds = es.enter_context(nc.sbuf_tensor("bnds", [128, NCH], F32))
    bmsks = es.enter_context(nc.sbuf_tensor("bmsks", [128, NCH], F32))
    ldix = p.dma("Y", ixs[:], ix, group="ldix")
    ldbm = p.dma("Y", bmsks[:], bmask, group="ldbm")

    RG = [list(range(NCORES))]

    _dyncache = {}

    def dyn(g, k, ekey="G"):
        if (ekey, k) not in _dyncache:
            _dyncache[(ekey, k)] = g.value_load(ixs[0:1, k:k + 1])
        return _dyncache[(ekey, k)]

    def repack_qkv(coll_tok):
        first = [coll_tok, ldix]
        for a in range(2):
            def fqk(e, a=a):
                bb = dyn(e, 0, "G")
                hb = dyn(e, 1, "G")
                return e.dma_start(
                    out=qkl5[a].transpose([2, 0, 1, 3]),
                    in_=qkg5[bass.ts(bb, 4)][:, a][:, bass.ts(hb, 4)])
            p.raw_dma("G", fqk, waits=first, group="rpqk")
            first = []
        for ri in range(4):
            def fv(e, ri=ri):
                bb = dyn(e, 0, "S")
                hb = dyn(e, 1, "S")
                return e.dma_start(
                    out=vloc4[:, ri],
                    in_=vg4[bass.ts(bb, 4)][ri][bass.ts(hb, 4)])
            p.raw_dma("S", fv, waits=[coll_tok, ldix] if ri == 0 else (), group="rpv")
        return [p.gtotal("rpqk"), p.gtotal("rpv")]

    def mega_flash_loader(rp_toks):
        def loader(pair, prev_done, qe, ke, ve):
            w = list(prev_done) + list(rp_toks)
            ldq = p.dma("G", qe[0:64, :], qkl5[0, pair].rearrange("p r t -> p (r t)"),
                        waits=w, group="ldq")
            ldk = p.dma("G", ke[0:64, :], qkl5[1, pair].rearrange("p r t -> p (r t)"),
                        waits=(), group="ldk")
            ldv = p.dma("G", ve[:].rearrange("p (c e2) -> p c e2", e2=65)[:, :, 0:64],
                        vloc4[pair].rearrange("r (c p) e -> p (r c) e", p=128),
                        waits=(), group="ldv")
            return ldq, ldk, ldv
        return loader

    def mega_o_loader(coll_tok):
        def o_loader(obf):
            def fo(e):
                bb = dyn(e, 0, "G")
                so = dyn(e, 3, "G")
                return e.dma_start(
                    out=oloc.rearrange("(r s p) t -> r s p t", r=4, s=4),
                    in_=og6[so][bass.ts(bb, 4)])
            t1 = p.raw_dma("G", fo, waits=[coll_tok, ldix], group="ldo1")
            p.dma("G", obf[0:64, :].rearrange("p (c t) -> p c t", t=TPC),
                  oloc4[:, 0].transpose([1, 0, 2]), waits=[t1], group="ldo")
            tokl = p.dma("G", obf[64:128, :].rearrange("p (c t) -> p c t", t=TPC),
                         oloc4[:, 1].transpose([1, 0, 2]), waits=(), group="ldo")
            return tokl
        return o_loader

    def dense_block(lbl, x_src):
        pn = NCProxy(nc, f"dq{lbl}_")
        with ExitStack() as pes:
            emit_qkv(p, pn, pes, x_src, dw["wq" + lbl], dw["wk" + lbl], dw["wv" + lbl],
                     dw["bq" + lbl], dw["bk" + lbl], dw["bvrow" + lbl],
                     qkc[0:D, :], qkc[D:2 * D, :], vc)
            p.barrier()
        wv_ = [p.gtotal("oh0_q"), p.gtotal("oh1_q"), p.gtotal("oh0_k"),
               p.gtotal("oh1_k"), p.gtotal("vout")]
        p.collective("AllGather", [qkc], [qkg], waits=wv_)
        ct = p.collective("AllGather", [vc], [vg], waits=[])
        rp = repack_qkv(ct)
        pn = NCProxy(nc, f"df{lbl}_")
        with ExitStack() as pes:
            emit_flash(p, pn, pes, ident, tri, oc4.rearrange("(f p) t -> f p t", f=4),
                       mega_flash_loader(rp))
            p.barrier()
        ct2 = p.collective("AllGather", [oc4], [og], waits=[p.gtotal("ob0"), p.gtotal("ob1")])
        pn = NCProxy(nc, f"dp{lbl}_")
        with ExitStack() as pes:
            emit_proj_ln(p, pn, pes, None, dw["wo" + lbl], dw["bo" + lbl], x_src, xcur,
                         o_loader=mega_o_loader(ct2))
            p.barrier()

    def stage_boundary(src_dram):
        t1 = p.dma("G", bndc, src_dram.rearrange("(c p) t -> p c t", p=128)[:, :, TPC - 1],
                   group="bstg", allow_slow_non_contiguous=True)
        ct = p.collective("AllGather", [bndc], [bndg], waits=[t1])

        def fb(e):
            rk = dyn(e, 2, "S")
            return e.dma_start(out=bnds[:], in_=bndg3[rk])
        t2 = p.raw_dma("S", fb, waits=[ct, ldix], group="bld")
        t3 = p.op("V", lambda v: v.tensor_mul(bnds[:], bnds[:], bmsks[:]), [t2, ldbm])
        t4 = p.dma("G", bndloc.rearrange("(c p) o -> p c o", p=128), bnds[:].unsqueeze(2),
                   waits=[t3], group="bst2", allow_slow_non_contiguous=True)
        p.barrier()

    def lffn_layer(lbl):
        dbg = MEGA_DEBUG and lbl == "0"
        stage_boundary(xcur)
        if dbg:
            p.dma("G", bdbg, bndloc, group="xdbg", allow_slow_non_contiguous=True)
            p.barrier()
        pn = NCProxy(nc, f"lq{lbl}_")
        with ExitStack() as pes:
            emit_lffn_qkv(p, pn, pes, xcur, bndloc, lw["wqkv" + lbl], lw["qs2" + lbl],
                          lw["ks2" + lbl], tabs, E2, qlxs, kfbs[:, 512:TPC + 512],
                          vfbs[512:TPC + 512, :])
            p.barrier()
        if dbg:
            p.dma("G", qdbg, qlxs, group="xdbg")
            p.barrier()
        t1 = p.dma("G", kvwc[0:512, :], kfbs[:, TPC:TPC + 512], group="kvstg")
        t2 = p.dma("G", kvwc[512:1024, :], vfbs[TPC:TPC + 512, :], group="kvstg")
        ct = p.collective("AllGather", [kvwc], [kvwg], waits=[t1, t2])

        kes = ExitStack()
        kvtmp = kes.enter_context(nc.sbuf_tensor(f"kvtmp{lbl}", [128, 4 * 512], BF16))

        def fk(e):
            rk = dyn(e, 2, "Y")
            return e.dma_start(out=kvtmp[:].rearrange("p (a t) -> p a t", t=512),
                               in_=kvwg3[rk][0].rearrange("(a p) t -> p a t", p=128))
        def fv(e):
            rk = dyn(e, 2, "Y")
            return e.dma_start(out=vfbs[0:512, :].rearrange("(a p) t -> p a t", p=128),
                               in_=kvwg3[rk][1].rearrange("(a p) t -> p a t", p=128))
        t3 = p.raw_dma("Y", fk, waits=[ct, ldix], group="kvld")
        t4 = p.raw_dma("Y", fv, waits=(), group="kvld")
        tz = p.op("V", lambda v: v.tensor_scalar_mul(kvtmp[:], kvtmp[:], bmsks[:, 0:1]),
                  [t3, ldbm])
        p.dma("G", kfbs[:, 0:512].rearrange("(a p) t -> p a t", p=128),
              kvtmp[:].rearrange("p (a t) -> p a t", t=512), waits=[tz], group="kvld2")
        p.barrier()
        kes.close()
        if dbg:
            p.dma("G", kdbg, kfbs, group="xdbg")
            p.dma("G", vdbg, vfbs, group="xdbg")
            p.barrier()
        pn = NCProxy(nc, f"la{lbl}_")
        with ExitStack() as pes:
            emit_lffn_attn(p, pn, pes, qlxs, kfbs, vfbs, lmasks, ident, oxl)
            p.barrier()
        if dbg:
            p.dma("G", odbg, oxl, group="xdbg")
            p.barrier()
        pn = NCProxy(nc, f"lp{lbl}_")
        with ExitStack() as pes:
            emit_lffn_proj(p, pn, pes, oxl, lw["lwo" + lbl], xcur, x2s)
            p.barrier()
        if dbg:
            p.dma("G", x2dbg, x2s, group="xdbg")
            p.barrier()
        stage_boundary(x2s)
        pn = NCProxy(nc, f"lf{lbl}_")
        with ExitStack() as pes:
            emit_lffn_ffn(p, pn, pes, x2s, bndloc, lw["lg" + lbl], lw["lb" + lbl],
                          lw["w1p" + lbl], lw["b1p" + lbl], lw["w2p" + lbl],
                          lw["b2" + lbl], xcur)
            p.barrier()

    def tap(i):
        if MEGA_DEBUG:
            p.dma("G", xdbg[i], xcur, group="xdbg")
            p.barrier()

    dense_block("0", x0T)
    cp = p.dma("G", xpre, xcur, group="xpre")
    p.barrier()
    tap(0)
    lffn_layer("0")
    tap(1)
    lffn_layer("1")
    tap(2)
    pn = NCProxy(nc, "al_")
    with ExitStack() as pes:
        emit_addln(p, pn, pes, xpre, xcur, xcur)
        p.barrier()
    tap(3)
    dense_block("1", xcur)
    tap(4)
    pn = NCProxy(nc, "mo_")
    with ExitStack() as pes:
        emit_moe(p, pn, pes, xcur, rw, rbrow, upT, downR, ident, rowsel, out)
        p.barrier()
    p.emit()
    es.close()
    return nc



def _cache_build(key, fn):
    if key not in _NC_CACHE:
        _NC_CACHE[key] = fn()
    return _NC_CACHE[key]


def _fm(x):
    return np.ascontiguousarray(x.T)


def host_tri_masks():
    ks = np.arange(128)
    qs = np.arange(512)
    m = np.zeros((4, 128, 512), dtype=BF)
    for s in range(4):
        m[s] = ((128 * s + ks[:, None]) <= qs[None, :]).astype(BF)
    return m


def host_local_masks(first_window_of_seq):
    ks = np.arange(128)
    qs = np.arange(512)
    out = np.zeros((4, 8, 128, 512), dtype=BF)
    for w in range(4):
        for kc in range(2):
            for s in range(4):
                klocal = kc * 512 + s * 128 + ks[:, None]
                ql = qs[None, :]
                ok = (klocal > ql) & (klocal <= ql + 512)
                if w == 0 and first_window_of_seq:
                    ok = ok & (klocal >= 512)
                out[w, kc * 4 + s] = ok.astype(BF)
    return out.reshape(32, 128, 512)


def host_rotary_tables():
    DH, WIN = 64, 512
    half = np.arange(0, DH, 2, dtype=np.float64)
    inv_freq = 1.0 / (10000.0 ** (half / DH))
    t = np.arange(T, dtype=np.float64)
    freqs = t[:, None] * inv_freq[None, :]
    freqs = np.concatenate([freqs, freqs], axis=1)
    base = (half + 0.4 * DH) / (1.4 * DH)
    power = (t - T // 2) / (WIN // 2)
    scale = base[None, :] ** power[:, None]
    scale = np.concatenate([scale, scale], axis=1)
    cos, sin = np.cos(freqs), np.sin(freqs)
    AQ = (cos * scale).T.astype(np.float32)
    BQr = (sin * scale).T.astype(np.float32)
    AK = (cos / scale).T.astype(np.float32)
    BKr = (sin / scale).T.astype(np.float32)
    sgn = np.concatenate([-np.ones(32), np.ones(32)]).astype(np.float32)[:, None]
    return AQ, BQr * sgn, AK, BKr * sgn


def host_E2():
    e = np.zeros((128, 128), dtype=np.float32)
    e[0:64, 0:64] = 1.0
    e[64:128, 64:128] = 1.0
    return e


_IDENT = np.eye(128, dtype=np.float32)
_ROWSEL = np.zeros((8, 8 * 128), dtype=np.float32)
for _e in range(8):
    _ROWSEL[_e, _e * 128:(_e + 1) * 128] = 1.0


def run_dense_attn_block(x, Wq, bq, Wk, bk, Wv, bv, Wo, bo):
    nc1 = _cache_build("qkv", build_qkv_launch)
    in1 = []
    for c in range(NCORES):
        b, s = divmod(c, 4)
        sl = x[b, s * TPC:(s + 1) * TPC]
        in1.append(dict(xT=_fm(sl), wq=Wq, wk=Wk, wv=Wv,
                        bq=np.ascontiguousarray(bq.reshape(NCH, 128).T),
                        bk=np.ascontiguousarray(bk.reshape(NCH, 128).T),
                        bvrow=bv.reshape(1, D)))
    r1 = run_bass_kernel_spmd(nc1, in1, core_ids=list(range(NCORES)))
    QF = np.zeros((B, D, T), dtype=BF)
    KF = np.zeros((B, D, T), dtype=BF)
    VT = np.zeros((B, T, D), dtype=BF)
    for c in range(NCORES):
        b, s = divmod(c, 4)
        QF[b][:, s * TPC:(s + 1) * TPC] = r1.results[c]["qx"]
        KF[b][:, s * TPC:(s + 1) * TPC] = r1.results[c]["kx"]
        VT[b][s * TPC:(s + 1) * TPC] = r1.results[c]["vx"]
    nc2 = _cache_build("flash", build_flash_dense)
    tri = host_tri_masks()
    in2 = []
    for c in range(NCORES):
        pairs = [(c * 4 + i) for i in range(4)]
        q4 = np.stack([np.ascontiguousarray(QF[pr // 16][(pr % 16) * 64:(pr % 16) * 64 + 64]) for pr in pairs])
        k4 = np.stack([np.ascontiguousarray(KF[pr // 16][(pr % 16) * 64:(pr % 16) * 64 + 64]) for pr in pairs])
        v4 = np.stack([np.ascontiguousarray(VT[pr // 16][:, (pr % 16) * 64:(pr % 16) * 64 + 64]) for pr in pairs])
        in2.append(dict(q4=q4, k4=k4, v4=v4, ident=_IDENT, trimask=tri))
    r2 = run_bass_kernel_spmd(nc2, in2, core_ids=list(range(NCORES)))
    OF = np.zeros((B, D, T), dtype=BF)
    for c in range(NCORES):
        for i in range(4):
            pr = c * 4 + i
            OF[pr // 16][(pr % 16) * 64:(pr % 16) * 64 + 64] = r2.results[c]["o4"][i]
    nc3 = _cache_build("projln", build_proj_ln)
    in3 = []
    for c in range(NCORES):
        b, s = divmod(c, 4)
        in3.append(dict(oT=np.ascontiguousarray(OF[b][:, s * TPC:(s + 1) * TPC]),
                        wo=Wo, bo=np.ascontiguousarray(bo.reshape(NCH, 128).T),
                        xT=_fm(x[b, s * TPC:(s + 1) * TPC])))
    r3 = run_bass_kernel_spmd(nc3, in3, core_ids=list(range(NCORES)))
    xo = np.zeros((B, T, D), dtype=np.float32)
    for c in range(NCORES):
        b, s = divmod(c, 4)
        xo[b, s * TPC:(s + 1) * TPC] = r3.results[c]["xn"].T
    return xo


def run_lffn_layer(x, wqkv_l, qs_l, ks_l, wo_l, lg_l, lb_l, w1_l, b1_l, w2_l, b2_l):
    AQ, BQ, AK, BK = host_rotary_tables()
    E2 = host_E2()
    HID = 2730
    w1p = np.zeros((D, 2 * HID_P), dtype=np.float32)
    w1p[:, 0:HID] = w1_l[:, 0:HID]
    w1p[:, HID_P:HID_P + HID] = w1_l[:, HID:2 * HID]
    b1p = np.zeros(2 * HID_P, dtype=np.float32)
    b1p[0:HID] = b1_l[0:HID]
    b1p[HID_P:HID_P + HID] = b1_l[HID:2 * HID]
    w2p = np.zeros((HID_P, D), dtype=np.float32)
    w2p[0:HID] = w2_l

    nc = _cache_build("lffn_qkv", build_lffn_qkv)
    ins = []
    for c in range(NCORES):
        b, s = divmod(c, 4)
        sl = x[b, s * TPC:(s + 1) * TPC]
        xprev = (x[b, s * TPC - 1] if s > 0 else np.zeros(D, dtype=np.float32)).reshape(D, 1)
        t0 = s * TPC
        tiles = {n: np.ascontiguousarray(np.tile(tab[:, t0:t0 + TPC], (2, 1)))
                 for n, tab in (("AQ", AQ), ("BQ", BQ), ("AK", AK), ("BK", BK))}
        ins.append(dict(xT=_fm(sl), xprevT=np.ascontiguousarray(xprev), wqkv=wqkv_l,
                        qs2=np.tile(qs_l, 2).reshape(128, 1).astype(np.float32),
                        ks2=np.tile(ks_l, 2).reshape(128, 1).astype(np.float32),
                        E2=E2, **tiles))
    r1 = run_bass_kernel_spmd(nc, ins, core_ids=list(range(NCORES)))
    QF = np.zeros((B, INNER, T), dtype=BF)
    KF = np.zeros((B, INNER, T), dtype=BF)
    VT = np.zeros((B, T, INNER), dtype=BF)
    for c in range(NCORES):
        b, s = divmod(c, 4)
        QF[b][:, s * TPC:(s + 1) * TPC] = r1.results[c]["qx"]
        KF[b][:, s * TPC:(s + 1) * TPC] = r1.results[c]["kx"]
        VT[b][s * TPC:(s + 1) * TPC] = r1.results[c]["vx"]

    nc2 = _cache_build("lffn_attn", build_lffn_attn)
    ins2 = []
    for c in range(NCORES):
        b, s = divmod(c, 4)
        t0 = s * TPC
        if s > 0:
            kfb = KF[b][:, t0 - 512: t0 + TPC]
            vfb = VT[b][t0 - 512: t0 + TPC]
        else:
            kfb = np.concatenate([np.zeros((INNER, 512), dtype=BF), KF[b][:, 0:TPC]], axis=1)
            vfb = np.concatenate([np.zeros((512, INNER), dtype=BF), VT[b][0:TPC]], axis=0)
        ins2.append(dict(qx=np.ascontiguousarray(QF[b][:, t0:t0 + TPC]),
                         kfb=np.ascontiguousarray(kfb), vfb=np.ascontiguousarray(vfb),
                         masks=host_local_masks(first_window_of_seq=(s == 0)),
                         ident=_IDENT))
    r2 = run_bass_kernel_spmd(nc2, ins2, core_ids=list(range(NCORES)))

    nc3 = _cache_build("lffn_proj", build_lffn_proj)
    ins3 = []
    for c in range(NCORES):
        b, s = divmod(c, 4)
        ins3.append(dict(oT=r2.results[c]["ox"], wo=wo_l,
                         xT=_fm(x[b, s * TPC:(s + 1) * TPC])))
    r3 = run_bass_kernel_spmd(nc3, ins3, core_ids=list(range(NCORES)))
    x2 = np.zeros((B, T, D), dtype=np.float32)
    for c in range(NCORES):
        b, s = divmod(c, 4)
        x2[b, s * TPC:(s + 1) * TPC] = r3.results[c]["x2"].T

    nc4 = _cache_build("lffn_ffn", build_lffn_ffn)
    ins4 = []
    for c in range(NCORES):
        b, s = divmod(c, 4)
        xprev = (x2[b, s * TPC - 1] if s > 0 else np.zeros(D, dtype=np.float32)).reshape(D, 1)
        ins4.append(dict(x2T=_fm(x2[b, s * TPC:(s + 1) * TPC]),
                         x2prevT=np.ascontiguousarray(xprev),
                         lg=np.ascontiguousarray(lg_l.reshape(NCH, 128).T),
                         lb=np.ascontiguousarray(lb_l.reshape(NCH, 128).T),
                         w1p=_w1R(w1p),
                         b1p=np.ascontiguousarray(b1p.reshape(2 * HID_P // 128, 128).T),
                         w2p=_w2R(w2p),
                         b2=np.ascontiguousarray(b2_l.reshape(NCH, 128).T)))
    r4 = run_bass_kernel_spmd(nc4, ins4, core_ids=list(range(NCORES)))
    x3 = np.zeros((B, T, D), dtype=np.float32)
    for c in range(NCORES):
        b, s = divmod(c, 4)
        x3[b, s * TPC:(s + 1) * TPC] = r4.results[c]["x3"].T
    return x3


def run_moe_block(x, router_w, router_b, expert_up, expert_down):
    upTh = np.ascontiguousarray(expert_up.transpose(0, 2, 1))
    downTh = _downR(np.asarray(expert_down, dtype=np.float32))
    nc = _cache_build("moe", build_moe)
    ins = []
    for c in range(NCORES):
        b, s = divmod(c, 4)
        ins.append(dict(xT=_fm(x[b, s * TPC:(s + 1) * TPC]), rw=router_w,
                        rbrow=router_b.reshape(1, NE), upT=upTh, downT=downTh,
                        ident=_IDENT, rowsel=_ROWSEL))
    r = run_bass_kernel_spmd(nc, ins, core_ids=list(range(NCORES)))
    xo = np.zeros((B, T, D), dtype=np.float32)
    for c in range(NCORES):
        b, s = divmod(c, 4)
        xo[b, s * TPC:(s + 1) * TPC] = r.results[c]["x4"].T
    return xo


def run_addln(x, y):
    nc = _cache_build("addln", build_addln)
    ins = []
    for c in range(NCORES):
        b, s = divmod(c, 4)
        ins.append(dict(xT=_fm(x[b, s * TPC:(s + 1) * TPC]), yT=_fm(y[b, s * TPC:(s + 1) * TPC])))
    r = run_bass_kernel_spmd(nc, ins, core_ids=list(range(NCORES)))
    xo = np.zeros((B, T, D), dtype=np.float32)
    for c in range(NCORES):
        b, s = divmod(c, 4)
        xo[b, s * TPC:(s + 1) * TPC] = r.results[c]["xn"].T
    return xo


def _w1R(w1p):
    return np.ascontiguousarray(
        w1p.reshape(NCH, 128, 2, HID_P // 128, 128).transpose(2, 3, 1, 0, 4))


def _w2R(w2p):
    return np.ascontiguousarray(
        w2p.reshape(HID_P // 128, 128, NCH, 128).transpose(2, 1, 0, 3))


def _downR(ed):
    return np.ascontiguousarray(
        ed.reshape(NE, NCH, 128, DFF // 128, 128).transpose(0, 1, 4, 3, 2))


def kernel_fused(inp):
    g = {k: np.asarray(v) for k, v in inp.items()}
    x0 = np.asarray(g["L0_attn_x"], dtype=np.float32)
    wA = np.asarray(g["L1_attn_w_attn"], dtype=np.float32)
    bA = np.asarray(g["L1_attn_b_attn"], dtype=np.float32)
    AQ, BQ, AK, BK = host_rotary_tables()
    E2h = host_E2()
    HID = 2730
    trih = host_tri_masks()
    upTh = np.ascontiguousarray(np.asarray(g["L1_ffn_expert_up"], dtype=np.float32).transpose(0, 2, 1))
    downRh = _downR(np.asarray(g["L1_ffn_expert_down"], dtype=np.float32))

    def colmat(v):
        return np.ascontiguousarray(np.asarray(v, dtype=np.float32).reshape(NCH, 128).T)

    lws = []
    for l in range(2):
        w1_l = np.asarray(g["L0_ffn_w1"][l], dtype=np.float32)
        b1_l = np.asarray(g["L0_ffn_b1"][l], dtype=np.float32)
        w2_l = np.asarray(g["L0_ffn_w2"][l], dtype=np.float32)
        w1p = np.zeros((D, 2 * HID_P), dtype=np.float32)
        w1p[:, 0:HID] = w1_l[:, 0:HID]
        w1p[:, HID_P:HID_P + HID] = w1_l[:, HID:2 * HID]
        b1p = np.zeros(2 * HID_P, dtype=np.float32)
        b1p[0:HID] = b1_l[0:HID]
        b1p[HID_P:HID_P + HID] = b1_l[HID:2 * HID]
        w2p = np.zeros((HID_P, D), dtype=np.float32)
        w2p[0:HID] = w2_l
        sl = str(l)
        lws.append({
            "wqkv" + sl: np.ascontiguousarray(np.asarray(g["L0_ffn_wqkv"][l], dtype=np.float32)),
            "qs2" + sl: np.tile(np.asarray(g["L0_ffn_q_scale"][l], dtype=np.float32), 2).reshape(128, 1),
            "ks2" + sl: np.tile(np.asarray(g["L0_ffn_k_scale"][l], dtype=np.float32), 2).reshape(128, 1),
            "lwo" + sl: np.ascontiguousarray(np.asarray(g["L0_ffn_wo"][l], dtype=np.float32)),
            "lg" + sl: colmat(g["L0_ffn_ln_g"][l]),
            "lb" + sl: colmat(g["L0_ffn_ln_b"][l]),
            "w1p" + sl: _w1R(w1p),
            "b1p" + sl: np.ascontiguousarray(b1p.reshape(2 * HID_P // 128, 128).T),
            "w2p" + sl: _w2R(w2p),
            "b2" + sl: colmat(g["L0_ffn_b2"][l]),
        })

    dense0 = dict(
        wq0=np.asarray(g["L0_attn_Wq"], dtype=np.float32), wk0=np.asarray(g["L0_attn_Wk"], dtype=np.float32),
        wv0=np.asarray(g["L0_attn_Wv"], dtype=np.float32), wo0=np.asarray(g["L0_attn_Wo"], dtype=np.float32),
        bq0=colmat(g["L0_attn_bq"]), bk0=colmat(g["L0_attn_bk"]), bo0=colmat(g["L0_attn_bo"]),
        bvrow0=np.asarray(g["L0_attn_bv"], dtype=np.float32).reshape(1, D))
    dense1 = dict(
        wq1=np.ascontiguousarray(wA[:, 0:D]), wk1=np.ascontiguousarray(wA[:, D:2 * D]),
        wv1=np.ascontiguousarray(wA[:, 2 * D:3 * D]),
        wo1=np.asarray(g["L1_attn_w_proj"], dtype=np.float32),
        bq1=colmat(bA[0:D]), bk1=colmat(bA[D:2 * D]), bo1=colmat(g["L1_attn_b_proj"]),
        bvrow1=bA[2 * D:3 * D].reshape(1, D))

    nc = _cache_build("mega", build_mega)
    ins = []
    for c in range(NCORES):
        b, s = divmod(c, 4)
        t0 = s * TPC
        tiles = {"t" + n: np.ascontiguousarray(np.tile(tab[:, t0:t0 + TPC], (2, 1)))
                 for n, tab in (("AQ", AQ), ("BQ", BQ), ("AK", AK), ("BK", BK))}
        ixv = np.zeros((1, 64), dtype=np.uint32)
        ixv[0, 0] = b
        ixv[0, 1] = s
        ixv[0, 2] = (c + 7) % 8
        ixv[0, 3] = s
        bm = np.ones((128, NCH), dtype=np.float32) if s > 0 else np.zeros((128, NCH), dtype=np.float32)
        m = dict(x0T=_fm(x0[b, t0:t0 + TPC]), **dense0, **dense1, **lws[0], **lws[1],
                 **tiles, E2=E2h, lmasks=host_local_masks(first_window_of_seq=(s == 0)),
                 tri=trih, ident=_IDENT, rowsel=_ROWSEL,
                 rw=np.asarray(g["L1_ffn_router_w"], dtype=np.float32),
                 rbrow=np.asarray(g["L1_ffn_router_b"], dtype=np.float32).reshape(1, NE),
                 upT=upTh, downR=downRh, bmask=bm, ix=ixv)
        ins.append(m)
    r = run_bass_kernel_spmd(nc, ins, core_ids=list(range(NCORES)))
    xo = np.zeros((B, T, D), dtype=np.float32)
    for c in range(NCORES):
        b, s = divmod(c, 4)
        xo[b, s * TPC:(s + 1) * TPC] = r.results[c]["out"].T
    if MEGA_DEBUG:
        global MEGA_TAPS, MEGA_RAW
        MEGA_TAPS = np.zeros((5, B, T, D), dtype=np.float32)
        for c in range(NCORES):
            b, s = divmod(c, 4)
            for i in range(5):
                MEGA_TAPS[i, b, s * TPC:(s + 1) * TPC] = r.results[c]["xdbg"][i].T
        MEGA_RAW = r.results
    return xo


def kernel(L0_attn_x, L0_attn_Wq, L0_attn_bq, L0_attn_Wk, L0_attn_bk, L0_attn_Wv,
           L0_attn_bv, L0_attn_Wo, L0_attn_bo, L0_ffn_wqkv, L0_ffn_q_scale,
           L0_ffn_k_scale, L0_ffn_wo, L0_ffn_ln_g, L0_ffn_ln_b, L0_ffn_w1, L0_ffn_b1,
           L0_ffn_w2, L0_ffn_b2, L1_attn_w_attn, L1_attn_b_attn, L1_attn_w_proj,
           L1_attn_b_proj, L1_ffn_router_w, L1_ffn_router_b, L1_ffn_expert_up,
           L1_ffn_expert_down):
    g = {k: np.ascontiguousarray(np.asarray(v), dtype=None) for k, v in list(locals().items())
         if k != "g"}
    return kernel_fused(g)
    x = np.asarray(g["L0_attn_x"], dtype=np.float32)
    x = run_dense_attn_block(x, g["L0_attn_Wq"], g["L0_attn_bq"], g["L0_attn_Wk"],
                             g["L0_attn_bk"], g["L0_attn_Wv"], g["L0_attn_bv"],
                             g["L0_attn_Wo"], g["L0_attn_bo"])
    y = x
    for l in range(2):
        y = run_lffn_layer(y, g["L0_ffn_wqkv"][l], g["L0_ffn_q_scale"][l],
                           g["L0_ffn_k_scale"][l], g["L0_ffn_wo"][l], g["L0_ffn_ln_g"][l],
                           g["L0_ffn_ln_b"][l], g["L0_ffn_w1"][l], g["L0_ffn_b1"][l],
                           g["L0_ffn_w2"][l], g["L0_ffn_b2"][l])
    x = run_addln(x, y)
    wA = np.asarray(g["L1_attn_w_attn"], dtype=np.float32)
    bA = np.asarray(g["L1_attn_b_attn"], dtype=np.float32)
    x = run_dense_attn_block(x, np.ascontiguousarray(wA[:, 0:D]), bA[0:D],
                             np.ascontiguousarray(wA[:, D:2 * D]), bA[D:2 * D],
                             np.ascontiguousarray(wA[:, 2 * D:3 * D]), bA[2 * D:3 * D],
                             g["L1_attn_w_proj"], g["L1_attn_b_proj"])
    x = run_moe_block(x, g["L1_ffn_router_w"], g["L1_ffn_router_b"],
                      np.asarray(g["L1_ffn_expert_up"], dtype=np.float32),
                      np.asarray(g["L1_ffn_expert_down"], dtype=np.float32))
    return x


# version 07 (final)
# speedup vs baseline: 1.2094x; 1.0006x over previous
import numpy as np
import ml_dtypes
from contextlib import ExitStack

import concourse.bass as bass
import concourse.mybir as mybir
from concourse.bass_utils import run_bass_kernel_spmd

F32 = mybir.dt.float32
BF16 = mybir.dt.bfloat16
AF = mybir.ActivationFunctionType
ALU = mybir.AluOpType
AX = mybir.AxisListType
BF = ml_dtypes.bfloat16

NCORES = 8
B, T, D = 2, 8192, 1024
TPC = 2048
NCH = D // 128
NTT = TPC // 512

_NC_CACHE = {}


class Prog:

    ENGS = ("T", "V", "S", "G", "Y")

    def __init__(self, nc, es):
        self.nc = nc
        self.es = es
        self.q = {e: [] for e in self.ENGS}
        self.sem = {e: es.enter_context(nc.semaphore(f"sem{e}")) for e in self.ENGS}
        self.cnt = {e: 0 for e in self.ENGS}
        self.gsem = {}

    def op(self, eng, fn, waits=(), inc=True):
        waits = [w for w in waits if w is not None]
        tok = None
        if inc:
            self.cnt[eng] += 1
            tok = (eng, self.cnt[eng])
        self.q[eng].append((fn, tuple(waits), tok))
        return tok

    def dma(self, eng, out, in_, waits=(), group=None, **kw):
        assert group is not None
        waits = [w for w in waits if w is not None]
        if group not in self.gsem:
            self.gsem[group] = [self.es.enter_context(self.nc.semaphore(f"g_{group}")), 0]
        ent = self.gsem[group]
        ent[1] += 16
        tok = ("g:" + group, ent[1])
        sem = ent[0]

        def fn(e, out=out, in_=in_, kw=kw):
            return e.dma_start(out=out, in_=in_, **kw).then_inc(sem, 16)

        self.q[eng].append((fn, tuple(waits), None))
        return tok

    def collective(self, kind, ins, outs, waits=()):
        waits = [w for w in waits if w is not None]
        self.cnt["G"] += 1
        tok = ("G", self.cnt["G"])
        sem = self.sem["G"]

        def fn(e, ins=ins, outs=outs):
            return e.collective_compute(
                kind, ALU.bypass, replica_groups=[list(range(NCORES))],
                ins=ins, outs=outs).then_inc(sem, 1)

        self.q["G"].append((fn, tuple(waits), None))
        return tok

    def raw_dma(self, eng, fn, waits=(), group=None):
        assert group is not None
        waits = [w for w in waits if w is not None]
        if group not in self.gsem:
            self.gsem[group] = [self.es.enter_context(self.nc.semaphore(f"g_{group}")), 0]
        ent = self.gsem[group]
        ent[1] += 16
        tok = ("g:" + group, ent[1])
        sem = ent[0]

        def fn2(e, fn=fn, sem=sem):
            return fn(e).then_inc(sem, 16)

        self.q[eng].append((fn2, tuple(waits), None))
        return tok

    def gtotal(self, group):
        return ("g:" + group, self.gsem[group][1])

    def barrier(self):
        waits = tuple(("" + e, self.cnt[e]) for e in self.ENGS if self.cnt[e] > 0)
        gwaits = tuple(("g:" + g, ent[1]) for g, ent in self.gsem.items())
        for e in self.ENGS:
            self.q[e].append(((lambda engine: None), waits + gwaits, None))

    def _resolve(self, t):
        name, v = t
        if name.startswith("g:"):
            return self.gsem[name[2:]][0], v
        return self.sem[name], v

    def emit(self):
        for gname, (sem, cnt) in self.gsem.items():
            self.q["Y"].append(((lambda e: None), (("g:" + gname, cnt),), None))
        block = self.es.enter_context(self.nc.Block())

        def body(engine, key):
            for fn, waits, tok in self.q[key]:
                for t in waits:
                    s, v = self._resolve(t)
                    engine.wait_ge(s, v)
                ins = fn(engine)
                if tok is not None:
                    ins.then_inc(self.sem[tok[0]], 1)

        block.tensor(lambda e: body(e, "T"))
        block.vector(lambda e: body(e, "V"))
        block.scalar(lambda e: body(e, "S"))
        block.gpsimd(lambda e: body(e, "G"))
        block.sync(lambda e: body(e, "Y"))


class NCProxy:

    def __init__(self, nc, tag):
        self._nc = nc
        self._tag = tag

    def sbuf_tensor(self, name, shape, dtype, **kw):
        return self._nc.sbuf_tensor(self._tag + name, shape, dtype, **kw)

    def psum_tensor(self, name, shape, dtype=F32, **kw):
        return self._nc.psum_tensor(self._tag + name, shape, dtype, **kw)

    def __getattr__(self, k):
        return getattr(self._nc, k)



def alloc_ln_bufs(nc, es, tag="", reuse=None):
    b = {}
    b["ln_sq"] = es.enter_context(nc.sbuf_tensor(f"lnsq{tag}", [128, 512], F32))
    b["ln_stat"] = es.enter_context(nc.sbuf_tensor(f"lnst{tag}", [1, 1024], F32))
    b["ln_tmp"] = es.enter_context(nc.sbuf_tensor(f"lntmp{tag}", [128, 512], F32))
    b["ones_col"] = es.enter_context(nc.sbuf_tensor(f"lnoc{tag}", [128, 1], F32))
    b["ones1"] = es.enter_context(nc.sbuf_tensor(f"lno1{tag}", [1, 128], F32))
    if reuse is None:
        b["ln_psS"] = es.enter_context(nc.psum_tensor(f"lnpsS{tag}", [1, 512], F32))[:]
        b["ln_psS2"] = es.enter_context(nc.psum_tensor(f"lnpsS2{tag}", [1, 512], F32))[:]
        b["ln_psB"] = es.enter_context(nc.psum_tensor(f"lnpsB{tag}", [128, 512], F32))[:]
        b["ln_psB2"] = es.enter_context(nc.psum_tensor(f"lnpsB2{tag}", [128, 512], F32))[:]
    else:
        b["ln_psS"] = reuse["psS"]
        b["ln_psS2"] = reuse["psS2"]
        b["ln_psB"] = reuse["psB"]
        b["ln_psB2"] = reuse["psB2"]
    return b


def emit_ln_init(p, b):
    t1 = p.op("V", lambda v: v.memset(b["ones_col"][:], 1.0))
    t2 = p.op("V", lambda v: v.memset(b["ones1"][:], 1.0))
    return [t1, t2]


def emit_ln(p, b, x_ap_fn, n_tok, out_fn, gamma_scale=None, gamma_bias=None,
            eps=1e-5, pre_waits=(), extra_out_fn=None, init_toks=()):
    sq, stat, tmp = b["ln_sq"], b["ln_stat"], b["ln_tmp"]
    psS, psS2, psB, psB2 = b["ln_psS"], b["ln_psS2"], b["ln_psB"], b["ln_psB2"]
    ones_col, ones1 = b["ones_col"], b["ones1"]
    n_tt = n_tok // 512
    out_toks = {}
    prev_apply = []
    prev_stats = []
    last = None
    for tt in range(n_tt):
        mu = stat[:, 0:512]
        va = stat[:, 512:1024]
        s_tok = None
        for c in range(NCH):
            w = list(pre_waits) + list(init_toks) + prev_stats if c == 0 else []
            s_tok = p.op("T", (lambda t, c=c, tt=tt: t.matmul(
                psS[:], ones_col[:], x_ap_fn(c, tt), start=(c == 0), stop=(c == NCH - 1))), w)
        prev_mt = None
        for c in range(NCH):
            w = (list(pre_waits) + prev_stats) if c == 0 else [prev_mt]
            st = p.op("S", (lambda s, c=c, tt=tt: s.activation(sq[:], x_ap_fn(c, tt), AF.Square)), w)
            prev_mt = p.op("T", (lambda t, c=c: t.matmul(
                psS2[:], ones_col[:], sq[:], start=(c == 0), stop=(c == NCH - 1))), [st])
        t1 = p.op("V", (lambda v, mu=mu: v.tensor_scalar_mul(mu, psS, 1.0 / D)), [s_tok])
        t2 = p.op("V", (lambda v, va=va: v.tensor_scalar_mul(va, psS2, 1.0 / D)), [prev_mt])
        t3 = p.op("V", (lambda v, mu=mu: v.tensor_mul(sq[0:1, 0:512], mu, mu)), [t1, t2])
        t4 = p.op("V", (lambda v, va=va: v.tensor_sub(va, va, sq[0:1, 0:512])), [t3])
        t4b = p.op("V", (lambda v, va=va: v.tensor_scalar_add(va, va, float(eps))), [t4])
        t5 = p.op("S", (lambda s, va=va: s.activation(va, va, AF.Sqrt)), [t4b])
        t6 = p.op("V", (lambda v, va=va: v.reciprocal(va, va)), [t5])
        b1 = p.op("T", (lambda t, mu=mu: t.matmul(psB, ones1[:], mu, start=True, stop=True)),
                  [t6] + prev_apply)
        b2 = p.op("T", (lambda t, va=va: t.matmul(psB2, ones1[:], va, start=True, stop=True)), [b1])
        prev_stats = [t3, t6]
        prev_apply = []
        for c in range(NCH):
            tA = p.op("V", (lambda v, c=c, tt=tt: v.tensor_sub(tmp[:], x_ap_fn(c, tt), psB)),
                      [b2] + ([prev_apply[-1]] if prev_apply else []))
            if gamma_scale is None:
                tB = p.op("V", (lambda v, c=c, tt=tt: v.tensor_mul(out_fn(c, tt), tmp[:], psB2)), [tA])
            else:
                tB0 = p.op("V", (lambda v: v.tensor_mul(tmp[:], tmp[:], psB2)), [tA])
                tB = p.op("V", (lambda v, c=c, tt=tt: v.tensor_scalar(
                    out_fn(c, tt), tmp[:], gamma_scale(c), gamma_bias(c), ALU.mult, ALU.add)), [tB0])
            if extra_out_fn is not None:
                tB = p.op("S", (lambda s, c=c, tt=tt: s.activation(
                    extra_out_fn(c, tt), out_fn(c, tt), AF.Copy)), [tB])
            out_toks[(c, tt)] = tB
            prev_apply.append(tB)
            last = tB
        prev_stats = prev_stats + [b1, b2]
    return out_toks, last



def emit_qkv(p, nc, es, xT, wq, wk, wv, bq, bk, bvrow, qx, kx, vx, tag=""):
    xbf = es.enter_context(nc.sbuf_tensor("xbf", [128, NCH * TPC], BF16))
    wqb = es.enter_context(nc.sbuf_tensor("wqb", [128, NCH * D], BF16))
    wkb = es.enter_context(nc.sbuf_tensor("wkb", [128, NCH * D], BF16))
    wvb = es.enter_context(nc.sbuf_tensor("wvb", [128, NCH * D], BF16))
    bqs = es.enter_context(nc.sbuf_tensor("bqs", [128, NCH], F32))
    bks = es.enter_context(nc.sbuf_tensor("bks", [128, NCH], F32))
    bvr = es.enter_context(nc.sbuf_tensor("bvr", [1, D], F32))
    bvb = es.enter_context(nc.sbuf_tensor("bvb", [128, D], F32))
    ones1 = es.enter_context(nc.sbuf_tensor("ones1", [1, 128], F32))
    osb = es.enter_context(nc.sbuf_tensor("osb", [128, 2 * TPC], BF16))
    vsb = es.enter_context(nc.sbuf_tensor("vsb", [128, 2 * D], BF16))
    ps = [es.enter_context(nc.psum_tensor(f"ps{i}", [128, 512], F32)) for i in range(2)]
    psv = [es.enter_context(nc.psum_tensor(f"psv{i}", [128, 512], F32)) for i in range(2)]
    psb = es.enter_context(nc.psum_tensor("psb", [128, 512], F32))

    ldx = p.dma("G", xbf[:].rearrange("p (c t) -> p c t", t=TPC),
                xT.rearrange("(c p) t -> p c t", p=128), group="ldx")
    ldwq = p.dma("G", wqb[:].rearrange("p (c o) -> p c o", o=D), wq.rearrange("(c p) o -> p c o", p=128), group="wq")
    ldwk = p.dma("G", wkb[:].rearrange("p (c o) -> p c o", o=D), wk.rearrange("(c p) o -> p c o", p=128), group="wk")
    ldwv = p.dma("G", wvb[:].rearrange("p (c o) -> p c o", o=D), wv.rearrange("(c p) o -> p c o", p=128), group="wv")
    ldbq = p.dma("Y", bqs[:], bq, group="bq")
    ldbk = p.dma("Y", bks[:], bk, group="bk")
    ldbv = p.dma("Y", bvr[:], bvrow, group="bv")
    onet = p.op("V", lambda v: v.memset(ones1[:], 1.0))
    prevc = onet
    bvb_toks = []
    for h in range(2):
        tb = p.op("T", (lambda t, h=h: t.matmul(psb[:], ones1[:], bvr[:, h * 512:(h + 1) * 512],
                                                start=True, stop=True)), [prevc, ldbv])
        prevc = p.op("V", (lambda v, h=h: v.tensor_copy(bvb[:, h * 512:(h + 1) * 512], psb[:])), [tb])
        bvb_toks.append(prevc)

    bank_free = [None, None]
    half_free = [None, None]
    for name, wb, bsbuf, ldwt, ldbt, out_dram in (
            ("q", wqb, bqs, ldwq, ldbq, qx), ("k", wkb, bks, ldwk, ldbk, kx)):
        for oc in range(NCH):
            half = oc % 2
            st_toks = []
            for tt in range(NTT):
                bi = tt % 2
                pt = ps[bi]
                w0 = [ldx, ldwt, ldbt, bank_free[bi], half_free[half]]
                mm = None
                for ic in range(NCH):
                    mm = p.op("T", (lambda t, ic=ic, oc=oc, tt=tt, pt=pt, wb=wb: t.matmul(
                        pt[:], wb[:, ic * D + oc * 128: ic * D + oc * 128 + 128],
                        xbf[:, ic * TPC + tt * 512: ic * TPC + (tt + 1) * 512],
                        start=(ic == 0), stop=(ic == NCH - 1))), w0 if ic == 0 else [])
                    w0 = []
                ev = p.op("V", (lambda v, tt=tt, pt=pt, oc=oc, half=half, bsbuf=bsbuf: v.tensor_scalar_add(
                    osb[:, half * TPC + tt * 512: half * TPC + (tt + 1) * 512], pt[:],
                    bsbuf[:, oc:oc + 1])), [mm])
                bank_free[bi] = ev
                st_toks.append(ev)
            dt = p.dma("G", out_dram.rearrange("(c p) t -> p c t", p=128)[:, oc, :],
                       osb[:, half * TPC: half * TPC + TPC], waits=st_toks,
                       group=f"oh{half}_{name}")
            half_free[half] = dt

    vbank_free = [None, None]
    vhalf_free = None
    for tc in range(TPC // 128):
        st_toks = []
        for half in range(2):
            pt = psv[half]
            w0 = [ldx, ldwv, vbank_free[half], vhalf_free]
            mm = None
            for ic in range(NCH):
                mm = p.op("T", (lambda t, ic=ic, tc=tc, half=half, pt=pt: t.matmul(
                    pt[:], xbf[:, ic * TPC + tc * 128: ic * TPC + (tc + 1) * 128],
                    wvb[:, ic * D + half * 512: ic * D + (half + 1) * 512],
                    start=(ic == 0), stop=(ic == NCH - 1))), w0 if ic == 0 else [])
                w0 = []
            ev = p.op("V", (lambda v, half=half, pt=pt: v.tensor_add(
                vsb[:, half * D // 2 + 0: 0] if False else vsb[:, half * 512:(half + 1) * 512],
                pt[:], bvb[:, half * 512:(half + 1) * 512])), [mm] + bvb_toks)
            vbank_free[half] = ev
            st_toks.append(ev)
        vhalf_free = p.dma("G", vx[tc * 128:(tc + 1) * 128, :], vsb[:, 0:D], waits=st_toks,
                           group="vout")

    return


def build_qkv_launch():
    nc = bass.Bass("TRN2", target_bir_lowering=False)
    names_i = [("xT", [D, TPC], F32), ("wq", [D, D], F32), ("wk", [D, D], F32),
               ("wv", [D, D], F32), ("bq", [128, NCH], F32), ("bk", [128, NCH], F32),
               ("bvrow", [1, D], F32)]
    names_o = [("qx", [D, TPC], BF16), ("kx", [D, TPC], BF16), ("vx", [TPC, D], BF16)]
    aps = {n: nc.dram_tensor(n, s, d, kind="ExternalInput").ap() for n, s, d in names_i}
    aps.update({n: nc.dram_tensor(n, s, d, kind="ExternalOutput").ap() for n, s, d in names_o})
    es = ExitStack()
    p = Prog(nc, es)
    emit_qkv(p, nc, es, **aps)
    p.emit()
    es.close()
    return nc



def emit_flash(p, nc, es, ident, trimask, o4, loader):
    SC = 0.125
    qe = es.enter_context(nc.sbuf_tensor("qe", [65, T], BF16))
    ke = es.enter_context(nc.sbuf_tensor("ke", [65, T], BF16))
    ve = es.enter_context(nc.sbuf_tensor("ve", [128, 65 * (T // 128)], BF16))
    idn = es.enter_context(nc.sbuf_tensor("idn", [128, 128], F32))
    tri = es.enter_context(nc.sbuf_tensor("tri", [128, 4 * 512], BF16))
    mneg = es.enter_context(nc.sbuf_tensor("mneg", [128, 4], F32))
    mtmp = es.enter_context(nc.sbuf_tensor("mtmp", [128, 2], F32))
    pt_sb = es.enter_context(nc.sbuf_tensor("pt_sb", [128, 2048], BF16))
    rb = es.enter_context(nc.sbuf_tensor("rb", [64, 512], F32))
    rec = es.enter_context(nc.sbuf_tensor("rec", [1, 512], F32))
    ones1 = es.enter_context(nc.sbuf_tensor("ones1", [1, 64], F32))
    ob = es.enter_context(nc.sbuf_tensor("ob", [64, 2 * 512], BF16))
    psA = [es.enter_context(nc.psum_tensor(f"psA{i}", [128, 512], F32)) for i in range(2)]
    psM = es.enter_context(nc.psum_tensor("psM", [1, 128], F32))
    psT = es.enter_context(nc.psum_tensor("psT", [128, 2048], F32))
    psO = es.enter_context(nc.psum_tensor("psO", [65, 512], F32))
    psR = psT[0:64, 0:512]
    p_ = p

    ldi = p.dma("Y", idn[:], ident, group="ldi")
    ldt = p.dma("Y", tri[:].rearrange("p (s n) -> p s n", n=512), trimask.rearrange("s p n -> p s n"),
                group="ldt")
    one1 = p.op("V", lambda v: v.memset(ones1[:], 1.0))
    onek = p.op("V", lambda v: v.memset(ke[64:65, :], 1.0))
    onev = p.op("V", lambda v: v.memset(ve[:].rearrange("p (c e) -> p c e", e=65)[:, :, 64], 1.0))

    NQG = T // 512
    prev_pair_done = [onek, onev]
    prev_cpm = None
    ob_last = [None, None]
    pending_norm = [None]
    last_st0 = [None]

    for pair in range(4):
        ldq, ldk, ldv = loader(pair, prev_pair_done, qe, ke, ve)
        prev_avm = [None]
        prev_red = [None, None]
        for g in range(NQG):
            nkc = g + 1
            if pending_norm[0] is not None:
                pending_norm[0]()
                pending_norm[0] = None
            tr_toks = []
            for qt in range(4):
                qsl = qe[0:64, g * 512 + qt * 128: g * 512 + qt * 128 + 128]
                red_prev = None
                for kc in range(nkc):
                    bi = kc % 2
                    w = [ldq, ldk] if (qt == 0 and kc == 0) else []
                    w = w + [prev_red[bi]]
                    mm = p.op("T", (lambda t, qsl=qsl, kc=kc, bi=bi: t.matmul(
                        psA[bi][:], qsl, ke[0:64, kc * 512:(kc + 1) * 512],
                        start=True, stop=True)), w)
                    if kc == 0:
                        red_prev = p.op("V", (lambda v, qt=qt, bi=bi: v.reduce_max(
                            mneg[:, qt:qt + 1], psA[bi][:], axis=AX.X, negate=True)), [mm])
                    else:
                        r1 = p.op("V", (lambda v, bi=bi: v.reduce_max(
                            mtmp[:, bi:bi + 1], psA[bi][:], axis=AX.X, negate=True)), [mm])
                        red_prev = p.op("V", (lambda v, qt=qt, bi=bi: v.tensor_tensor(
                            mneg[:, qt:qt + 1], mneg[:, qt:qt + 1], mtmp[:, bi:bi + 1],
                            ALU.min)), [r1])
                    prev_red[bi] = red_prev
                trm = p.op("T", (lambda t, qt=qt: t.transpose(psM[:], mneg[:, qt:qt + 1], idn[:])),
                           [red_prev, ldi, prev_cpm])
                cpm = p.op("V", (lambda v, g=g, qt=qt: v.tensor_copy(
                    qe[64:65, g * 512 + qt * 128: g * 512 + qt * 128 + 128], psM[:])), [trm])
                prev_cpm = cpm
                tr_toks.append(cpm)
            qg = qe[:, g * 512:(g + 1) * 512]
            first_st_waits = tr_toks + [prev_avm[0]]
            ex_prev = [None, None]
            avm = None
            for kc in range(nkc):
                for hf in range(2):
                    mm2 = None
                    for s2 in range(2):
                        s = hf * 2 + s2
                        w = list(first_st_waits) + [ex_prev[hf]]
                        if s == 0:
                            w = w + [last_st0[0]]
                        first_st_waits = []
                        mm2 = p.op("T", (lambda t, kc=kc, s=s, qg=qg: t.matmul(
                            psT[:, s * 512:(s + 1) * 512],
                            ke[:, kc * 512 + s * 128: kc * 512 + s * 128 + 128], qg,
                            start=True, stop=True)), w)
                        if s == 0:
                            last_st0[0] = mm2
                    ex = p.op("S", (lambda s_, hf=hf: s_.activation(
                        pt_sb[:, hf * 1024:(hf + 1) * 1024],
                        psT[:, hf * 1024:(hf + 1) * 1024], AF.Exp, scale=SC)),
                        [mm2, ("avdep", None)] if False else [mm2, prev_avm[0]])
                    ex_prev[hf] = ex
                    dep = [ex]
                    if kc == g:
                        for s2 in range(2):
                            s = hf * 2 + s2
                            dep = [p.op("V", (lambda v, s=s: v.tensor_mul(
                                pt_sb[:, s * 512:(s + 1) * 512], pt_sb[:, s * 512:(s + 1) * 512],
                                tri[:, s * 512:(s + 1) * 512])), dep + ([ldt] if s2 == 0 else []))]
                    for s2 in range(2):
                        s = hf * 2 + s2
                        w = (dep + [ldv]) if s2 == 0 else []
                        avm = p.op("T", (lambda t, kc=kc, s=s, g=g: t.matmul(
                            psO[:], ve[:, (kc * 4 + s) * 65:(kc * 4 + s) * 65 + 65],
                            pt_sb[:, s * 512:(s + 1) * 512],
                            start=(kc == 0 and s == 0), stop=(kc == nkc - 1 and s == 3))), w)
            prev_avm[0] = avm

            def make_norm(g=g, pair=pair, avm=avm):
                def donorm():
                    rc = p_.op("V", (lambda v: v.reciprocal(rec[:], psO[64:65, :])), [avm])
                    bc = p_.op("T", (lambda t: t.matmul(psR, ones1[:], rec[:],
                                                        start=True, stop=True)),
                               [rc, one1, last_st0[0], ex_prev[0]])
                    last_st0[0] = None
                    cb = p_.op("V", (lambda v: v.tensor_copy(rb[:], psR)), [bc])
                    ot = p_.op("V", (lambda v, g=g: v.tensor_mul(
                        ob[:, (g % 2) * 512:(g % 2) * 512 + 512], psO[0:64, :], rb[:])),
                        [cb, ob_last[g % 2]])
                    ob_last[g % 2] = p_.dma("G", o4[pair][:, g * 512:(g + 1) * 512],
                                            ob[:, (g % 2) * 512:(g % 2) * 512 + 512],
                                            waits=[ot], group=f"ob{g % 2}")
                    prev_avm[0] = ot
                    last_st0[0] = cb
                return donorm
            pending_norm[0] = make_norm()
        pending_norm[0]()
        pending_norm[0] = None
        prev_pair_done = [prev_avm[0], ob_last[0], ob_last[1], prev_cpm]
    return


def default_flash_loader(p, q4, k4, v4):
    def loader(pair, prev_done, qe, ke, ve):
        ldq = p.dma("G", qe[0:64, :], q4[pair], waits=prev_done, group="ldq")
        ldk = p.dma("G", ke[0:64, :], k4[pair], waits=prev_done, group="ldk")
        ldv = p.dma("G", ve[:].rearrange("p (c e) -> p c e", e=65)[:, :, 0:64],
                    v4[pair].rearrange("(c p) d -> p c d", p=128), waits=prev_done,
                    group="ldv")
        return ldq, ldk, ldv
    return loader


def build_flash_dense():
    nc = bass.Bass("TRN2", target_bir_lowering=False)
    q4 = nc.dram_tensor("q4", [4, 64, T], BF16, kind="ExternalInput").ap()
    k4 = nc.dram_tensor("k4", [4, 64, T], BF16, kind="ExternalInput").ap()
    v4 = nc.dram_tensor("v4", [4, T, 64], BF16, kind="ExternalInput").ap()
    ident = nc.dram_tensor("ident", [128, 128], F32, kind="ExternalInput").ap()
    trimask = nc.dram_tensor("trimask", [4, 128, 512], BF16, kind="ExternalInput").ap()
    o4 = nc.dram_tensor("o4", [4, 64, T], BF16, kind="ExternalOutput").ap()
    es = ExitStack()
    p = Prog(nc, es)
    emit_flash(p, nc, es, ident, trimask, o4, default_flash_loader(p, q4, k4, v4))
    p.emit()
    es.close()
    return nc



def emit_proj_ln(p, nc, es, oT, wo, bo, xT, xn, o_loader=None):
    obf = es.enter_context(nc.sbuf_tensor("obf", [128, NCH * TPC], BF16))
    wob = es.enter_context(nc.sbuf_tensor("wob", [128, NCH * D], BF16))
    bos = es.enter_context(nc.sbuf_tensor("bos", [128, NCH], F32))
    xr = es.enter_context(nc.sbuf_tensor("xr", [128, NCH * TPC], F32))
    xo = es.enter_context(nc.sbuf_tensor("xo", [128, 512], F32))
    ps = [es.enter_context(nc.psum_tensor(f"ps{i}", [128, 512], F32)) for i in range(2)]
    lnb = alloc_ln_bufs(nc, es)

    if o_loader is None:
        ldo = p.dma("G", obf[:].rearrange("p (c t) -> p c t", t=TPC),
                    oT.rearrange("(c p) t -> p c t", p=128), group="ldo")
    else:
        ldo = o_loader(obf)
    ldw = p.dma("G", wob[:].rearrange("p (c o) -> p c o", o=D), wo.rearrange("(c p) o -> p c o", p=128), group="ldw")
    ldb = p.dma("Y", bos[:], bo, group="ldb")
    ldxr = p.dma("G", xr[:].rearrange("p (c t) -> p c t", t=TPC),
                 xT.rearrange("(c p) t -> p c t", p=128), group="ldxr")
    init = emit_ln_init(p, lnb)

    bank_free = [None, None]
    res_toks = []
    for oc in range(NCH):
        for tt in range(NTT):
            bi = (oc * NTT + tt) % 2
            pt = ps[bi]
            w0 = [ldo, ldw, ldb, ldxr, bank_free[bi]]
            mm = None
            for ic in range(NCH):
                mm = p.op("T", (lambda t, ic=ic, oc=oc, tt=tt, pt=pt: t.matmul(
                    pt[:], wob[:, ic * D + oc * 128: ic * D + oc * 128 + 128],
                    obf[:, ic * TPC + tt * 512: ic * TPC + (tt + 1) * 512],
                    start=(ic == 0), stop=(ic == NCH - 1))), w0 if ic == 0 else [])
                w0 = []
            ev = p.op("V", (lambda v, pt=pt, oc=oc: v.tensor_scalar_add(
                xo[:], pt[:], bos[:, oc:oc + 1])), [mm])
            bank_free[bi] = ev
            ad = p.op("V", (lambda v, oc=oc, tt=tt: v.tensor_add(
                xr[:, oc * TPC + tt * 512: oc * TPC + (tt + 1) * 512],
                xr[:, oc * TPC + tt * 512: oc * TPC + (tt + 1) * 512], xo[:])), [ev])
            bank_free[bi] = ad
            res_toks.append(ad)

    def x_ap(c, tt):
        return xr[:, c * TPC + tt * 512: c * TPC + (tt + 1) * 512]

    toks, last = emit_ln(p, lnb, x_ap, TPC, out_fn=x_ap, pre_waits=res_toks, init_toks=init)
    for c in range(NCH):
        for tt in range(NTT):
            p.dma("G", xn.rearrange("(c p) t -> p c t", p=128)[:, c, tt * 512:(tt + 1) * 512],
                  x_ap(c, tt), waits=[toks[(c, tt)]], group="xout")
    return



INNER = 512
NQC = 4


def emit_lffn_qkv(p, nc, es, xT, xprevT, wqkv, qs2, ks2, tabs, E2, qx, kx, vx):
    W = 2049
    xbf = es.enter_context(nc.sbuf_tensor("xbf", [128, NCH * W], BF16))
    wb = es.enter_context(nc.sbuf_tensor("wb", [128, NCH * 3 * INNER], BF16))
    qss = es.enter_context(nc.sbuf_tensor("qss", [128, 1], F32))
    kss = es.enter_context(nc.sbuf_tensor("kss", [128, 1], F32))
    tb = {n: es.enter_context(nc.sbuf_tensor("t" + n, [128, TPC], F32)) for n in ("AQ", "BQ", "AK", "BK")}
    e2s = es.enter_context(nc.sbuf_tensor("e2s", [128, 128], F32))
    sqb = es.enter_context(nc.sbuf_tensor("sqb", [128, 512], F32))
    rstd = es.enter_context(nc.sbuf_tensor("rstd", [128, 512], F32))
    qsq = es.enter_context(nc.sbuf_tensor("qsq", [128, 512], F32))
    qrot = es.enter_context(nc.sbuf_tensor("qrot", [128, 512], F32))
    u = es.enter_context(nc.sbuf_tensor("u", [128, 512], F32))
    w_ = es.enter_context(nc.sbuf_tensor("w_", [128, 512], F32))
    osb = es.enter_context(nc.sbuf_tensor("osb", [128, 2 * TPC], BF16))
    vsb = es.enter_context(nc.sbuf_tensor("vsb", [128, 512], BF16))
    pQ = [es.enter_context(nc.psum_tensor(f"pQ{i}", [128, 512], F32)) for i in range(2)]
    pS = es.enter_context(nc.psum_tensor("pS", [128, 512], F32))
    pV = [es.enter_context(nc.psum_tensor(f"pV{i}", [128, 512], F32)) for i in range(2)]

    ldx = p.dma("G", xbf[:].rearrange("p (c t) -> p c t", t=W)[:, :, 1:W],
                xT.rearrange("(c p) t -> p c t", p=128), group="ldx")
    ldxp = p.dma("G", xbf[:].rearrange("p (c t) -> p c t", t=W)[:, :, 0:1],
                 xprevT.rearrange("(c p) t -> p c t", p=128), allow_slow_non_contiguous=True,
                 group="ldxp")
    ldw = p.dma("G", wb[:].rearrange("p (c o) -> p c o", o=3 * INNER),
                wqkv.rearrange("(c p) o -> p c o", p=128), group="ldw")
    ldqs = p.dma("Y", qss[:], qs2, group="ldqs")
    ldks = p.dma("Y", kss[:], ks2, group="ldks")
    ldtb = [p.dma("Y", tb[n][:], tabs[n], group="t" + n) for n in ("AQ", "BQ", "AK", "BK")]
    lde2 = p.dma("Y", e2s[:], E2, group="lde2")
    lds = [ldx, ldxp, ldw, ldqs, ldks, lde2] + ldtb

    def h_ap(ic, t0, n):
        off = 0 if ic >= 4 else 1
        return xbf[:, ic * W + off + t0: ic * W + off + t0 + n]

    Q0, Q1, Q2, Q3 = (slice(0, 32), slice(32, 64), slice(64, 96), slice(96, 128))
    QPART = [(Q0, Q1), (Q1, Q0), (Q2, Q3), (Q3, Q2)]

    bank_free = [None, None]
    half_free = [None, None]
    prev_t3 = None
    for name, qkoff, ssb, A, Bt, out_dram in (
            ("q", 0, qss, "AQ", "BQ", qx), ("k", 4, kss, "AK", "BK", kx)):
        for oc in range(NQC):
            half = oc % 2
            st_toks = []
            for tt in range(NTT):
                bi = tt % 2
                pt = pQ[bi]
                w0 = lds + [bank_free[bi], half_free[half]]
                mm = None
                for ic in range(NCH):
                    mm = p.op("T", (lambda t, ic=ic, oc=oc, tt=tt, pt=pt, qkoff=qkoff: t.matmul(
                        pt[:], wb[:, ic * 3 * INNER + (qkoff + oc) * 128: ic * 3 * INNER + (qkoff + oc) * 128 + 128],
                        h_ap(ic, tt * 512, 512),
                        start=(ic == 0), stop=(ic == NCH - 1))), w0 if ic == 0 else [])
                    w0 = []
                sg = p.op("S", (lambda s, pt=pt: s.activation(sqb[:], pt[:], AF.Square)), [mm])
                sm = p.op("T", (lambda t: t.matmul(pS[:], e2s[:], sqb[:], start=True, stop=True)), [sg])
                nr = p.op("S", (lambda s: s.activation(rstd[:], pS[:], AF.Sqrt)), [sm, prev_t3])
                nr2 = p.op("V", (lambda v: v.tensor_scalar_max(rstd[:], rstd[:], 1e-12)), [nr])
                nr3 = p.op("V", (lambda v: v.reciprocal(rstd[:], rstd[:])), [nr2])
                ev = p.op("V", (lambda v, pt=pt, ssb=ssb: v.tensor_scalar_mul(qsq[:], pt[:], ssb[:])), [mm])
                bank_free[bi] = ev
                t1 = p.op("V", (lambda v, tt=tt, A=A: v.tensor_mul(
                    u[:], qsq[:], tb[A][:, tt * 512:(tt + 1) * 512])), [ev])
                rt = None
                for (dst, srcq) in QPART:
                    rt = p.op("V", (lambda v, dst=dst, srcq=srcq: v.tensor_copy(
                        qrot[dst, :], qsq[srcq, :])), [ev])
                rt = p.op("V", (lambda v, tt=tt, Bt=Bt: v.tensor_mul(
                    w_[:], qrot[:], tb[Bt][:, tt * 512:(tt + 1) * 512])), [rt])
                t2 = p.op("V", (lambda v: v.tensor_add(u[:], u[:], w_[:])), [t1, rt])
                t3 = p.op("V", (lambda v, oc=oc, tt=tt, half=half: v.tensor_mul(
                    osb[:, half * TPC + tt * 512: half * TPC + (tt + 1) * 512], u[:], rstd[:])),
                    [t2, nr3])
                prev_t3 = t3
                st_toks.append(t3)
            dt = p.dma("G", out_dram.rearrange("(c p) t -> p c t", p=128)[:, oc, :],
                       osb[:, half * TPC: half * TPC + TPC], waits=st_toks,
                       group=f"oh{half}_{name}")
            half_free[half] = dt

    vbank_free = [None, None]
    vhalf_free = None
    for tc in range(TPC // 128):
        bi = tc % 2
        pt = pV[bi]
        w0 = lds + [vbank_free[bi], vhalf_free]
        mm = None
        for ic in range(NCH):
            mm = p.op("T", (lambda t, ic=ic, tc=tc, pt=pt: t.matmul(
                pt[:], h_ap(ic, tc * 128, 128),
                wb[:, ic * 3 * INNER + 2 * INNER: ic * 3 * INNER + 3 * INNER],
                start=(ic == 0), stop=(ic == NCH - 1))), w0 if ic == 0 else [])
            w0 = []
        ev = p.op("V", (lambda v, pt=pt: v.tensor_copy(vsb[:], pt[:])), [mm])
        vbank_free[bi] = ev
        vhalf_free = p.dma("G", vx[tc * 128:(tc + 1) * 128, :], vsb[:], waits=[ev], group="vout")
    return


def build_lffn_qkv():
    nc = bass.Bass("TRN2", target_bir_lowering=False)
    xT = nc.dram_tensor("xT", [D, TPC], F32, kind="ExternalInput").ap()
    xprevT = nc.dram_tensor("xprevT", [D, 1], F32, kind="ExternalInput").ap()
    wqkv = nc.dram_tensor("wqkv", [D, 3 * INNER], F32, kind="ExternalInput").ap()
    qs2 = nc.dram_tensor("qs2", [128, 1], F32, kind="ExternalInput").ap()
    ks2 = nc.dram_tensor("ks2", [128, 1], F32, kind="ExternalInput").ap()
    tabs = {n: nc.dram_tensor(n, [128, TPC], F32, kind="ExternalInput").ap()
            for n in ("AQ", "BQ", "AK", "BK")}
    E2 = nc.dram_tensor("E2", [128, 128], F32, kind="ExternalInput").ap()
    qx = nc.dram_tensor("qx", [INNER, TPC], BF16, kind="ExternalOutput").ap()
    kx = nc.dram_tensor("kx", [INNER, TPC], BF16, kind="ExternalOutput").ap()
    vx = nc.dram_tensor("vx", [TPC, INNER], BF16, kind="ExternalOutput").ap()
    es = ExitStack()
    p = Prog(nc, es)
    emit_lffn_qkv(p, nc, es, xT, xprevT, wqkv, qs2, ks2, tabs, E2, qx, kx, vx)
    p.emit()
    es.close()
    return nc



def emit_lffn_attn(p, nc, es, qx, kfb, vfb, masks, ident, ox, kv_loader=None):
    KL = TPC + 512
    qeb = [es.enter_context(nc.sbuf_tensor(f"qe{i}", [65, TPC], BF16)) for i in range(2)]
    keb = [es.enter_context(nc.sbuf_tensor(f"ke{i}", [65, KL], BF16)) for i in range(2)]
    veb = [es.enter_context(nc.sbuf_tensor(f"ve{i}", [128, 65 * (KL // 128)], BF16)) for i in range(2)]
    msk = es.enter_context(nc.sbuf_tensor("msk", [128, 32 * 512], BF16))
    idn = es.enter_context(nc.sbuf_tensor("idn", [128, 128], F32))
    mneg = es.enter_context(nc.sbuf_tensor("mneg", [128, 4], F32))
    mtmp = es.enter_context(nc.sbuf_tensor("mtmp", [128, 2], F32))
    pt_sb = es.enter_context(nc.sbuf_tensor("pt_sb", [128, 2048], BF16))
    rb = es.enter_context(nc.sbuf_tensor("rb", [64, 512], F32))
    rec = es.enter_context(nc.sbuf_tensor("rec", [1, 512], F32))
    ones1 = es.enter_context(nc.sbuf_tensor("ones1", [1, 64], F32))
    ob = es.enter_context(nc.sbuf_tensor("ob", [64, 2 * 512], BF16))
    psA = [es.enter_context(nc.psum_tensor(f"psA{i}", [128, 512], F32)) for i in range(2)]
    psM = es.enter_context(nc.psum_tensor("psM", [1, 128], F32))
    psT = es.enter_context(nc.psum_tensor("psT", [128, 2048], F32))
    psO = es.enter_context(nc.psum_tensor("psO", [65, 512], F32))
    psR = psT[0:64, 0:512]
    p_ = p

    ldi = p.dma("Y", idn[:], ident, group="ldi")
    ldm = p.dma("Y", msk[:].rearrange("p (s n) -> p s n", n=512), masks.rearrange("s p n -> p s n"),
                group="ldm")
    one1 = p.op("V", lambda v: v.memset(ones1[:], 1.0))
    prev_done_b = {}
    for i in range(2):
        onek = p.op("V", (lambda v, i=i: v.memset(keb[i][64:65, :], 1.0)))
        onev = p.op("V", (lambda v, i=i: v.memset(
            veb[i][:].rearrange("p (c e) -> p c e", e=65)[:, :, 64], 1.0)))
        prev_done_b[i] = [onek, onev]
    prev_cpm = None
    ob_last = [None, None]
    pending_norm = [None]
    last_st0 = [None]

    def head_body(h, qe, ke, ve, prev_head_done):
        nonlocal prev_cpm
        ldq = p.dma("G", qe[0:64, :], qx[h * 64:(h + 1) * 64, :], waits=prev_head_done,
                    group=f"ldq{h % 2}")
        if kv_loader is None:
            ldk = p.dma("G", ke[0:64, :], kfb[h * 64:(h + 1) * 64, :], waits=prev_head_done,
                        group=f"ldk{h % 2}")
            ldv = p.dma("G", ve[:].rearrange("p (c e) -> p c e", e=65)[:, :, 0:64],
                        vfb.rearrange("(c p) d -> p c d", p=128)[:, :, h * 64:(h + 1) * 64],
                        waits=prev_head_done, group=f"ldv{h % 2}")
        else:
            ldk, ldv = kv_loader(h, prev_head_done, ke, ve)
        prev_avm = [None]
        prev_red = [None, None]
        for wi in range(4):
            if pending_norm[0] is not None:
                pending_norm[0]()
                pending_norm[0] = None
            tr_toks = []
            for qt in range(4):
                qsl = qe[0:64, wi * 512 + qt * 128: wi * 512 + qt * 128 + 128]
                red_prev = None
                for kc in range(2):
                    bi = kc
                    w = [ldq, ldk] if (qt == 0 and kc == 0) else []
                    w = w + [prev_red[bi]]
                    mm = p.op("T", (lambda t, qsl=qsl, kc=kc, wi=wi, bi=bi: t.matmul(
                        psA[bi][:], qsl, ke[0:64, wi * 512 + kc * 512: wi * 512 + (kc + 1) * 512],
                        start=True, stop=True)), w)
                    if kc == 0:
                        red_prev = p.op("V", (lambda v, qt=qt, bi=bi: v.reduce_max(
                            mneg[:, qt:qt + 1], psA[bi][:], axis=AX.X, negate=True)), [mm])
                    else:
                        r1 = p.op("V", (lambda v, bi=bi: v.reduce_max(
                            mtmp[:, bi:bi + 1], psA[bi][:], axis=AX.X, negate=True)), [mm])
                        red_prev = p.op("V", (lambda v, qt=qt, bi=bi: v.tensor_tensor(
                            mneg[:, qt:qt + 1], mneg[:, qt:qt + 1], mtmp[:, bi:bi + 1],
                            ALU.min)), [r1])
                    prev_red[bi] = red_prev
                trm = p.op("T", (lambda t, qt=qt: t.transpose(psM[:], mneg[:, qt:qt + 1], idn[:])),
                           [red_prev, ldi, prev_cpm])
                cpm = p.op("V", (lambda v, wi=wi, qt=qt: v.tensor_copy(
                    qe[64:65, wi * 512 + qt * 128: wi * 512 + qt * 128 + 128], psM[:])), [trm])
                prev_cpm = cpm
                tr_toks.append(cpm)
            qg = qe[:, wi * 512:(wi + 1) * 512]
            first_st_waits = tr_toks + [prev_avm[0]]
            ex_prev = [None, None]
            avm = None
            for kc in range(2):
                for hf in range(2):
                    mm2 = None
                    for s2 in range(2):
                        s = hf * 2 + s2
                        w = list(first_st_waits) + [ex_prev[hf]]
                        if s == 0:
                            w = w + [last_st0[0]]
                        first_st_waits = []
                        mm2 = p.op("T", (lambda t, kc=kc, s=s, qg=qg, wi=wi: t.matmul(
                            psT[:, s * 512:(s + 1) * 512],
                            ke[:, wi * 512 + kc * 512 + s * 128: wi * 512 + kc * 512 + s * 128 + 128],
                            qg, start=True, stop=True)), w)
                        if s == 0:
                            last_st0[0] = mm2
                    ex = p.op("S", (lambda s_, hf=hf: s_.activation(
                        pt_sb[:, hf * 1024:(hf + 1) * 1024],
                        psT[:, hf * 1024:(hf + 1) * 1024], AF.Exp, scale=8.0)),
                        [mm2, prev_avm[0]])
                    ex_prev[hf] = ex
                    dep = [ex]
                    for s2 in range(2):
                        s = hf * 2 + s2
                        mi = wi * 8 + kc * 4 + s
                        dep = [p.op("V", (lambda v, s=s, mi=mi: v.tensor_mul(
                            pt_sb[:, s * 512:(s + 1) * 512], pt_sb[:, s * 512:(s + 1) * 512],
                            msk[:, mi * 512:(mi + 1) * 512])), dep + ([ldm] if s2 == 0 else []))]
                    for s2 in range(2):
                        s = hf * 2 + s2
                        w = (dep + [ldv]) if s2 == 0 else []
                        avm = p.op("T", (lambda t, kc=kc, s=s, wi=wi: t.matmul(
                            psO[:], ve[:, (wi * 4 + kc * 4 + s) * 65:(wi * 4 + kc * 4 + s) * 65 + 65],
                            pt_sb[:, s * 512:(s + 1) * 512],
                            start=(kc == 0 and s == 0), stop=(kc == 1 and s == 3))), w)
            prev_avm[0] = avm

            def make_norm(wi=wi, h=h, avm=avm, ex_prev=ex_prev):
                def donorm():
                    rc = p_.op("V", (lambda v: v.reciprocal(rec[:], psO[64:65, :])), [avm])
                    bc = p_.op("T", (lambda t: t.matmul(psR, ones1[:], rec[:],
                                                        start=True, stop=True)),
                               [rc, one1, last_st0[0], ex_prev[0]])
                    cb = p_.op("V", (lambda v: v.tensor_copy(rb[:], psR)), [bc])
                    ot = p_.op("V", (lambda v, wi=wi: v.tensor_mul(
                        ob[:, (wi % 2) * 512:(wi % 2) * 512 + 512], psO[0:64, :], rb[:])),
                        [cb, ob_last[wi % 2]])
                    ob_last[wi % 2] = p_.dma("G", ox[h * 64:(h + 1) * 64, wi * 512:(wi + 1) * 512],
                                             ob[:, (wi % 2) * 512:(wi % 2) * 512 + 512],
                                             waits=[ot], group=f"ob{wi % 2}")
                    prev_avm[0] = ot
                    last_st0[0] = cb
                return donorm
            pending_norm[0] = make_norm()
        pending_norm[0]()
        pending_norm[0] = None
        return [prev_avm[0], ob_last[0], ob_last[1], prev_cpm]

    for h in range(8):
        prev_done_b[h % 2] = head_body(h, qeb[h % 2], keb[h % 2], veb[h % 2],
                                       prev_done_b[h % 2])
    return


def build_lffn_attn():
    nc = bass.Bass("TRN2", target_bir_lowering=False)
    KL = TPC + 512
    qx = nc.dram_tensor("qx", [INNER, TPC], BF16, kind="ExternalInput").ap()
    kfb = nc.dram_tensor("kfb", [INNER, KL], BF16, kind="ExternalInput").ap()
    vfb = nc.dram_tensor("vfb", [KL, INNER], BF16, kind="ExternalInput").ap()
    masks = nc.dram_tensor("masks", [32, 128, 512], BF16, kind="ExternalInput").ap()
    ident = nc.dram_tensor("ident", [128, 128], F32, kind="ExternalInput").ap()
    ox = nc.dram_tensor("ox", [INNER, TPC], BF16, kind="ExternalOutput").ap()
    es = ExitStack()
    p = Prog(nc, es)
    emit_lffn_attn(p, nc, es, qx, kfb, vfb, masks, ident, ox)
    p.emit()
    es.close()
    return nc



def emit_lffn_proj(p, nc, es, oT, wo, xT, x2):
    obf = es.enter_context(nc.sbuf_tensor("obf", [128, 4 * TPC], BF16))
    wob = es.enter_context(nc.sbuf_tensor("wob", [128, 4 * D], BF16))
    xr = es.enter_context(nc.sbuf_tensor("xr", [128, NCH * TPC], F32))
    xo = es.enter_context(nc.sbuf_tensor("xo", [128, 512], F32))
    ps = [es.enter_context(nc.psum_tensor(f"ps{i}", [128, 512], F32)) for i in range(2)]
    ldo = p.dma("G", obf[:].rearrange("p (c t) -> p c t", t=TPC),
                oT.rearrange("(c p) t -> p c t", p=128), group="ldo")
    ldw = p.dma("G", wob[:].rearrange("p (c o) -> p c o", o=D), wo.rearrange("(c p) o -> p c o", p=128), group="ldw")
    ldxr = p.dma("G", xr[:].rearrange("p (c t) -> p c t", t=TPC),
                 xT.rearrange("(c p) t -> p c t", p=128), group="ldxr")
    bank_free = [None, None]
    for oc in range(NCH):
        for tt in range(NTT):
            bi = (oc * NTT + tt) % 2
            pt = ps[bi]
            w0 = [ldo, ldw, ldxr, bank_free[bi]]
            mm = None
            for ic in range(4):
                mm = p.op("T", (lambda t, ic=ic, oc=oc, tt=tt, pt=pt: t.matmul(
                    pt[:], wob[:, ic * D + oc * 128: ic * D + oc * 128 + 128],
                    obf[:, ic * TPC + tt * 512: ic * TPC + (tt + 1) * 512],
                    start=(ic == 0), stop=(ic == 3))), w0 if ic == 0 else [])
                w0 = []
            ad = p.op("V", (lambda v, oc=oc, tt=tt, pt=pt: v.tensor_add(
                xr[:, oc * TPC + tt * 512: oc * TPC + (tt + 1) * 512],
                xr[:, oc * TPC + tt * 512: oc * TPC + (tt + 1) * 512], pt[:])), [mm])
            bank_free[bi] = ad
            p.dma("G", x2.rearrange("(c p) t -> p c t", p=128)[:, oc, tt * 512:(tt + 1) * 512],
                  xr[:, oc * TPC + tt * 512: oc * TPC + (tt + 1) * 512], waits=[ad], group="xout")
    return


def build_lffn_proj():
    nc = bass.Bass("TRN2", target_bir_lowering=False)
    oT = nc.dram_tensor("oT", [INNER, TPC], BF16, kind="ExternalInput").ap()
    wo = nc.dram_tensor("wo", [INNER, D], F32, kind="ExternalInput").ap()
    xT = nc.dram_tensor("xT", [D, TPC], F32, kind="ExternalInput").ap()
    x2 = nc.dram_tensor("x2", [D, TPC], F32, kind="ExternalOutput").ap()
    es = ExitStack()
    p = Prog(nc, es)
    emit_lffn_proj(p, nc, es, oT, wo, xT, x2)
    p.emit()
    es.close()
    return nc



HID_P = 2816


def emit_lffn_ffn(p, nc, es, x2T, x2prevT, lg, lb, w1p, b1p, w2p, b2, x3):
    W = 2049
    NA = HID_P // 128
    xsh = es.enter_context(nc.sbuf_tensor("xsh", [128, NCH * W], F32))
    hbf = es.enter_context(nc.sbuf_tensor("hbf", [128, NCH * TPC], BF16))
    lgs = es.enter_context(nc.sbuf_tensor("lgs", [128, NCH], F32))
    lbs = es.enter_context(nc.sbuf_tensor("lbs", [128, NCH], F32))
    b1s = es.enter_context(nc.sbuf_tensor("b1s", [128, 2 * NA], F32))
    b2s = es.enter_context(nc.sbuf_tensor("b2s", [128, NCH], F32))
    w1sb = es.enter_context(nc.sbuf_tensor("w1sb", [128, 3 * 2 * NCH * 128], BF16))
    w2sb = es.enter_context(nc.sbuf_tensor("w2sb", [128, 3 * NA * 128], BF16))
    pbuf = es.enter_context(nc.sbuf_tensor("pbuf", [128, 2 * NA * 512], BF16))
    gg = es.enter_context(nc.sbuf_tensor("gg", [128, 512], BF16))
    atmp = es.enter_context(nc.sbuf_tensor("atmp", [128, 512], F32))
    xo = es.enter_context(nc.sbuf_tensor("xo", [128, 512], F32))
    xo2 = es.enter_context(nc.sbuf_tensor("xo2", [128, 512], F32))
    ps = [es.enter_context(nc.psum_tensor(f"ps{i}", [128, 512], F32)) for i in range(2)]
    lnb = alloc_ln_bufs(nc, es)

    ldx = p.dma("G", xsh[:].rearrange("p (c t) -> p c t", t=W)[:, :, 1:W],
                x2T.rearrange("(c p) t -> p c t", p=128), group="ldx")
    ldxp = p.dma("G", xsh[:].rearrange("p (c t) -> p c t", t=W)[:, :, 0:1],
                 x2prevT.rearrange("(c p) t -> p c t", p=128), allow_slow_non_contiguous=True,
                 group="ldxp")
    ldlg = p.dma("Y", lgs[:], lg, group="ldlg")
    ldlb = p.dma("Y", lbs[:], lb, group="ldlb")
    ldb1 = p.dma("Y", b1s[:], b1p, group="ldb1")
    ldb2 = p.dma("Y", b2s[:], b2, group="ldb2")
    init = emit_ln_init(p, lnb)

    def xs_ap(c, tt):
        off = 0 if c >= 4 else 1
        return xsh[:, c * W + off + tt * 512: c * W + off + tt * 512 + 512]

    def xu_ap(c, tt):
        return xsh[:, c * W + 1 + tt * 512: c * W + 1 + tt * 512 + 512]

    def hb_ap(c, tt):
        return hbf[:, c * TPC + tt * 512: c * TPC + (tt + 1) * 512]

    toks, last_ln = emit_ln(p, lnb, xs_ap, TPC, out_fn=hb_ap,
                            gamma_scale=lambda c: lgs[:, c:c + 1],
                            gamma_bias=lambda c: lbs[:, c:c + 1],
                            pre_waits=[ldx, ldxp, ldlg, ldlb], init_toks=init)
    ln_done = [toks[(c, tt)] for c in range(NCH) for tt in range(NTT)]

    w1v = w1sb[:].rearrange("p (b a c o) -> p b a c o", b=3, a=2, o=128)
    w2v = w2sb[:].rearrange("p (b c o) -> p b c o", b=3, o=128)
    prev_pb_use = None
    last_x3 = None
    w1_free = [None, None, None]
    w2_free = [None, None, None]
    gbank = [None, None]
    w1t = {}
    w2t = {}

    def issue_w1(ocp, wtok):
        b3 = ocp % 3
        la = p.dma("G", w1v[:, b3, 0], w1p[0][ocp], waits=[wtok], group=f"w1_{b3}")
        lg_ = p.dma("G", w1v[:, b3, 1], w1p[1][ocp], group=f"w1_{b3}")
        w1t[ocp] = (la, lg_)

    def issue_w2(oc2, wtok):
        w2t[oc2] = p.dma("G", w2v[:, oc2 % 3], w2p[oc2], waits=[wtok], group=f"w2_{oc2 % 3}")

    xov = [xo, xo2]
    x3r = x3.rearrange("(c p) t -> p c t", p=128)

    def pb_ap(j, kc):
        return pbuf[:, (j * NA + kc) * 512: (j * NA + kc) * 512 + 512]

    for tp in range(2):
        tts = (2 * tp, 2 * tp + 1)
        pbt = [[], []]
        for j3 in range(3):
            issue_w1(j3, w1_free[j3])
        for j3 in range(3):
            issue_w2(j3, w2_free[j3])
        for ocp in range(NA):
            par = ocp % 3
            lwa, lwg = w1t.pop(ocp)
            pm = None
            for j, tt in enumerate(tts):
                w0 = (ln_done + [lwa, lwg, prev_pb_use]) if j == 0 else []
                w0 = w0 + [gbank[0]]
                mmg = None
                for ic in range(NCH):
                    mmg = p.op("T", (lambda t, ic=ic, par=par, tt=tt: t.matmul(
                        ps[0][:], w1v[:, par, 1, ic], hb_ap(ic, tt),
                        start=(ic == 0), stop=(ic == NCH - 1))), w0 if ic == 0 else [])
                    w0 = []
                ge = p.op("S", (lambda s, ocp=ocp: s.activation(
                    gg[:], ps[0][:], AF.Gelu, bias=b1s[:, NA + ocp:NA + ocp + 1])), [mmg, ldb1])
                gbank[0] = ge
                w0 = [gbank[1], ge]
                mma = None
                for ic in range(NCH):
                    mma = p.op("T", (lambda t, ic=ic, par=par, tt=tt: t.matmul(
                        ps[1][:], w1v[:, par, 0, ic], hb_ap(ic, tt),
                        start=(ic == 0), stop=(ic == NCH - 1))), w0 if ic == 0 else [])
                    w0 = []
                ae = p.op("V", (lambda v, ocp=ocp: v.tensor_scalar_add(
                    atmp[:], ps[1][:], b1s[:, ocp:ocp + 1])), [mma, ldb1])
                gbank[1] = ae
                pm = p.op("V", (lambda v, ocp=ocp, j=j: v.tensor_mul(
                    pb_ap(j, ocp), atmp[:], gg[:])), [ae, ge])
                pbt[j].append(pm)
            w1_free[par] = pm
            if ocp + 3 < NA:
                issue_w1(ocp + 3, pm)
        x3_toks = []
        last_w2mm = None
        for oc2 in range(NCH):
            lw2 = w2t.pop(oc2)
            mm = None
            for j, tt in enumerate(tts):
                w0 = pbt[j] + [lw2, gbank[j]]
                for kc in range(NA):
                    mm = p.op("T", (lambda t, kc=kc, j=j, oc2=oc2: t.matmul(
                        ps[j][:], w2v[:, oc2 % 3, kc], pb_ap(j, kc),
                        start=(kc == 0), stop=(kc == NA - 1))), w0 if kc == 0 else [])
                    w0 = []
                ev = p.op("V", (lambda v, oc2=oc2, j=j: v.tensor_scalar_add(
                    xov[j][:], ps[j][:], b2s[:, oc2:oc2 + 1])), [mm, ldb2])
                ad = p.op("V", (lambda v, oc2=oc2, tt=tt, j=j: v.tensor_add(
                    xov[j][:], xov[j][:], xu_ap(oc2, tt))), [ev])
                dt = p.dma("Y", x3r[:, oc2, tt * 512:(tt + 1) * 512],
                           xov[j][:], waits=[ad], group="xout")
                gbank[j] = dt
                x3_toks.append(ad)
            last_w2mm = mm
            w2_free[oc2 % 3] = mm
            if oc2 + 3 < NCH:
                issue_w2(oc2 + 3, mm)
        prev_pb_use = last_w2mm
        last_x3 = x3_toks[-1]
        last_x3 = x3_toks[-1]
    return


def build_lffn_ffn():
    nc = bass.Bass("TRN2", target_bir_lowering=False)
    x2T = nc.dram_tensor("x2T", [D, TPC], F32, kind="ExternalInput").ap()
    x2prevT = nc.dram_tensor("x2prevT", [D, 1], F32, kind="ExternalInput").ap()
    lg = nc.dram_tensor("lg", [128, NCH], F32, kind="ExternalInput").ap()
    lb = nc.dram_tensor("lb", [128, NCH], F32, kind="ExternalInput").ap()
    w1p = nc.dram_tensor("w1p", [2, HID_P // 128, 128, NCH, 128], F32, kind="ExternalInput").ap()
    b1p = nc.dram_tensor("b1p", [128, 2 * HID_P // 128], F32, kind="ExternalInput").ap()
    w2p = nc.dram_tensor("w2p", [NCH, 128, HID_P // 128, 128], F32, kind="ExternalInput").ap()
    b2 = nc.dram_tensor("b2", [128, NCH], F32, kind="ExternalInput").ap()
    x3 = nc.dram_tensor("x3", [D, TPC], F32, kind="ExternalOutput").ap()
    es = ExitStack()
    p = Prog(nc, es)
    emit_lffn_ffn(p, nc, es, x2T, x2prevT, lg, lb, w1p, b1p, w2p, b2, x3)
    p.emit()
    es.close()
    return nc





DFF = 4096
NE = 8


def emit_moe(p, nc, es, xT, rw, rbrow, upT, downR, ident, rowsel, x4):
    NDC = DFF // 128
    TH = 1024
    xbf = es.enter_context(nc.sbuf_tensor("xbf", [128, NCH * TPC], BF16))
    rwb = es.enter_context(nc.sbuf_tensor("rwb", [128, NCH * NE], F32))
    rbr = es.enter_context(nc.sbuf_tensor("rbr", [1, NE], F32))
    onesr = es.enter_context(nc.sbuf_tensor("onesr", [1, 128], F32))
    xf = es.enter_context(nc.sbuf_tensor("xf", [128, NCH * 128], F32))
    ones1 = es.enter_context(nc.sbuf_tensor("ones1f", [1, 128], F32))
    idn = es.enter_context(nc.sbuf_tensor("idn", [128, 128], F32))
    rsel = es.enter_context(nc.sbuf_tensor("rsel", [NE, NE * 128], BF16))
    probs = es.enter_context(nc.sbuf_tensor("probs", [128, NE], F32))
    m8 = es.enter_context(nc.sbuf_tensor("m8", [128, NE], F32))
    mneg = es.enter_context(nc.sbuf_tensor("mnegM", [128, 1], F32))
    ssum = es.enter_context(nc.sbuf_tensor("ssum", [128, 1], F32))
    comb = es.enter_context(nc.sbuf_tensor("comb", [128, NE], F32))
    comb_fm = es.enter_context(nc.sbuf_tensor("comb_fm", [NE, TPC], BF16))
    cbc = es.enter_context(nc.sbuf_tensor("cbc", [128, TH], F32))
    uph = es.enter_context(nc.sbuf_tensor("uph", [128, 2 * NCH * 1024], BF16))
    hb = es.enter_context(nc.sbuf_tensor("hb", [128, NDC * TH], BF16))
    dsl = es.enter_context(nc.sbuf_tensor("dsl", [128, 2 * NDC * 128], BF16))
    acc = es.enter_context(nc.sbuf_tensor("acc", [128, NCH * TH], F32))
    tmpv = es.enter_context(nc.sbuf_tensor("tmpv", [128, 512], F32))
    xtile = es.enter_context(nc.sbuf_tensor("xtile", [128, 2 * 512], F32))
    psL = es.enter_context(nc.psum_tensor("psL", [128, 512], F32))
    psC = es.enter_context(nc.psum_tensor("psC", [128, 512], F32))
    psU = [es.enter_context(nc.psum_tensor(f"psU{i}", [128, 512], F32)) for i in range(2)]
    psD = [es.enter_context(nc.psum_tensor(f"psD{i}", [128, 512], F32)) for i in range(2)]
    lnb = alloc_ln_bufs(nc, es, reuse=dict(psS=psL[0:1, 0:512], psS2=psC[0:1, 0:512],
                                           psB=psU[0][:], psB2=psU[1][:]))

    ldx = p.dma("G", xbf[:].rearrange("p (c t) -> p c t", t=TPC),
                xT.rearrange("(c p) t -> p c t", p=128), group="ldx")
    ldrw = p.dma("Y", rwb[:].rearrange("p (c o) -> p c o", o=NE),
                 rw.rearrange("(c p) o -> p c o", p=128), group="ldrw")
    ldrb = p.dma("Y", rbr[:], rbrow, group="ldrb")
    ldi = p.dma("Y", idn[:], ident, group="ldi")
    ldrs = p.dma("G", rsel[:], rowsel, group="ldrs")
    o1 = p.op("V", lambda v: v.memset(onesr[:], 1.0))
    o2 = p.op("V", lambda v: v.memset(ones1[:], 1.0))
    init = emit_ln_init(p, lnb)

    prev_tr = None
    last_cf = None
    prev_t2 = None
    xf_free = None
    for tc in range(TPC // 128):
        lxf = p.dma("G", xf[:].rearrange("p (c o) -> p c o", o=128),
                    xT.rearrange("(c p) t -> p c t", p=128)[:, :, tc * 128:(tc + 1) * 128],
                    waits=[xf_free], group="xf")
        w0 = [lxf, ldrw, ldrb, o1, prev_t2]
        for ic in range(NCH):
            mmr = p.op("T", (lambda t, ic=ic, tc=tc: t.matmul(
                psL[:, 0:NE],
                xf[:, ic * 128:(ic + 1) * 128],
                rwb[:, ic * NE:(ic + 1) * NE], start=(ic == 0), stop=False)),
                w0 if ic == 0 else [])
            w0 = []
        xf_free = mmr
        mm = p.op("T", (lambda t: t.matmul(
            psL[:, 0:NE], onesr[:], rbr[:], start=False, stop=True)), [])
        t1 = p.op("V", (lambda v: v.reduce_max(mneg[:], psL[:, 0:NE], axis=AX.X, negate=True)),
                  [mm, prev_tr])
        t2 = p.op("S", (lambda s: s.activation(probs[:], psL[:, 0:NE], AF.Exp, bias=mneg[:])), [t1])
        prev_t2 = t2
        t3 = p.op("V", (lambda v: v.reduce_sum(ssum[:], probs[:], axis=AX.X)), [t2])
        t4 = p.op("V", (lambda v: v.reciprocal(ssum[:], ssum[:])), [t3])
        t5 = p.op("V", (lambda v: v.tensor_scalar_mul(probs[:], probs[:], ssum[:])), [t4])
        t6 = p.op("V", (lambda v: v.max(m8[:], probs[:])), [t5])
        t7 = p.op("V", (lambda v: v.scalar_tensor_tensor(
            comb[:], probs[:], m8[:, 1:2], probs[:], ALU.is_ge, ALU.mult)), [t6])
        tr = p.op("T", (lambda t: t.transpose(psC[0:NE, 0:128], comb[:], idn[:])),
                  [t7, ldi, prev_tr])
        cf = p.op("V", (lambda v, tc=tc: v.tensor_copy(
            comb_fm[:, tc * 128:(tc + 1) * 128], psC[0:NE, 0:128])), [tr])
        prev_tr = cf
        last_cf = cf

    upr = upT.rearrange("e (c p) f -> e p c f", p=128)
    dwr = downR
    uphv = uph[:].rearrange("p (b c f) -> p b c f", b=2, f=1024)
    dslv = dsl[:].rearrange("p (b c o) -> p b c o", b=2, o=128)
    up_free = None
    dsl_free = [None, None]
    psD_reader = [None, None]
    psU_reader = [None, None]
    prev_cbc = None
    cbc_reader = None
    prev_acc_use = []
    x4_toks = []
    up_mm_last = {0: None, 1: None}
    pref_lu = {}
    for th in range(2):
        acc_last = {}
        tmp_last = None
        for e in range(NE):
            h_last = []
            for dq in range(4):
                qb = dq % 2
                lu = pref_lu.pop((th, e, dq), None)
                if lu is None:
                    lu = p.dma("G", uphv[:, qb], upr[e][:, :, dq * 1024:(dq + 1) * 1024],
                               waits=[up_mm_last[qb]], group=f"lu{qb}")
                last_ev = None
                for dc in range(8):
                    for ts in range(2):
                        bi = (dc * 2 + ts) % 2
                        pt = psU[bi]
                        w0 = [ldx, lu, psU_reader[bi]] + \
                             (prev_acc_use if (e == 0 and dq == 0 and dc == 0 and ts == 0) else [])
                        mm = None
                        for ic in range(NCH):
                            mm = p.op("T", (lambda t, ic=ic, dc=dc, ts=ts, th=th, pt=pt, qb=qb: t.matmul(
                                pt[:], uphv[:, qb, ic, dc * 128:(dc + 1) * 128],
                                xbf[:, ic * TPC + th * TH + ts * 512: ic * TPC + th * TH + ts * 512 + 512],
                                start=(ic == 0), stop=(ic == NCH - 1))), w0 if ic == 0 else [])
                            w0 = []
                        up_mm_last[qb] = mm
                        ev = p.op("S", (lambda s, dq=dq, dc=dc, ts=ts, pt=pt: s.activation(
                            hb[:, (dq * 8 + dc) * TH + ts * 512: (dq * 8 + dc) * TH + ts * 512 + 512],
                            pt[:], AF.Gelu)), [mm])
                        psU_reader[bi] = ev
                        last_ev = ev
                h_last.append(last_ev)
            h_last = h_last[-2:]
            cb_toks = []
            for ts in range(2):
                bcm = p.op("T", (lambda t, e=e, th=th, ts=ts: t.matmul(
                    psC[:, 0:512], rsel[:, e * 128:(e + 1) * 128],
                    comb_fm[0:NE, th * TH + ts * 512: th * TH + ts * 512 + 512],
                    start=True, stop=True)), [last_cf, ldrs, prev_cbc] + cb_toks +
                    ([cbc_reader] if cbc_reader else []))
                cbc_t = p.op("V", (lambda v, ts=ts: v.tensor_copy(
                    cbc[:, ts * 512:(ts + 1) * 512], psC[:, 0:512])), [bcm])
                prev_cbc = cbc_t
                cb_toks.append(cbc_t)
            last_mm = None
            ldt = {0: p.dma("G", dslv[:, 0], dwr[e][0], waits=[dsl_free[0]], group="dsl0")}
            for oc in range(NCH):
                par = oc % 2
                if oc + 1 < NCH:
                    nb = (oc + 1) % 2
                    ldt[oc + 1] = p.dma("G", dslv[:, nb], dwr[e][oc + 1],
                                        waits=[dsl_free[nb]], group=f"dsl{nb}")
                if oc in (2, 3) and not (th == 1 and e == NE - 1):
                    ne_, nth = (e + 1, th) if e + 1 < NE else (0, th + 1)
                    qb_ = oc - 2
                    pref_lu[(nth, ne_, qb_)] = p.dma(
                        "G", uphv[:, qb_], upr[ne_][:, :, qb_ * 1024:(qb_ + 1) * 1024],
                        waits=[up_mm_last[qb_]], group=f"lu{qb_}")
                ld2 = ldt[oc]
                for ts in range(2):
                    pt = psD[ts]
                    w0 = (h_last if oc == 0 else []) + [ld2, psD_reader[ts]] + \
                         (prev_acc_use if (oc == 0 and e == 0) else [])
                    mm = None
                    for kc in range(NDC):
                        mm = p.op("T", (lambda t, kc=kc, ts=ts, pt=pt, par=par: t.matmul(
                            pt[:], dslv[:, par, kc],
                            hb[:, kc * TH + ts * 512: kc * TH + ts * 512 + 512],
                            start=(kc == 0), stop=(kc == NDC - 1))), w0 if kc == 0 else [])
                        w0 = []
                    last_mm = mm
                    aslice = acc[:, oc * TH + ts * 512: oc * TH + ts * 512 + 512]
                    if e == 0:
                        rd = p.op("V", (lambda v, aslice=aslice, ts=ts, pt=pt: v.tensor_mul(
                            aslice, pt[:], cbc[:, ts * 512:(ts + 1) * 512])), [mm] + cb_toks)
                        acc_last[(oc, ts)] = rd
                    else:
                        rd = p.op("V", (lambda v, ts=ts, pt=pt: v.tensor_mul(
                            tmpv[:], pt[:], cbc[:, ts * 512:(ts + 1) * 512])),
                            [mm, tmp_last] + cb_toks)
                        at = p.op("G", (lambda g, aslice=aslice: g.tensor_add(
                            aslice, aslice, tmpv[:])), [rd, acc_last[(oc, ts)]])
                        tmp_last = at
                        acc_last[(oc, ts)] = at
                    psD_reader[ts] = rd
                    cbc_reader = rd
                dsl_free[par] = last_mm
        def a_ap(c, tt, th=th):
            return acc[:, c * TH + tt * 512: c * TH + tt * 512 + 512]
        res_toks = []
        xl_prev = [None, None]
        for c in range(NCH):
            for tt in range(2):
                par = tt % 2
                lx = p.dma("G", xtile[:, par * 512:(par + 1) * 512],
                           xT.rearrange("(c p) t -> p c t", p=128)[:, c,
                                        th * TH + tt * 512: th * TH + tt * 512 + 512],
                           waits=[xl_prev[par]], group=f"xt{par}")
                ad = p.op("V", (lambda v, c=c, tt=tt, par=par: v.tensor_add(
                    a_ap(c, tt), a_ap(c, tt), xtile[:, par * 512:(par + 1) * 512])),
                    [lx, acc_last[(c, tt)]])
                xl_prev[par] = ad
                res_toks.append(ad)
        toks, last_ln = emit_ln(p, lnb, a_ap, TH, out_fn=a_ap, pre_waits=[res_toks[-1]],
                                init_toks=init)
        for c in range(NCH):
            for tt in range(2):
                p.dma("G", x4.rearrange("(c p) t -> p c t", p=128)[:, c,
                                        th * TH + tt * 512: th * TH + tt * 512 + 512],
                      a_ap(c, tt), waits=[toks[(c, tt)]], group="x4out")
        prev_acc_use = [last_ln, p.gtotal("x4out")]
    return


def build_moe():
    nc = bass.Bass("TRN2", target_bir_lowering=False)
    xT = nc.dram_tensor("xT", [D, TPC], F32, kind="ExternalInput").ap()
    rw = nc.dram_tensor("rw", [D, NE], F32, kind="ExternalInput").ap()
    rbrow = nc.dram_tensor("rbrow", [1, NE], F32, kind="ExternalInput").ap()
    upT = nc.dram_tensor("upT", [NE, D, DFF], F32, kind="ExternalInput").ap()
    downT = nc.dram_tensor("downT", [NE, NCH, 128, DFF // 128, 128], F32, kind="ExternalInput").ap()
    ident = nc.dram_tensor("ident", [128, 128], F32, kind="ExternalInput").ap()
    rowsel = nc.dram_tensor("rowsel", [NE, NE * 128], F32, kind="ExternalInput").ap()
    x4 = nc.dram_tensor("x4", [D, TPC], F32, kind="ExternalOutput").ap()
    es = ExitStack()
    p = Prog(nc, es)
    emit_moe(p, nc, es, xT, rw, rbrow, upT, downT, ident, rowsel, x4)
    p.emit()
    es.close()
    return nc



def emit_addln(p, nc, es, xT, yT, xn):
    xr = es.enter_context(nc.sbuf_tensor("xr", [128, NCH * TPC], F32))
    yb = es.enter_context(nc.sbuf_tensor("yb", [128, NCH * TPC], F32))
    lnb = alloc_ln_bufs(nc, es)
    ldx = p.dma("G", xr[:].rearrange("p (c t) -> p c t", t=TPC), xT.rearrange("(c p) t -> p c t", p=128), group="ldx")
    ldy = p.dma("G", yb[:].rearrange("p (c t) -> p c t", t=TPC), yT.rearrange("(c p) t -> p c t", p=128), group="ldy")
    init = emit_ln_init(p, lnb)
    add_toks = []
    for c in range(NCH):
        for tt in range(NTT):
            ad = p.op("V", (lambda v, c=c, tt=tt: v.tensor_add(
                xr[:, c * TPC + tt * 512: c * TPC + (tt + 1) * 512],
                xr[:, c * TPC + tt * 512: c * TPC + (tt + 1) * 512],
                yb[:, c * TPC + tt * 512: c * TPC + (tt + 1) * 512])),
                [ldx, ldy] if (c == 0 and tt == 0) else [])
            add_toks.append(ad)

    def x_ap(c, tt):
        return xr[:, c * TPC + tt * 512: c * TPC + (tt + 1) * 512]

    toks, last = emit_ln(p, lnb, x_ap, TPC, out_fn=x_ap, pre_waits=[add_toks[-1]], init_toks=init)
    for c in range(NCH):
        for tt in range(NTT):
            p.dma("G", xn.rearrange("(c p) t -> p c t", p=128)[:, c, tt * 512:(tt + 1) * 512],
                  x_ap(c, tt), waits=[toks[(c, tt)]], group="xout")
    return


def build_addln():
    nc = bass.Bass("TRN2", target_bir_lowering=False)
    xT = nc.dram_tensor("xT", [D, TPC], F32, kind="ExternalInput").ap()
    yT = nc.dram_tensor("yT", [D, TPC], F32, kind="ExternalInput").ap()
    xn = nc.dram_tensor("xn", [D, TPC], F32, kind="ExternalOutput").ap()
    es = ExitStack()
    p = Prog(nc, es)
    emit_addln(p, nc, es, xT, yT, xn)
    p.emit()
    es.close()
    return nc


def build_proj_ln():
    nc = bass.Bass("TRN2", target_bir_lowering=False)
    oT = nc.dram_tensor("oT", [D, TPC], BF16, kind="ExternalInput").ap()
    wo = nc.dram_tensor("wo", [D, D], F32, kind="ExternalInput").ap()
    bo = nc.dram_tensor("bo", [128, NCH], F32, kind="ExternalInput").ap()
    xT = nc.dram_tensor("xT", [D, TPC], F32, kind="ExternalInput").ap()
    xn = nc.dram_tensor("xn", [D, TPC], F32, kind="ExternalOutput").ap()
    es = ExitStack()
    p = Prog(nc, es)
    emit_proj_ln(p, nc, es, oT, wo, bo, xT, xn)
    p.emit()
    es.close()
    return nc



MEGA_DEBUG = False


def build_mega():
    nc = bass.Bass("TRN2", target_bir_lowering=False)
    EI, EO = "ExternalInput", "ExternalOutput"
    A = lambda n, s, d=F32, k=EI: nc.dram_tensor(n, s, d, kind=k).ap()
    x0T = A("x0T", [D, TPC])
    out = A("out", [D, TPC], F32, EO)
    xdbg = A("xdbg", [5, D, TPC], F32, EO) if MEGA_DEBUG else None
    if MEGA_DEBUG:
        bdbg = A("bdbg", [D, 1], F32, EO)
        qdbg = A("qdbg", [INNER, TPC], BF16, EO)
        kdbg = A("kdbg", [INNER, TPC + 512], BF16, EO)
        vdbg = A("vdbg", [TPC + 512, INNER], BF16, EO)
        odbg = A("odbg", [INNER, TPC], BF16, EO)
        x2dbg = A("x2dbg", [D, TPC], F32, EO)
    dw = {}
    for l in ("0", "1"):
        for n in ("wq", "wk", "wv", "wo"):
            dw[n + l] = A(n + l, [D, D])
        for n in ("bq", "bk", "bo"):
            dw[n + l] = A(n + l, [128, NCH])
        dw["bvrow" + l] = A("bvrow" + l, [1, D])
    lw = {}
    for l in ("0", "1"):
        lw["wqkv" + l] = A("wqkv" + l, [D, 3 * INNER])
        lw["qs2" + l] = A("qs2" + l, [128, 1])
        lw["ks2" + l] = A("ks2" + l, [128, 1])
        lw["lwo" + l] = A("lwo" + l, [INNER, D])
        lw["lg" + l] = A("lg" + l, [128, NCH])
        lw["lb" + l] = A("lb" + l, [128, NCH])
        lw["w1p" + l] = A("w1p" + l, [2, HID_P // 128, 128, NCH, 128])
        lw["b1p" + l] = A("b1p" + l, [128, 2 * HID_P // 128])
        lw["w2p" + l] = A("w2p" + l, [NCH, 128, HID_P // 128, 128])
        lw["b2" + l] = A("b2" + l, [128, NCH])
    tabs = {n: A("t" + n, [128, TPC]) for n in ("AQ", "BQ", "AK", "BK")}
    E2 = A("E2", [128, 128])
    lmasks = A("lmasks", [32, 128, 512], BF16)
    tri = A("tri", [4, 128, 512], BF16)
    ident = A("ident", [128, 128])
    rowsel = A("rowsel", [NE, NE * 128])
    rw = A("rw", [D, NE])
    rbrow = A("rbrow", [1, NE])
    upT = A("upT", [NE, D, DFF])
    downR = A("downR", [NE, NCH, 128, DFF // 128, 128])
    bmask = A("bmask", [128, NCH])
    ix = A("ix", [1, 64], mybir.dt.uint32)
    I = lambda n, s, d=F32: nc.dram_tensor(n, s, d, kind="Internal").ap()
    xcur = I("xcur", [D, TPC])
    xpre = I("xpre", [D, TPC])
    x2s = I("x2s", [D, TPC])
    qkc = I("qkc", [2 * D, TPC], BF16)
    vc = I("vc", [TPC, D], BF16)
    qkg = nc.dram_tensor("qkg", [8 * 2 * D, TPC], BF16, kind="Internal", addr_space="Shared").ap()
    vg = nc.dram_tensor("vg", [8 * TPC, D], BF16, kind="Internal", addr_space="Shared").ap()
    oc4 = I("oc4", [4 * 64, T], BF16)
    og = nc.dram_tensor("og", [8 * 4 * 64, T], BF16, kind="Internal", addr_space="Shared").ap()
    bndc = I("bndc", [128, NCH])
    bndg = nc.dram_tensor("bndg", [8 * 128, NCH], F32, kind="Internal", addr_space="Shared").ap()
    bndloc = I("bndloc", [D, 1])
    kvwc = I("kvwc", [2 * 512, 512], BF16)
    kvwg = nc.dram_tensor("kvwg", [8 * 2 * 512, 512], BF16, kind="Internal", addr_space="Shared").ap()
    qkl = I("qkl", [2 * 4 * 64, T], BF16)
    vloc = I("vloc", [4 * T, 64], BF16)
    oloc = I("oloc", [16 * 64, TPC], BF16)
    qlxs = I("qlxs", [INNER, TPC], BF16)
    kfbs = I("kfbs", [INNER, TPC + 512], BF16)
    vfbs = I("vfbs", [TPC + 512, INNER], BF16)
    oxl = I("oxl", [INNER, TPC], BF16)
    qkg5 = qkg.rearrange("(r a h p) t -> r a h p t", r=8, a=2, h=16)
    vg4 = vg.rearrange("(r t) (h e) -> r h t e", r=8, h=16)
    og6 = og.rearrange("(r s p) (b t) -> b r s p t", r=8, s=4, t=TPC)
    qkl5 = qkl.rearrange("(a h p) (r t) -> a h p r t", a=2, h=4, r=4)
    vloc4 = vloc.rearrange("(h r t) e -> h r t e", h=4, r=4)
    oloc4 = oloc.rearrange("(c b p) t -> c b p t", c=8, b=2)
    bndg3 = bndg.rearrange("(r p) c -> r p c", r=8)
    kvwg3 = kvwg.rearrange("(r a p) t -> r a p t", r=8, a=2)

    es = ExitStack()
    p = Prog(nc, es)
    ixs = es.enter_context(nc.sbuf_tensor("ixs", [1, 64], mybir.dt.uint32))
    bnds = es.enter_context(nc.sbuf_tensor("bnds", [128, NCH], F32))
    bmsks = es.enter_context(nc.sbuf_tensor("bmsks", [128, NCH], F32))
    ldix = p.dma("Y", ixs[:], ix, group="ldix")
    ldbm = p.dma("Y", bmsks[:], bmask, group="ldbm")

    RG = [list(range(NCORES))]

    _dyncache = {}

    def dyn(g, k, ekey="G"):
        if (ekey, k) not in _dyncache:
            _dyncache[(ekey, k)] = g.value_load(ixs[0:1, k:k + 1])
        return _dyncache[(ekey, k)]

    def repack_qkv(coll_tok):
        first = [coll_tok, ldix]
        for a in range(2):
            def fqk(e, a=a):
                bb = dyn(e, 0, "G")
                hb = dyn(e, 1, "G")
                return e.dma_start(
                    out=qkl5[a].transpose([2, 0, 1, 3]),
                    in_=qkg5[bass.ts(bb, 4)][:, a][:, bass.ts(hb, 4)])
            p.raw_dma("G", fqk, waits=first, group="rpqk")
            first = []
        for ri in range(4):
            def fv(e, ri=ri):
                bb = dyn(e, 0, "S")
                hb = dyn(e, 1, "S")
                return e.dma_start(
                    out=vloc4[:, ri],
                    in_=vg4[bass.ts(bb, 4)][ri][bass.ts(hb, 4)])
            p.raw_dma("S", fv, waits=[coll_tok, ldix] if ri == 0 else (), group="rpv")
        return [p.gtotal("rpqk"), p.gtotal("rpv")]

    def mega_flash_loader(rp_toks):
        def loader(pair, prev_done, qe, ke, ve):
            w = list(prev_done) + list(rp_toks)
            ldq = p.dma("G", qe[0:64, :], qkl5[0, pair].rearrange("p r t -> p (r t)"),
                        waits=w, group="ldq")
            ldk = p.dma("G", ke[0:64, :], qkl5[1, pair].rearrange("p r t -> p (r t)"),
                        waits=(), group="ldk")
            ldv = p.dma("G", ve[:].rearrange("p (c e2) -> p c e2", e2=65)[:, :, 0:64],
                        vloc4[pair].rearrange("r (c p) e -> p (r c) e", p=128),
                        waits=(), group="ldv")
            return ldq, ldk, ldv
        return loader

    def mega_o_loader(coll_tok):
        def o_loader(obf):
            def fo(e):
                bb = dyn(e, 0, "G")
                so = dyn(e, 3, "G")
                return e.dma_start(
                    out=oloc.rearrange("(r s p) t -> r s p t", r=4, s=4),
                    in_=og6[so][bass.ts(bb, 4)])
            t1 = p.raw_dma("G", fo, waits=[coll_tok, ldix], group="ldo1")
            p.dma("G", obf[0:64, :].rearrange("p (c t) -> p c t", t=TPC),
                  oloc4[:, 0].transpose([1, 0, 2]), waits=[t1], group="ldo")
            tokl = p.dma("G", obf[64:128, :].rearrange("p (c t) -> p c t", t=TPC),
                         oloc4[:, 1].transpose([1, 0, 2]), waits=(), group="ldo")
            return tokl
        return o_loader

    def dense_block(lbl, x_src):
        pn = NCProxy(nc, f"dq{lbl}_")
        with ExitStack() as pes:
            emit_qkv(p, pn, pes, x_src, dw["wq" + lbl], dw["wk" + lbl], dw["wv" + lbl],
                     dw["bq" + lbl], dw["bk" + lbl], dw["bvrow" + lbl],
                     qkc[0:D, :], qkc[D:2 * D, :], vc)
            p.barrier()
        wv_ = [p.gtotal("oh0_q"), p.gtotal("oh1_q"), p.gtotal("oh0_k"),
               p.gtotal("oh1_k"), p.gtotal("vout")]
        p.collective("AllGather", [qkc], [qkg], waits=wv_)
        ct = p.collective("AllGather", [vc], [vg], waits=[])
        rp = repack_qkv(ct)
        pn = NCProxy(nc, f"df{lbl}_")
        with ExitStack() as pes:
            emit_flash(p, pn, pes, ident, tri, oc4.rearrange("(f p) t -> f p t", f=4),
                       mega_flash_loader(rp))
            p.barrier()
        ct2 = p.collective("AllGather", [oc4], [og], waits=[p.gtotal("ob0"), p.gtotal("ob1")])
        pn = NCProxy(nc, f"dp{lbl}_")
        with ExitStack() as pes:
            emit_proj_ln(p, pn, pes, None, dw["wo" + lbl], dw["bo" + lbl], x_src, xcur,
                         o_loader=mega_o_loader(ct2))
            p.barrier()

    def stage_boundary(src_dram):
        t1 = p.dma("G", bndc, src_dram.rearrange("(c p) t -> p c t", p=128)[:, :, TPC - 1],
                   group="bstg", allow_slow_non_contiguous=True)
        ct = p.collective("AllGather", [bndc], [bndg], waits=[t1])

        def fb(e):
            rk = dyn(e, 2, "S")
            return e.dma_start(out=bnds[:], in_=bndg3[rk])
        t2 = p.raw_dma("S", fb, waits=[ct, ldix], group="bld")
        t3 = p.op("V", lambda v: v.tensor_mul(bnds[:], bnds[:], bmsks[:]), [t2, ldbm])
        t4 = p.dma("G", bndloc.rearrange("(c p) o -> p c o", p=128), bnds[:].unsqueeze(2),
                   waits=[t3], group="bst2", allow_slow_non_contiguous=True)
        p.barrier()

    def lffn_layer(lbl):
        dbg = MEGA_DEBUG and lbl == "0"
        stage_boundary(xcur)
        if dbg:
            p.dma("G", bdbg, bndloc, group="xdbg", allow_slow_non_contiguous=True)
            p.barrier()
        pn = NCProxy(nc, f"lq{lbl}_")
        with ExitStack() as pes:
            emit_lffn_qkv(p, pn, pes, xcur, bndloc, lw["wqkv" + lbl], lw["qs2" + lbl],
                          lw["ks2" + lbl], tabs, E2, qlxs, kfbs[:, 512:TPC + 512],
                          vfbs[512:TPC + 512, :])
            p.barrier()
        if dbg:
            p.dma("G", qdbg, qlxs, group="xdbg")
            p.barrier()
        t1 = p.dma("G", kvwc[0:512, :], kfbs[:, TPC:TPC + 512], group="kvstg")
        t2 = p.dma("G", kvwc[512:1024, :], vfbs[TPC:TPC + 512, :], group="kvstg")
        ct = p.collective("AllGather", [kvwc], [kvwg], waits=[t1, t2])

        kes = ExitStack()
        kvtmp = kes.enter_context(nc.sbuf_tensor(f"kvtmp{lbl}", [128, 4 * 512], BF16))

        def fk(e):
            rk = dyn(e, 2, "Y")
            return e.dma_start(out=kvtmp[:].rearrange("p (a t) -> p a t", t=512),
                               in_=kvwg3[rk][0].rearrange("(a p) t -> p a t", p=128))
        def fv(e):
            rk = dyn(e, 2, "Y")
            return e.dma_start(out=vfbs[0:512, :].rearrange("(a p) t -> p a t", p=128),
                               in_=kvwg3[rk][1].rearrange("(a p) t -> p a t", p=128))
        t3 = p.raw_dma("Y", fk, waits=[ct, ldix], group="kvld")
        t4 = p.raw_dma("Y", fv, waits=(), group="kvld")
        tz = p.op("V", lambda v: v.tensor_scalar_mul(kvtmp[:], kvtmp[:], bmsks[:, 0:1]),
                  [t3, ldbm])
        p.dma("G", kfbs[:, 0:512].rearrange("(a p) t -> p a t", p=128),
              kvtmp[:].rearrange("p (a t) -> p a t", t=512), waits=[tz], group="kvld2")
        p.barrier()
        kes.close()
        if dbg:
            p.dma("G", kdbg, kfbs, group="xdbg")
            p.dma("G", vdbg, vfbs, group="xdbg")
            p.barrier()
        pn = NCProxy(nc, f"la{lbl}_")
        with ExitStack() as pes:
            emit_lffn_attn(p, pn, pes, qlxs, kfbs, vfbs, lmasks, ident, oxl)
            p.barrier()
        if dbg:
            p.dma("G", odbg, oxl, group="xdbg")
            p.barrier()
        pn = NCProxy(nc, f"lp{lbl}_")
        with ExitStack() as pes:
            emit_lffn_proj(p, pn, pes, oxl, lw["lwo" + lbl], xcur, x2s)
            p.barrier()
        if dbg:
            p.dma("G", x2dbg, x2s, group="xdbg")
            p.barrier()
        stage_boundary(x2s)
        pn = NCProxy(nc, f"lf{lbl}_")
        with ExitStack() as pes:
            emit_lffn_ffn(p, pn, pes, x2s, bndloc, lw["lg" + lbl], lw["lb" + lbl],
                          lw["w1p" + lbl], lw["b1p" + lbl], lw["w2p" + lbl],
                          lw["b2" + lbl], xcur)
            p.barrier()

    def tap(i):
        if MEGA_DEBUG:
            p.dma("G", xdbg[i], xcur, group="xdbg")
            p.barrier()

    dense_block("0", x0T)
    cp = p.dma("G", xpre, xcur, group="xpre")
    p.barrier()
    tap(0)
    lffn_layer("0")
    tap(1)
    lffn_layer("1")
    tap(2)
    pn = NCProxy(nc, "al_")
    with ExitStack() as pes:
        emit_addln(p, pn, pes, xpre, xcur, xcur)
        p.barrier()
    tap(3)
    dense_block("1", xcur)
    tap(4)
    pn = NCProxy(nc, "mo_")
    with ExitStack() as pes:
        emit_moe(p, pn, pes, xcur, rw, rbrow, upT, downR, ident, rowsel, out)
        p.barrier()
    p.emit()
    es.close()
    return nc



def _cache_build(key, fn):
    if key not in _NC_CACHE:
        _NC_CACHE[key] = fn()
    return _NC_CACHE[key]


def _fm(x):
    return np.ascontiguousarray(x.T)


def host_tri_masks():
    ks = np.arange(128)
    qs = np.arange(512)
    m = np.zeros((4, 128, 512), dtype=BF)
    for s in range(4):
        m[s] = ((128 * s + ks[:, None]) <= qs[None, :]).astype(BF)
    return m


def host_local_masks(first_window_of_seq):
    ks = np.arange(128)
    qs = np.arange(512)
    out = np.zeros((4, 8, 128, 512), dtype=BF)
    for w in range(4):
        for kc in range(2):
            for s in range(4):
                klocal = kc * 512 + s * 128 + ks[:, None]
                ql = qs[None, :]
                ok = (klocal > ql) & (klocal <= ql + 512)
                if w == 0 and first_window_of_seq:
                    ok = ok & (klocal >= 512)
                out[w, kc * 4 + s] = ok.astype(BF)
    return out.reshape(32, 128, 512)


def host_rotary_tables():
    DH, WIN = 64, 512
    half = np.arange(0, DH, 2, dtype=np.float64)
    inv_freq = 1.0 / (10000.0 ** (half / DH))
    t = np.arange(T, dtype=np.float64)
    freqs = t[:, None] * inv_freq[None, :]
    freqs = np.concatenate([freqs, freqs], axis=1)
    base = (half + 0.4 * DH) / (1.4 * DH)
    power = (t - T // 2) / (WIN // 2)
    scale = base[None, :] ** power[:, None]
    scale = np.concatenate([scale, scale], axis=1)
    cos, sin = np.cos(freqs), np.sin(freqs)
    AQ = (cos * scale).T.astype(np.float32)
    BQr = (sin * scale).T.astype(np.float32)
    AK = (cos / scale).T.astype(np.float32)
    BKr = (sin / scale).T.astype(np.float32)
    sgn = np.concatenate([-np.ones(32), np.ones(32)]).astype(np.float32)[:, None]
    return AQ, BQr * sgn, AK, BKr * sgn


def host_E2():
    e = np.zeros((128, 128), dtype=np.float32)
    e[0:64, 0:64] = 1.0
    e[64:128, 64:128] = 1.0
    return e


_IDENT = np.eye(128, dtype=np.float32)
_ROWSEL = np.zeros((8, 8 * 128), dtype=np.float32)
for _e in range(8):
    _ROWSEL[_e, _e * 128:(_e + 1) * 128] = 1.0


def run_dense_attn_block(x, Wq, bq, Wk, bk, Wv, bv, Wo, bo):
    nc1 = _cache_build("qkv", build_qkv_launch)
    in1 = []
    for c in range(NCORES):
        b, s = divmod(c, 4)
        sl = x[b, s * TPC:(s + 1) * TPC]
        in1.append(dict(xT=_fm(sl), wq=Wq, wk=Wk, wv=Wv,
                        bq=np.ascontiguousarray(bq.reshape(NCH, 128).T),
                        bk=np.ascontiguousarray(bk.reshape(NCH, 128).T),
                        bvrow=bv.reshape(1, D)))
    r1 = run_bass_kernel_spmd(nc1, in1, core_ids=list(range(NCORES)))
    QF = np.zeros((B, D, T), dtype=BF)
    KF = np.zeros((B, D, T), dtype=BF)
    VT = np.zeros((B, T, D), dtype=BF)
    for c in range(NCORES):
        b, s = divmod(c, 4)
        QF[b][:, s * TPC:(s + 1) * TPC] = r1.results[c]["qx"]
        KF[b][:, s * TPC:(s + 1) * TPC] = r1.results[c]["kx"]
        VT[b][s * TPC:(s + 1) * TPC] = r1.results[c]["vx"]
    nc2 = _cache_build("flash", build_flash_dense)
    tri = host_tri_masks()
    in2 = []
    for c in range(NCORES):
        pairs = [(c * 4 + i) for i in range(4)]
        q4 = np.stack([np.ascontiguousarray(QF[pr // 16][(pr % 16) * 64:(pr % 16) * 64 + 64]) for pr in pairs])
        k4 = np.stack([np.ascontiguousarray(KF[pr // 16][(pr % 16) * 64:(pr % 16) * 64 + 64]) for pr in pairs])
        v4 = np.stack([np.ascontiguousarray(VT[pr // 16][:, (pr % 16) * 64:(pr % 16) * 64 + 64]) for pr in pairs])
        in2.append(dict(q4=q4, k4=k4, v4=v4, ident=_IDENT, trimask=tri))
    r2 = run_bass_kernel_spmd(nc2, in2, core_ids=list(range(NCORES)))
    OF = np.zeros((B, D, T), dtype=BF)
    for c in range(NCORES):
        for i in range(4):
            pr = c * 4 + i
            OF[pr // 16][(pr % 16) * 64:(pr % 16) * 64 + 64] = r2.results[c]["o4"][i]
    nc3 = _cache_build("projln", build_proj_ln)
    in3 = []
    for c in range(NCORES):
        b, s = divmod(c, 4)
        in3.append(dict(oT=np.ascontiguousarray(OF[b][:, s * TPC:(s + 1) * TPC]),
                        wo=Wo, bo=np.ascontiguousarray(bo.reshape(NCH, 128).T),
                        xT=_fm(x[b, s * TPC:(s + 1) * TPC])))
    r3 = run_bass_kernel_spmd(nc3, in3, core_ids=list(range(NCORES)))
    xo = np.zeros((B, T, D), dtype=np.float32)
    for c in range(NCORES):
        b, s = divmod(c, 4)
        xo[b, s * TPC:(s + 1) * TPC] = r3.results[c]["xn"].T
    return xo


def run_lffn_layer(x, wqkv_l, qs_l, ks_l, wo_l, lg_l, lb_l, w1_l, b1_l, w2_l, b2_l):
    AQ, BQ, AK, BK = host_rotary_tables()
    E2 = host_E2()
    HID = 2730
    w1p = np.zeros((D, 2 * HID_P), dtype=np.float32)
    w1p[:, 0:HID] = w1_l[:, 0:HID]
    w1p[:, HID_P:HID_P + HID] = w1_l[:, HID:2 * HID]
    b1p = np.zeros(2 * HID_P, dtype=np.float32)
    b1p[0:HID] = b1_l[0:HID]
    b1p[HID_P:HID_P + HID] = b1_l[HID:2 * HID]
    w2p = np.zeros((HID_P, D), dtype=np.float32)
    w2p[0:HID] = w2_l

    nc = _cache_build("lffn_qkv", build_lffn_qkv)
    ins = []
    for c in range(NCORES):
        b, s = divmod(c, 4)
        sl = x[b, s * TPC:(s + 1) * TPC]
        xprev = (x[b, s * TPC - 1] if s > 0 else np.zeros(D, dtype=np.float32)).reshape(D, 1)
        t0 = s * TPC
        tiles = {n: np.ascontiguousarray(np.tile(tab[:, t0:t0 + TPC], (2, 1)))
                 for n, tab in (("AQ", AQ), ("BQ", BQ), ("AK", AK), ("BK", BK))}
        ins.append(dict(xT=_fm(sl), xprevT=np.ascontiguousarray(xprev), wqkv=wqkv_l,
                        qs2=np.tile(qs_l, 2).reshape(128, 1).astype(np.float32),
                        ks2=np.tile(ks_l, 2).reshape(128, 1).astype(np.float32),
                        E2=E2, **tiles))
    r1 = run_bass_kernel_spmd(nc, ins, core_ids=list(range(NCORES)))
    QF = np.zeros((B, INNER, T), dtype=BF)
    KF = np.zeros((B, INNER, T), dtype=BF)
    VT = np.zeros((B, T, INNER), dtype=BF)
    for c in range(NCORES):
        b, s = divmod(c, 4)
        QF[b][:, s * TPC:(s + 1) * TPC] = r1.results[c]["qx"]
        KF[b][:, s * TPC:(s + 1) * TPC] = r1.results[c]["kx"]
        VT[b][s * TPC:(s + 1) * TPC] = r1.results[c]["vx"]

    nc2 = _cache_build("lffn_attn", build_lffn_attn)
    ins2 = []
    for c in range(NCORES):
        b, s = divmod(c, 4)
        t0 = s * TPC
        if s > 0:
            kfb = KF[b][:, t0 - 512: t0 + TPC]
            vfb = VT[b][t0 - 512: t0 + TPC]
        else:
            kfb = np.concatenate([np.zeros((INNER, 512), dtype=BF), KF[b][:, 0:TPC]], axis=1)
            vfb = np.concatenate([np.zeros((512, INNER), dtype=BF), VT[b][0:TPC]], axis=0)
        ins2.append(dict(qx=np.ascontiguousarray(QF[b][:, t0:t0 + TPC]),
                         kfb=np.ascontiguousarray(kfb), vfb=np.ascontiguousarray(vfb),
                         masks=host_local_masks(first_window_of_seq=(s == 0)),
                         ident=_IDENT))
    r2 = run_bass_kernel_spmd(nc2, ins2, core_ids=list(range(NCORES)))

    nc3 = _cache_build("lffn_proj", build_lffn_proj)
    ins3 = []
    for c in range(NCORES):
        b, s = divmod(c, 4)
        ins3.append(dict(oT=r2.results[c]["ox"], wo=wo_l,
                         xT=_fm(x[b, s * TPC:(s + 1) * TPC])))
    r3 = run_bass_kernel_spmd(nc3, ins3, core_ids=list(range(NCORES)))
    x2 = np.zeros((B, T, D), dtype=np.float32)
    for c in range(NCORES):
        b, s = divmod(c, 4)
        x2[b, s * TPC:(s + 1) * TPC] = r3.results[c]["x2"].T

    nc4 = _cache_build("lffn_ffn", build_lffn_ffn)
    ins4 = []
    for c in range(NCORES):
        b, s = divmod(c, 4)
        xprev = (x2[b, s * TPC - 1] if s > 0 else np.zeros(D, dtype=np.float32)).reshape(D, 1)
        ins4.append(dict(x2T=_fm(x2[b, s * TPC:(s + 1) * TPC]),
                         x2prevT=np.ascontiguousarray(xprev),
                         lg=np.ascontiguousarray(lg_l.reshape(NCH, 128).T),
                         lb=np.ascontiguousarray(lb_l.reshape(NCH, 128).T),
                         w1p=_w1R(w1p),
                         b1p=np.ascontiguousarray(b1p.reshape(2 * HID_P // 128, 128).T),
                         w2p=_w2R(w2p),
                         b2=np.ascontiguousarray(b2_l.reshape(NCH, 128).T)))
    r4 = run_bass_kernel_spmd(nc4, ins4, core_ids=list(range(NCORES)))
    x3 = np.zeros((B, T, D), dtype=np.float32)
    for c in range(NCORES):
        b, s = divmod(c, 4)
        x3[b, s * TPC:(s + 1) * TPC] = r4.results[c]["x3"].T
    return x3


def run_moe_block(x, router_w, router_b, expert_up, expert_down):
    upTh = np.ascontiguousarray(expert_up.transpose(0, 2, 1))
    downTh = _downR(np.asarray(expert_down, dtype=np.float32))
    nc = _cache_build("moe", build_moe)
    ins = []
    for c in range(NCORES):
        b, s = divmod(c, 4)
        ins.append(dict(xT=_fm(x[b, s * TPC:(s + 1) * TPC]), rw=router_w,
                        rbrow=router_b.reshape(1, NE), upT=upTh, downT=downTh,
                        ident=_IDENT, rowsel=_ROWSEL))
    r = run_bass_kernel_spmd(nc, ins, core_ids=list(range(NCORES)))
    xo = np.zeros((B, T, D), dtype=np.float32)
    for c in range(NCORES):
        b, s = divmod(c, 4)
        xo[b, s * TPC:(s + 1) * TPC] = r.results[c]["x4"].T
    return xo


def run_addln(x, y):
    nc = _cache_build("addln", build_addln)
    ins = []
    for c in range(NCORES):
        b, s = divmod(c, 4)
        ins.append(dict(xT=_fm(x[b, s * TPC:(s + 1) * TPC]), yT=_fm(y[b, s * TPC:(s + 1) * TPC])))
    r = run_bass_kernel_spmd(nc, ins, core_ids=list(range(NCORES)))
    xo = np.zeros((B, T, D), dtype=np.float32)
    for c in range(NCORES):
        b, s = divmod(c, 4)
        xo[b, s * TPC:(s + 1) * TPC] = r.results[c]["xn"].T
    return xo


def _w1R(w1p):
    return np.ascontiguousarray(
        w1p.reshape(NCH, 128, 2, HID_P // 128, 128).transpose(2, 3, 1, 0, 4))


def _w2R(w2p):
    return np.ascontiguousarray(
        w2p.reshape(HID_P // 128, 128, NCH, 128).transpose(2, 1, 0, 3))


def _downR(ed):
    return np.ascontiguousarray(
        ed.reshape(NE, NCH, 128, DFF // 128, 128).transpose(0, 1, 4, 3, 2))


def kernel_fused(inp):
    g = {k: np.asarray(v) for k, v in inp.items()}
    x0 = np.asarray(g["L0_attn_x"], dtype=np.float32)
    wA = np.asarray(g["L1_attn_w_attn"], dtype=np.float32)
    bA = np.asarray(g["L1_attn_b_attn"], dtype=np.float32)
    AQ, BQ, AK, BK = host_rotary_tables()
    E2h = host_E2()
    HID = 2730
    trih = host_tri_masks()
    upTh = np.ascontiguousarray(np.asarray(g["L1_ffn_expert_up"], dtype=np.float32).transpose(0, 2, 1))
    downRh = _downR(np.asarray(g["L1_ffn_expert_down"], dtype=np.float32))

    def colmat(v):
        return np.ascontiguousarray(np.asarray(v, dtype=np.float32).reshape(NCH, 128).T)

    lws = []
    for l in range(2):
        w1_l = np.asarray(g["L0_ffn_w1"][l], dtype=np.float32)
        b1_l = np.asarray(g["L0_ffn_b1"][l], dtype=np.float32)
        w2_l = np.asarray(g["L0_ffn_w2"][l], dtype=np.float32)
        w1p = np.zeros((D, 2 * HID_P), dtype=np.float32)
        w1p[:, 0:HID] = w1_l[:, 0:HID]
        w1p[:, HID_P:HID_P + HID] = w1_l[:, HID:2 * HID]
        b1p = np.zeros(2 * HID_P, dtype=np.float32)
        b1p[0:HID] = b1_l[0:HID]
        b1p[HID_P:HID_P + HID] = b1_l[HID:2 * HID]
        w2p = np.zeros((HID_P, D), dtype=np.float32)
        w2p[0:HID] = w2_l
        sl = str(l)
        lws.append({
            "wqkv" + sl: np.ascontiguousarray(np.asarray(g["L0_ffn_wqkv"][l], dtype=np.float32)),
            "qs2" + sl: np.tile(np.asarray(g["L0_ffn_q_scale"][l], dtype=np.float32), 2).reshape(128, 1),
            "ks2" + sl: np.tile(np.asarray(g["L0_ffn_k_scale"][l], dtype=np.float32), 2).reshape(128, 1),
            "lwo" + sl: np.ascontiguousarray(np.asarray(g["L0_ffn_wo"][l], dtype=np.float32)),
            "lg" + sl: colmat(g["L0_ffn_ln_g"][l]),
            "lb" + sl: colmat(g["L0_ffn_ln_b"][l]),
            "w1p" + sl: _w1R(w1p),
            "b1p" + sl: np.ascontiguousarray(b1p.reshape(2 * HID_P // 128, 128).T),
            "w2p" + sl: _w2R(w2p),
            "b2" + sl: colmat(g["L0_ffn_b2"][l]),
        })

    dense0 = dict(
        wq0=np.asarray(g["L0_attn_Wq"], dtype=np.float32), wk0=np.asarray(g["L0_attn_Wk"], dtype=np.float32),
        wv0=np.asarray(g["L0_attn_Wv"], dtype=np.float32), wo0=np.asarray(g["L0_attn_Wo"], dtype=np.float32),
        bq0=colmat(g["L0_attn_bq"]), bk0=colmat(g["L0_attn_bk"]), bo0=colmat(g["L0_attn_bo"]),
        bvrow0=np.asarray(g["L0_attn_bv"], dtype=np.float32).reshape(1, D))
    dense1 = dict(
        wq1=np.ascontiguousarray(wA[:, 0:D]), wk1=np.ascontiguousarray(wA[:, D:2 * D]),
        wv1=np.ascontiguousarray(wA[:, 2 * D:3 * D]),
        wo1=np.asarray(g["L1_attn_w_proj"], dtype=np.float32),
        bq1=colmat(bA[0:D]), bk1=colmat(bA[D:2 * D]), bo1=colmat(g["L1_attn_b_proj"]),
        bvrow1=bA[2 * D:3 * D].reshape(1, D))

    nc = _cache_build("mega", build_mega)
    ins = []
    for c in range(NCORES):
        b, s = divmod(c, 4)
        t0 = s * TPC
        tiles = {"t" + n: np.ascontiguousarray(np.tile(tab[:, t0:t0 + TPC], (2, 1)))
                 for n, tab in (("AQ", AQ), ("BQ", BQ), ("AK", AK), ("BK", BK))}
        ixv = np.zeros((1, 64), dtype=np.uint32)
        ixv[0, 0] = b
        ixv[0, 1] = s
        ixv[0, 2] = (c + 7) % 8
        ixv[0, 3] = s
        bm = np.ones((128, NCH), dtype=np.float32) if s > 0 else np.zeros((128, NCH), dtype=np.float32)
        m = dict(x0T=_fm(x0[b, t0:t0 + TPC]), **dense0, **dense1, **lws[0], **lws[1],
                 **tiles, E2=E2h, lmasks=host_local_masks(first_window_of_seq=(s == 0)),
                 tri=trih, ident=_IDENT, rowsel=_ROWSEL,
                 rw=np.asarray(g["L1_ffn_router_w"], dtype=np.float32),
                 rbrow=np.asarray(g["L1_ffn_router_b"], dtype=np.float32).reshape(1, NE),
                 upT=upTh, downR=downRh, bmask=bm, ix=ixv)
        ins.append(m)
    r = run_bass_kernel_spmd(nc, ins, core_ids=list(range(NCORES)))
    xo = np.zeros((B, T, D), dtype=np.float32)
    for c in range(NCORES):
        b, s = divmod(c, 4)
        xo[b, s * TPC:(s + 1) * TPC] = r.results[c]["out"].T
    if MEGA_DEBUG:
        global MEGA_TAPS, MEGA_RAW
        MEGA_TAPS = np.zeros((5, B, T, D), dtype=np.float32)
        for c in range(NCORES):
            b, s = divmod(c, 4)
            for i in range(5):
                MEGA_TAPS[i, b, s * TPC:(s + 1) * TPC] = r.results[c]["xdbg"][i].T
        MEGA_RAW = r.results
    return xo


def kernel(L0_attn_x, L0_attn_Wq, L0_attn_bq, L0_attn_Wk, L0_attn_bk, L0_attn_Wv,
           L0_attn_bv, L0_attn_Wo, L0_attn_bo, L0_ffn_wqkv, L0_ffn_q_scale,
           L0_ffn_k_scale, L0_ffn_wo, L0_ffn_ln_g, L0_ffn_ln_b, L0_ffn_w1, L0_ffn_b1,
           L0_ffn_w2, L0_ffn_b2, L1_attn_w_attn, L1_attn_b_attn, L1_attn_w_proj,
           L1_attn_b_proj, L1_ffn_router_w, L1_ffn_router_b, L1_ffn_expert_up,
           L1_ffn_expert_down):
    g = {k: np.ascontiguousarray(np.asarray(v), dtype=None) for k, v in list(locals().items())
         if k != "g"}
    return kernel_fused(g)
    x = np.asarray(g["L0_attn_x"], dtype=np.float32)
    x = run_dense_attn_block(x, g["L0_attn_Wq"], g["L0_attn_bq"], g["L0_attn_Wk"],
                             g["L0_attn_bk"], g["L0_attn_Wv"], g["L0_attn_bv"],
                             g["L0_attn_Wo"], g["L0_attn_bo"])
    y = x
    for l in range(2):
        y = run_lffn_layer(y, g["L0_ffn_wqkv"][l], g["L0_ffn_q_scale"][l],
                           g["L0_ffn_k_scale"][l], g["L0_ffn_wo"][l], g["L0_ffn_ln_g"][l],
                           g["L0_ffn_ln_b"][l], g["L0_ffn_w1"][l], g["L0_ffn_b1"][l],
                           g["L0_ffn_w2"][l], g["L0_ffn_b2"][l])
    x = run_addln(x, y)
    wA = np.asarray(g["L1_attn_w_attn"], dtype=np.float32)
    bA = np.asarray(g["L1_attn_b_attn"], dtype=np.float32)
    x = run_dense_attn_block(x, np.ascontiguousarray(wA[:, 0:D]), bA[0:D],
                             np.ascontiguousarray(wA[:, D:2 * D]), bA[D:2 * D],
                             np.ascontiguousarray(wA[:, 2 * D:3 * D]), bA[2 * D:3 * D],
                             g["L1_attn_w_proj"], g["L1_attn_b_proj"])
    x = run_moe_block(x, g["L1_ffn_router_w"], g["L1_ffn_router_b"],
                      np.asarray(g["L1_ffn_expert_up"], dtype=np.float32),
                      np.asarray(g["L1_ffn_expert_down"], dtype=np.float32))
    return x
